# version 02
# baseline (speedup 1.0000x reference)
import os
import numpy as np
import ml_dtypes
from contextlib import ExitStack

import concourse.bass as bass
import concourse.mybir as mybir
from concourse.bass_utils import run_bass_kernel_spmd

F32 = mybir.dt.float32
BF16 = mybir.dt.bfloat16
I32 = mybir.dt.int32
AF = mybir.ActivationFunctionType
ALU = mybir.AluOpType
AX = mybir.AxisListType.X
BF16NP = ml_dtypes.bfloat16

S = 4096
D = 1024
SC0 = float(1.0 / np.sqrt(128.0))
SC1 = 0.125
EPS_LN = 1e-5
EPS_RMS = 1e-6
FF0 = 3072
FF1 = 4096
NE = 8
INV127 = float(1.0 / 127.0)
INV127SQ = float(1.0 / (127.0 * 127.0))

ENGINES = ("sync", "tensor", "scalar", "vector", "gpsimd")
OWN_SEM = {"tensor": "T", "scalar": "A", "vector": "V", "gpsimd": "G", "sync": "SY"}


class Sched:
    def __init__(self):
        self.q = {e: [] for e in ENGINES}
        self.cnt = {}
        self.pending = {e: [] for e in ENGINES}

    def tok(self, sem):
        return (sem, self.cnt.get(sem, 0))

    def emit(self, eng, fn, deps=(), sig=None, inc=1):
        if sig is None:
            sig = OWN_SEM[eng]
        for d in deps:
            if d is None:
                continue
            s_, v = d
            assert v <= self.cnt.get(s_, 0), f"dep on future token {s_}={v}"
        self.cnt[sig] = self.cnt.get(sig, 0) + inc
        token = (sig, self.cnt[sig])
        dd = self.pending[eng] + [d for d in deps if d is not None]
        self.pending[eng] = []
        self.q[eng].append((dd, fn, sig, inc))
        return token

    def dma(self, eng, fn, deps=(), sem="DS"):
        return self.emit(eng, fn, deps=deps, sig=sem, inc=16)

    def barrier(self):
        snap = [(s_, v) for (s_, v) in self.cnt.items()]
        for e in ENGINES:
            self.pending[e] = self.pending[e] + snap

    def realize(self, nc, ctx):
        sems = {name: ctx.enter_context(nc.semaphore(name)) for name in self.cnt}
        block = ctx.enter_context(nc.Block())

        def mk(eng):
            def body(h):
                waited = {}
                for (deps, fn, sig, inc) in self.q[eng]:
                    need = {}
                    for (s_, v) in deps:
                        if v > waited.get(s_, 0) and v > need.get(s_, 0):
                            need[s_] = v
                    for s_ in sorted(need):
                        h.wait_ge(sems[s_], need[s_])
                        waited[s_] = need[s_]
                    fn(h).then_inc(sems[sig], inc)
            return body

        block.sync(mk("sync"))
        block.tensor(mk("tensor"))
        block.scalar(mk("scalar"))
        block.vector(mk("vector"))
        block.gpsimd(mk("gpsimd"))


class BankGuard:
    def __init__(self):
        self.readers = {}

    def read(self, b, token):
        self.readers.setdefault(b, []).append(token)

    def wdeps(self, b):
        d = self.readers.get(b, [])
        self.readers[b] = []
        return d


def rearr_cpf(ap, p=128):
    return ap.rearrange("(c p) f -> p c f", p=p)


def rearr_groups(ap, t=S):
    return ap.rearrange("p (c t) -> p c t", t=t)


class Env:
    pass


def emit_postln(E, r_sb, T, dst_fn, r_deps, eps_col, rbf, sqb):
    s = E.s
    nTT = T // 512
    vcp = s.emit("vector", lambda h: h.tensor_copy(rbf[:, 0:8 * T], r_sb[:, 0:8 * T]),
                 deps=list(r_deps) + E.ln_guard)
    acp = s.emit("scalar", lambda h: h.activation(out=sqb[:, 0:8 * T], in_=r_sb[:, 0:8 * T],
                                                  func=AF.Square),
                 deps=list(r_deps) + E.ln_guard)
    R = E.ln_rows
    for tt in range(nTT):
        mean_ps = E.pb[2][0:1, 0:512]
        msq_ps = E.pb[3][0:1, 0:512]

        def statmm(h, src, dst, tt=tt):
            ins = None
            for dc in range(8):
                ins = h.matmul(dst, E.invD_bf[:, 0:1],
                               src[:, dc * T + tt * 512: dc * T + tt * 512 + 512],
                               start=(dc == 0), stop=(dc == 7))
            return ins
        t_mean = s.emit("tensor", lambda h, tt=tt: statmm(h, rbf, E.pb[2][0:1, 0:512], tt),
                        deps=[vcp] + E.bank.wdeps(2))
        t_msq = s.emit("tensor", lambda h, tt=tt: statmm(h, sqb, E.pb[3][0:1, 0:512], tt),
                       deps=[acp] + E.bank.wdeps(3))
        mrow = R[0:1, 0:512]
        m2 = R[0:1, 512:1024]
        vrow = R[0:1, 1024:1536]
        rstd = R[0:1, 1536:2048]
        v1 = s.emit("vector", lambda h: h.tensor_copy(mrow, mean_ps),
                    deps=[t_mean] + E.rows_guard)
        E.rows_guard = []
        E.bank.read(2, v1)
        v2 = s.emit("vector", lambda h: h.tensor_mul(m2, mrow, mrow), deps=[v1])
        v3 = s.emit("vector", lambda h: h.tensor_sub(vrow, msq_ps, m2), deps=[t_msq, v2])
        E.bank.read(3, v3)
        a1 = s.emit("scalar", lambda h: h.activation(out=vrow, in_=vrow, func=AF.Ln,
                                                     bias=eps_col), deps=[v3])
        a2 = s.emit("scalar", lambda h: h.activation(out=rstd, in_=vrow, func=AF.Exp,
                                                     scale=-0.5), deps=[a1])
        v4 = s.emit("vector", lambda h: h.tensor_mul(m2, mrow, rstd), deps=[a2])
        bca = E.pb[6][:, 0:512]
        bcb = E.pb[7][:, 0:512]
        t_a = s.emit("tensor", lambda h: h.matmul(bca, E.ones_f[0:1, 0:128], rstd,
                                                  start=True, stop=True),
                     deps=[a2] + E.bank.wdeps(6))
        t_b = s.emit("tensor", lambda h: h.matmul(bcb, E.neg_f[0:1, 0:128], m2,
                                                  start=True, stop=True),
                     deps=[v4] + E.bank.wdeps(7))
        E.rows_guard = [t_a, t_b]
        for dc in range(8):
            slot = (tt * 8 + dc) % 2
            stg = E.ln_stg[slot]
            vv1 = s.emit("vector",
                         lambda h, stg=stg, dc=dc, tt=tt: h.tensor_mul(
                             stg[:], r_sb[:, dc * T + tt * 512: dc * T + tt * 512 + 512], bca),
                         deps=[t_a, t_b] + E.stg_guard[slot])
            vv2 = s.emit("vector", lambda h, stg=stg: h.tensor_add(stg[:], stg[:], bcb),
                         deps=[vv1])
            E.bank.read(6, vv1)
            E.bank.read(7, vv2)
            dtok = s.dma("sync", lambda h, stg=stg, dc=dc, tt=tt: h.dma_start(
                out=dst_fn(dc, tt), in_=stg[:]), deps=[vv2], sem=f"DST{slot}")
            E.stg_guard[slot] = [dtok]
    E.ln_guard = [s.tok("T")]


def emit_p1(E):
    BIS = os.environ.get("P1BISECT", "")
    s = E.s
    wq, wk, wv, wo = E.p1_w
    ldtoks = []
    for (sb_t, src) in ((wq, "w0q"), (wk, "w0k"), (wv, "w0v"), (wo, "w0o")):
        ldtoks.append(s.dma("gpsimd", lambda h, d=sb_t, src=src: h.dma_start(
            out=d[:].rearrange("p (c f) -> p c f", f=1024), in_=rearr_cpf(E.ap[src])),
            sem="DW1"))
    wtok = ldtoks[-1]
    bvr = s.dma("sync", lambda h: h.dma_start(out=E.p1_bvrow[:], in_=E.ap["b0v_row"]), sem="DC")
    bvbc = E.p1_bvbc
    bv_toks = []
    for half in range(2):
        t = s.emit("tensor", lambda h, half=half: h.matmul(
            E.pb[2][:, 0:512], E.ones_f[0:1, 0:128],
            E.p1_bvrow[0:1, half * 512:half * 512 + 512], start=True, stop=True),
            deps=[bvr] + E.bank.wdeps(2))
        a = s.emit("scalar", lambda h, half=half: h.copy(bvbc[:, half * 512:half * 512 + 512],
                                                         E.pb[2][:, 0:512]), deps=[t])
        E.bank.read(2, a)
        bv_toks.append(a)

    xg32 = E.p1_xg32
    xgbf = E.p1_xgbf
    q_sb, k_sb, v_sb, att_sb = E.p1_q, E.p1_k, E.p1_v, E.p1_att
    xg32_guard = []
    xgbf_guard = []
    gg = []

    ngroups = 1 if BIS else 8
    for g in range(ngroups):
        ld32 = s.dma("sync", lambda h, g=g: h.dma_start(
            out=xg32[:].rearrange("p (c t) -> p c t", t=512),
            in_=rearr_groups(E.ap["x_in"])[:, :, g * 512:(g + 1) * 512]),
            deps=xg32_guard, sem="DX32")
        ldbf = s.dma("gpsimd", lambda h, g=g: h.dma_start(
            out=xgbf[:].rearrange("p (c t) -> p c t", t=512),
            in_=rearr_groups(E.ap["x_in"])[:, :, g * 512:(g + 1) * 512]),
            deps=xgbf_guard + [wtok], sem="DXBF")
        xg32_guard = []
        xgbf_guard = []

        pe_toks = []
        for (w_sb, dst, bname) in ((wq, q_sb, "b0q"), (wk, k_sb, "b0k")):
            for dc in range(8):
                bank = len(pe_toks) % 2

                def pmm(h, w_sb=w_sb, dc=dc, bank=bank):
                    ins = None
                    for ci in range(8):
                        ins = h.matmul(E.pb[bank][:, 0:512],
                                       w_sb[:, ci * 1024 + dc * 128: ci * 1024 + dc * 128 + 128],
                                       xgbf[:, ci * 512:(ci + 1) * 512],
                                       start=(ci == 0), stop=(ci == 7))
                    return ins
                t = s.emit("tensor", lambda h, f=pmm: f(h),
                           deps=[ldbf] + E.bank.wdeps(bank))
                pe_toks.append(t)
                a = s.emit("scalar", lambda h, dst=dst, dc=dc, bank=bank, bname=bname:
                           h.activation(out=dst[:, dc * 512:(dc + 1) * 512],
                                        in_=E.pb[bank][:, 0:512], func=AF.Identity,
                                        bias=E.bias[bname][:, dc:dc + 1]),
                           deps=[t, E.bias_tok] + gg)
                E.bank.read(bank, a)
        for tc in range(4):
            for half in range(2):
                bank = len(pe_toks) % 2

                def vmm(h, tc=tc, half=half, bank=bank):
                    ins = None
                    for ci in range(8):
                        ins = h.matmul(E.pb[bank][:, 0:512],
                                       xgbf[:, ci * 512 + tc * 128: ci * 512 + tc * 128 + 128],
                                       wv[:, ci * 1024 + half * 512: ci * 1024 + half * 512 + 512],
                                       start=(ci == 0), stop=(ci == 7))
                    return ins
                t = s.emit("tensor", lambda h, f=vmm: f(h), deps=[ldbf] + E.bank.wdeps(bank))
                pe_toks.append(t)
                v = s.emit("vector", lambda h, tc=tc, half=half, bank=bank: h.tensor_add(
                    v_sb[:, tc * 1024 + half * 512: tc * 1024 + half * 512 + 512],
                    E.pb[bank][:, 0:512], bvbc[:, half * 512:half * 512 + 512]),
                    deps=[t] + bv_toks + gg)
                E.bank.read(bank, v)
        proj_done_v = s.tok("V")
        proj_done_a = s.tok("A")
        if BIS == "proj":
            s.dma("gpsimd", lambda h: h.dma_start(
                out=rearr_groups(E.ap["r1"])[:, :, 0:512],
                in_=q_sb[:].rearrange("p (c t) -> p c t", t=512)),
                deps=[proj_done_v, proj_done_a], sem="DXBF")
            return

        it = 0
        if g == 0:
            z0 = s.emit("vector", lambda h: h.memset(E.p1_expst[0][0][64:128, :], 0.0))
            z1 = s.emit("vector", lambda h: h.memset(E.p1_expst[1][0][64:128, :], 0.0))
            z2 = s.emit("vector", lambda h: h.memset(E.p1_expst[0][1][0:64, :], 0.0))
            z3 = s.emit("vector", lambda h: h.memset(E.p1_expst[1][1][0:64, :], 0.0))
            E.p1_expst_init = [z0, z1, z2, z3]
        expst_g = [list(E.p1_expst_init), list(E.p1_expst_init)]
        recip_g = [[], []]
        bcsb_g = [[], []]
        natt = int(os.environ.get("P1ATT_N", "32"))
        aops = int(os.environ.get("P1ATT_OPS", "7"))
        for tc in range(4):
            for hh in range(8):
                if it >= natt:
                    continue
                sl = it % 2
                st_ps = E.pb[2 + sl]
                da_ps = E.pb[4 + sl]
                bc_ps = E.pb[6 + sl]
                exp0 = E.p1_expst[sl][0]
                exp1 = E.p1_expst[sl][1]
                recip = E.p1_recip[sl]
                bc_sb = E.p1_bcsb[sl]

                def stmm(h, tc=tc, hh=hh, st_ps=st_ps):
                    base = hh * 512 + tc * 128
                    h.matmul(st_ps[0:64, 0:64], k_sb[:, base:base + 64],
                             q_sb[:, base:base + 64], start=True, stop=True)
                    return h.matmul(st_ps[64:128, 0:64], k_sb[:, base + 64:base + 128],
                                    q_sb[:, base + 64:base + 128], start=True, stop=True,
                                    tile_position=(0, 64))
                t1 = s.emit("tensor", lambda h, f=stmm: f(h),
                            deps=[proj_done_v, proj_done_a] + E.bank.wdeps(2 + sl))
                if aops < 2:
                    it += 1
                    continue
                e1a = s.emit("scalar", lambda h, st_ps=st_ps, exp0=exp0: h.activation(
                    out=exp0[0:64, :], in_=st_ps[0:64, 0:64], func=AF.Exp, scale=SC0),
                    deps=[t1] + expst_g[sl])
                e1 = s.emit("scalar", lambda h, st_ps=st_ps, exp1=exp1: h.activation(
                    out=exp1[64:128, :], in_=st_ps[64:128, 0:64], func=AF.Exp, scale=SC0),
                    deps=[t1])
                E.bank.read(2 + sl, e1)
                if aops < 3:
                    it += 1
                    continue

                def dpv(h, tc=tc, hh=hh, da_ps=da_ps, exp0=exp0, exp1=exp1):
                    h.matmul(da_ps[0:1, 128:192], E.ones_bf[:, 0:1], exp0[:, :],
                             start=True, stop=True)
                    h.matmul(da_ps[0:1, 192:256], E.ones_bf[:, 0:1], exp1[:, :],
                             start=True, stop=True)
                    vb = tc * 1024 + hh * 128
                    h.matmul(da_ps[:, 0:64], v_sb[:, vb:vb + 128], exp0[:, :],
                             start=True, stop=True)
                    return h.matmul(da_ps[:, 64:128], v_sb[:, vb:vb + 128],
                                    exp1[:, :], start=True, stop=True)
                t2 = s.emit("tensor", lambda h, f=dpv: f(h),
                            deps=[e1a, e1] + E.bank.wdeps(4 + sl))
                expst_g[sl] = [t2]
                if aops < 4:
                    it += 1
                    continue
                r1 = s.emit("vector", lambda h, da_ps=da_ps, recip=recip: h.reciprocal(
                    recip[0:1, 0:128], da_ps[0:1, 128:256]), deps=[t2] + recip_g[sl])
                E.bank.read(4 + sl, r1)
                if aops < 5:
                    it += 1
                    continue
                t3 = s.emit("tensor", lambda h, bc_ps=bc_ps, recip=recip: h.matmul(
                    bc_ps[:, 0:128], E.ones_f[0:1, 0:128], recip[0:1, 0:128],
                    start=True, stop=True), deps=[r1] + E.bank.wdeps(6 + sl))
                recip_g[sl] = [t3]
                if aops < 6:
                    it += 1
                    continue
                a3 = s.emit("scalar", lambda h, bc_ps=bc_ps, bc_sb=bc_sb: h.copy(
                    bc_sb[:], bc_ps[:, 0:128]), deps=[t3] + bcsb_g[sl])
                E.bank.read(6 + sl, a3)
                if aops < 7:
                    it += 1
                    continue
                v3 = s.emit("vector", lambda h, tc=tc, hh=hh, da_ps=da_ps, bc_sb=bc_sb:
                            h.tensor_mul(att_sb[:, hh * 512 + tc * 128: hh * 512 + tc * 128 + 128],
                                         da_ps[:, 0:128], bc_sb[:]),
                            deps=[a3, t2])
                E.bank.read(4 + sl, v3)
                expst_g[sl] = [t2]
                bcsb_g[sl] = [v3]
                it += 1
        att_done = s.tok("V")
        if BIS == "attn":
            s.dma("gpsimd", lambda h: h.dma_start(
                out=rearr_groups(E.ap["r1"])[:, :, 0:512],
                in_=att_sb[:].rearrange("p (c t) -> p c t", t=512)),
                deps=[att_done], sem="DXBF")
            return

        wo_last = None
        for dc in range(8):
            bank = dc % 2

            def womm(h, dc=dc, bank=bank):
                ins = None
                for ci in range(8):
                    ins = h.matmul(E.pb[bank][:, 0:512],
                                   wo[:, ci * 1024 + dc * 128: ci * 1024 + dc * 128 + 128],
                                   att_sb[:, ci * 512:(ci + 1) * 512],
                                   start=(ci == 0), stop=(ci == 7))
                return ins
            t = s.emit("tensor", lambda h, f=womm: f(h),
                       deps=[att_done] + E.bank.wdeps(bank))
            v = s.emit("vector", lambda h, dc=dc, bank=bank: h.scalar_tensor_tensor(
                xg32[:, dc * 512:(dc + 1) * 512], E.pb[bank][:, 0:512],
                E.bias["b0o"][:, dc:dc + 1], xg32[:, dc * 512:(dc + 1) * 512],
                ALU.add, ALU.add),
                deps=[t, ld32, E.bias_tok])
            E.bank.read(bank, v)
            wo_last = v

        if BIS == "wo":
            s.dma("sync", lambda h: h.dma_start(
                out=rearr_groups(E.ap["r1"])[:, :, 0:512],
                in_=xg32[:].rearrange("p (c t) -> p c t", t=512)),
                deps=[wo_last], sem="DST0")
            return

        def dst_fn(dc, tt, g=g):
            return rearr_groups(E.ap["r1"])[:, dc, g * 512:(g + 1) * 512]
        emit_postln(E, xg32, 512, dst_fn, [wo_last], E.eps[0:1, 0:1], rbf=att_sb, sqb=xgbf)
        xg32_guard = [s.tok("V"), s.tok("A")]
        xgbf_guard = list(E.ln_guard)
        gg = [s.tok("T")]


def emit_p2(E):
    s = E.s
    wd = E.p2_wd
    wd_toks = []
    for part in range(3):
        wd_toks.append(s.dma("gpsimd", lambda h, part=part: h.dma_start(
            out=wd[:, part * 8192:(part + 1) * 8192].rearrange("p (c f) -> p c f", f=1024),
            in_=rearr_cpf(E.ap["w0d_t"])[:, part * 8:(part + 1) * 8, :]), sem="DWD"))
    wg4, wu4 = E.p2_wg, E.p2_wu
    h_sb = E.p2_h
    x32 = E.p2_x32
    xbf = E.p2_xbf
    x32_guard = []
    xbf_guard = []
    w_guard = [[], []]
    h_guard = []
    for ei in range(8):
        ld32 = s.dma("sync", lambda h, ei=ei: h.dma_start(
            out=x32[:].rearrange("p (c t) -> p c t", t=512),
            in_=rearr_groups(E.ap["r1"])[:, :, ei * 512:(ei + 1) * 512]),
            deps=x32_guard, sem="DX32")
        ldbf = s.dma("gpsimd", lambda h, ei=ei: h.dma_start(
            out=xbf[:].rearrange("p (c t) -> p c t", t=512),
            in_=rearr_groups(E.ap["r1"])[:, :, ei * 512:(ei + 1) * 512]),
            deps=xbf_guard + [wd_toks[-1]], sem="DXBF")
        x32_guard = []
        xbf_guard = []
        for wgrp in range(6):
            slot = wgrp % 2
            ldg = s.dma("gpsimd", lambda h, wgrp=wgrp, slot=slot: h.dma_start(
                out=wg4[slot][:].rearrange("p (c f) -> p c f", f=512),
                in_=rearr_cpf(E.ap["w0g_t"])[:, :, wgrp * 512:(wgrp + 1) * 512]),
                deps=w_guard[slot], sem=f"DWG{slot}")
            ldu = s.dma("gpsimd", lambda h, wgrp=wgrp, slot=slot: h.dma_start(
                out=wu4[slot][:].rearrange("p (c f) -> p c f", f=512),
                in_=rearr_cpf(E.ap["w0u_t"])[:, :, wgrp * 512:(wgrp + 1) * 512]),
                deps=w_guard[slot], sem=f"DWU{slot}")
            w_guard[slot] = []
            for h4 in range(4):
                hc = wgrp * 4 + h4
                bkg = hc % 2
                bku = 2 + hc % 2

                def gmm(h, h4=h4, bkg=bkg, slot=slot):
                    ins = None
                    for ci in range(8):
                        ins = h.matmul(E.pb[bkg][:, 0:512],
                                       wg4[slot][:, ci * 512 + h4 * 128: ci * 512 + h4 * 128 + 128],
                                       xbf[:, ci * 512:(ci + 1) * 512],
                                       start=(ci == 0), stop=(ci == 7))
                    return ins
                tg = s.emit("tensor", lambda h, f=gmm: f(h),
                            deps=[ldbf, ldg] + E.bank.wdeps(bkg))

                def umm(h, h4=h4, bku=bku, slot=slot):
                    ins = None
                    for ci in range(8):
                        ins = h.matmul(E.pb[bku][:, 0:512],
                                       wu4[slot][:, ci * 512 + h4 * 128: ci * 512 + h4 * 128 + 128],
                                       xbf[:, ci * 512:(ci + 1) * 512],
                                       start=(ci == 0), stop=(ci == 7))
                    return ins
                tu = s.emit("tensor", lambda h, f=umm: f(h),
                            deps=[ldbf, ldu] + E.bank.wdeps(bku))
                ssl = hc % 2
                sgs = E.sg[ssl]
                ag = s.emit("scalar", lambda h, bkg=bkg, sgs=sgs: h.activation(
                    out=sgs[:], in_=E.pb[bkg][:, 0:512], func=AF.Silu, scale=INV127),
                    deps=[tg] + E.sg_guard[ssl])
                E.bank.read(bkg, ag)
                vh = s.emit("vector", lambda h, hc=hc, bku=bku, sgs=sgs: h.tensor_mul(
                    h_sb[:, hc * 512:(hc + 1) * 512], E.pb[bku][:, 0:512], sgs[:]),
                    deps=[tu, ag] + h_guard)
                h_guard = []
                E.bank.read(bku, vh)
                E.sg_guard[ssl] = [vh]
            w_guard[slot] = [s.tok("T")]
        h_done = s.tok("V")
        dn_last = None
        for dc in range(8):
            bank = 4 + dc % 2

            def dmm(h, dc=dc, bank=bank):
                ins = None
                for hc in range(24):
                    ins = h.matmul(E.pb[bank][:, 0:512],
                                   wd[:, hc * 1024 + dc * 128: hc * 1024 + dc * 128 + 128],
                                   h_sb[:, hc * 512:(hc + 1) * 512],
                                   start=(hc == 0), stop=(hc == 23))
                return ins
            t = s.emit("tensor", lambda h, f=dmm: f(h),
                       deps=[h_done, wd_toks[-1]] + E.bank.wdeps(bank))
            v = s.emit("vector", lambda h, dc=dc, bank=bank: h.scalar_tensor_tensor(
                x32[:, dc * 512:(dc + 1) * 512], E.pb[bank][:, 0:512], INV127SQ,
                x32[:, dc * 512:(dc + 1) * 512], ALU.mult, ALU.add),
                deps=[t, ld32])
            E.bank.read(bank, v)
            dn_last = v
        h_guard = [s.tok("T")]

        def dst_fn(dc, tt, ei=ei):
            return rearr_groups(E.ap["r2"])[:, dc, ei * 512:(ei + 1) * 512]
        emit_postln(E, x32, 512, dst_fn, [dn_last], E.eps[0:1, 0:1],
                    rbf=h_sb, sqb=xbf)
        h_guard = h_guard + E.ln_guard
        x32_guard = [s.tok("V"), s.tok("A")]
        xbf_guard = list(E.ln_guard)


def emit_p3(E, pc):
    s = E.s
    nc = E.nc
    kfm = E.p3_k
    vext = E.p3_vext
    wbuf = E.p3_wbuf
    actx = ExitStack()
    wvb = actx.enter_context(nc.sbuf_tensor("p3_wv", [128, 8 * 1024], BF16))
    E.p3_bvrow = actx.enter_context(nc.sbuf_tensor("p3_bvrow", [1, D], F32))
    bvbc = actx.enter_context(nc.sbuf_tensor("p3_bvbc", [128, 1024], F32))
    xbf1 = actx.enter_context(nc.sbuf_tensor("p3_xbf1", [128, 8 * 512], BF16))
    ms = s.emit("vector", lambda h: h.memset(vext[:], 1.0))
    wk_t = s.dma("gpsimd", lambda h: h.dma_start(
        out=wbuf[:].rearrange("p (c f) -> p c f", f=1024), in_=rearr_cpf(E.ap["w1k_t"])),
        sem="DW3")
    wv_t = s.dma("gpsimd", lambda h: h.dma_start(
        out=wvb[:].rearrange("p (c f) -> p c f", f=1024), in_=rearr_cpf(E.ap["w1v_t"])),
        sem="DWV")
    bvr = s.dma("sync", lambda h: h.dma_start(out=E.p3_bvrow[:], in_=E.ap["b1v_row"]), sem="DC")
    bv_toks = []
    for half in range(2):
        t = s.emit("tensor", lambda h, half=half: h.matmul(
            E.pb[2][:, 0:512], E.ones_f[0:1, 0:128],
            E.p3_bvrow[0:1, half * 512:half * 512 + 512], start=True, stop=True),
            deps=[bvr] + E.bank.wdeps(2))
        a = s.emit("scalar", lambda h, half=half: h.copy(bvbc[:, half * 512:half * 512 + 512],
                                                         E.pb[2][:, 0:512]), deps=[t])
        E.bank.read(2, a)
        bv_toks.append(a)

    xbf = [E.p3_xbf0, xbf1]
    xbf_guard = [[], []]
    for tt in range(8):
        slot = tt % 2
        ldbf = s.dma("gpsimd", lambda h, tt=tt, slot=slot: h.dma_start(
            out=xbf[slot][:].rearrange("p (c t) -> p c t", t=512),
            in_=rearr_groups(E.ap["r2"])[:, :, tt * 512:(tt + 1) * 512]),
            deps=xbf_guard[slot] + [wv_t], sem=f"DXB{slot}")
        xbf_guard[slot] = []
        for dc in range(8):
            bank = dc % 2

            def kmm(h, dc=dc, bank=bank, slot=slot):
                ins = None
                for ci in range(8):
                    ins = h.matmul(E.pb[bank][:, 0:512],
                                   wbuf[:, ci * 1024 + dc * 128: ci * 1024 + dc * 128 + 128],
                                   xbf[slot][:, ci * 512:(ci + 1) * 512],
                                   start=(ci == 0), stop=(ci == 7))
                return ins
            t = s.emit("tensor", lambda h, f=kmm: f(h), deps=[ldbf, wk_t] + E.bank.wdeps(bank))
            v = s.emit("vector", lambda h, tt=tt, dc=dc, bank=bank: h.tensor_scalar(
                kfm[:, dc * 4096 + tt * 512: dc * 4096 + tt * 512 + 512],
                E.pb[bank][:, 0:512], E.bias["b1k"][:, dc:dc + 1], None, ALU.add),
                deps=[t, E.bias_tok])
            E.bank.read(bank, v)
        for tc in range(4):
            for half in range(2):
                bank = 2 + (tc * 2 + half) % 2

                def vmm(h, tc=tc, half=half, bank=bank, slot=slot):
                    ins = None
                    for ci in range(8):
                        ins = h.matmul(E.pb[bank][:, 0:512],
                                       xbf[slot][:, ci * 512 + tc * 128: ci * 512 + tc * 128 + 128],
                                       wvb[:, ci * 1024 + half * 512: ci * 1024 + half * 512 + 512],
                                       start=(ci == 0), stop=(ci == 7))
                    return ins
                t = s.emit("tensor", lambda h, f=vmm: f(h), deps=[ldbf, wv_t] + E.bank.wdeps(bank))
                base = (tt * 4 + tc) * 1040 + half * 520

                def vev(h, bank=bank, base=base, half=half):
                    dst = vext[:, base:base + 520].rearrange("p (hh e) -> p hh e", e=65)[:, :, 0:64]
                    src = E.pb[bank][:, 0:512].rearrange("p (hh e) -> p hh e", e=64)
                    bcv = bvbc[:, half * 512:half * 512 + 512].rearrange("p (hh e) -> p hh e", e=64)
                    return h.tensor_add(dst, src, bcv)
                v = s.emit("vector", lambda h, f=vev: f(h), deps=[t, ms] + bv_toks)
                E.bank.read(bank, v)
        xbf_guard[slot] = [s.tok("T")]
    a_done_v = s.tok("V")
    actx.close()
    a_fin = [s.tok("T"), s.tok("V"), s.tok("A")]

    p = lambda *a: pc.enter_context(nc.sbuf_tensor(*a))
    q_sb = p("p3_q", [128, 8 * 512], BF16)
    att_sb = p("p3_att", [128, 8 * 512], BF16)
    r32 = p("p3_r32", [128, 8 * 512], F32)
    E.p3_expst = [p(f"p3_expst{i}", [128, 512], BF16) for i in range(2)]
    E.p3_rec = p("p3_rec", [128, 512], F32)
    E.p3_bcsb = [p("p3_bcsb0", [64, 512], F32)]
    E.p3_bcsb = E.p3_bcsb * 2
    E.p3_tmp = [p(f"p3_tmp{i}", [64, 512], BF16) for i in range(2)]
    wload_guard = [s.tok("T")]
    q_guard = list(a_fin)
    att_guard = list(a_fin)
    r32_guard = list(a_fin)
    expst_g = [list(a_fin), list(a_fin)]
    rec_g = list(a_fin)
    bcsb_g = [list(a_fin), list(a_fin)]
    tmp_g = [list(a_fin), list(a_fin)]
    for qt in range(8):
        wq_t = s.dma("gpsimd", lambda h: h.dma_start(
            out=wbuf[:].rearrange("p (c f) -> p c f", f=1024), in_=rearr_cpf(E.ap["w1q_t"])),
            deps=wload_guard, sem="DW3")
        ldr = s.dma("sync", lambda h, qt=qt: h.dma_start(
            out=r32[:].rearrange("p (c t) -> p c t", t=512),
            in_=rearr_groups(E.ap["r2"])[:, :, qt * 512:(qt + 1) * 512]),
            deps=r32_guard, sem="DR32")
        ldq = s.dma("gpsimd", lambda h, qt=qt: h.dma_start(
            out=xbf[0][:].rearrange("p (c t) -> p c t", t=512),
            in_=rearr_groups(E.ap["r2"])[:, :, qt * 512:(qt + 1) * 512]),
            deps=xbf_guard[0], sem="DXB0")
        r32_guard = []
        xbf_guard[0] = []
        q_last = None
        for dc in range(8):
            bank = dc % 2

            def qmm(h, dc=dc, bank=bank):
                ins = None
                for ci in range(8):
                    ins = h.matmul(E.pb[bank][:, 0:512],
                                   wbuf[:, ci * 1024 + dc * 128: ci * 1024 + dc * 128 + 128],
                                   xbf[0][:, ci * 512:(ci + 1) * 512],
                                   start=(ci == 0), stop=(ci == 7))
                return ins
            t = s.emit("tensor", lambda h, f=qmm: f(h),
                       deps=[ldq, wq_t, a_done_v] + E.bank.wdeps(bank))
            v = s.emit("vector", lambda h, dc=dc, bank=bank: h.tensor_scalar(
                q_sb[:, dc * 512:(dc + 1) * 512], E.pb[bank][:, 0:512],
                E.bias["b1q"][:, dc:dc + 1], None, ALU.add),
                deps=[t, E.bias_tok] + q_guard)
            q_guard = []
            E.bank.read(bank, v)
            q_last = v
        qmm_done = s.tok("T")
        wo_t = s.dma("gpsimd", lambda h: h.dma_start(
            out=wbuf[:].rearrange("p (c f) -> p c f", f=1024), in_=rearr_cpf(E.ap["w1o_t"])),
            deps=[qmm_done], sem="DW3")

        prev_bc = None
        for hh in range(16):
            hp = hh // 2
            hlo = (hh % 2) * 64
            pvb = 4 + hh % 2
            pv_ps = E.pb[pvb]
            pv_w = E.bank.wdeps(pvb)
            pv_last = None
            for ktc in range(32):
                esl = ktc % 2
                stb = 2 + ktc % 2
                st_ps = E.pb[stb]

                def stmm(h, hp=hp, hlo=hlo, ktc=ktc, st_ps=st_ps):
                    return h.matmul(st_ps[:, 0:512],
                                    kfm[hlo:hlo + 64, hp * 4096 + ktc * 128: hp * 4096 + ktc * 128 + 128],
                                    q_sb[hlo:hlo + 64, hp * 512:(hp + 1) * 512],
                                    start=True, stop=True)
                t1 = s.emit("tensor", lambda h, f=stmm: f(h),
                            deps=[q_last, a_done_v] + E.bank.wdeps(stb)
                                 + ([prev_bc] if (ktc == 0 and prev_bc) else []))
                e1 = s.emit("scalar", lambda h, st_ps=st_ps, esl=esl: h.activation(
                    out=E.p3_expst[esl][:], in_=st_ps[:, 0:512], func=AF.Exp, scale=SC1),
                    deps=[t1] + expst_g[esl])
                E.bank.read(stb, e1)

                def pvmm(h, hh=hh, ktc=ktc, esl=esl, pv_ps=pv_ps):
                    return h.matmul(pv_ps[0:65, 0:512],
                                    vext[:, ktc * 1040 + hh * 65: ktc * 1040 + hh * 65 + 65],
                                    E.p3_expst[esl][:],
                                    start=(ktc == 0), stop=(ktc == 31))
                pv_last = s.emit("tensor", lambda h, f=pvmm: f(h),
                                 deps=[e1] + (pv_w if ktc == 0 else []))
                expst_g[esl] = [pv_last]
            rec = E.p3_rec
            r1 = s.emit("vector", lambda h, pv_ps=pv_ps: h.reciprocal(
                rec[64:65, 0:512], pv_ps[64:65, 0:512]), deps=[pv_last] + rec_g)
            bcb = 6 + hh % 2
            t3 = s.emit("tensor", lambda h, bcb=bcb: h.matmul(
                E.pb[bcb][0:64, 0:512], E.ones_f[64:65, 0:64], rec[64:65, 0:512],
                start=True, stop=True), deps=[r1] + E.bank.wdeps(bcb))
            rec_g = [t3]
            prev_bc = t3
            bsl = 0
            a3 = s.emit("scalar", lambda h, bcb=bcb, bsl=bsl: h.copy(
                E.p3_bcsb[bsl][0:64, :], E.pb[bcb][0:64, 0:512]),
                deps=[t3] + bcsb_g[0] + bcsb_g[1])
            E.bank.read(bcb, a3)
            if hh % 2 == 0:
                v3 = s.emit("vector", lambda h, hp=hp, pv_ps=pv_ps, bsl=bsl: h.tensor_mul(
                    att_sb[0:64, hp * 512:(hp + 1) * 512], pv_ps[0:64, 0:512],
                    E.p3_bcsb[bsl][0:64, :]),
                    deps=[a3, pv_last] + att_guard)
                E.bank.read(pvb, v3)
                bcsb_g = [[v3], []]
            else:
                tsl = (hh // 2) % 2
                v3 = s.emit("vector", lambda h, pv_ps=pv_ps, bsl=bsl, tsl=tsl: h.tensor_mul(
                    E.p3_tmp[tsl][0:64, :], pv_ps[0:64, 0:512], E.p3_bcsb[bsl][0:64, :]),
                    deps=[a3, pv_last] + tmp_g[tsl])
                E.bank.read(pvb, v3)
                bcsb_g = [[v3], []]
                d3 = s.dma("sync", lambda h, hp=hp, tsl=tsl: h.dma_start(
                    out=att_sb[64:128, hp * 512:(hp + 1) * 512], in_=E.p3_tmp[tsl][0:64, :]),
                    deps=[v3] + att_guard, sem=f"DTMP{tsl}")
                tmp_g[tsl] = [d3]
        att_done = [s.tok("V"), s.tok("DTMP0"), s.tok("DTMP1")]

        wo_last = None
        for dc in range(8):
            bank = dc % 2

            def womm(h, dc=dc, bank=bank):
                ins = None
                for ci in range(8):
                    ins = h.matmul(E.pb[bank][:, 0:512],
                                   wbuf[:, ci * 1024 + dc * 128: ci * 1024 + dc * 128 + 128],
                                   att_sb[:, ci * 512:(ci + 1) * 512],
                                   start=(ci == 0), stop=(ci == 7))
                return ins
            t = s.emit("tensor", lambda h, f=womm: f(h),
                       deps=att_done + [wo_t] + E.bank.wdeps(bank))
            v = s.emit("vector", lambda h, dc=dc, bank=bank: h.scalar_tensor_tensor(
                r32[:, dc * 512:(dc + 1) * 512], E.pb[bank][:, 0:512],
                E.bias["b1o"][:, dc:dc + 1], r32[:, dc * 512:(dc + 1) * 512],
                ALU.add, ALU.add),
                deps=[t, ldr, E.bias_tok])
            E.bank.read(bank, v)
            wo_last = v
        wload_guard = [s.tok("T")]

        def dst_fn(dc, tt, qt=qt):
            return rearr_groups(E.ap["r3"])[:, dc, qt * 512:(qt + 1) * 512]
        emit_postln(E, r32, 512, dst_fn, [wo_last], E.eps[0:1, 0:1],
                    rbf=att_sb, sqb=xbf[0])
        att_guard = list(E.ln_guard)
        q_guard = [s.tok("T")]
        xbf_guard[0] = list(E.ln_guard)
        r32_guard = [s.tok("V"), s.tok("A")]


def emit_p4(E, out_name):
    s = E.s
    r3q = E.p4_r3
    xn = E.p4_xn
    y_acc = E.p4_yacc
    h_sb = E.p4_h
    mfm = E.p4_mfm
    mbc = E.p4_mbc
    w1b = E.p4_w1
    w2b = E.p4_w2
    gw_tok = s.dma("gpsimd", lambda h: h.dma_start(
        out=E.gw_sb[:].rearrange("p (c e) -> p c e", e=8), in_=rearr_cpf(E.ap["gw_t"])),
        sem="DGW")
    s.dma("sync", lambda h: h.dma_start(out=E.b2_sb[:], in_=E.ap["b2"]), sem="DC")
    s.dma("sync", lambda h: h.dma_start(out=E.sel_f[:], in_=E.c_sel), sem="DC")
    s.dma("sync", lambda h: h.dma_start(out=E.b1f1[:], in_=E.ap["b1f1"]), sem="DC")
    b2_tok = sel_tok = b1f_tok = s.tok("DC")

    r3_guard = []
    xn_guard = []
    yacc_guard = []
    h_guard = []
    mfm_guard = []
    mbc_guard = []
    w_guard = [[], []]
    lg_guard = []
    for qtr in range(4):
        ld32 = s.dma("sync", lambda h, qtr=qtr: h.dma_start(
            out=r3q[:].rearrange("p (c t) -> p c t", t=1024),
            in_=rearr_groups(E.ap["r3"])[:, :, qtr * 1024:(qtr + 1) * 1024]),
            deps=r3_guard, sem="DR32")
        r3_guard = []
        asq = s.emit("scalar", lambda h: h.activation(out=xn[:], in_=r3q[:], func=AF.Square),
                     deps=[ld32] + xn_guard)
        xn_guard = []
        stat_toks = []
        bc_toks = []
        for tt in range(2):
            def mms(h, tt=tt):
                ins = None
                for dc in range(8):
                    ins = h.matmul(E.pb[2][0:1, 0:512], E.invD_bf[:, 0:1],
                                   xn[:, dc * 1024 + tt * 512: dc * 1024 + tt * 512 + 512],
                                   start=(dc == 0), stop=(dc == 7))
                return ins
            tm = s.emit("tensor", lambda h, f=mms: f(h), deps=[asq] + E.bank.wdeps(2))
            vrow = E.ln_rows[0:1, 1024:1536]
            rstd = E.ln_rows[0:1, 1536:2048]
            a1 = s.emit("scalar", lambda h: h.activation(
                out=vrow, in_=E.pb[2][0:1, 0:512], func=AF.Ln, bias=E.eps[0:1, 1:2]),
                deps=[tm] + E.rows_guard)
            E.rows_guard = []
            E.bank.read(2, a1)
            a2 = s.emit("scalar", lambda h: h.activation(
                out=rstd, in_=vrow, func=AF.Exp, scale=-0.5), deps=[a1])
            t_b = s.emit("tensor", lambda h, tt=tt: h.matmul(
                E.pb[6 + tt][:, 0:512], E.ones_f[0:1, 0:128], rstd, start=True, stop=True),
                deps=[a2] + E.bank.wdeps(6 + tt))
            E.rows_guard = [t_b]
            stat_toks.append(tm)
            bc_toks.append(t_b)
        xn_last = None
        for tt in range(2):
            for dc in range(8):
                v = s.emit("vector", lambda h, dc=dc, tt=tt: h.scalar_tensor_tensor(
                    xn[:, dc * 1024 + tt * 512: dc * 1024 + tt * 512 + 512],
                    r3q[:, dc * 1024 + tt * 512: dc * 1024 + tt * 512 + 512],
                    E.bias["rmsw"][:, dc:dc + 1], E.pb[6 + tt][:, 0:512],
                    ALU.mult, ALU.mult),
                    deps=[bc_toks[tt], stat_toks[0], stat_toks[1], ld32, E.bias_tok])
                E.bank.read(6 + tt, v)
                xn_last = v
        xn_done = xn_last

        prev_a1 = None
        prev_t2 = None
        mask_last = None
        for tcl in range(8):
            def lmm(h, tcl=tcl):
                ins = None
                for ci in range(8):
                    ins = h.matmul(E.pb[2][0:128, 0:8],
                                   xn[:, ci * 1024 + tcl * 128: ci * 1024 + tcl * 128 + 128],
                                   E.gw_sb[:, ci * 8:(ci + 1) * 8],
                                   start=(ci == 0), stop=(ci == 7))
                return ins
            t1 = s.emit("tensor", lambda h, f=lmm: f(h),
                        deps=[xn_done, gw_tok] + E.bank.wdeps(2))
            G = E.p4_g
            v1 = s.emit("vector", lambda h: h.tensor_copy(G[:, 0:8], E.pb[2][0:128, 0:8]),
                        deps=[t1] + lg_guard + ([prev_a1] if prev_a1 else []))
            lg_guard = []
            E.bank.read(2, v1)
            v2 = s.emit("vector", lambda h: h.reduce_max(G[:, 8:9], G[:, 0:8], axis=AX),
                        deps=[v1])
            v3 = s.emit("vector", lambda h: h.tensor_scalar_mul(G[:, 9:10], G[:, 8:9], -1.0),
                        deps=[v2])
            a1 = s.emit("scalar", lambda h: h.activation(out=G[:, 10:18], in_=G[:, 0:8],
                                                         func=AF.Exp, bias=G[:, 9:10]),
                        deps=[v3])
            v4 = s.emit("vector", lambda h: h.reduce_sum(G[:, 18:19], G[:, 10:18], axis=AX),
                        deps=[a1])
            v5 = s.emit("vector", lambda h: h.reciprocal(G[:, 19:20], G[:, 18:19]), deps=[v4])
            v6 = s.emit("vector", lambda h: h.tensor_scalar(G[:, 20:28], G[:, 0:8], G[:, 8:9],
                                                            None, ALU.is_equal),
                        deps=[v5] + ([prev_t2] if prev_t2 else []))
            v7 = s.emit("vector", lambda h: h.tensor_scalar(G[:, 20:28], G[:, 20:28],
                                                            G[:, 19:20], None, ALU.mult),
                        deps=[v6])
            t2 = s.emit("tensor", lambda h: h.transpose(E.pb[3][0:8, 0:128], G[:, 20:28],
                                                        E.ident_f[:]),
                        deps=[v7, E.ident_tok] + E.bank.wdeps(3))
            v8 = s.emit("vector", lambda h, tcl=tcl: h.tensor_copy(
                mfm[0:8, tcl * 128:(tcl + 1) * 128], E.pb[3][0:8, 0:128]),
                deps=[t2] + mfm_guard)
            mfm_guard = []
            E.bank.read(3, v8)
            prev_a1 = a1
            prev_t2 = t2
            mask_last = v8
        lg_guard = [s.tok("T")]
        mask_done = mask_last

        y_last = None
        for tt in range(2):
            for dc in range(8):
                bank = (tt * 8 + dc) % 2
                t = s.emit("tensor", lambda h, tt=tt, dc=dc, bank=bank: h.matmul(
                    E.pb[bank][:, 0:512], E.b2_sb[0:8, dc * 128:(dc + 1) * 128],
                    mfm[0:8, tt * 512:(tt + 1) * 512], start=True, stop=True),
                    deps=[mask_done, b2_tok] + E.bank.wdeps(bank))
                v = s.emit("vector", lambda h, tt=tt, dc=dc, bank=bank: h.tensor_copy(
                    y_acc[:, dc * 1024 + tt * 512: dc * 1024 + tt * 512 + 512],
                    E.pb[bank][:, 0:512]),
                    deps=[t] + yacc_guard)
                yacc_guard = []
                E.bank.read(bank, v)
                y_last = v

        for e in range(NE):
            mb_last = None
            for tt in range(2):
                bank = 6 + tt % 2
                t = s.emit("tensor", lambda h, e=e, tt=tt, bank=bank: h.matmul(
                    E.pb[bank][:, 0:512], E.sel_f[0:8, e * 128:(e + 1) * 128],
                    mfm[0:8, tt * 512:(tt + 1) * 512], start=True, stop=True),
                    deps=[mask_done, sel_tok] + E.bank.wdeps(bank))
                a = s.emit("scalar", lambda h, tt=tt, bank=bank: h.copy(
                    mbc[:, tt * 512:(tt + 1) * 512], E.pb[bank][:, 0:512]),
                    deps=[t] + mbc_guard)
                mbc_guard = []
                E.bank.read(bank, a)
                mb_last = a
            for hcg in range(4):
                slot = hcg % 2
                ld1 = s.dma("gpsimd", lambda h, e=e, hcg=hcg, slot=slot: h.dma_start(
                    out=w1b[slot][:].rearrange("p (c f) -> p c f", f=1024),
                    in_=rearr_cpf(E.ap["w1ff1"][e])[:, :, hcg * 1024:(hcg + 1) * 1024]),
                    deps=w_guard[slot], sem=f"D41{slot}")
                ld2 = s.dma("gpsimd", lambda h, e=e, hcg=hcg, slot=slot: h.dma_start(
                    out=w2b[slot][:].rearrange("p (c f) -> p c f", f=1024),
                    in_=rearr_cpf(E.ap["w1ff2"][e])[:, hcg * 8:(hcg + 1) * 8, :]),
                    deps=w_guard[slot], sem=f"D42{slot}")
                w_guard[slot] = []
                h_last = None
                for hcl in range(8):
                    for tt in range(2):
                        bank = (hcl * 2 + tt) % 2

                        def gmm(h, hcl=hcl, tt=tt, bank=bank, slot=slot):
                            ins = None
                            for ci in range(8):
                                ins = h.matmul(
                                    E.pb[bank][:, 0:512],
                                    w1b[slot][:, ci * 1024 + hcl * 128: ci * 1024 + hcl * 128 + 128],
                                    xn[:, ci * 1024 + tt * 512: ci * 1024 + tt * 512 + 512],
                                    start=(ci == 0), stop=(ci == 7))
                            return ins
                        tg = s.emit("tensor", lambda h, f=gmm: f(h),
                                    deps=[ld1, xn_done] + E.bank.wdeps(bank))
                        ssl = (hcl * 2 + tt) % 2
                        sgs = E.sg[ssl]
                        bcol = E.b1f1[:, e * 32 + hcg * 8 + hcl: e * 32 + hcg * 8 + hcl + 1]
                        ag = s.emit("scalar", lambda h, bank=bank, sgs=sgs, bcol=bcol:
                                    h.activation(out=sgs[:], in_=E.pb[bank][:, 0:512],
                                                 func=AF.Silu, bias=bcol),
                                    deps=[tg, b1f_tok] + E.sg_guard[ssl])
                        E.bank.read(bank, ag)
                        vh = s.emit("vector", lambda h, hcl=hcl, tt=tt, sgs=sgs: h.tensor_mul(
                            h_sb[:, hcl * 1024 + tt * 512: hcl * 1024 + tt * 512 + 512],
                            sgs[:], mbc[:, tt * 512:(tt + 1) * 512]),
                            deps=[ag, mb_last] + h_guard)
                        h_guard = []
                        E.sg_guard[ssl] = [vh]
                        h_last = vh
                for tt in range(2):
                    for dc in range(8):
                        bank = 2 + (tt * 8 + dc) % 2

                        def ymm(h, tt=tt, dc=dc, bank=bank, slot=slot):
                            ins = None
                            for hcl in range(8):
                                ins = h.matmul(
                                    E.pb[bank][:, 0:512],
                                    w2b[slot][:, hcl * 1024 + dc * 128: hcl * 1024 + dc * 128 + 128],
                                    h_sb[:, hcl * 1024 + tt * 512: hcl * 1024 + tt * 512 + 512],
                                    start=(hcl == 0), stop=(hcl == 7))
                            return ins
                        t = s.emit("tensor", lambda h, f=ymm: f(h),
                                   deps=[ld2, h_last] + E.bank.wdeps(bank))
                        v = s.emit("vector", lambda h, tt=tt, dc=dc, bank=bank: h.tensor_add(
                            y_acc[:, dc * 1024 + tt * 512: dc * 1024 + tt * 512 + 512],
                            y_acc[:, dc * 1024 + tt * 512: dc * 1024 + tt * 512 + 512],
                            E.pb[bank][:, 0:512]),
                            deps=[t, y_last])
                        E.bank.read(bank, v)
                        y_last = v
                h_guard = [s.tok("T")]
                w_guard[slot] = [s.tok("T")]
            mbc_guard = [s.tok("V")]

        radd_last = None
        for i in range(2):
            radd_last = s.emit("vector", lambda h, i=i: h.tensor_add(
                y_acc[:, i * 4096:(i + 1) * 4096], y_acc[:, i * 4096:(i + 1) * 4096],
                r3q[:, i * 4096:(i + 1) * 4096]), deps=[y_last, ld32])

        def dst_fn(dc, tt, qtr=qtr):
            return rearr_groups(E.ap[out_name])[:, dc, qtr * 1024 + tt * 512: qtr * 1024 + tt * 512 + 512]
        emit_postln(E, y_acc, 1024, dst_fn, [radd_last], E.eps[0:1, 0:1],
                    rbf=h_sb, sqb=xn)
        vtok, atok = s.tok("V"), s.tok("A")
        r3_guard = [vtok, atok]
        yacc_guard = [vtok, atok]
        xn_guard = list(E.ln_guard)
        h_guard = list(E.ln_guard)
        mfm_guard = [s.tok("T")]
        mbc_guard = [s.tok("V")]


def build_nc(phases=("p1", "p2", "p3", "p4")):
    nc = bass.Bass("TRN2", target_bir_lowering=False, debug=False)
    E = Env()
    E.nc = nc
    s = Sched()
    E.s = s
    E.bank = BankGuard()

    ap = {}
    ap["x0"] = nc.dram_tensor("x0", [128, 8 * S], F32, kind="ExternalInput").ap()
    chain = {"p1": ("x_in", "r1"), "p2": ("r1", "r2"), "p3": ("r2", "r3"), "p4": ("r3", "y")}
    needed = set()
    for ph in phases:
        needed.add(chain[ph][0])
        needed.add(chain[ph][1])
    needed.discard("x_in")
    out_hbm = chain[phases[-1]][1]
    in_hbm = chain[phases[0]][0]
    for name in sorted(needed):
        if name == out_hbm:
            ap[name] = nc.dram_tensor(name, [128, 8 * S], F32, kind="ExternalOutput").ap()
        else:
            ap[name] = nc.dram_tensor(name, [128, 8 * S], F32, kind="Internal").ap()
    if in_hbm != "x_in":
        ap[in_hbm] = ap["x0"]
    else:
        ap["x_in"] = ap["x0"]

    def din(name, shape, dt):
        ap[name] = nc.dram_tensor(name, shape, dt, kind="ExternalInput").ap()
    din("w0q", [D, D], F32); din("w0k", [D, D], F32); din("w0v", [D, D], F32); din("w0o", [D, D], F32)
    din("b0q", [128, 8], F32); din("b0k", [128, 8], F32); din("b0o", [128, 8], F32)
    din("b0v_row", [1, D], F32)
    din("w0g_t", [D, FF0], I32); din("w0u_t", [D, FF0], I32); din("w0d_t", [FF0, D], I32)
    din("w1q_t", [D, D], F32); din("w1k_t", [D, D], F32); din("w1v_t", [D, D], F32)
    din("w1o_t", [D, D], F32)
    din("b1q", [128, 8], F32); din("b1k", [128, 8], F32); din("b1o", [128, 8], F32)
    din("b1v_row", [1, D], F32)
    din("rmsw", [128, 8], F32)
    din("gw_t", [D, NE], F32)
    din("b1f1", [128, 256], F32)
    din("b2", [NE, D], F32)
    ap["w1ff1"] = [nc.dram_tensor(f"w1ff1_{e}", [D, FF1], F32, kind="ExternalInput").ap()
                   for e in range(NE)]
    ap["w1ff2"] = [nc.dram_tensor(f"w1ff2_{e}", [FF1, D], F32, kind="ExternalInput").ap()
                   for e in range(NE)]
    c_ones_bf = nc.inline_tensor(np.ones((128, 128), dtype=BF16NP), "c_ones_bf").ap()
    c_ones_f = nc.inline_tensor(np.ones((128, 128), dtype=np.float32), "c_ones_f").ap()
    c_neg_f = nc.inline_tensor(np.full((1, 128), -1.0, dtype=np.float32), "c_neg_f").ap()
    c_invd = nc.inline_tensor(np.full((128, 1), 1.0 / 1024.0, dtype=BF16NP), "c_invd").ap()
    c_ident = nc.inline_tensor(np.eye(128, dtype=np.float32), "c_ident").ap()
    selm = np.zeros((8, 8 * 128), dtype=np.float32)
    for e in range(8):
        selm[e, e * 128:(e + 1) * 128] = 1.0
    E.c_sel = nc.inline_tensor(selm, "c_sel").ap()
    c_eps = nc.inline_tensor(np.array([[EPS_LN, EPS_RMS]], dtype=np.float32), "c_eps").ap()
    E.ap = ap

    with ExitStack() as ctx:
        sb = lambda *a: ctx.enter_context(nc.sbuf_tensor(*a))
        psb = lambda *a: ctx.enter_context(nc.psum_tensor(*a))
        E.pb = [psb(f"pb{i}", [128, 512], F32) for i in range(8)]
        E.ones_bf = sb("ones_bf", [128, 128], BF16)
        E.ones_f = sb("ones_f", [128, 128], F32)
        E.neg_f = sb("neg_f", [1, 128], F32)
        E.invD_bf = sb("invD_bf", [128, 1], BF16)
        E.ident_f = sb("ident_f", [128, 128], F32)
        E.eps = sb("eps", [1, 2], F32)
        E.bias = {}
        for name in ("b0q", "b0k", "b0o", "b1q", "b1k", "b1o", "rmsw"):
            E.bias[name] = sb("t_" + name, [128, 8], F32)
        E.ln_rows = sb("ln_rows", [1, 2048], F32)
        E.ln_stg = [sb(f"ln_stg{i}", [128, 512], F32) for i in range(2)]
        E.stg_guard = [[], []]
        E.ln_guard = []
        E.rows_guard = []

        s.dma("sync", lambda h: h.dma_start(out=E.ones_bf[:], in_=c_ones_bf), sem="DC")
        s.dma("sync", lambda h: h.dma_start(out=E.ones_f[:], in_=c_ones_f), sem="DC")
        s.dma("sync", lambda h: h.dma_start(out=E.neg_f[:], in_=c_neg_f), sem="DC")
        s.dma("sync", lambda h: h.dma_start(out=E.invD_bf[:], in_=c_invd), sem="DC")
        E.ident_tok = s.dma("sync", lambda h: h.dma_start(out=E.ident_f[:], in_=c_ident), sem="DC")
        s.dma("sync", lambda h: h.dma_start(out=E.eps[:], in_=c_eps), sem="DC")
        for name in ("b0q", "b0k", "b0o", "b1q", "b1k", "b1o", "rmsw"):
            s.dma("sync", lambda h, name=name: h.dma_start(out=E.bias[name][:], in_=ap[name]),
                  sem="DC")
        E.bias_tok = s.tok("DC")
        s.barrier()

        if "p1" in phases:
            with ExitStack() as pc:
                p = lambda *a: pc.enter_context(nc.sbuf_tensor(*a))
                E.p1_w = [p(f"p1w{i}", [128, 8 * 1024], BF16) for i in range(4)]
                E.p1_bvrow = p("p1_bvrow", [1, D], F32)
                E.p1_bvbc = p("p1_bvbc", [128, 1024], F32)
                E.p1_xg32 = p("p1_xg32", [128, 8 * 512], F32)
                E.p1_xgbf = p("p1_xgbf", [128, 8 * 512], BF16)
                E.p1_q = p("p1_q", [128, 8 * 512], BF16)
                E.p1_k = p("p1_k", [128, 8 * 512], BF16)
                E.p1_v = p("p1_v", [128, 4 * 1024], BF16)
                E.p1_att = p("p1_att", [128, 8 * 512], BF16)
                E.p1_expst = [[p(f"p1_expst{i}_{b}", [128, 64], BF16) for b in range(2)]
                              for i in range(2)]
                E.p1_recip = [p(f"p1_recip{i}", [1, 128], F32) for i in range(2)]
                E.p1_bcsb = [p(f"p1_bcsb{i}", [128, 128], F32) for i in range(2)]
                emit_p1(E)
            s.barrier()
        if "p2" in phases:
            with ExitStack() as pc:
                p = lambda *a: pc.enter_context(nc.sbuf_tensor(*a))
                E.p2_wd = p("p2_wd", [128, 24 * 1024], BF16)
                E.p2_wg = [p(f"p2_wg{i}", [128, 8 * 512], BF16) for i in range(2)]
                E.p2_wu = [p(f"p2_wu{i}", [128, 8 * 512], BF16) for i in range(2)]
                E.p2_h = p("p2_h", [128, 24 * 512], BF16)
                E.p2_x32 = p("p2_x32", [128, 8 * 512], F32)
                E.p2_xbf = p("p2_xbf", [128, 8 * 512], BF16)
                E.sg = [p(f"p2sg{i}", [128, 512], F32) for i in range(2)]
                E.sg_guard = [[], []]
                emit_p2(E)
            s.barrier()
        if "p3" in phases:
            with ExitStack() as pc:
                p = lambda *a: pc.enter_context(nc.sbuf_tensor(*a))
                E.p3_k = p("p3_k", [128, 8 * 4096], BF16)
                E.p3_vext = p("p3_vext", [128, 32 * 1040], BF16)
                E.p3_wbuf = p("p3_wbuf", [128, 8 * 1024], BF16)
                E.p3_xbf0 = p("p3_xbf0", [128, 8 * 512], BF16)
                emit_p3(E, pc)
            s.barrier()
        if "p4" in phases:
            with ExitStack() as pc:
                p = lambda *a: pc.enter_context(nc.sbuf_tensor(*a))
                E.p4_r3 = p("p4_r3", [128, 8 * 1024], F32)
                E.p4_xn = p("p4_xn", [128, 8 * 1024], BF16)
                E.p4_yacc = p("p4_yacc", [128, 8 * 1024], F32)
                E.p4_h = p("p4_h", [128, 8 * 1024], BF16)
                E.p4_mfm = p("p4_mfm", [8, 1024], F32)
                E.p4_mbc = p("p4_mbc", [128, 1024], F32)
                E.p4_w1 = [p(f"p4_w1{i}", [128, 8 * 1024], BF16) for i in range(2)]
                E.p4_w2 = [p(f"p4_w2{i}", [128, 8 * 1024], BF16) for i in range(2)]
                E.p4_g = p("p4_g", [128, 32], F32)
                E.gw_sb = p("gw_sb", [128, 64], BF16)
                E.b2_sb = p("b2_sb", [8, 1024], F32)
                E.sel_f = p("sel_f", [8, 1024], F32)
                E.b1f1 = p("t_b1f1", [128, 256], F32)
                E.sg = [p(f"p4sg{i}", [128, 512], F32) for i in range(2)]
                E.sg_guard = [[], []]
                emit_p4(E, out_hbm)
            s.barrier()

        fence_deps = [s.tok(name) for name in list(s.cnt.keys())]
        s.emit("sync", lambda h: h.nop(), deps=fence_deps)
        s.realize(nc, ctx)
    return nc


def _prep_shared(inputs):
    f = np.ascontiguousarray
    d = {}
    d["w0q"] = f(np.asarray(inputs["L0_attn_Wq"], np.float32))
    d["w0k"] = f(np.asarray(inputs["L0_attn_Wk"], np.float32))
    d["w0v"] = f(np.asarray(inputs["L0_attn_Wv"], np.float32))
    d["w0o"] = f(np.asarray(inputs["L0_attn_Wo"], np.float32))
    d["b0q"] = f(np.asarray(inputs["L0_attn_bq"], np.float32).reshape(8, 128).T)
    d["b0k"] = f(np.asarray(inputs["L0_attn_bk"], np.float32).reshape(8, 128).T)
    d["b0o"] = f(np.asarray(inputs["L0_attn_bo"], np.float32).reshape(8, 128).T)
    d["b0v_row"] = f(np.asarray(inputs["L0_attn_bv"], np.float32).reshape(1, 1024))
    d["w0g_t"] = f(np.asarray(inputs["L0_ffn_w_gate"], np.int32).T)
    d["w0u_t"] = f(np.asarray(inputs["L0_ffn_w_up"], np.int32).T)
    d["w0d_t"] = f(np.asarray(inputs["L0_ffn_w_down"], np.int32).T)
    d["w1q_t"] = f(np.asarray(inputs["L1_attn_wq"], np.float32).transpose(1, 0, 2).reshape(1024, 1024))
    d["w1k_t"] = f(np.asarray(inputs["L1_attn_wk"], np.float32).transpose(1, 0, 2).reshape(1024, 1024))
    d["w1v_t"] = f(np.asarray(inputs["L1_attn_wv"], np.float32).transpose(1, 0, 2).reshape(1024, 1024))
    d["w1o_t"] = f(np.asarray(inputs["L1_attn_wo"], np.float32).T)
    d["b1q"] = f(np.asarray(inputs["L1_attn_bq"], np.float32).reshape(1024).reshape(8, 128).T)
    d["b1k"] = f(np.asarray(inputs["L1_attn_bk"], np.float32).reshape(1024).reshape(8, 128).T)
    d["b1o"] = f(np.asarray(inputs["L1_attn_bo"], np.float32).reshape(8, 128).T)
    d["b1v_row"] = f(np.asarray(inputs["L1_attn_bv"], np.float32).reshape(1, 1024))
    d["rmsw"] = f(np.asarray(inputs["L1_ffn_rms_w"], np.float32).reshape(8, 128).T)
    d["gw_t"] = f(np.asarray(inputs["L1_ffn_gate_w"], np.float32).T)
    d["b1f1"] = f(np.asarray(inputs["L1_ffn_b1"], np.float32).reshape(8, 32, 128)
                  .transpose(2, 0, 1).reshape(128, 256))
    d["b2"] = f(np.asarray(inputs["L1_ffn_b2"], np.float32))
    for e in range(NE):
        d[f"w1ff1_{e}"] = f(np.asarray(inputs["L1_ffn_W1"][e], np.float32))
        d[f"w1ff2_{e}"] = f(np.asarray(inputs["L1_ffn_W2"][e], np.float32))
    return d


def fm_layout(x2d):
    fm = np.ascontiguousarray(np.asarray(x2d, np.float32).T)
    return np.ascontiguousarray(fm.reshape(8, 128, S).transpose(1, 0, 2).reshape(128, 8 * S))


def fm_unlayout(y):
    fm = y.reshape(128, 8, S).transpose(1, 0, 2).reshape(1024, S)
    return np.ascontiguousarray(fm.T)


def kernel(**inputs):
    shared = _prep_shared(inputs)
    x = np.asarray(inputs["L0_attn_src"], dtype=np.float32)
    nc = build_nc(phases=("p1", "p2", "p3", "p4"))
    in_maps = []
    for b in range(8):
        m = dict(shared)
        m["x0"] = fm_layout(x[b])
        in_maps.append(m)
    res = run_bass_kernel_spmd(nc, in_maps, core_ids=list(range(8)))
    return np.stack([fm_unlayout(np.asarray(r["y"], dtype=np.float32)) for r in res.results])


# version 04
# speedup vs baseline: 1.1682x; 1.1682x over previous
import os
import numpy as np
import ml_dtypes
from contextlib import ExitStack

import concourse.bass as bass
import concourse.mybir as mybir
from concourse.bass_utils import run_bass_kernel_spmd

F32 = mybir.dt.float32
BF16 = mybir.dt.bfloat16
I32 = mybir.dt.int32
AF = mybir.ActivationFunctionType
ALU = mybir.AluOpType
AX = mybir.AxisListType.X
BF16NP = ml_dtypes.bfloat16

S = 4096
D = 1024
SC0 = float(1.0 / np.sqrt(128.0))
SC1 = 0.125
EPS_LN = 1e-5
EPS_RMS = 1e-6
FF0 = 3072
FF1 = 4096
NE = 8
INV127 = float(1.0 / 127.0)
INV127SQ = float(1.0 / (127.0 * 127.0))

ENGINES = ("sync", "tensor", "scalar", "vector", "gpsimd")
OWN_SEM = {"tensor": "T", "scalar": "A", "vector": "V", "gpsimd": "G", "sync": "SY"}


class Sched:
    def __init__(self):
        self.q = {e: [] for e in ENGINES}
        self.cnt = {}
        self.pending = {e: [] for e in ENGINES}

    def tok(self, sem):
        return (sem, self.cnt.get(sem, 0))

    def emit(self, eng, fn, deps=(), sig=None, inc=1):
        if sig is None:
            sig = OWN_SEM[eng]
        for d in deps:
            if d is None:
                continue
            s_, v = d
            assert v <= self.cnt.get(s_, 0), f"dep on future token {s_}={v}"
        self.cnt[sig] = self.cnt.get(sig, 0) + inc
        token = (sig, self.cnt[sig])
        dd = self.pending[eng] + [d for d in deps if d is not None]
        self.pending[eng] = []
        self.q[eng].append((dd, fn, sig, inc))
        return token

    def dma(self, eng, fn, deps=(), sem="DS"):
        return self.emit(eng, fn, deps=deps, sig=sem, inc=16)

    def barrier(self):
        snap = [(s_, v) for (s_, v) in self.cnt.items()]
        for e in ENGINES:
            self.pending[e] = self.pending[e] + snap

    def realize(self, nc, ctx):
        sems = {name: ctx.enter_context(nc.semaphore(name)) for name in self.cnt}
        block = ctx.enter_context(nc.Block())

        def mk(eng):
            def body(h):
                waited = {}
                for (deps, fn, sig, inc) in self.q[eng]:
                    need = {}
                    for (s_, v) in deps:
                        if v > waited.get(s_, 0) and v > need.get(s_, 0):
                            need[s_] = v
                    for s_ in sorted(need):
                        h.wait_ge(sems[s_], need[s_])
                        waited[s_] = need[s_]
                    fn(h).then_inc(sems[sig], inc)
            return body

        block.sync(mk("sync"))
        block.tensor(mk("tensor"))
        block.scalar(mk("scalar"))
        block.vector(mk("vector"))
        block.gpsimd(mk("gpsimd"))


class BankGuard:
    def __init__(self):
        self.readers = {}

    def read(self, b, token):
        self.readers.setdefault(b, []).append(token)

    def wdeps(self, b):
        d = self.readers.get(b, [])
        self.readers[b] = []
        return d


def rearr_cpf(ap, p=128):
    return ap.rearrange("(c p) f -> p c f", p=p)


def rearr_groups(ap, t=S):
    return ap.rearrange("p (c t) -> p c t", t=t)


class Env:
    pass


def emit_postln(E, r_sb, T, dst_fn, r_deps, eps_col, rbf, sqb):
    s = E.s
    nTT = T // 512
    vcp = s.emit("vector", lambda h: h.tensor_copy(rbf[:, 0:8 * T], r_sb[:, 0:8 * T]),
                 deps=list(r_deps) + E.ln_guard)
    acp = s.emit("scalar", lambda h: h.activation(out=sqb[:, 0:8 * T], in_=r_sb[:, 0:8 * T],
                                                  func=AF.Square),
                 deps=list(r_deps) + E.ln_guard)
    R = E.ln_rows
    for tt in range(nTT):
        mean_ps = E.pb[2][0:1, 0:512]
        msq_ps = E.pb[3][0:1, 0:512]

        def statmm(h, src, dst, tt=tt):
            ins = None
            for dc in range(8):
                ins = h.matmul(dst, E.invD_bf[:, 0:1],
                               src[:, dc * T + tt * 512: dc * T + tt * 512 + 512],
                               start=(dc == 0), stop=(dc == 7))
            return ins
        t_mean = s.emit("tensor", lambda h, tt=tt: statmm(h, rbf, E.pb[2][0:1, 0:512], tt),
                        deps=[vcp] + E.bank.wdeps(2))
        t_msq = s.emit("tensor", lambda h, tt=tt: statmm(h, sqb, E.pb[3][0:1, 0:512], tt),
                       deps=[acp] + E.bank.wdeps(3))
        mrow = R[0:1, 0:512]
        m2 = R[0:1, 512:1024]
        vrow = R[0:1, 1024:1536]
        rstd = vrow
        v1 = s.emit("vector", lambda h: h.tensor_copy(mrow, mean_ps),
                    deps=[t_mean] + E.rows_guard)
        E.rows_guard = []
        E.bank.read(2, v1)
        v2 = s.emit("vector", lambda h: h.tensor_mul(m2, mrow, mrow), deps=[v1])
        v3 = s.emit("vector", lambda h: h.tensor_sub(vrow, msq_ps, m2), deps=[t_msq, v2])
        E.bank.read(3, v3)
        a1 = s.emit("scalar", lambda h: h.activation(out=vrow, in_=vrow, func=AF.Ln,
                                                     bias=eps_col), deps=[v3])
        a2 = s.emit("scalar", lambda h: h.activation(out=rstd, in_=vrow, func=AF.Exp,
                                                     scale=-0.5), deps=[a1])
        v4 = s.emit("vector", lambda h: h.tensor_mul(m2, mrow, rstd), deps=[a2])
        bca = E.pb[6][:, 0:512]
        bcb = E.pb[7][:, 0:512]
        t_a = s.emit("tensor", lambda h: h.matmul(bca, E.ones_f[0:1, 0:128], rstd,
                                                  start=True, stop=True),
                     deps=[a2] + E.bank.wdeps(6))
        t_b = s.emit("tensor", lambda h: h.matmul(bcb, E.neg_f[0:1, 0:128], m2,
                                                  start=True, stop=True),
                     deps=[v4] + E.bank.wdeps(7))
        E.rows_guard = [t_a, t_b]
        for dc in range(8):
            slot = (tt * 8 + dc) % 2
            stg = E.ln_stg[slot]
            vv1 = s.emit("vector",
                         lambda h, stg=stg, dc=dc, tt=tt: h.tensor_mul(
                             stg[:], r_sb[:, dc * T + tt * 512: dc * T + tt * 512 + 512], bca),
                         deps=[t_a, t_b] + E.stg_guard[slot])
            vv2 = s.emit("vector", lambda h, stg=stg: h.tensor_add(stg[:], stg[:], bcb),
                         deps=[vv1])
            E.bank.read(6, vv1)
            E.bank.read(7, vv2)
            dtok = s.dma("sync", lambda h, stg=stg, dc=dc, tt=tt: h.dma_start(
                out=dst_fn(dc, tt), in_=stg[:]), deps=[vv2], sem=f"DST{slot}")
            E.stg_guard[slot] = [dtok]
    E.ln_guard = [s.tok("T")]


def emit_p1(E):
    BIS = os.environ.get("P1BISECT", "")
    s = E.s
    wq, wk, wv, wo = E.p1_w
    ldtoks = []
    for (sb_t, src) in ((wq, "w0q"), (wk, "w0k"), (wv, "w0v"), (wo, "w0o")):
        ldtoks.append(s.dma("gpsimd", lambda h, d=sb_t, src=src: h.dma_start(
            out=d[:].rearrange("p (c f) -> p c f", f=1024), in_=rearr_cpf(E.ap[src])),
            sem="DW1"))
    wtok = ldtoks[-1]
    bvr = s.dma("sync", lambda h: h.dma_start(out=E.p1_bvrow[:], in_=E.ap["b0v_row"]), sem="DC")
    bvbc = E.p1_bvbc
    bv_toks = []
    for half in range(2):
        t = s.emit("tensor", lambda h, half=half: h.matmul(
            E.pb[2][:, 0:512], E.ones_f[0:1, 0:128],
            E.p1_bvrow[0:1, half * 512:half * 512 + 512], start=True, stop=True),
            deps=[bvr] + E.bank.wdeps(2))
        a = s.emit("scalar", lambda h, half=half: h.copy(bvbc[:, half * 512:half * 512 + 512],
                                                         E.pb[2][:, 0:512]), deps=[t])
        E.bank.read(2, a)
        bv_toks.append(a)

    xg32 = E.p1_xg32
    xgbf = E.p1_xgbf
    q_sb, k_sb, v_sb, att_sb = E.p1_q, E.p1_k, E.p1_v, E.p1_att
    xg32_guard = []
    xgbf_guard = []
    gg = []

    ngroups = 1 if BIS else 8
    for g in range(ngroups):
        ld32 = s.dma("sync", lambda h, g=g: h.dma_start(
            out=xg32[:].rearrange("p (c t) -> p c t", t=512),
            in_=rearr_groups(E.ap["x_in"])[:, :, g * 512:(g + 1) * 512]),
            deps=xg32_guard, sem="DX32")
        ldbf = s.dma("gpsimd", lambda h, g=g: h.dma_start(
            out=xgbf[:].rearrange("p (c t) -> p c t", t=512),
            in_=rearr_groups(E.ap["x_in"])[:, :, g * 512:(g + 1) * 512]),
            deps=xgbf_guard + [wtok], sem="DXBF")
        xg32_guard = []
        xgbf_guard = []

        pe_toks = []
        for (w_sb, dst, bname) in ((wq, q_sb, "b0q"), (wk, k_sb, "b0k")):
            for dc in range(8):
                bank = len(pe_toks) % 2

                def pmm(h, w_sb=w_sb, dc=dc, bank=bank):
                    ins = None
                    for ci in range(8):
                        ins = h.matmul(E.pb[bank][:, 0:512],
                                       w_sb[:, ci * 1024 + dc * 128: ci * 1024 + dc * 128 + 128],
                                       xgbf[:, ci * 512:(ci + 1) * 512],
                                       start=(ci == 0), stop=(ci == 7))
                    return ins
                t = s.emit("tensor", lambda h, f=pmm: f(h),
                           deps=[ldbf] + E.bank.wdeps(bank))
                pe_toks.append(t)
                a = s.emit("scalar", lambda h, dst=dst, dc=dc, bank=bank, bname=bname:
                           h.activation(out=dst[:, dc * 512:(dc + 1) * 512],
                                        in_=E.pb[bank][:, 0:512], func=AF.Identity,
                                        bias=E.bias[bname][:, dc:dc + 1]),
                           deps=[t, E.bias_tok] + gg)
                E.bank.read(bank, a)
        for tc in range(4):
            for half in range(2):
                bank = len(pe_toks) % 2

                def vmm(h, tc=tc, half=half, bank=bank):
                    ins = None
                    for ci in range(8):
                        ins = h.matmul(E.pb[bank][:, 0:512],
                                       xgbf[:, ci * 512 + tc * 128: ci * 512 + tc * 128 + 128],
                                       wv[:, ci * 1024 + half * 512: ci * 1024 + half * 512 + 512],
                                       start=(ci == 0), stop=(ci == 7))
                    return ins
                t = s.emit("tensor", lambda h, f=vmm: f(h), deps=[ldbf] + E.bank.wdeps(bank))
                pe_toks.append(t)
                v = s.emit("vector", lambda h, tc=tc, half=half, bank=bank: h.tensor_add(
                    v_sb[:, tc * 1024 + half * 512: tc * 1024 + half * 512 + 512],
                    E.pb[bank][:, 0:512], bvbc[:, half * 512:half * 512 + 512]),
                    deps=[t] + bv_toks + gg)
                E.bank.read(bank, v)
        proj_done_v = s.tok("V")
        proj_done_a = s.tok("A")
        if BIS == "proj":
            s.dma("gpsimd", lambda h: h.dma_start(
                out=rearr_groups(E.ap["r1"])[:, :, 0:512],
                in_=q_sb[:].rearrange("p (c t) -> p c t", t=512)),
                deps=[proj_done_v, proj_done_a], sem="DXBF")
            return

        it = 0
        if g == 0:
            z0 = s.emit("vector", lambda h: h.memset(E.p1_expst[0][0][64:128, :], 0.0))
            z1 = s.emit("vector", lambda h: h.memset(E.p1_expst[1][0][64:128, :], 0.0))
            z2 = s.emit("vector", lambda h: h.memset(E.p1_expst[0][1][0:64, :], 0.0))
            z3 = s.emit("vector", lambda h: h.memset(E.p1_expst[1][1][0:64, :], 0.0))
            E.p1_expst_init = [z0, z1, z2, z3]
        expst_g = [list(E.p1_expst_init), list(E.p1_expst_init)]
        recip_g = [[], []]
        bcsb_g = [[], []]
        natt = int(os.environ.get("P1ATT_N", "32"))
        aops = int(os.environ.get("P1ATT_OPS", "7"))
        for tc in range(4):
            for hh in range(8):
                if it >= natt:
                    continue
                sl = it % 2
                st_ps = E.pb[2 + sl]
                da_ps = E.pb[4 + sl]
                bc_ps = E.pb[6 + sl]
                exp0 = E.p1_expst[sl][0]
                exp1 = E.p1_expst[sl][1]
                recip = E.p1_recip[sl]
                bc_sb = E.p1_bcsb[sl]

                def stmm(h, tc=tc, hh=hh, st_ps=st_ps):
                    base = hh * 512 + tc * 128
                    h.matmul(st_ps[0:64, 0:64], k_sb[:, base:base + 64],
                             q_sb[:, base:base + 64], start=True, stop=True)
                    return h.matmul(st_ps[64:128, 0:64], k_sb[:, base + 64:base + 128],
                                    q_sb[:, base + 64:base + 128], start=True, stop=True,
                                    tile_position=(0, 64))
                t1 = s.emit("tensor", lambda h, f=stmm: f(h),
                            deps=[proj_done_v, proj_done_a] + E.bank.wdeps(2 + sl))
                if aops < 2:
                    it += 1
                    continue
                e1a = s.emit("scalar", lambda h, st_ps=st_ps, exp0=exp0: h.activation(
                    out=exp0[0:64, :], in_=st_ps[0:64, 0:64], func=AF.Exp, scale=SC0),
                    deps=[t1] + expst_g[sl])
                e1 = s.emit("scalar", lambda h, st_ps=st_ps, exp1=exp1: h.activation(
                    out=exp1[64:128, :], in_=st_ps[64:128, 0:64], func=AF.Exp, scale=SC0),
                    deps=[t1])
                E.bank.read(2 + sl, e1)
                if aops < 3:
                    it += 1
                    continue

                def dpv(h, tc=tc, hh=hh, da_ps=da_ps, exp0=exp0, exp1=exp1):
                    h.matmul(da_ps[0:1, 128:192], E.ones_bf[:, 0:1], exp0[:, :],
                             start=True, stop=True)
                    h.matmul(da_ps[0:1, 192:256], E.ones_bf[:, 0:1], exp1[:, :],
                             start=True, stop=True)
                    vb = tc * 1024 + hh * 128
                    h.matmul(da_ps[:, 0:64], v_sb[:, vb:vb + 128], exp0[:, :],
                             start=True, stop=True)
                    return h.matmul(da_ps[:, 64:128], v_sb[:, vb:vb + 128],
                                    exp1[:, :], start=True, stop=True)
                t2 = s.emit("tensor", lambda h, f=dpv: f(h),
                            deps=[e1a, e1] + E.bank.wdeps(4 + sl))
                expst_g[sl] = [t2]
                if aops < 4:
                    it += 1
                    continue
                r1 = s.emit("vector", lambda h, da_ps=da_ps, recip=recip: h.reciprocal(
                    recip[0:1, 0:128], da_ps[0:1, 128:256]), deps=[t2] + recip_g[sl])
                E.bank.read(4 + sl, r1)
                if aops < 5:
                    it += 1
                    continue
                t3 = s.emit("tensor", lambda h, bc_ps=bc_ps, recip=recip: h.matmul(
                    bc_ps[:, 0:128], E.ones_f[0:1, 0:128], recip[0:1, 0:128],
                    start=True, stop=True), deps=[r1] + E.bank.wdeps(6 + sl))
                recip_g[sl] = [t3]
                if aops < 6:
                    it += 1
                    continue
                a3 = s.emit("scalar", lambda h, bc_ps=bc_ps, bc_sb=bc_sb: h.copy(
                    bc_sb[:], bc_ps[:, 0:128]), deps=[t3] + bcsb_g[sl])
                E.bank.read(6 + sl, a3)
                if aops < 7:
                    it += 1
                    continue
                v3 = s.emit("vector", lambda h, tc=tc, hh=hh, da_ps=da_ps, bc_sb=bc_sb:
                            h.tensor_mul(att_sb[:, hh * 512 + tc * 128: hh * 512 + tc * 128 + 128],
                                         da_ps[:, 0:128], bc_sb[:]),
                            deps=[a3, t2])
                E.bank.read(4 + sl, v3)
                expst_g[sl] = [t2]
                bcsb_g[sl] = [v3]
                it += 1
        att_done = s.tok("V")
        if BIS == "attn":
            s.dma("gpsimd", lambda h: h.dma_start(
                out=rearr_groups(E.ap["r1"])[:, :, 0:512],
                in_=att_sb[:].rearrange("p (c t) -> p c t", t=512)),
                deps=[att_done], sem="DXBF")
            return

        wo_last = None
        for dc in range(8):
            bank = dc % 2

            def womm(h, dc=dc, bank=bank):
                ins = None
                for ci in range(8):
                    ins = h.matmul(E.pb[bank][:, 0:512],
                                   wo[:, ci * 1024 + dc * 128: ci * 1024 + dc * 128 + 128],
                                   att_sb[:, ci * 512:(ci + 1) * 512],
                                   start=(ci == 0), stop=(ci == 7))
                return ins
            t = s.emit("tensor", lambda h, f=womm: f(h),
                       deps=[att_done] + E.bank.wdeps(bank))
            v = s.emit("vector", lambda h, dc=dc, bank=bank: h.scalar_tensor_tensor(
                xg32[:, dc * 512:(dc + 1) * 512], E.pb[bank][:, 0:512],
                E.bias["b0o"][:, dc:dc + 1], xg32[:, dc * 512:(dc + 1) * 512],
                ALU.add, ALU.add),
                deps=[t, ld32, E.bias_tok])
            E.bank.read(bank, v)
            wo_last = v

        if BIS == "wo":
            s.dma("sync", lambda h: h.dma_start(
                out=rearr_groups(E.ap["r1"])[:, :, 0:512],
                in_=xg32[:].rearrange("p (c t) -> p c t", t=512)),
                deps=[wo_last], sem="DST0")
            return

        def dst_fn(dc, tt, g=g):
            return rearr_groups(E.ap["r1"])[:, dc, g * 512:(g + 1) * 512]
        emit_postln(E, xg32, 512, dst_fn, [wo_last], E.eps[0:1, 0:1], rbf=att_sb, sqb=xgbf)
        xg32_guard = [s.tok("V"), s.tok("A")]
        xgbf_guard = list(E.ln_guard)
        gg = [s.tok("T")]


def emit_p2(E):
    s = E.s
    wd = E.p2_wd
    wd_toks = []
    for part in range(3):
        wd_toks.append(s.dma("gpsimd", lambda h, part=part: h.dma_start(
            out=wd[:, part * 8192:(part + 1) * 8192].rearrange("p (c f) -> p c f", f=1024),
            in_=rearr_cpf(E.ap["w0d_t"])[:, part * 8:(part + 1) * 8, :]), sem="DWD"))
    wg4, wu4 = E.p2_wg, E.p2_wu
    h_sb = E.p2_h
    x32 = E.p2_x32
    xbf = E.p2_xbf
    x32_guard = []
    xbf_guard = []
    w_guard = [[], []]
    h_guard = []
    for ei in range(8):
        ld32 = s.dma("sync", lambda h, ei=ei: h.dma_start(
            out=x32[:].rearrange("p (c t) -> p c t", t=512),
            in_=rearr_groups(E.ap["r1"])[:, :, ei * 512:(ei + 1) * 512]),
            deps=x32_guard, sem="DX32")
        ldbf = s.dma("gpsimd", lambda h, ei=ei: h.dma_start(
            out=xbf[:].rearrange("p (c t) -> p c t", t=512),
            in_=rearr_groups(E.ap["r1"])[:, :, ei * 512:(ei + 1) * 512]),
            deps=xbf_guard + [wd_toks[-1]], sem="DXBF")
        x32_guard = []
        xbf_guard = []
        for wgrp in range(6):
            slot = wgrp % 2
            ldg = s.dma("gpsimd", lambda h, wgrp=wgrp, slot=slot: h.dma_start(
                out=wg4[slot][:].rearrange("p (c f) -> p c f", f=512),
                in_=rearr_cpf(E.ap["w0g_t"])[:, :, wgrp * 512:(wgrp + 1) * 512]),
                deps=w_guard[slot], sem=f"DWG{slot}")
            ldu = s.dma("gpsimd", lambda h, wgrp=wgrp, slot=slot: h.dma_start(
                out=wu4[slot][:].rearrange("p (c f) -> p c f", f=512),
                in_=rearr_cpf(E.ap["w0u_t"])[:, :, wgrp * 512:(wgrp + 1) * 512]),
                deps=w_guard[slot], sem=f"DWU{slot}")
            w_guard[slot] = []
            for h4 in range(4):
                hc = wgrp * 4 + h4
                bkg = hc % 2
                bku = 2 + hc % 2

                def gmm(h, h4=h4, bkg=bkg, slot=slot):
                    ins = None
                    for ci in range(8):
                        ins = h.matmul(E.pb[bkg][:, 0:512],
                                       wg4[slot][:, ci * 512 + h4 * 128: ci * 512 + h4 * 128 + 128],
                                       xbf[:, ci * 512:(ci + 1) * 512],
                                       start=(ci == 0), stop=(ci == 7))
                    return ins
                tg = s.emit("tensor", lambda h, f=gmm: f(h),
                            deps=[ldbf, ldg] + E.bank.wdeps(bkg))

                def umm(h, h4=h4, bku=bku, slot=slot):
                    ins = None
                    for ci in range(8):
                        ins = h.matmul(E.pb[bku][:, 0:512],
                                       wu4[slot][:, ci * 512 + h4 * 128: ci * 512 + h4 * 128 + 128],
                                       xbf[:, ci * 512:(ci + 1) * 512],
                                       start=(ci == 0), stop=(ci == 7))
                    return ins
                tu = s.emit("tensor", lambda h, f=umm: f(h),
                            deps=[ldbf, ldu] + E.bank.wdeps(bku))
                ssl = hc % 2
                sgs = E.sg[ssl]
                ag = s.emit("scalar", lambda h, bkg=bkg, sgs=sgs: h.activation(
                    out=sgs[:], in_=E.pb[bkg][:, 0:512], func=AF.Silu, scale=INV127),
                    deps=[tg] + E.sg_guard[ssl])
                E.bank.read(bkg, ag)
                vh = s.emit("vector", lambda h, hc=hc, bku=bku, sgs=sgs: h.tensor_mul(
                    h_sb[:, hc * 512:(hc + 1) * 512], E.pb[bku][:, 0:512], sgs[:]),
                    deps=[tu, ag] + h_guard)
                h_guard = []
                E.bank.read(bku, vh)
                E.sg_guard[ssl] = [vh]
            w_guard[slot] = [s.tok("T")]
        h_done = s.tok("V")
        dn_last = None
        for dc in range(8):
            bank = 4 + dc % 2

            def dmm(h, dc=dc, bank=bank):
                ins = None
                for hc in range(24):
                    ins = h.matmul(E.pb[bank][:, 0:512],
                                   wd[:, hc * 1024 + dc * 128: hc * 1024 + dc * 128 + 128],
                                   h_sb[:, hc * 512:(hc + 1) * 512],
                                   start=(hc == 0), stop=(hc == 23))
                return ins
            t = s.emit("tensor", lambda h, f=dmm: f(h),
                       deps=[h_done, wd_toks[-1]] + E.bank.wdeps(bank))
            v = s.emit("vector", lambda h, dc=dc, bank=bank: h.scalar_tensor_tensor(
                x32[:, dc * 512:(dc + 1) * 512], E.pb[bank][:, 0:512], INV127SQ,
                x32[:, dc * 512:(dc + 1) * 512], ALU.mult, ALU.add),
                deps=[t, ld32])
            E.bank.read(bank, v)
            dn_last = v
        h_guard = [s.tok("T")]

        def dst_fn(dc, tt, ei=ei):
            return rearr_groups(E.ap["r2"])[:, dc, ei * 512:(ei + 1) * 512]
        emit_postln(E, x32, 512, dst_fn, [dn_last], E.eps[0:1, 0:1],
                    rbf=h_sb, sqb=xbf)
        h_guard = h_guard + E.ln_guard
        x32_guard = [s.tok("V"), s.tok("A")]
        xbf_guard = list(E.ln_guard)


def emit_p3(E, pc):
    s = E.s
    nc = E.nc
    kfm = E.p3_k
    vext = E.p3_vext
    wbuf = E.p3_wbuf
    actx = ExitStack()
    wvb = actx.enter_context(nc.sbuf_tensor("p3_wv", [128, 8 * 1024], BF16))
    E.p3_bvrow = actx.enter_context(nc.sbuf_tensor("p3_bvrow", [1, D], F32))
    bvbc = actx.enter_context(nc.sbuf_tensor("p3_bvbc", [128, 1024], F32))
    xbf1 = actx.enter_context(nc.sbuf_tensor("p3_xbf1", [128, 8 * 512], BF16))
    ms = s.emit("vector", lambda h: h.memset(vext[:], 1.0))
    wk_t = s.dma("gpsimd", lambda h: h.dma_start(
        out=wbuf[:].rearrange("p (c f) -> p c f", f=1024), in_=rearr_cpf(E.ap["w1k_t"])),
        sem="DW3")
    wv_t = s.dma("gpsimd", lambda h: h.dma_start(
        out=wvb[:].rearrange("p (c f) -> p c f", f=1024), in_=rearr_cpf(E.ap["w1v_t"])),
        sem="DWV")
    bvr = s.dma("sync", lambda h: h.dma_start(out=E.p3_bvrow[:], in_=E.ap["b1v_row"]), sem="DC")
    bv_toks = []
    for half in range(2):
        t = s.emit("tensor", lambda h, half=half: h.matmul(
            E.pb[2][:, 0:512], E.ones_f[0:1, 0:128],
            E.p3_bvrow[0:1, half * 512:half * 512 + 512], start=True, stop=True),
            deps=[bvr] + E.bank.wdeps(2))
        a = s.emit("scalar", lambda h, half=half: h.copy(bvbc[:, half * 512:half * 512 + 512],
                                                         E.pb[2][:, 0:512]), deps=[t])
        E.bank.read(2, a)
        bv_toks.append(a)

    xbf = [E.p3_xbf0, xbf1]
    xbf_guard = [[], []]
    for tt in range(8):
        slot = tt % 2
        ldbf = s.dma("gpsimd", lambda h, tt=tt, slot=slot: h.dma_start(
            out=xbf[slot][:].rearrange("p (c t) -> p c t", t=512),
            in_=rearr_groups(E.ap["r2"])[:, :, tt * 512:(tt + 1) * 512]),
            deps=xbf_guard[slot] + [wv_t], sem=f"DXB{slot}")
        xbf_guard[slot] = []
        for dc in range(8):
            bank = dc % 2

            def kmm(h, dc=dc, bank=bank, slot=slot):
                ins = None
                for ci in range(8):
                    ins = h.matmul(E.pb[bank][:, 0:512],
                                   wbuf[:, ci * 1024 + dc * 128: ci * 1024 + dc * 128 + 128],
                                   xbf[slot][:, ci * 512:(ci + 1) * 512],
                                   start=(ci == 0), stop=(ci == 7))
                return ins
            t = s.emit("tensor", lambda h, f=kmm: f(h), deps=[ldbf, wk_t] + E.bank.wdeps(bank))
            v = s.emit("vector", lambda h, tt=tt, dc=dc, bank=bank: h.tensor_scalar(
                kfm[:, dc * 4096 + tt * 512: dc * 4096 + tt * 512 + 512],
                E.pb[bank][:, 0:512], E.bias["b1k"][:, dc:dc + 1], None, ALU.add),
                deps=[t, E.bias_tok])
            E.bank.read(bank, v)
        for tc in range(4):
            for half in range(2):
                bank = 2 + (tc * 2 + half) % 2

                def vmm(h, tc=tc, half=half, bank=bank, slot=slot):
                    ins = None
                    for ci in range(8):
                        ins = h.matmul(E.pb[bank][:, 0:512],
                                       xbf[slot][:, ci * 512 + tc * 128: ci * 512 + tc * 128 + 128],
                                       wvb[:, ci * 1024 + half * 512: ci * 1024 + half * 512 + 512],
                                       start=(ci == 0), stop=(ci == 7))
                    return ins
                t = s.emit("tensor", lambda h, f=vmm: f(h), deps=[ldbf, wv_t] + E.bank.wdeps(bank))
                base = (tt * 4 + tc) * 1040 + half * 520

                def vev(h, bank=bank, base=base, half=half):
                    dst = vext[:, base:base + 520].rearrange("p (hh e) -> p hh e", e=65)[:, :, 0:64]
                    src = E.pb[bank][:, 0:512].rearrange("p (hh e) -> p hh e", e=64)
                    bcv = bvbc[:, half * 512:half * 512 + 512].rearrange("p (hh e) -> p hh e", e=64)
                    return h.tensor_add(dst, src, bcv)
                v = s.emit("vector", lambda h, f=vev: f(h), deps=[t, ms] + bv_toks)
                E.bank.read(bank, v)
        xbf_guard[slot] = [s.tok("T")]
    a_done_v = s.tok("V")
    actx.close()
    a_fin = [s.tok("T"), s.tok("V"), s.tok("A")]

    p = lambda *a: pc.enter_context(nc.sbuf_tensor(*a))
    q_sb = p("p3_q", [128, 8 * 512], BF16)
    att_sb = p("p3_att", [128, 8 * 512], BF16)
    r32 = p("p3_r32", [128, 8 * 512], F32)
    E.p3_expst = [p(f"p3_expst{i}", [128, 1024], BF16) for i in range(2)]
    E.p3_rec = p("p3_rec", [128, 512], F32)
    E.p3_bcsb = [p("p3_bcsb0", [64, 512], F32)]
    E.p3_bcsb = E.p3_bcsb * 2
    E.p3_tmp = [p(f"p3_tmp{i}", [64, 512], BF16) for i in range(2)]
    wload_guard = [s.tok("T")]
    E.p3_expg = [list(a_fin) for _ in range(2)]
    q_guard = list(a_fin)
    att_guard = list(a_fin)
    r32_guard = list(a_fin)
    expst_g = [list(a_fin), list(a_fin)]
    rec_g = list(a_fin)
    bcsb_g = [list(a_fin), list(a_fin)]
    tmp_g = [list(a_fin), list(a_fin)]
    for qt in range(8):
        wq_t = s.dma("gpsimd", lambda h: h.dma_start(
            out=wbuf[:].rearrange("p (c f) -> p c f", f=1024), in_=rearr_cpf(E.ap["w1q_t"])),
            deps=wload_guard, sem="DW3")
        ldr = s.dma("sync", lambda h, qt=qt: h.dma_start(
            out=r32[:].rearrange("p (c t) -> p c t", t=512),
            in_=rearr_groups(E.ap["r2"])[:, :, qt * 512:(qt + 1) * 512]),
            deps=r32_guard, sem="DR32")
        ldq = s.dma("gpsimd", lambda h, qt=qt: h.dma_start(
            out=xbf[0][:].rearrange("p (c t) -> p c t", t=512),
            in_=rearr_groups(E.ap["r2"])[:, :, qt * 512:(qt + 1) * 512]),
            deps=xbf_guard[0], sem="DXB0")
        r32_guard = []
        xbf_guard[0] = []
        q_last = None
        for dc in range(8):
            bank = dc % 2

            def qmm(h, dc=dc, bank=bank):
                ins = None
                for ci in range(8):
                    ins = h.matmul(E.pb[bank][:, 0:512],
                                   wbuf[:, ci * 1024 + dc * 128: ci * 1024 + dc * 128 + 128],
                                   xbf[0][:, ci * 512:(ci + 1) * 512],
                                   start=(ci == 0), stop=(ci == 7))
                return ins
            t = s.emit("tensor", lambda h, f=qmm: f(h),
                       deps=[ldq, wq_t, a_done_v] + E.bank.wdeps(bank))
            v = s.emit("vector", lambda h, dc=dc, bank=bank: h.tensor_scalar(
                q_sb[:, dc * 512:(dc + 1) * 512], E.pb[bank][:, 0:512],
                E.bias["b1q"][:, dc:dc + 1], None, ALU.add),
                deps=[t, E.bias_tok] + q_guard)
            q_guard = []
            E.bank.read(bank, v)
            q_last = v
        qmm_done = s.tok("T")
        wo_t = s.dma("gpsimd", lambda h: h.dma_start(
            out=wbuf[:].rearrange("p (c f) -> p c f", f=1024), in_=rearr_cpf(E.ap["w1o_t"])),
            deps=[qmm_done], sem="DW3")

        prev_bc = None
        for hp in range(8):
            pvA = E.pb[4]
            pvB = E.pb[5]
            pvA_w = E.bank.wdeps(4)
            pvB_w = E.bank.wdeps(5)
            pv_tok = {"A": None, "B": None}
            e_tok = {"A": None, "B": None}
            for ktc in range(33):
                par = ktc % 2
                if ktc < 32:
                    bA = 0 + 2 * par
                    bB = 1 + 2 * par

                    def stm(h, hlo, ktc=ktc, hp=hp, bank=None):
                        return h.matmul(E.pb[bank][:, 0:512],
                                        kfm[hlo:hlo + 64, hp * 4096 + ktc * 128: hp * 4096 + ktc * 128 + 128],
                                        q_sb[hlo:hlo + 64, hp * 512:(hp + 1) * 512],
                                        start=True, stop=True)
                    tA = s.emit("tensor", lambda h, f=stm, bA=bA: f(h, 0, bank=bA),
                                deps=[q_last, a_done_v] + E.bank.wdeps(bA)
                                     + ([prev_bc] if (ktc == 0 and prev_bc) else []))
                if ktc > 0:
                    kprev = ktc - 1

                    def pvm(h, hh, ktc_, side, dst):
                        par_ = ktc_ % 2
                        return h.matmul(dst[0:65, 0:512],
                                        vext[:, ktc_ * 1040 + hh * 65: ktc_ * 1040 + hh * 65 + 65],
                                        E.p3_expst[par_][:, side * 512: side * 512 + 512],
                                        start=(ktc_ == 0), stop=(ktc_ == 31))
                    pv_tok["A"] = s.emit("tensor",
                                         lambda h, f=pvm, kprev=kprev, hp=hp: f(
                                             h, 2 * hp, kprev, 0, pvA),
                                         deps=[e_tok["A"]] + (pvA_w if kprev == 0 else []))
                if ktc < 32:
                    tB = s.emit("tensor", lambda h, f=stm, bB=bB: f(h, 64, bank=bB),
                                deps=E.bank.wdeps(bB))
                if ktc > 0:
                    pv_tok["B"] = s.emit("tensor",
                                         lambda h, f=pvm, kprev=kprev, hp=hp: f(
                                             h, 2 * hp + 1, kprev, 1, pvB),
                                         deps=[e_tok["B"]] + (pvB_w if kprev == 0 else []))
                    E.p3_expg[kprev % 2] = [pv_tok["B"]]
                if ktc < 32:
                    ee = s.emit("scalar", lambda h, par=par: h.activation(
                        out=E.p3_expst[par][:], in_=E.pbig[:, 1024 * par: 1024 * par + 1024],
                        func=AF.Exp, scale=SC1),
                        deps=[tA, tB] + E.p3_expg[par])
                    E.bank.read(bA, ee)
                    E.bank.read(bB, ee)
                    e_tok["A"] = e_tok["B"] = ee
            for (side, hh, pvb, pv_ps) in (("A", 2 * hp, 4, pvA), ("B", 2 * hp + 1, 5, pvB)):
                rec = E.p3_rec
                r1 = s.emit("vector", lambda h, pv_ps=pv_ps: h.reciprocal(
                    rec[64:65, 0:512], pv_ps[64:65, 0:512]), deps=[pv_tok[side]] + rec_g)
                t3 = s.emit("tensor", lambda h: h.matmul(
                    E.pb[6][0:64, 0:512], E.ones_f[64:65, 0:64], rec[64:65, 0:512],
                    start=True, stop=True), deps=[r1] + E.bank.wdeps(6))
                rec_g = [t3]
                prev_bc = t3
                a3 = s.emit("scalar", lambda h: h.copy(
                    E.p3_bcsb[0][0:64, :], E.pb[6][0:64, 0:512]),
                    deps=[t3] + bcsb_g[0] + bcsb_g[1])
                E.bank.read(6, a3)
                if side == "A":
                    v3 = s.emit("vector", lambda h, hp=hp, pv_ps=pv_ps: h.tensor_mul(
                        att_sb[0:64, hp * 512:(hp + 1) * 512], pv_ps[0:64, 0:512],
                        E.p3_bcsb[0][0:64, :]),
                        deps=[a3, pv_tok[side]] + att_guard)
                    E.bank.read(pvb, v3)
                    bcsb_g = [[v3], []]
                else:
                    tsl = hp % 2
                    v3 = s.emit("vector", lambda h, pv_ps=pv_ps, tsl=tsl: h.tensor_mul(
                        E.p3_tmp[tsl][0:64, :], pv_ps[0:64, 0:512], E.p3_bcsb[0][0:64, :]),
                        deps=[a3, pv_tok[side]] + tmp_g[tsl])
                    E.bank.read(pvb, v3)
                    bcsb_g = [[v3], []]
                    d3 = s.dma("sync", lambda h, hp=hp, tsl=tsl: h.dma_start(
                        out=att_sb[64:128, hp * 512:(hp + 1) * 512], in_=E.p3_tmp[tsl][0:64, :]),
                        deps=[v3] + att_guard, sem=f"DTMP{tsl}")
                    tmp_g[tsl] = [d3]
        att_done = [s.tok("V"), s.tok("DTMP0"), s.tok("DTMP1")]

        wo_last = None
        for dc in range(8):
            bank = dc % 2

            def womm(h, dc=dc, bank=bank):
                ins = None
                for ci in range(8):
                    ins = h.matmul(E.pb[bank][:, 0:512],
                                   wbuf[:, ci * 1024 + dc * 128: ci * 1024 + dc * 128 + 128],
                                   att_sb[:, ci * 512:(ci + 1) * 512],
                                   start=(ci == 0), stop=(ci == 7))
                return ins
            t = s.emit("tensor", lambda h, f=womm: f(h),
                       deps=att_done + [wo_t] + E.bank.wdeps(bank))
            v = s.emit("vector", lambda h, dc=dc, bank=bank: h.scalar_tensor_tensor(
                r32[:, dc * 512:(dc + 1) * 512], E.pb[bank][:, 0:512],
                E.bias["b1o"][:, dc:dc + 1], r32[:, dc * 512:(dc + 1) * 512],
                ALU.add, ALU.add),
                deps=[t, ldr, E.bias_tok])
            E.bank.read(bank, v)
            wo_last = v
        wload_guard = [s.tok("T")]

        def dst_fn(dc, tt, qt=qt):
            return rearr_groups(E.ap["r3"])[:, dc, qt * 512:(qt + 1) * 512]
        emit_postln(E, r32, 512, dst_fn, [wo_last], E.eps[0:1, 0:1],
                    rbf=att_sb, sqb=xbf[0])
        att_guard = list(E.ln_guard)
        q_guard = [s.tok("T")]
        xbf_guard[0] = list(E.ln_guard)
        r32_guard = [s.tok("V"), s.tok("A")]


def emit_p4(E, out_name):
    s = E.s
    r3q = E.p4_r3
    xn = E.p4_xn
    y_acc = E.p4_yacc
    h_sb = E.p4_h
    mfm = E.p4_mfm
    mbc = E.p4_mbc
    w1b = E.p4_w1
    w2b = E.p4_w2
    gw_tok = s.dma("gpsimd", lambda h: h.dma_start(
        out=E.gw_sb[:].rearrange("p (c e) -> p c e", e=8), in_=rearr_cpf(E.ap["gw_t"])),
        sem="DGW")
    s.dma("sync", lambda h: h.dma_start(out=E.b2_sb[:], in_=E.ap["b2"]), sem="DC")
    s.dma("sync", lambda h: h.dma_start(out=E.sel_f[:], in_=E.c_sel), sem="DC")
    s.dma("sync", lambda h: h.dma_start(out=E.b1f1[:], in_=E.ap["b1f1"]), sem="DC")
    b2_tok = sel_tok = b1f_tok = s.tok("DC")

    r3_guard = []
    xn_guard = []
    yacc_guard = []
    h_guard = []
    mfm_guard = []
    mbc_guard = []
    w_guard = [[], []]
    lg_guard = []
    for qtr in range(4):
        ld32 = s.dma("sync", lambda h, qtr=qtr: h.dma_start(
            out=r3q[:].rearrange("p (c t) -> p c t", t=1024),
            in_=rearr_groups(E.ap["r3"])[:, :, qtr * 1024:(qtr + 1) * 1024]),
            deps=r3_guard, sem="DR32")
        r3_guard = []
        asq = s.emit("scalar", lambda h: h.activation(out=xn[:], in_=r3q[:], func=AF.Square),
                     deps=[ld32] + xn_guard)
        xn_guard = []
        stat_toks = []
        bc_toks = []
        for tt in range(2):
            def mms(h, tt=tt):
                ins = None
                for dc in range(8):
                    ins = h.matmul(E.pb[2][0:1, 0:512], E.invD_bf[:, 0:1],
                                   xn[:, dc * 1024 + tt * 512: dc * 1024 + tt * 512 + 512],
                                   start=(dc == 0), stop=(dc == 7))
                return ins
            tm = s.emit("tensor", lambda h, f=mms: f(h), deps=[asq] + E.bank.wdeps(2))
            vrow = E.ln_rows[0:1, 1024:1536]
            rstd = vrow
            a1 = s.emit("scalar", lambda h: h.activation(
                out=vrow, in_=E.pb[2][0:1, 0:512], func=AF.Ln, bias=E.eps[0:1, 1:2]),
                deps=[tm] + E.rows_guard)
            E.rows_guard = []
            E.bank.read(2, a1)
            a2 = s.emit("scalar", lambda h: h.activation(
                out=rstd, in_=vrow, func=AF.Exp, scale=-0.5), deps=[a1])
            t_b = s.emit("tensor", lambda h, tt=tt: h.matmul(
                E.pb[6 + tt][:, 0:512], E.ones_f[0:1, 0:128], rstd, start=True, stop=True),
                deps=[a2] + E.bank.wdeps(6 + tt))
            E.rows_guard = [t_b]
            stat_toks.append(tm)
            bc_toks.append(t_b)
        xn_last = None
        for tt in range(2):
            for dc in range(8):
                v = s.emit("vector", lambda h, dc=dc, tt=tt: h.scalar_tensor_tensor(
                    xn[:, dc * 1024 + tt * 512: dc * 1024 + tt * 512 + 512],
                    r3q[:, dc * 1024 + tt * 512: dc * 1024 + tt * 512 + 512],
                    E.bias["rmsw"][:, dc:dc + 1], E.pb[6 + tt][:, 0:512],
                    ALU.mult, ALU.mult),
                    deps=[bc_toks[tt], stat_toks[0], stat_toks[1], ld32, E.bias_tok])
                E.bank.read(6 + tt, v)
                xn_last = v
        xn_done = xn_last

        prev_a1 = None
        prev_t2 = None
        mask_last = None
        for tcl in range(8):
            def lmm(h, tcl=tcl):
                ins = None
                for ci in range(8):
                    ins = h.matmul(E.pb[2][0:128, 0:8],
                                   xn[:, ci * 1024 + tcl * 128: ci * 1024 + tcl * 128 + 128],
                                   E.gw_sb[:, ci * 8:(ci + 1) * 8],
                                   start=(ci == 0), stop=(ci == 7))
                return ins
            t1 = s.emit("tensor", lambda h, f=lmm: f(h),
                        deps=[xn_done, gw_tok] + E.bank.wdeps(2))
            G = E.p4_g
            v1 = s.emit("vector", lambda h: h.tensor_copy(G[:, 0:8], E.pb[2][0:128, 0:8]),
                        deps=[t1] + lg_guard + ([prev_a1] if prev_a1 else []))
            lg_guard = []
            E.bank.read(2, v1)
            v2 = s.emit("vector", lambda h: h.reduce_max(G[:, 8:9], G[:, 0:8], axis=AX),
                        deps=[v1])
            v3 = s.emit("vector", lambda h: h.tensor_scalar_mul(G[:, 9:10], G[:, 8:9], -1.0),
                        deps=[v2])
            a1 = s.emit("scalar", lambda h: h.activation(out=G[:, 10:18], in_=G[:, 0:8],
                                                         func=AF.Exp, bias=G[:, 9:10]),
                        deps=[v3])
            v4 = s.emit("vector", lambda h: h.reduce_sum(G[:, 18:19], G[:, 10:18], axis=AX),
                        deps=[a1])
            v5 = s.emit("vector", lambda h: h.reciprocal(G[:, 19:20], G[:, 18:19]), deps=[v4])
            v6 = s.emit("vector", lambda h: h.tensor_scalar(G[:, 20:28], G[:, 0:8], G[:, 8:9],
                                                            None, ALU.is_equal),
                        deps=[v5] + ([prev_t2] if prev_t2 else []))
            v7 = s.emit("vector", lambda h: h.tensor_scalar(G[:, 20:28], G[:, 20:28],
                                                            G[:, 19:20], None, ALU.mult),
                        deps=[v6])
            t2 = s.emit("tensor", lambda h: h.transpose(E.pb[3][0:8, 0:128], G[:, 20:28],
                                                        E.ident_f[:]),
                        deps=[v7, E.ident_tok] + E.bank.wdeps(3))
            v8 = s.emit("vector", lambda h, tcl=tcl: h.tensor_copy(
                mfm[0:8, tcl * 128:(tcl + 1) * 128], E.pb[3][0:8, 0:128]),
                deps=[t2] + mfm_guard)
            mfm_guard = []
            E.bank.read(3, v8)
            prev_a1 = a1
            prev_t2 = t2
            mask_last = v8
        lg_guard = [s.tok("T")]
        mask_done = mask_last

        y_last = None
        for tt in range(2):
            for dc in range(8):
                bank = (tt * 8 + dc) % 2
                t = s.emit("tensor", lambda h, tt=tt, dc=dc, bank=bank: h.matmul(
                    E.pb[bank][:, 0:512], E.b2_sb[0:8, dc * 128:(dc + 1) * 128],
                    mfm[0:8, tt * 512:(tt + 1) * 512], start=True, stop=True),
                    deps=[mask_done, b2_tok] + E.bank.wdeps(bank))
                v = s.emit("vector", lambda h, tt=tt, dc=dc, bank=bank: h.tensor_copy(
                    y_acc[:, dc * 1024 + tt * 512: dc * 1024 + tt * 512 + 512],
                    E.pb[bank][:, 0:512]),
                    deps=[t] + yacc_guard)
                yacc_guard = []
                E.bank.read(bank, v)
                y_last = v

        for e in range(NE):
            mb_last = None
            for tt in range(2):
                bank = 6 + tt % 2
                t = s.emit("tensor", lambda h, e=e, tt=tt, bank=bank: h.matmul(
                    E.pb[bank][:, 0:512], E.sel_f[0:8, e * 128:(e + 1) * 128],
                    mfm[0:8, tt * 512:(tt + 1) * 512], start=True, stop=True),
                    deps=[mask_done, sel_tok] + E.bank.wdeps(bank))
                a = s.emit("scalar", lambda h, tt=tt, bank=bank: h.copy(
                    mbc[:, tt * 512:(tt + 1) * 512], E.pb[bank][:, 0:512]),
                    deps=[t] + mbc_guard)
                mbc_guard = []
                E.bank.read(bank, a)
                mb_last = a
            for hcg in range(4):
                slot = hcg % 2
                ld1 = s.dma("gpsimd", lambda h, e=e, hcg=hcg, slot=slot: h.dma_start(
                    out=w1b[slot][:].rearrange("p (c f) -> p c f", f=1024),
                    in_=rearr_cpf(E.ap["w1ff1"][e])[:, :, hcg * 1024:(hcg + 1) * 1024]),
                    deps=w_guard[slot], sem=f"D41{slot}")
                ld2 = s.dma("gpsimd", lambda h, e=e, hcg=hcg, slot=slot: h.dma_start(
                    out=w2b[slot][:].rearrange("p (c f) -> p c f", f=1024),
                    in_=rearr_cpf(E.ap["w1ff2"][e])[:, hcg * 8:(hcg + 1) * 8, :]),
                    deps=w_guard[slot], sem=f"D42{slot}")
                w_guard[slot] = []
                h_last = None
                for hcl in range(8):
                    for tt in range(2):
                        bank = (hcl * 2 + tt) % 2

                        def gmm(h, hcl=hcl, tt=tt, bank=bank, slot=slot):
                            ins = None
                            for ci in range(8):
                                ins = h.matmul(
                                    E.pb[bank][:, 0:512],
                                    w1b[slot][:, ci * 1024 + hcl * 128: ci * 1024 + hcl * 128 + 128],
                                    xn[:, ci * 1024 + tt * 512: ci * 1024 + tt * 512 + 512],
                                    start=(ci == 0), stop=(ci == 7))
                            return ins
                        tg = s.emit("tensor", lambda h, f=gmm: f(h),
                                    deps=[ld1, xn_done] + E.bank.wdeps(bank))
                        ssl = (hcl * 2 + tt) % 2
                        sgs = E.sg[ssl]
                        bcol = E.b1f1[:, e * 32 + hcg * 8 + hcl: e * 32 + hcg * 8 + hcl + 1]
                        ag = s.emit("scalar", lambda h, bank=bank, sgs=sgs, bcol=bcol:
                                    h.activation(out=sgs[:], in_=E.pb[bank][:, 0:512],
                                                 func=AF.Silu, bias=bcol),
                                    deps=[tg, b1f_tok] + E.sg_guard[ssl])
                        E.bank.read(bank, ag)
                        vh = s.emit("vector", lambda h, hcl=hcl, tt=tt, sgs=sgs: h.tensor_mul(
                            h_sb[:, hcl * 1024 + tt * 512: hcl * 1024 + tt * 512 + 512],
                            sgs[:], mbc[:, tt * 512:(tt + 1) * 512]),
                            deps=[ag, mb_last] + h_guard)
                        h_guard = []
                        E.sg_guard[ssl] = [vh]
                        h_last = vh
                for tt in range(2):
                    for dc in range(8):
                        bank = 2 + (tt * 8 + dc) % 2

                        def ymm(h, tt=tt, dc=dc, bank=bank, slot=slot):
                            ins = None
                            for hcl in range(8):
                                ins = h.matmul(
                                    E.pb[bank][:, 0:512],
                                    w2b[slot][:, hcl * 1024 + dc * 128: hcl * 1024 + dc * 128 + 128],
                                    h_sb[:, hcl * 1024 + tt * 512: hcl * 1024 + tt * 512 + 512],
                                    start=(hcl == 0), stop=(hcl == 7))
                            return ins
                        t = s.emit("tensor", lambda h, f=ymm: f(h),
                                   deps=[ld2, h_last] + E.bank.wdeps(bank))
                        v = s.emit("vector", lambda h, tt=tt, dc=dc, bank=bank: h.tensor_add(
                            y_acc[:, dc * 1024 + tt * 512: dc * 1024 + tt * 512 + 512],
                            y_acc[:, dc * 1024 + tt * 512: dc * 1024 + tt * 512 + 512],
                            E.pb[bank][:, 0:512]),
                            deps=[t, y_last])
                        E.bank.read(bank, v)
                        y_last = v
                h_guard = [s.tok("T")]
                w_guard[slot] = [s.tok("T")]
            mbc_guard = [s.tok("V")]

        radd_last = None
        for i in range(2):
            radd_last = s.emit("vector", lambda h, i=i: h.tensor_add(
                y_acc[:, i * 4096:(i + 1) * 4096], y_acc[:, i * 4096:(i + 1) * 4096],
                r3q[:, i * 4096:(i + 1) * 4096]), deps=[y_last, ld32])

        def dst_fn(dc, tt, qtr=qtr):
            return rearr_groups(E.ap[out_name])[:, dc, qtr * 1024 + tt * 512: qtr * 1024 + tt * 512 + 512]
        emit_postln(E, y_acc, 1024, dst_fn, [radd_last], E.eps[0:1, 0:1],
                    rbf=h_sb, sqb=xn)
        vtok, atok = s.tok("V"), s.tok("A")
        r3_guard = [vtok, atok]
        yacc_guard = [vtok, atok]
        xn_guard = list(E.ln_guard)
        h_guard = list(E.ln_guard)
        mfm_guard = [s.tok("T")]
        mbc_guard = [s.tok("V")]


def build_nc(phases=("p1", "p2", "p3", "p4")):
    nc = bass.Bass("TRN2", target_bir_lowering=False, debug=False)
    E = Env()
    E.nc = nc
    s = Sched()
    E.s = s
    E.bank = BankGuard()

    ap = {}
    ap["x0"] = nc.dram_tensor("x0", [128, 8 * S], F32, kind="ExternalInput").ap()
    chain = {"p1": ("x_in", "r1"), "p2": ("r1", "r2"), "p3": ("r2", "r3"), "p4": ("r3", "y")}
    needed = set()
    for ph in phases:
        needed.add(chain[ph][0])
        needed.add(chain[ph][1])
    needed.discard("x_in")
    out_hbm = chain[phases[-1]][1]
    in_hbm = chain[phases[0]][0]
    for name in sorted(needed):
        if name == out_hbm:
            ap[name] = nc.dram_tensor(name, [128, 8 * S], F32, kind="ExternalOutput").ap()
        else:
            ap[name] = nc.dram_tensor(name, [128, 8 * S], F32, kind="Internal").ap()
    if in_hbm != "x_in":
        ap[in_hbm] = ap["x0"]
    else:
        ap["x_in"] = ap["x0"]

    def din(name, shape, dt):
        ap[name] = nc.dram_tensor(name, shape, dt, kind="ExternalInput").ap()
    din("w0q", [D, D], F32); din("w0k", [D, D], F32); din("w0v", [D, D], F32); din("w0o", [D, D], F32)
    din("b0q", [128, 8], F32); din("b0k", [128, 8], F32); din("b0o", [128, 8], F32)
    din("b0v_row", [1, D], F32)
    din("w0g_t", [D, FF0], I32); din("w0u_t", [D, FF0], I32); din("w0d_t", [FF0, D], I32)
    din("w1q_t", [D, D], F32); din("w1k_t", [D, D], F32); din("w1v_t", [D, D], F32)
    din("w1o_t", [D, D], F32)
    din("b1q", [128, 8], F32); din("b1k", [128, 8], F32); din("b1o", [128, 8], F32)
    din("b1v_row", [1, D], F32)
    din("rmsw", [128, 8], F32)
    din("gw_t", [D, NE], F32)
    din("b1f1", [128, 256], F32)
    din("b2", [NE, D], F32)
    ap["w1ff1"] = [nc.dram_tensor(f"w1ff1_{e}", [D, FF1], F32, kind="ExternalInput").ap()
                   for e in range(NE)]
    ap["w1ff2"] = [nc.dram_tensor(f"w1ff2_{e}", [FF1, D], F32, kind="ExternalInput").ap()
                   for e in range(NE)]
    c_ones_bf = nc.inline_tensor(np.ones((128, 128), dtype=BF16NP), "c_ones_bf").ap()
    c_ones_f = nc.inline_tensor(np.ones((128, 128), dtype=np.float32), "c_ones_f").ap()
    c_neg_f = nc.inline_tensor(np.full((1, 128), -1.0, dtype=np.float32), "c_neg_f").ap()
    c_invd = nc.inline_tensor(np.full((128, 1), 1.0 / 1024.0, dtype=BF16NP), "c_invd").ap()
    c_ident = nc.inline_tensor(np.eye(128, dtype=np.float32), "c_ident").ap()
    selm = np.zeros((8, 8 * 128), dtype=np.float32)
    for e in range(8):
        selm[e, e * 128:(e + 1) * 128] = 1.0
    E.c_sel = nc.inline_tensor(selm, "c_sel").ap()
    c_eps = nc.inline_tensor(np.array([[EPS_LN, EPS_RMS]], dtype=np.float32), "c_eps").ap()
    E.ap = ap

    with ExitStack() as ctx:
        sb = lambda *a: ctx.enter_context(nc.sbuf_tensor(*a))
        psb = lambda *a: ctx.enter_context(nc.psum_tensor(*a))
        E.pbig = psb("pbig", [128, 4096], F32)
        E.pb = [E.pbig[:, i * 512:(i + 1) * 512] for i in range(8)]
        E.ones_bf = sb("ones_bf", [128, 128], BF16)
        E.ones_f = sb("ones_f", [128, 128], F32)
        E.neg_f = sb("neg_f", [1, 128], F32)
        E.invD_bf = sb("invD_bf", [128, 1], BF16)
        E.ident_f = sb("ident_f", [128, 128], F32)
        E.eps = sb("eps", [1, 2], F32)
        E.bias = {}
        for name in ("b0q", "b0k", "b0o", "b1q", "b1k", "b1o", "rmsw"):
            E.bias[name] = sb("t_" + name, [128, 8], F32)
        E.ln_rows = sb("ln_rows", [1, 1536], F32)
        E.ln_stg = [sb(f"ln_stg{i}", [128, 512], F32) for i in range(2)]
        E.stg_guard = [[], []]
        E.ln_guard = []
        E.rows_guard = []

        s.dma("sync", lambda h: h.dma_start(out=E.ones_bf[:], in_=c_ones_bf), sem="DC")
        s.dma("sync", lambda h: h.dma_start(out=E.ones_f[:], in_=c_ones_f), sem="DC")
        s.dma("sync", lambda h: h.dma_start(out=E.neg_f[:], in_=c_neg_f), sem="DC")
        s.dma("sync", lambda h: h.dma_start(out=E.invD_bf[:], in_=c_invd), sem="DC")
        E.ident_tok = s.dma("sync", lambda h: h.dma_start(out=E.ident_f[:], in_=c_ident), sem="DC")
        s.dma("sync", lambda h: h.dma_start(out=E.eps[:], in_=c_eps), sem="DC")
        for name in ("b0q", "b0k", "b0o", "b1q", "b1k", "b1o", "rmsw"):
            s.dma("sync", lambda h, name=name: h.dma_start(out=E.bias[name][:], in_=ap[name]),
                  sem="DC")
        E.bias_tok = s.tok("DC")
        s.barrier()

        if "p1" in phases:
            with ExitStack() as pc:
                p = lambda *a: pc.enter_context(nc.sbuf_tensor(*a))
                E.p1_w = [p(f"p1w{i}", [128, 8 * 1024], BF16) for i in range(4)]
                E.p1_bvrow = p("p1_bvrow", [1, D], F32)
                E.p1_bvbc = p("p1_bvbc", [128, 1024], F32)
                E.p1_xg32 = p("p1_xg32", [128, 8 * 512], F32)
                E.p1_xgbf = p("p1_xgbf", [128, 8 * 512], BF16)
                E.p1_q = p("p1_q", [128, 8 * 512], BF16)
                E.p1_k = p("p1_k", [128, 8 * 512], BF16)
                E.p1_v = p("p1_v", [128, 4 * 1024], BF16)
                E.p1_att = p("p1_att", [128, 8 * 512], BF16)
                E.p1_expst = [[p(f"p1_expst{i}_{b}", [128, 64], BF16) for b in range(2)]
                              for i in range(2)]
                E.p1_recip = [p(f"p1_recip{i}", [1, 128], F32) for i in range(2)]
                E.p1_bcsb = [p(f"p1_bcsb{i}", [128, 128], F32) for i in range(2)]
                emit_p1(E)
            s.barrier()
        if "p2" in phases:
            with ExitStack() as pc:
                p = lambda *a: pc.enter_context(nc.sbuf_tensor(*a))
                E.p2_wd = p("p2_wd", [128, 24 * 1024], BF16)
                E.p2_wg = [p(f"p2_wg{i}", [128, 8 * 512], BF16) for i in range(2)]
                E.p2_wu = [p(f"p2_wu{i}", [128, 8 * 512], BF16) for i in range(2)]
                E.p2_h = p("p2_h", [128, 24 * 512], BF16)
                E.p2_x32 = p("p2_x32", [128, 8 * 512], F32)
                E.p2_xbf = p("p2_xbf", [128, 8 * 512], BF16)
                E.sg = [p(f"p2sg{i}", [128, 512], F32) for i in range(2)]
                E.sg_guard = [[], []]
                emit_p2(E)
            s.barrier()
        if "p3" in phases:
            with ExitStack() as pc:
                p = lambda *a: pc.enter_context(nc.sbuf_tensor(*a))
                E.p3_k = p("p3_k", [128, 8 * 4096], BF16)
                E.p3_vext = p("p3_vext", [128, 32 * 1040], BF16)
                E.p3_wbuf = p("p3_wbuf", [128, 8 * 1024], BF16)
                E.p3_xbf0 = p("p3_xbf0", [128, 8 * 512], BF16)
                emit_p3(E, pc)
            s.barrier()
        if "p4" in phases:
            with ExitStack() as pc:
                p = lambda *a: pc.enter_context(nc.sbuf_tensor(*a))
                E.p4_r3 = p("p4_r3", [128, 8 * 1024], F32)
                E.p4_xn = p("p4_xn", [128, 8 * 1024], BF16)
                E.p4_yacc = p("p4_yacc", [128, 8 * 1024], F32)
                E.p4_h = p("p4_h", [128, 8 * 1024], BF16)
                E.p4_mfm = p("p4_mfm", [8, 1024], F32)
                E.p4_mbc = p("p4_mbc", [128, 1024], F32)
                E.p4_w1 = [p(f"p4_w1{i}", [128, 8 * 1024], BF16) for i in range(2)]
                E.p4_w2 = [p(f"p4_w2{i}", [128, 8 * 1024], BF16) for i in range(2)]
                E.p4_g = p("p4_g", [128, 32], F32)
                E.gw_sb = p("gw_sb", [128, 64], BF16)
                E.b2_sb = p("b2_sb", [8, 1024], F32)
                E.sel_f = p("sel_f", [8, 1024], F32)
                E.b1f1 = p("t_b1f1", [128, 256], F32)
                E.sg = [p(f"p4sg{i}", [128, 512], F32) for i in range(3)]
                E.sg_guard = [[], [], []]
                emit_p4(E, out_hbm)
            s.barrier()

        fence_deps = [s.tok(name) for name in list(s.cnt.keys())]
        s.emit("sync", lambda h: h.nop(), deps=fence_deps)
        s.realize(nc, ctx)
    return nc


def _prep_shared(inputs):
    f = np.ascontiguousarray
    d = {}
    d["w0q"] = f(np.asarray(inputs["L0_attn_Wq"], np.float32))
    d["w0k"] = f(np.asarray(inputs["L0_attn_Wk"], np.float32))
    d["w0v"] = f(np.asarray(inputs["L0_attn_Wv"], np.float32))
    d["w0o"] = f(np.asarray(inputs["L0_attn_Wo"], np.float32))
    d["b0q"] = f(np.asarray(inputs["L0_attn_bq"], np.float32).reshape(8, 128).T)
    d["b0k"] = f(np.asarray(inputs["L0_attn_bk"], np.float32).reshape(8, 128).T)
    d["b0o"] = f(np.asarray(inputs["L0_attn_bo"], np.float32).reshape(8, 128).T)
    d["b0v_row"] = f(np.asarray(inputs["L0_attn_bv"], np.float32).reshape(1, 1024))
    d["w0g_t"] = f(np.asarray(inputs["L0_ffn_w_gate"], np.int32).T)
    d["w0u_t"] = f(np.asarray(inputs["L0_ffn_w_up"], np.int32).T)
    d["w0d_t"] = f(np.asarray(inputs["L0_ffn_w_down"], np.int32).T)
    d["w1q_t"] = f(np.asarray(inputs["L1_attn_wq"], np.float32).transpose(1, 0, 2).reshape(1024, 1024))
    d["w1k_t"] = f(np.asarray(inputs["L1_attn_wk"], np.float32).transpose(1, 0, 2).reshape(1024, 1024))
    d["w1v_t"] = f(np.asarray(inputs["L1_attn_wv"], np.float32).transpose(1, 0, 2).reshape(1024, 1024))
    d["w1o_t"] = f(np.asarray(inputs["L1_attn_wo"], np.float32).T)
    d["b1q"] = f(np.asarray(inputs["L1_attn_bq"], np.float32).reshape(1024).reshape(8, 128).T)
    d["b1k"] = f(np.asarray(inputs["L1_attn_bk"], np.float32).reshape(1024).reshape(8, 128).T)
    d["b1o"] = f(np.asarray(inputs["L1_attn_bo"], np.float32).reshape(8, 128).T)
    d["b1v_row"] = f(np.asarray(inputs["L1_attn_bv"], np.float32).reshape(1, 1024))
    d["rmsw"] = f(np.asarray(inputs["L1_ffn_rms_w"], np.float32).reshape(8, 128).T)
    d["gw_t"] = f(np.asarray(inputs["L1_ffn_gate_w"], np.float32).T)
    d["b1f1"] = f(np.asarray(inputs["L1_ffn_b1"], np.float32).reshape(8, 32, 128)
                  .transpose(2, 0, 1).reshape(128, 256))
    d["b2"] = f(np.asarray(inputs["L1_ffn_b2"], np.float32))
    for e in range(NE):
        d[f"w1ff1_{e}"] = f(np.asarray(inputs["L1_ffn_W1"][e], np.float32))
        d[f"w1ff2_{e}"] = f(np.asarray(inputs["L1_ffn_W2"][e], np.float32))
    return d


def fm_layout(x2d):
    fm = np.ascontiguousarray(np.asarray(x2d, np.float32).T)
    return np.ascontiguousarray(fm.reshape(8, 128, S).transpose(1, 0, 2).reshape(128, 8 * S))


def fm_unlayout(y):
    fm = y.reshape(128, 8, S).transpose(1, 0, 2).reshape(1024, S)
    return np.ascontiguousarray(fm.T)


def kernel(**inputs):
    shared = _prep_shared(inputs)
    x = np.asarray(inputs["L0_attn_src"], dtype=np.float32)
    nc = build_nc(phases=("p1", "p2", "p3", "p4"))
    in_maps = []
    for b in range(8):
        m = dict(shared)
        m["x0"] = fm_layout(x[b])
        in_maps.append(m)
    res = run_bass_kernel_spmd(nc, in_maps, core_ids=list(range(8)))
    return np.stack([fm_unlayout(np.asarray(r["y"], dtype=np.float32)) for r in res.results])


# version 05
# speedup vs baseline: 1.1968x; 1.0245x over previous
import os
import numpy as np
import ml_dtypes
from contextlib import ExitStack

import concourse.bass as bass
import concourse.mybir as mybir
from concourse.bass_utils import run_bass_kernel_spmd

F32 = mybir.dt.float32
BF16 = mybir.dt.bfloat16
I32 = mybir.dt.int32
AF = mybir.ActivationFunctionType
ALU = mybir.AluOpType
AX = mybir.AxisListType.X
BF16NP = ml_dtypes.bfloat16

S = 4096
D = 1024
SC0 = float(1.0 / np.sqrt(128.0))
SC1 = 0.125
EPS_LN = 1e-5
EPS_RMS = 1e-6
FF0 = 3072
FF1 = 4096
NE = 8
INV127 = float(1.0 / 127.0)
INV127SQ = float(1.0 / (127.0 * 127.0))

ENGINES = ("sync", "tensor", "scalar", "vector", "gpsimd")
OWN_SEM = {"tensor": "T", "scalar": "A", "vector": "V", "gpsimd": "G", "sync": "SY"}


class Sched:
    def __init__(self):
        self.q = {e: [] for e in ENGINES}
        self.cnt = {}
        self.pending = {e: [] for e in ENGINES}

    def tok(self, sem):
        return (sem, self.cnt.get(sem, 0))

    def emit(self, eng, fn, deps=(), sig=None, inc=1):
        if sig is None:
            sig = OWN_SEM[eng]
        for d in deps:
            if d is None:
                continue
            s_, v = d
            assert v <= self.cnt.get(s_, 0), f"dep on future token {s_}={v}"
        self.cnt[sig] = self.cnt.get(sig, 0) + inc
        token = (sig, self.cnt[sig])
        dd = self.pending[eng] + [d for d in deps if d is not None]
        self.pending[eng] = []
        self.q[eng].append((dd, fn, sig, inc))
        return token

    def dma(self, eng, fn, deps=(), sem="DS"):
        return self.emit(eng, fn, deps=deps, sig=sem, inc=16)

    def barrier(self):
        snap = [(s_, v) for (s_, v) in self.cnt.items()]
        for e in ENGINES:
            self.pending[e] = self.pending[e] + snap

    def realize(self, nc, ctx):
        sems = {name: ctx.enter_context(nc.semaphore(name)) for name in self.cnt}
        block = ctx.enter_context(nc.Block())

        def mk(eng):
            def body(h):
                waited = {}
                for (deps, fn, sig, inc) in self.q[eng]:
                    need = {}
                    for (s_, v) in deps:
                        if v > waited.get(s_, 0) and v > need.get(s_, 0):
                            need[s_] = v
                    for s_ in sorted(need):
                        h.wait_ge(sems[s_], need[s_])
                        waited[s_] = need[s_]
                    fn(h).then_inc(sems[sig], inc)
            return body

        block.sync(mk("sync"))
        block.tensor(mk("tensor"))
        block.scalar(mk("scalar"))
        block.vector(mk("vector"))
        block.gpsimd(mk("gpsimd"))


class BankGuard:
    def __init__(self):
        self.readers = {}

    def read(self, b, token):
        self.readers.setdefault(b, []).append(token)

    def wdeps(self, b):
        d = self.readers.get(b, [])
        self.readers[b] = []
        return d


def rearr_cpf(ap, p=128):
    return ap.rearrange("(c p) f -> p c f", p=p)


def rearr_groups(ap, t=S):
    return ap.rearrange("p (c t) -> p c t", t=t)


class Env:
    pass


def emit_postln(E, r_sb, T, dst_fn, r_deps, eps_col, rbf, sqb):
    s = E.s
    nTT = T // 512
    vcp = s.emit("vector", lambda h: h.tensor_copy(rbf[:, 0:8 * T], r_sb[:, 0:8 * T]),
                 deps=list(r_deps) + E.ln_guard)
    acp = s.emit("scalar", lambda h: h.activation(out=sqb[:, 0:8 * T], in_=r_sb[:, 0:8 * T],
                                                  func=AF.Square),
                 deps=list(r_deps) + E.ln_guard)
    R = E.ln_rows
    for tt in range(nTT):
        mean_ps = E.pb[2][0:1, 0:512]
        msq_ps = E.pb[3][0:1, 0:512]

        def statmm(h, src, dst, tt=tt):
            ins = None
            for dc in range(8):
                ins = h.matmul(dst, E.invD_bf[:, 0:1],
                               src[:, dc * T + tt * 512: dc * T + tt * 512 + 512],
                               start=(dc == 0), stop=(dc == 7))
            return ins
        t_mean = s.emit("tensor", lambda h, tt=tt: statmm(h, rbf, E.pb[2][0:1, 0:512], tt),
                        deps=[vcp] + E.bank.wdeps(2))
        t_msq = s.emit("tensor", lambda h, tt=tt: statmm(h, sqb, E.pb[3][0:1, 0:512], tt),
                       deps=[acp] + E.bank.wdeps(3))
        mrow = R[0:1, 0:512]
        m2 = R[0:1, 512:1024]
        vrow = R[0:1, 1024:1536]
        rstd = vrow
        v1 = s.emit("vector", lambda h: h.tensor_copy(mrow, mean_ps),
                    deps=[t_mean] + E.rows_guard)
        E.rows_guard = []
        E.bank.read(2, v1)
        v2 = s.emit("vector", lambda h: h.tensor_mul(m2, mrow, mrow), deps=[v1])
        v3 = s.emit("vector", lambda h: h.tensor_sub(vrow, msq_ps, m2), deps=[t_msq, v2])
        E.bank.read(3, v3)
        a1 = s.emit("scalar", lambda h: h.activation(out=vrow, in_=vrow, func=AF.Ln,
                                                     bias=eps_col), deps=[v3])
        a2 = s.emit("scalar", lambda h: h.activation(out=rstd, in_=vrow, func=AF.Exp,
                                                     scale=-0.5), deps=[a1])
        v4 = s.emit("vector", lambda h: h.tensor_mul(m2, mrow, rstd), deps=[a2])
        bca = E.pb[6][:, 0:512]
        bcb = E.pb[7][:, 0:512]
        t_a = s.emit("tensor", lambda h: h.matmul(bca, E.ones_f[0:1, 0:128], rstd,
                                                  start=True, stop=True),
                     deps=[a2] + E.bank.wdeps(6))
        t_b = s.emit("tensor", lambda h: h.matmul(bcb, E.neg_f[0:1, 0:128], m2,
                                                  start=True, stop=True),
                     deps=[v4] + E.bank.wdeps(7))
        E.rows_guard = [t_a, t_b]
        for dc in range(8):
            slot = (tt * 8 + dc) % 2
            stg = E.ln_stg[slot]
            vv1 = s.emit("vector",
                         lambda h, stg=stg, dc=dc, tt=tt: h.tensor_mul(
                             stg[:], r_sb[:, dc * T + tt * 512: dc * T + tt * 512 + 512], bca),
                         deps=[t_a, t_b] + E.stg_guard[slot])
            vv2 = s.emit("vector", lambda h, stg=stg: h.tensor_add(stg[:], stg[:], bcb),
                         deps=[vv1])
            E.bank.read(6, vv1)
            E.bank.read(7, vv2)
            dtok = s.dma("sync", lambda h, stg=stg, dc=dc, tt=tt: h.dma_start(
                out=dst_fn(dc, tt), in_=stg[:]), deps=[vv2], sem=f"DST{slot}")
            E.stg_guard[slot] = [dtok]
    E.ln_guard = [s.tok("T")]


def emit_p1(E):
    BIS = os.environ.get("P1BISECT", "")
    s = E.s
    wq, wk, wv, wo = E.p1_w
    ldtoks = []
    for (sb_t, src) in ((wq, "w0q"), (wk, "w0k"), (wv, "w0v"), (wo, "w0o")):
        ldtoks.append(s.dma("gpsimd", lambda h, d=sb_t, src=src: h.dma_start(
            out=d[:].rearrange("p (c f) -> p c f", f=1024), in_=rearr_cpf(E.ap[src])),
            sem="DW1"))
    wtok = ldtoks[-1]
    bvr = s.dma("sync", lambda h: h.dma_start(out=E.p1_bvrow[:], in_=E.ap["b0v_row"]), sem="DC")
    bvbc = E.p1_bvbc
    bv_toks = []
    for half in range(2):
        t = s.emit("tensor", lambda h, half=half: h.matmul(
            E.pb[2][:, 0:512], E.ones_f[0:1, 0:128],
            E.p1_bvrow[0:1, half * 512:half * 512 + 512], start=True, stop=True),
            deps=[bvr] + E.bank.wdeps(2))
        a = s.emit("scalar", lambda h, half=half: h.copy(bvbc[:, half * 512:half * 512 + 512],
                                                         E.pb[2][:, 0:512]), deps=[t])
        E.bank.read(2, a)
        bv_toks.append(a)

    xg32 = E.p1_xg32
    xgbf = E.p1_xgbf
    q_sb, k_sb, v_sb, att_sb = E.p1_q, E.p1_k, E.p1_v, E.p1_att
    xg32_guard = []
    xgbf_guard = []
    gg = []

    ngroups = 1 if BIS else 8
    for g in range(ngroups):
        ld32 = s.dma("sync", lambda h, g=g: h.dma_start(
            out=xg32[:].rearrange("p (c t) -> p c t", t=512),
            in_=rearr_groups(E.ap["x_in"])[:, :, g * 512:(g + 1) * 512]),
            deps=xg32_guard, sem="DX32")
        ldbf = s.dma("gpsimd", lambda h, g=g: h.dma_start(
            out=xgbf[:].rearrange("p (c t) -> p c t", t=512),
            in_=rearr_groups(E.ap["x_in"])[:, :, g * 512:(g + 1) * 512]),
            deps=xgbf_guard + [wtok], sem="DXBF")
        xg32_guard = []
        xgbf_guard = []

        pe_toks = []
        for (w_sb, dst, bname) in ((wq, q_sb, "b0q"), (wk, k_sb, "b0k")):
            for dc in range(8):
                bank = len(pe_toks) % 2

                def pmm(h, w_sb=w_sb, dc=dc, bank=bank):
                    ins = None
                    for ci in range(8):
                        ins = h.matmul(E.pb[bank][:, 0:512],
                                       w_sb[:, ci * 1024 + dc * 128: ci * 1024 + dc * 128 + 128],
                                       xgbf[:, ci * 512:(ci + 1) * 512],
                                       start=(ci == 0), stop=(ci == 7))
                    return ins
                t = s.emit("tensor", lambda h, f=pmm: f(h),
                           deps=[ldbf] + E.bank.wdeps(bank))
                pe_toks.append(t)
                a = s.emit("scalar", lambda h, dst=dst, dc=dc, bank=bank, bname=bname:
                           h.activation(out=dst[:, dc * 512:(dc + 1) * 512],
                                        in_=E.pb[bank][:, 0:512], func=AF.Identity,
                                        bias=E.bias[bname][:, dc:dc + 1]),
                           deps=[t, E.bias_tok] + gg)
                E.bank.read(bank, a)
        for tc in range(4):
            for half in range(2):
                bank = len(pe_toks) % 2

                def vmm(h, tc=tc, half=half, bank=bank):
                    ins = None
                    for ci in range(8):
                        ins = h.matmul(E.pb[bank][:, 0:512],
                                       xgbf[:, ci * 512 + tc * 128: ci * 512 + tc * 128 + 128],
                                       wv[:, ci * 1024 + half * 512: ci * 1024 + half * 512 + 512],
                                       start=(ci == 0), stop=(ci == 7))
                    return ins
                t = s.emit("tensor", lambda h, f=vmm: f(h), deps=[ldbf] + E.bank.wdeps(bank))
                pe_toks.append(t)
                v = s.emit("vector", lambda h, tc=tc, half=half, bank=bank: h.tensor_add(
                    v_sb[:, tc * 1024 + half * 512: tc * 1024 + half * 512 + 512],
                    E.pb[bank][:, 0:512], bvbc[:, half * 512:half * 512 + 512]),
                    deps=[t] + bv_toks + gg)
                E.bank.read(bank, v)
        proj_done_v = s.tok("V")
        proj_done_a = s.tok("A")
        if BIS == "proj":
            s.dma("gpsimd", lambda h: h.dma_start(
                out=rearr_groups(E.ap["r1"])[:, :, 0:512],
                in_=q_sb[:].rearrange("p (c t) -> p c t", t=512)),
                deps=[proj_done_v, proj_done_a], sem="DXBF")
            return

        it = 0
        if g == 0:
            z0 = s.emit("vector", lambda h: h.memset(E.p1_expst[0][0][64:128, :], 0.0))
            z1 = s.emit("vector", lambda h: h.memset(E.p1_expst[1][0][64:128, :], 0.0))
            z2 = s.emit("vector", lambda h: h.memset(E.p1_expst[0][1][0:64, :], 0.0))
            z3 = s.emit("vector", lambda h: h.memset(E.p1_expst[1][1][0:64, :], 0.0))
            E.p1_expst_init = [z0, z1, z2, z3]
        expst_g = [list(E.p1_expst_init), list(E.p1_expst_init)]
        recip_g = [[], []]
        bcsb_g = [[], []]
        natt = int(os.environ.get("P1ATT_N", "32"))
        aops = int(os.environ.get("P1ATT_OPS", "7"))
        for tc in range(4):
            for hh in range(8):
                if it >= natt:
                    continue
                sl = it % 2
                st_ps = E.pb[2 + sl]
                da_ps = E.pb[4 + sl]
                bc_ps = E.pb[6 + sl]
                exp0 = E.p1_expst[sl][0]
                exp1 = E.p1_expst[sl][1]
                recip = E.p1_recip[sl]
                bc_sb = E.p1_bcsb[sl]

                def stmm(h, tc=tc, hh=hh, st_ps=st_ps):
                    base = hh * 512 + tc * 128
                    h.matmul(st_ps[0:64, 0:64], k_sb[:, base:base + 64],
                             q_sb[:, base:base + 64], start=True, stop=True)
                    return h.matmul(st_ps[64:128, 0:64], k_sb[:, base + 64:base + 128],
                                    q_sb[:, base + 64:base + 128], start=True, stop=True,
                                    tile_position=(0, 64))
                t1 = s.emit("tensor", lambda h, f=stmm: f(h),
                            deps=[proj_done_v, proj_done_a] + E.bank.wdeps(2 + sl))
                if aops < 2:
                    it += 1
                    continue
                e1a = s.emit("scalar", lambda h, st_ps=st_ps, exp0=exp0: h.activation(
                    out=exp0[0:64, :], in_=st_ps[0:64, 0:64], func=AF.Exp, scale=SC0),
                    deps=[t1] + expst_g[sl])
                e1 = s.emit("scalar", lambda h, st_ps=st_ps, exp1=exp1: h.activation(
                    out=exp1[64:128, :], in_=st_ps[64:128, 0:64], func=AF.Exp, scale=SC0),
                    deps=[t1])
                E.bank.read(2 + sl, e1)
                if aops < 3:
                    it += 1
                    continue

                def dpv(h, tc=tc, hh=hh, da_ps=da_ps, exp0=exp0, exp1=exp1):
                    h.matmul(da_ps[0:1, 128:192], E.ones_bf[:, 0:1], exp0[:, :],
                             start=True, stop=True)
                    h.matmul(da_ps[0:1, 192:256], E.ones_bf[:, 0:1], exp1[:, :],
                             start=True, stop=True)
                    vb = tc * 1024 + hh * 128
                    h.matmul(da_ps[:, 0:64], v_sb[:, vb:vb + 128], exp0[:, :],
                             start=True, stop=True)
                    return h.matmul(da_ps[:, 64:128], v_sb[:, vb:vb + 128],
                                    exp1[:, :], start=True, stop=True)
                t2 = s.emit("tensor", lambda h, f=dpv: f(h),
                            deps=[e1a, e1] + E.bank.wdeps(4 + sl))
                expst_g[sl] = [t2]
                if aops < 4:
                    it += 1
                    continue
                r1 = s.emit("vector", lambda h, da_ps=da_ps, recip=recip: h.reciprocal(
                    recip[0:1, 0:128], da_ps[0:1, 128:256]), deps=[t2] + recip_g[sl])
                E.bank.read(4 + sl, r1)
                if aops < 5:
                    it += 1
                    continue
                t3 = s.emit("tensor", lambda h, bc_ps=bc_ps, recip=recip: h.matmul(
                    bc_ps[:, 0:128], E.ones_f[0:1, 0:128], recip[0:1, 0:128],
                    start=True, stop=True), deps=[r1] + E.bank.wdeps(6 + sl))
                recip_g[sl] = [t3]
                if aops < 6:
                    it += 1
                    continue
                a3 = s.emit("scalar", lambda h, bc_ps=bc_ps, bc_sb=bc_sb: h.copy(
                    bc_sb[:], bc_ps[:, 0:128]), deps=[t3] + bcsb_g[sl])
                E.bank.read(6 + sl, a3)
                if aops < 7:
                    it += 1
                    continue
                v3 = s.emit("vector", lambda h, tc=tc, hh=hh, da_ps=da_ps, bc_sb=bc_sb:
                            h.tensor_mul(att_sb[:, hh * 512 + tc * 128: hh * 512 + tc * 128 + 128],
                                         da_ps[:, 0:128], bc_sb[:]),
                            deps=[a3, t2])
                E.bank.read(4 + sl, v3)
                expst_g[sl] = [t2]
                bcsb_g[sl] = [v3]
                it += 1
        att_done = s.tok("V")
        if BIS == "attn":
            s.dma("gpsimd", lambda h: h.dma_start(
                out=rearr_groups(E.ap["r1"])[:, :, 0:512],
                in_=att_sb[:].rearrange("p (c t) -> p c t", t=512)),
                deps=[att_done], sem="DXBF")
            return

        wo_last = None
        for dc in range(8):
            bank = dc % 2

            def womm(h, dc=dc, bank=bank):
                ins = None
                for ci in range(8):
                    ins = h.matmul(E.pb[bank][:, 0:512],
                                   wo[:, ci * 1024 + dc * 128: ci * 1024 + dc * 128 + 128],
                                   att_sb[:, ci * 512:(ci + 1) * 512],
                                   start=(ci == 0), stop=(ci == 7))
                return ins
            t = s.emit("tensor", lambda h, f=womm: f(h),
                       deps=[att_done] + E.bank.wdeps(bank))
            v = s.emit("vector", lambda h, dc=dc, bank=bank: h.scalar_tensor_tensor(
                xg32[:, dc * 512:(dc + 1) * 512], E.pb[bank][:, 0:512],
                E.bias["b0o"][:, dc:dc + 1], xg32[:, dc * 512:(dc + 1) * 512],
                ALU.add, ALU.add),
                deps=[t, ld32, E.bias_tok])
            E.bank.read(bank, v)
            wo_last = v

        if BIS == "wo":
            s.dma("sync", lambda h: h.dma_start(
                out=rearr_groups(E.ap["r1"])[:, :, 0:512],
                in_=xg32[:].rearrange("p (c t) -> p c t", t=512)),
                deps=[wo_last], sem="DST0")
            return

        def dst_fn(dc, tt, g=g):
            return rearr_groups(E.ap["r1"])[:, dc, g * 512:(g + 1) * 512]
        emit_postln(E, xg32, 512, dst_fn, [wo_last], E.eps[0:1, 0:1], rbf=att_sb, sqb=xgbf)
        xg32_guard = [s.tok("V"), s.tok("A")]
        xgbf_guard = list(E.ln_guard)
        gg = [s.tok("T")]


def emit_p2(E):
    s = E.s
    wd = E.p2_wd
    wd_toks = []
    for part in range(3):
        wd_toks.append(s.dma("gpsimd", lambda h, part=part: h.dma_start(
            out=wd[:, part * 8192:(part + 1) * 8192].rearrange("p (c f) -> p c f", f=1024),
            in_=rearr_cpf(E.ap["w0d_t"])[:, part * 8:(part + 1) * 8, :]), sem="DWD"))
    wg4, wu4 = E.p2_wg, E.p2_wu
    h_sb = E.p2_h
    x32 = E.p2_x32
    xbf = E.p2_xbf
    x32_guard = []
    xbf_guard = []
    w_guard = [[], []]
    h_guard = []
    for ei in range(8):
        ld32 = s.dma("sync", lambda h, ei=ei: h.dma_start(
            out=x32[:].rearrange("p (c t) -> p c t", t=512),
            in_=rearr_groups(E.ap["r1"])[:, :, ei * 512:(ei + 1) * 512]),
            deps=x32_guard, sem="DX32")
        ldbf = s.dma("gpsimd", lambda h, ei=ei: h.dma_start(
            out=xbf[:].rearrange("p (c t) -> p c t", t=512),
            in_=rearr_groups(E.ap["r1"])[:, :, ei * 512:(ei + 1) * 512]),
            deps=xbf_guard + [wd_toks[-1]], sem="DXBF")
        x32_guard = []
        xbf_guard = []
        for wgrp in range(6):
            slot = wgrp % 2
            ldg = s.dma("gpsimd", lambda h, wgrp=wgrp, slot=slot: h.dma_start(
                out=wg4[slot][:].rearrange("p (c f) -> p c f", f=512),
                in_=rearr_cpf(E.ap["w0g_t"])[:, :, wgrp * 512:(wgrp + 1) * 512]),
                deps=w_guard[slot], sem=f"DWG{slot}")
            ldu = s.dma("gpsimd", lambda h, wgrp=wgrp, slot=slot: h.dma_start(
                out=wu4[slot][:].rearrange("p (c f) -> p c f", f=512),
                in_=rearr_cpf(E.ap["w0u_t"])[:, :, wgrp * 512:(wgrp + 1) * 512]),
                deps=w_guard[slot], sem=f"DWU{slot}")
            w_guard[slot] = []
            for h4 in range(4):
                hc = wgrp * 4 + h4
                bkg = hc % 2
                bku = 2 + hc % 2

                def gmm(h, h4=h4, bkg=bkg, slot=slot):
                    ins = None
                    for ci in range(8):
                        ins = h.matmul(E.pb[bkg][:, 0:512],
                                       wg4[slot][:, ci * 512 + h4 * 128: ci * 512 + h4 * 128 + 128],
                                       xbf[:, ci * 512:(ci + 1) * 512],
                                       start=(ci == 0), stop=(ci == 7))
                    return ins
                tg = s.emit("tensor", lambda h, f=gmm: f(h),
                            deps=[ldbf, ldg] + E.bank.wdeps(bkg))

                def umm(h, h4=h4, bku=bku, slot=slot):
                    ins = None
                    for ci in range(8):
                        ins = h.matmul(E.pb[bku][:, 0:512],
                                       wu4[slot][:, ci * 512 + h4 * 128: ci * 512 + h4 * 128 + 128],
                                       xbf[:, ci * 512:(ci + 1) * 512],
                                       start=(ci == 0), stop=(ci == 7))
                    return ins
                tu = s.emit("tensor", lambda h, f=umm: f(h),
                            deps=[ldbf, ldu] + E.bank.wdeps(bku))
                ssl = hc % 2
                sgs = E.sg[ssl]
                ag = s.emit("scalar", lambda h, bkg=bkg, sgs=sgs: h.activation(
                    out=sgs[:], in_=E.pb[bkg][:, 0:512], func=AF.Silu, scale=INV127),
                    deps=[tg] + E.sg_guard[ssl])
                E.bank.read(bkg, ag)
                vh = s.emit("vector", lambda h, hc=hc, bku=bku, sgs=sgs: h.tensor_mul(
                    h_sb[:, hc * 512:(hc + 1) * 512], E.pb[bku][:, 0:512], sgs[:]),
                    deps=[tu, ag] + h_guard)
                h_guard = []
                E.bank.read(bku, vh)
                E.sg_guard[ssl] = [vh]
            w_guard[slot] = [s.tok("T")]
        h_done = s.tok("V")
        dn_last = None
        for dc in range(8):
            bank = 4 + dc % 2

            def dmm(h, dc=dc, bank=bank):
                ins = None
                for hc in range(24):
                    ins = h.matmul(E.pb[bank][:, 0:512],
                                   wd[:, hc * 1024 + dc * 128: hc * 1024 + dc * 128 + 128],
                                   h_sb[:, hc * 512:(hc + 1) * 512],
                                   start=(hc == 0), stop=(hc == 23))
                return ins
            t = s.emit("tensor", lambda h, f=dmm: f(h),
                       deps=[h_done, wd_toks[-1]] + E.bank.wdeps(bank))
            v = s.emit("vector", lambda h, dc=dc, bank=bank: h.scalar_tensor_tensor(
                x32[:, dc * 512:(dc + 1) * 512], E.pb[bank][:, 0:512], INV127SQ,
                x32[:, dc * 512:(dc + 1) * 512], ALU.mult, ALU.add),
                deps=[t, ld32])
            E.bank.read(bank, v)
            dn_last = v
        h_guard = [s.tok("T")]

        def dst_fn(dc, tt, ei=ei):
            return rearr_groups(E.ap["r2"])[:, dc, ei * 512:(ei + 1) * 512]
        emit_postln(E, x32, 512, dst_fn, [dn_last], E.eps[0:1, 0:1],
                    rbf=h_sb, sqb=xbf)
        h_guard = h_guard + E.ln_guard
        x32_guard = [s.tok("V"), s.tok("A")]
        xbf_guard = list(E.ln_guard)


def emit_p3(E, pc):
    s = E.s
    nc = E.nc
    kfm = E.p3_k
    vext = E.p3_vext
    wbuf = E.p3_wbuf
    actx = ExitStack()
    wvb = actx.enter_context(nc.sbuf_tensor("p3_wv", [128, 8 * 1024], BF16))
    E.p3_bvrow = actx.enter_context(nc.sbuf_tensor("p3_bvrow", [1, D], F32))
    bvbc = actx.enter_context(nc.sbuf_tensor("p3_bvbc", [128, 1024], F32))
    xbf1 = actx.enter_context(nc.sbuf_tensor("p3_xbf1", [128, 8 * 512], BF16))
    ms = s.emit("vector", lambda h: h.memset(vext[:], 1.0))
    wk_t = s.dma("gpsimd", lambda h: h.dma_start(
        out=wbuf[:].rearrange("p (c f) -> p c f", f=1024), in_=rearr_cpf(E.ap["w1k_t"])),
        sem="DW3")
    wv_t = s.dma("gpsimd", lambda h: h.dma_start(
        out=wvb[:].rearrange("p (c f) -> p c f", f=1024), in_=rearr_cpf(E.ap["w1v_t"])),
        sem="DWV")
    bvr = s.dma("sync", lambda h: h.dma_start(out=E.p3_bvrow[:], in_=E.ap["b1v_row"]), sem="DC")
    bv_toks = []
    for half in range(2):
        t = s.emit("tensor", lambda h, half=half: h.matmul(
            E.pb[2][:, 0:512], E.ones_f[0:1, 0:128],
            E.p3_bvrow[0:1, half * 512:half * 512 + 512], start=True, stop=True),
            deps=[bvr] + E.bank.wdeps(2))
        a = s.emit("scalar", lambda h, half=half: h.copy(bvbc[:, half * 512:half * 512 + 512],
                                                         E.pb[2][:, 0:512]), deps=[t])
        E.bank.read(2, a)
        bv_toks.append(a)

    xbf = [E.p3_xbf0, xbf1]
    xbf_guard = [[], []]
    for tt in range(8):
        slot = tt % 2
        ldbf = s.dma("gpsimd", lambda h, tt=tt, slot=slot: h.dma_start(
            out=xbf[slot][:].rearrange("p (c t) -> p c t", t=512),
            in_=rearr_groups(E.ap["r2"])[:, :, tt * 512:(tt + 1) * 512]),
            deps=xbf_guard[slot] + [wv_t], sem=f"DXB{slot}")
        xbf_guard[slot] = []
        for dc in range(8):
            bank = dc % 2

            def kmm(h, dc=dc, bank=bank, slot=slot):
                ins = None
                for ci in range(8):
                    ins = h.matmul(E.pb[bank][:, 0:512],
                                   wbuf[:, ci * 1024 + dc * 128: ci * 1024 + dc * 128 + 128],
                                   xbf[slot][:, ci * 512:(ci + 1) * 512],
                                   start=(ci == 0), stop=(ci == 7))
                return ins
            t = s.emit("tensor", lambda h, f=kmm: f(h), deps=[ldbf, wk_t] + E.bank.wdeps(bank))
            v = s.emit("vector", lambda h, tt=tt, dc=dc, bank=bank: h.tensor_scalar(
                kfm[:, dc * 4096 + tt * 512: dc * 4096 + tt * 512 + 512],
                E.pb[bank][:, 0:512], E.bias["b1k"][:, dc:dc + 1], None, ALU.add),
                deps=[t, E.bias_tok])
            E.bank.read(bank, v)
        for tc in range(4):
            for half in range(2):
                bank = 2 + (tc * 2 + half) % 2

                def vmm(h, tc=tc, half=half, bank=bank, slot=slot):
                    ins = None
                    for ci in range(8):
                        ins = h.matmul(E.pb[bank][:, 0:512],
                                       xbf[slot][:, ci * 512 + tc * 128: ci * 512 + tc * 128 + 128],
                                       wvb[:, ci * 1024 + half * 512: ci * 1024 + half * 512 + 512],
                                       start=(ci == 0), stop=(ci == 7))
                    return ins
                t = s.emit("tensor", lambda h, f=vmm: f(h), deps=[ldbf, wv_t] + E.bank.wdeps(bank))
                base = (tt * 4 + tc) * 1040 + half * 520

                def vev(h, bank=bank, base=base, half=half):
                    dst = vext[:, base:base + 520].rearrange("p (hh e) -> p hh e", e=65)[:, :, 0:64]
                    src = E.pb[bank][:, 0:512].rearrange("p (hh e) -> p hh e", e=64)
                    bcv = bvbc[:, half * 512:half * 512 + 512].rearrange("p (hh e) -> p hh e", e=64)
                    return h.tensor_add(dst, src, bcv)
                v = s.emit("vector", lambda h, f=vev: f(h), deps=[t, ms] + bv_toks)
                E.bank.read(bank, v)
        xbf_guard[slot] = [s.tok("T")]
    a_done_v = s.tok("V")
    actx.close()
    a_fin = [s.tok("T"), s.tok("V"), s.tok("A")]

    p = lambda *a: pc.enter_context(nc.sbuf_tensor(*a))
    q_sb = p("p3_q", [128, 8 * 512], BF16)
    att_sb = p("p3_att", [128, 8 * 512], BF16)
    r32 = p("p3_r32", [128, 8 * 512], F32)
    E.p3_expst = [p(f"p3_expst{i}", [128, 1024], BF16) for i in range(2)]
    E.p3_rec = p("p3_rec", [128, 512], F32)
    E.p3_bcsb = [p("p3_bcsb0", [64, 512], F32)]
    E.p3_bcsb = E.p3_bcsb * 2
    E.p3_tmp = [p(f"p3_tmp{i}", [64, 512], BF16) for i in range(2)]
    wload_guard = [s.tok("T")]
    E.p3_expg = [list(a_fin) for _ in range(2)]
    q_guard = list(a_fin)
    att_guard = list(a_fin)
    r32_guard = list(a_fin)
    expst_g = [list(a_fin), list(a_fin)]
    rec_g = list(a_fin)
    bcsb_g = [list(a_fin), list(a_fin)]
    tmp_g = [list(a_fin), list(a_fin)]
    for qt in range(8):
        wq_t = s.dma("gpsimd", lambda h: h.dma_start(
            out=wbuf[:].rearrange("p (c f) -> p c f", f=1024), in_=rearr_cpf(E.ap["w1q_t"])),
            deps=wload_guard, sem="DW3")
        ldr = s.dma("sync", lambda h, qt=qt: h.dma_start(
            out=r32[:].rearrange("p (c t) -> p c t", t=512),
            in_=rearr_groups(E.ap["r2"])[:, :, qt * 512:(qt + 1) * 512]),
            deps=r32_guard, sem="DR32")
        ldq = s.dma("gpsimd", lambda h, qt=qt: h.dma_start(
            out=xbf[0][:].rearrange("p (c t) -> p c t", t=512),
            in_=rearr_groups(E.ap["r2"])[:, :, qt * 512:(qt + 1) * 512]),
            deps=xbf_guard[0], sem="DXB0")
        r32_guard = []
        xbf_guard[0] = []
        q_last = None
        for dc in range(8):
            bank = dc % 2

            def qmm(h, dc=dc, bank=bank):
                ins = None
                for ci in range(8):
                    ins = h.matmul(E.pb[bank][:, 0:512],
                                   wbuf[:, ci * 1024 + dc * 128: ci * 1024 + dc * 128 + 128],
                                   xbf[0][:, ci * 512:(ci + 1) * 512],
                                   start=(ci == 0), stop=(ci == 7))
                return ins
            t = s.emit("tensor", lambda h, f=qmm: f(h),
                       deps=[ldq, wq_t, a_done_v] + E.bank.wdeps(bank))
            v = s.emit("vector", lambda h, dc=dc, bank=bank: h.tensor_scalar(
                q_sb[:, dc * 512:(dc + 1) * 512], E.pb[bank][:, 0:512],
                E.bias["b1q"][:, dc:dc + 1], None, ALU.add),
                deps=[t, E.bias_tok] + q_guard)
            q_guard = []
            E.bank.read(bank, v)
            q_last = v
        qmm_done = s.tok("T")
        wo_t = s.dma("gpsimd", lambda h: h.dma_start(
            out=wbuf[:].rearrange("p (c f) -> p c f", f=1024), in_=rearr_cpf(E.ap["w1o_t"])),
            deps=[qmm_done], sem="DW3")

        prev_bc = None
        for hp in range(8):
            pvA = E.pb[4]
            pvB = E.pb[5]
            pvA_w = E.bank.wdeps(4)
            pvB_w = E.bank.wdeps(5)
            pv_tok = {"A": None, "B": None}
            e_tok = {}
            for ktc in range(34):
                par = ktc % 2
                if ktc < 32:
                    bA = 0 + 2 * par
                    bB = 1 + 2 * par

                    def stm(h, hlo, ktc=ktc, hp=hp, bank=None):
                        return h.matmul(E.pb[bank][:, 0:512],
                                        kfm[hlo:hlo + 64, hp * 4096 + ktc * 128: hp * 4096 + ktc * 128 + 128],
                                        q_sb[hlo:hlo + 64, hp * 512:(hp + 1) * 512],
                                        start=True, stop=True)
                    tA = s.emit("tensor", lambda h, f=stm, bA=bA: f(h, 0, bank=bA),
                                deps=[q_last, a_done_v] + E.bank.wdeps(bA)
                                     + ([prev_bc] if (ktc == 0 and prev_bc) else []))
                if ktc >= 2:
                    kprev = ktc - 2

                    def pvm(h, hh, ktc_, side, dst):
                        par_ = ktc_ % 2
                        return h.matmul(dst[0:65, 0:512],
                                        vext[:, ktc_ * 1040 + hh * 65: ktc_ * 1040 + hh * 65 + 65],
                                        E.p3_expst[par_][:, side * 512: side * 512 + 512],
                                        start=(ktc_ == 0), stop=(ktc_ == 31))
                    pv_tok["A"] = s.emit("tensor",
                                         lambda h, f=pvm, kprev=kprev, hp=hp: f(
                                             h, 2 * hp, kprev, 0, pvA),
                                         deps=[e_tok[kprev]] + (pvA_w if kprev == 0 else []))
                if ktc < 32:
                    tB = s.emit("tensor", lambda h, f=stm, bB=bB: f(h, 64, bank=bB),
                                deps=E.bank.wdeps(bB))
                if ktc >= 2:
                    pv_tok["B"] = s.emit("tensor",
                                         lambda h, f=pvm, kprev=kprev, hp=hp: f(
                                             h, 2 * hp + 1, kprev, 1, pvB),
                                         deps=[e_tok[kprev]] + (pvB_w if kprev == 0 else []))
                    E.p3_expg[kprev % 2] = [pv_tok["B"]]
                if ktc < 32:
                    ee = s.emit("scalar", lambda h, par=par: h.activation(
                        out=E.p3_expst[par][:], in_=E.pbig[:, 1024 * par: 1024 * par + 1024],
                        func=AF.Exp, scale=SC1),
                        deps=[tA, tB] + E.p3_expg[par])
                    E.bank.read(bA, ee)
                    E.bank.read(bB, ee)
                    e_tok[ktc] = ee
            for (side, hh, pvb, pv_ps) in (("A", 2 * hp, 4, pvA), ("B", 2 * hp + 1, 5, pvB)):
                rec = E.p3_rec
                r1 = s.emit("vector", lambda h, pv_ps=pv_ps: h.reciprocal(
                    rec[64:65, 0:512], pv_ps[64:65, 0:512]), deps=[pv_tok[side]] + rec_g)
                t3 = s.emit("tensor", lambda h: h.matmul(
                    E.pb[6][0:64, 0:512], E.ones_f[64:65, 0:64], rec[64:65, 0:512],
                    start=True, stop=True), deps=[r1] + E.bank.wdeps(6))
                rec_g = [t3]
                prev_bc = t3
                a3 = s.emit("scalar", lambda h: h.copy(
                    E.p3_bcsb[0][0:64, :], E.pb[6][0:64, 0:512]),
                    deps=[t3] + bcsb_g[0] + bcsb_g[1])
                E.bank.read(6, a3)
                if side == "A":
                    v3 = s.emit("vector", lambda h, hp=hp, pv_ps=pv_ps: h.tensor_mul(
                        att_sb[0:64, hp * 512:(hp + 1) * 512], pv_ps[0:64, 0:512],
                        E.p3_bcsb[0][0:64, :]),
                        deps=[a3, pv_tok[side]] + att_guard)
                    E.bank.read(pvb, v3)
                    bcsb_g = [[v3], []]
                else:
                    tsl = hp % 2
                    v3 = s.emit("vector", lambda h, pv_ps=pv_ps, tsl=tsl: h.tensor_mul(
                        E.p3_tmp[tsl][0:64, :], pv_ps[0:64, 0:512], E.p3_bcsb[0][0:64, :]),
                        deps=[a3, pv_tok[side]] + tmp_g[tsl])
                    E.bank.read(pvb, v3)
                    bcsb_g = [[v3], []]
                    d3 = s.dma("sync", lambda h, hp=hp, tsl=tsl: h.dma_start(
                        out=att_sb[64:128, hp * 512:(hp + 1) * 512], in_=E.p3_tmp[tsl][0:64, :]),
                        deps=[v3] + att_guard, sem=f"DTMP{tsl}")
                    tmp_g[tsl] = [d3]
        att_done = [s.tok("V"), s.tok("DTMP0"), s.tok("DTMP1")]

        wo_last = None
        for dc in range(8):
            bank = dc % 2

            def womm(h, dc=dc, bank=bank):
                ins = None
                for ci in range(8):
                    ins = h.matmul(E.pb[bank][:, 0:512],
                                   wbuf[:, ci * 1024 + dc * 128: ci * 1024 + dc * 128 + 128],
                                   att_sb[:, ci * 512:(ci + 1) * 512],
                                   start=(ci == 0), stop=(ci == 7))
                return ins
            t = s.emit("tensor", lambda h, f=womm: f(h),
                       deps=att_done + [wo_t] + E.bank.wdeps(bank))
            v = s.emit("vector", lambda h, dc=dc, bank=bank: h.scalar_tensor_tensor(
                r32[:, dc * 512:(dc + 1) * 512], E.pb[bank][:, 0:512],
                E.bias["b1o"][:, dc:dc + 1], r32[:, dc * 512:(dc + 1) * 512],
                ALU.add, ALU.add),
                deps=[t, ldr, E.bias_tok])
            E.bank.read(bank, v)
            wo_last = v
        wload_guard = [s.tok("T")]

        def dst_fn(dc, tt, qt=qt):
            return rearr_groups(E.ap["r3"])[:, dc, qt * 512:(qt + 1) * 512]
        emit_postln(E, r32, 512, dst_fn, [wo_last], E.eps[0:1, 0:1],
                    rbf=att_sb, sqb=xbf[0])
        att_guard = list(E.ln_guard)
        q_guard = [s.tok("T")]
        xbf_guard[0] = list(E.ln_guard)
        r32_guard = [s.tok("V"), s.tok("A")]


def emit_p4(E, out_name):
    s = E.s
    r3q = E.p4_r3
    xn = E.p4_xn
    y_acc = E.p4_yacc
    h_sb = E.p4_h
    mfm = E.p4_mfm
    mbc = E.p4_mbc
    w1b = E.p4_w1
    w2b = E.p4_w2
    gw_tok = s.dma("gpsimd", lambda h: h.dma_start(
        out=E.gw_sb[:].rearrange("p (c e) -> p c e", e=8), in_=rearr_cpf(E.ap["gw_t"])),
        sem="DGW")
    s.dma("sync", lambda h: h.dma_start(out=E.b2_sb[:], in_=E.ap["b2"]), sem="DC")
    s.dma("sync", lambda h: h.dma_start(out=E.sel_f[:], in_=E.c_sel), sem="DC")
    s.dma("sync", lambda h: h.dma_start(out=E.b1f1[:], in_=E.ap["b1f1"]), sem="DC")
    b2_tok = sel_tok = b1f_tok = s.tok("DC")

    r3_guard = []
    xn_guard = []
    yacc_guard = []
    h_guard = []
    mfm_guard = []
    mbc_guard = []
    w_guard = [[], []]
    lg_guard = []
    for qtr in range(4):
        ld32 = s.dma("sync", lambda h, qtr=qtr: h.dma_start(
            out=r3q[:].rearrange("p (c t) -> p c t", t=1024),
            in_=rearr_groups(E.ap["r3"])[:, :, qtr * 1024:(qtr + 1) * 1024]),
            deps=r3_guard, sem="DR32")
        r3_guard = []
        asq = s.emit("scalar", lambda h: h.activation(out=xn[:], in_=r3q[:], func=AF.Square),
                     deps=[ld32] + xn_guard)
        xn_guard = []
        stat_toks = []
        bc_toks = []
        for tt in range(2):
            def mms(h, tt=tt):
                ins = None
                for dc in range(8):
                    ins = h.matmul(E.pb[2][0:1, 0:512], E.invD_bf[:, 0:1],
                                   xn[:, dc * 1024 + tt * 512: dc * 1024 + tt * 512 + 512],
                                   start=(dc == 0), stop=(dc == 7))
                return ins
            tm = s.emit("tensor", lambda h, f=mms: f(h), deps=[asq] + E.bank.wdeps(2))
            vrow = E.ln_rows[0:1, 1024:1536]
            rstd = vrow
            a1 = s.emit("scalar", lambda h: h.activation(
                out=vrow, in_=E.pb[2][0:1, 0:512], func=AF.Ln, bias=E.eps[0:1, 1:2]),
                deps=[tm] + E.rows_guard)
            E.rows_guard = []
            E.bank.read(2, a1)
            a2 = s.emit("scalar", lambda h: h.activation(
                out=rstd, in_=vrow, func=AF.Exp, scale=-0.5), deps=[a1])
            t_b = s.emit("tensor", lambda h, tt=tt: h.matmul(
                E.pb[6 + tt][:, 0:512], E.ones_f[0:1, 0:128], rstd, start=True, stop=True),
                deps=[a2] + E.bank.wdeps(6 + tt))
            E.rows_guard = [t_b]
            stat_toks.append(tm)
            bc_toks.append(t_b)
        xn_last = None
        for tt in range(2):
            for dc in range(8):
                v = s.emit("vector", lambda h, dc=dc, tt=tt: h.scalar_tensor_tensor(
                    xn[:, dc * 1024 + tt * 512: dc * 1024 + tt * 512 + 512],
                    r3q[:, dc * 1024 + tt * 512: dc * 1024 + tt * 512 + 512],
                    E.bias["rmsw"][:, dc:dc + 1], E.pb[6 + tt][:, 0:512],
                    ALU.mult, ALU.mult),
                    deps=[bc_toks[tt], stat_toks[0], stat_toks[1], ld32, E.bias_tok])
                E.bank.read(6 + tt, v)
                xn_last = v
        xn_done = xn_last

        prev_a1 = None
        prev_t2 = None
        mask_last = None
        for tcl in range(8):
            def lmm(h, tcl=tcl):
                ins = None
                for ci in range(8):
                    ins = h.matmul(E.pb[2][0:128, 0:8],
                                   xn[:, ci * 1024 + tcl * 128: ci * 1024 + tcl * 128 + 128],
                                   E.gw_sb[:, ci * 8:(ci + 1) * 8],
                                   start=(ci == 0), stop=(ci == 7))
                return ins
            t1 = s.emit("tensor", lambda h, f=lmm: f(h),
                        deps=[xn_done, gw_tok] + E.bank.wdeps(2))
            G = E.p4_g
            v1 = s.emit("vector", lambda h: h.tensor_copy(G[:, 0:8], E.pb[2][0:128, 0:8]),
                        deps=[t1] + lg_guard + ([prev_a1] if prev_a1 else []))
            lg_guard = []
            E.bank.read(2, v1)
            v2 = s.emit("vector", lambda h: h.reduce_max(G[:, 8:9], G[:, 0:8], axis=AX),
                        deps=[v1])
            v3 = s.emit("vector", lambda h: h.tensor_scalar_mul(G[:, 9:10], G[:, 8:9], -1.0),
                        deps=[v2])
            a1 = s.emit("scalar", lambda h: h.activation(out=G[:, 10:18], in_=G[:, 0:8],
                                                         func=AF.Exp, bias=G[:, 9:10]),
                        deps=[v3])
            v4 = s.emit("vector", lambda h: h.reduce_sum(G[:, 18:19], G[:, 10:18], axis=AX),
                        deps=[a1])
            v5 = s.emit("vector", lambda h: h.reciprocal(G[:, 19:20], G[:, 18:19]), deps=[v4])
            v6 = s.emit("vector", lambda h: h.tensor_scalar(G[:, 20:28], G[:, 0:8], G[:, 8:9],
                                                            None, ALU.is_equal),
                        deps=[v5] + ([prev_t2] if prev_t2 else []))
            v7 = s.emit("vector", lambda h: h.tensor_scalar(G[:, 20:28], G[:, 20:28],
                                                            G[:, 19:20], None, ALU.mult),
                        deps=[v6])
            t2 = s.emit("tensor", lambda h: h.transpose(E.pb[3][0:8, 0:128], G[:, 20:28],
                                                        E.ident_f[:]),
                        deps=[v7, E.ident_tok] + E.bank.wdeps(3))
            v8 = s.emit("vector", lambda h, tcl=tcl: h.tensor_copy(
                mfm[0:8, tcl * 128:(tcl + 1) * 128], E.pb[3][0:8, 0:128]),
                deps=[t2] + mfm_guard)
            mfm_guard = []
            E.bank.read(3, v8)
            prev_a1 = a1
            prev_t2 = t2
            mask_last = v8
        lg_guard = [s.tok("T")]
        mask_done = mask_last

        y_last = None
        for tt in range(2):
            for dc in range(8):
                bank = (tt * 8 + dc) % 2
                t = s.emit("tensor", lambda h, tt=tt, dc=dc, bank=bank: h.matmul(
                    E.pb[bank][:, 0:512], E.b2_sb[0:8, dc * 128:(dc + 1) * 128],
                    mfm[0:8, tt * 512:(tt + 1) * 512], start=True, stop=True),
                    deps=[mask_done, b2_tok] + E.bank.wdeps(bank))
                v = s.emit("vector", lambda h, tt=tt, dc=dc, bank=bank: h.tensor_copy(
                    y_acc[:, dc * 1024 + tt * 512: dc * 1024 + tt * 512 + 512],
                    E.pb[bank][:, 0:512]),
                    deps=[t] + yacc_guard)
                yacc_guard = []
                E.bank.read(bank, v)
                y_last = v

        for e in range(NE):
            mb_last = None
            for tt in range(2):
                bank = 6 + tt % 2
                t = s.emit("tensor", lambda h, e=e, tt=tt, bank=bank: h.matmul(
                    E.pb[bank][:, 0:512], E.sel_f[0:8, e * 128:(e + 1) * 128],
                    mfm[0:8, tt * 512:(tt + 1) * 512], start=True, stop=True),
                    deps=[mask_done, sel_tok] + E.bank.wdeps(bank))
                a = s.emit("scalar", lambda h, tt=tt, bank=bank: h.copy(
                    mbc[:, tt * 512:(tt + 1) * 512], E.pb[bank][:, 0:512]),
                    deps=[t] + mbc_guard)
                mbc_guard = []
                E.bank.read(bank, a)
                mb_last = a
            for hcg in range(4):
                slot = hcg % 2
                ld1 = s.dma("gpsimd", lambda h, e=e, hcg=hcg, slot=slot: h.dma_start(
                    out=w1b[slot][:].rearrange("p (c f) -> p c f", f=1024),
                    in_=rearr_cpf(E.ap["w1ff1"][e])[:, :, hcg * 1024:(hcg + 1) * 1024]),
                    deps=w_guard[slot], sem=f"D41{slot}")
                ld2 = s.dma("gpsimd", lambda h, e=e, hcg=hcg, slot=slot: h.dma_start(
                    out=w2b[slot][:].rearrange("p (c f) -> p c f", f=1024),
                    in_=rearr_cpf(E.ap["w1ff2"][e])[:, hcg * 8:(hcg + 1) * 8, :]),
                    deps=w_guard[slot], sem=f"D42{slot}")
                w_guard[slot] = []
                h_last = None
                for hcl in range(8):
                    for tt in range(2):
                        bank = (hcl * 2 + tt) % 2

                        def gmm(h, hcl=hcl, tt=tt, bank=bank, slot=slot):
                            ins = None
                            for ci in range(8):
                                ins = h.matmul(
                                    E.pb[bank][:, 0:512],
                                    w1b[slot][:, ci * 1024 + hcl * 128: ci * 1024 + hcl * 128 + 128],
                                    xn[:, ci * 1024 + tt * 512: ci * 1024 + tt * 512 + 512],
                                    start=(ci == 0), stop=(ci == 7))
                            return ins
                        tg = s.emit("tensor", lambda h, f=gmm: f(h),
                                    deps=[ld1, xn_done] + E.bank.wdeps(bank))
                        ssl = (hcl * 2 + tt) % 2
                        sgs = E.sg[ssl]
                        bcol = E.b1f1[:, e * 32 + hcg * 8 + hcl: e * 32 + hcg * 8 + hcl + 1]
                        ag = s.emit("scalar", lambda h, bank=bank, sgs=sgs, bcol=bcol:
                                    h.activation(out=sgs[:], in_=E.pb[bank][:, 0:512],
                                                 func=AF.Silu, bias=bcol),
                                    deps=[tg, b1f_tok] + E.sg_guard[ssl])
                        E.bank.read(bank, ag)
                        vh = s.emit("vector", lambda h, hcl=hcl, tt=tt, sgs=sgs: h.tensor_mul(
                            h_sb[:, hcl * 1024 + tt * 512: hcl * 1024 + tt * 512 + 512],
                            sgs[:], mbc[:, tt * 512:(tt + 1) * 512]),
                            deps=[ag, mb_last] + h_guard)
                        h_guard = []
                        E.sg_guard[ssl] = [vh]
                        h_last = vh
                for tt in range(2):
                    for dc in range(8):
                        bank = 2 + (tt * 8 + dc) % 2

                        def ymm(h, tt=tt, dc=dc, bank=bank, slot=slot):
                            ins = None
                            for hcl in range(8):
                                ins = h.matmul(
                                    E.pb[bank][:, 0:512],
                                    w2b[slot][:, hcl * 1024 + dc * 128: hcl * 1024 + dc * 128 + 128],
                                    h_sb[:, hcl * 1024 + tt * 512: hcl * 1024 + tt * 512 + 512],
                                    start=(hcl == 0), stop=(hcl == 7))
                            return ins
                        t = s.emit("tensor", lambda h, f=ymm: f(h),
                                   deps=[ld2, h_last] + E.bank.wdeps(bank))
                        v = s.emit("vector", lambda h, tt=tt, dc=dc, bank=bank: h.tensor_add(
                            y_acc[:, dc * 1024 + tt * 512: dc * 1024 + tt * 512 + 512],
                            y_acc[:, dc * 1024 + tt * 512: dc * 1024 + tt * 512 + 512],
                            E.pb[bank][:, 0:512]),
                            deps=[t, y_last])
                        E.bank.read(bank, v)
                        y_last = v
                h_guard = [s.tok("T")]
                w_guard[slot] = [s.tok("T")]
            mbc_guard = [s.tok("V")]

        radd_last = None
        for i in range(2):
            radd_last = s.emit("vector", lambda h, i=i: h.tensor_add(
                y_acc[:, i * 4096:(i + 1) * 4096], y_acc[:, i * 4096:(i + 1) * 4096],
                r3q[:, i * 4096:(i + 1) * 4096]), deps=[y_last, ld32])

        def dst_fn(dc, tt, qtr=qtr):
            return rearr_groups(E.ap[out_name])[:, dc, qtr * 1024 + tt * 512: qtr * 1024 + tt * 512 + 512]
        emit_postln(E, y_acc, 1024, dst_fn, [radd_last], E.eps[0:1, 0:1],
                    rbf=h_sb, sqb=xn)
        vtok, atok = s.tok("V"), s.tok("A")
        r3_guard = [vtok, atok]
        yacc_guard = [vtok, atok]
        xn_guard = list(E.ln_guard)
        h_guard = list(E.ln_guard)
        mfm_guard = [s.tok("T")]
        mbc_guard = [s.tok("V")]


def build_nc(phases=("p1", "p2", "p3", "p4")):
    nc = bass.Bass("TRN2", target_bir_lowering=False, debug=False)
    E = Env()
    E.nc = nc
    s = Sched()
    E.s = s
    E.bank = BankGuard()

    ap = {}
    ap["x0"] = nc.dram_tensor("x0", [128, 8 * S], F32, kind="ExternalInput").ap()
    chain = {"p1": ("x_in", "r1"), "p2": ("r1", "r2"), "p3": ("r2", "r3"), "p4": ("r3", "y")}
    needed = set()
    for ph in phases:
        needed.add(chain[ph][0])
        needed.add(chain[ph][1])
    needed.discard("x_in")
    out_hbm = chain[phases[-1]][1]
    in_hbm = chain[phases[0]][0]
    for name in sorted(needed):
        if name == out_hbm:
            ap[name] = nc.dram_tensor(name, [128, 8 * S], F32, kind="ExternalOutput").ap()
        else:
            ap[name] = nc.dram_tensor(name, [128, 8 * S], F32, kind="Internal").ap()
    if in_hbm != "x_in":
        ap[in_hbm] = ap["x0"]
    else:
        ap["x_in"] = ap["x0"]

    def din(name, shape, dt):
        ap[name] = nc.dram_tensor(name, shape, dt, kind="ExternalInput").ap()
    din("w0q", [D, D], F32); din("w0k", [D, D], F32); din("w0v", [D, D], F32); din("w0o", [D, D], F32)
    din("b0q", [128, 8], F32); din("b0k", [128, 8], F32); din("b0o", [128, 8], F32)
    din("b0v_row", [1, D], F32)
    din("w0g_t", [D, FF0], I32); din("w0u_t", [D, FF0], I32); din("w0d_t", [FF0, D], I32)
    din("w1q_t", [D, D], F32); din("w1k_t", [D, D], F32); din("w1v_t", [D, D], F32)
    din("w1o_t", [D, D], F32)
    din("b1q", [128, 8], F32); din("b1k", [128, 8], F32); din("b1o", [128, 8], F32)
    din("b1v_row", [1, D], F32)
    din("rmsw", [128, 8], F32)
    din("gw_t", [D, NE], F32)
    din("b1f1", [128, 256], F32)
    din("b2", [NE, D], F32)
    ap["w1ff1"] = [nc.dram_tensor(f"w1ff1_{e}", [D, FF1], F32, kind="ExternalInput").ap()
                   for e in range(NE)]
    ap["w1ff2"] = [nc.dram_tensor(f"w1ff2_{e}", [FF1, D], F32, kind="ExternalInput").ap()
                   for e in range(NE)]
    c_ones_bf = nc.inline_tensor(np.ones((128, 128), dtype=BF16NP), "c_ones_bf").ap()
    c_ones_f = nc.inline_tensor(np.ones((128, 128), dtype=np.float32), "c_ones_f").ap()
    c_neg_f = nc.inline_tensor(np.full((1, 128), -1.0, dtype=np.float32), "c_neg_f").ap()
    c_invd = nc.inline_tensor(np.full((128, 1), 1.0 / 1024.0, dtype=BF16NP), "c_invd").ap()
    c_ident = nc.inline_tensor(np.eye(128, dtype=np.float32), "c_ident").ap()
    selm = np.zeros((8, 8 * 128), dtype=np.float32)
    for e in range(8):
        selm[e, e * 128:(e + 1) * 128] = 1.0
    E.c_sel = nc.inline_tensor(selm, "c_sel").ap()
    c_eps = nc.inline_tensor(np.array([[EPS_LN, EPS_RMS]], dtype=np.float32), "c_eps").ap()
    E.ap = ap

    with ExitStack() as ctx:
        sb = lambda *a: ctx.enter_context(nc.sbuf_tensor(*a))
        psb = lambda *a: ctx.enter_context(nc.psum_tensor(*a))
        E.pbig = psb("pbig", [128, 4096], F32)
        E.pb = [E.pbig[:, i * 512:(i + 1) * 512] for i in range(8)]
        E.ones_bf = sb("ones_bf", [128, 128], BF16)
        E.ones_f = sb("ones_f", [128, 128], F32)
        E.neg_f = sb("neg_f", [1, 128], F32)
        E.invD_bf = sb("invD_bf", [128, 1], BF16)
        E.ident_f = sb("ident_f", [128, 128], F32)
        E.eps = sb("eps", [1, 2], F32)
        E.bias = {}
        for name in ("b0q", "b0k", "b0o", "b1q", "b1k", "b1o", "rmsw"):
            E.bias[name] = sb("t_" + name, [128, 8], F32)
        E.ln_rows = sb("ln_rows", [1, 1536], F32)
        E.ln_stg = [sb(f"ln_stg{i}", [128, 512], F32) for i in range(2)]
        E.stg_guard = [[], []]
        E.ln_guard = []
        E.rows_guard = []

        s.dma("sync", lambda h: h.dma_start(out=E.ones_bf[:], in_=c_ones_bf), sem="DC")
        s.dma("sync", lambda h: h.dma_start(out=E.ones_f[:], in_=c_ones_f), sem="DC")
        s.dma("sync", lambda h: h.dma_start(out=E.neg_f[:], in_=c_neg_f), sem="DC")
        s.dma("sync", lambda h: h.dma_start(out=E.invD_bf[:], in_=c_invd), sem="DC")
        E.ident_tok = s.dma("sync", lambda h: h.dma_start(out=E.ident_f[:], in_=c_ident), sem="DC")
        s.dma("sync", lambda h: h.dma_start(out=E.eps[:], in_=c_eps), sem="DC")
        for name in ("b0q", "b0k", "b0o", "b1q", "b1k", "b1o", "rmsw"):
            s.dma("sync", lambda h, name=name: h.dma_start(out=E.bias[name][:], in_=ap[name]),
                  sem="DC")
        E.bias_tok = s.tok("DC")
        s.barrier()

        if "p1" in phases:
            with ExitStack() as pc:
                p = lambda *a: pc.enter_context(nc.sbuf_tensor(*a))
                E.p1_w = [p(f"p1w{i}", [128, 8 * 1024], BF16) for i in range(4)]
                E.p1_bvrow = p("p1_bvrow", [1, D], F32)
                E.p1_bvbc = p("p1_bvbc", [128, 1024], F32)
                E.p1_xg32 = p("p1_xg32", [128, 8 * 512], F32)
                E.p1_xgbf = p("p1_xgbf", [128, 8 * 512], BF16)
                E.p1_q = p("p1_q", [128, 8 * 512], BF16)
                E.p1_k = p("p1_k", [128, 8 * 512], BF16)
                E.p1_v = p("p1_v", [128, 4 * 1024], BF16)
                E.p1_att = p("p1_att", [128, 8 * 512], BF16)
                E.p1_expst = [[p(f"p1_expst{i}_{b}", [128, 64], BF16) for b in range(2)]
                              for i in range(2)]
                E.p1_recip = [p(f"p1_recip{i}", [1, 128], F32) for i in range(2)]
                E.p1_bcsb = [p(f"p1_bcsb{i}", [128, 128], F32) for i in range(2)]
                emit_p1(E)
            s.barrier()
        if "p2" in phases:
            with ExitStack() as pc:
                p = lambda *a: pc.enter_context(nc.sbuf_tensor(*a))
                E.p2_wd = p("p2_wd", [128, 24 * 1024], BF16)
                E.p2_wg = [p(f"p2_wg{i}", [128, 8 * 512], BF16) for i in range(2)]
                E.p2_wu = [p(f"p2_wu{i}", [128, 8 * 512], BF16) for i in range(2)]
                E.p2_h = p("p2_h", [128, 24 * 512], BF16)
                E.p2_x32 = p("p2_x32", [128, 8 * 512], F32)
                E.p2_xbf = p("p2_xbf", [128, 8 * 512], BF16)
                E.sg = [p(f"p2sg{i}", [128, 512], F32) for i in range(2)]
                E.sg_guard = [[], []]
                emit_p2(E)
            s.barrier()
        if "p3" in phases:
            with ExitStack() as pc:
                p = lambda *a: pc.enter_context(nc.sbuf_tensor(*a))
                E.p3_k = p("p3_k", [128, 8 * 4096], BF16)
                E.p3_vext = p("p3_vext", [128, 32 * 1040], BF16)
                E.p3_wbuf = p("p3_wbuf", [128, 8 * 1024], BF16)
                E.p3_xbf0 = p("p3_xbf0", [128, 8 * 512], BF16)
                emit_p3(E, pc)
            s.barrier()
        if "p4" in phases:
            with ExitStack() as pc:
                p = lambda *a: pc.enter_context(nc.sbuf_tensor(*a))
                E.p4_r3 = p("p4_r3", [128, 8 * 1024], F32)
                E.p4_xn = p("p4_xn", [128, 8 * 1024], BF16)
                E.p4_yacc = p("p4_yacc", [128, 8 * 1024], F32)
                E.p4_h = p("p4_h", [128, 8 * 1024], BF16)
                E.p4_mfm = p("p4_mfm", [8, 1024], F32)
                E.p4_mbc = p("p4_mbc", [128, 1024], F32)
                E.p4_w1 = [p(f"p4_w1{i}", [128, 8 * 1024], BF16) for i in range(2)]
                E.p4_w2 = [p(f"p4_w2{i}", [128, 8 * 1024], BF16) for i in range(2)]
                E.p4_g = p("p4_g", [128, 32], F32)
                E.gw_sb = p("gw_sb", [128, 64], BF16)
                E.b2_sb = p("b2_sb", [8, 1024], F32)
                E.sel_f = p("sel_f", [8, 1024], F32)
                E.b1f1 = p("t_b1f1", [128, 256], F32)
                E.sg = [p(f"p4sg{i}", [128, 512], F32) for i in range(3)]
                E.sg_guard = [[], [], []]
                emit_p4(E, out_hbm)
            s.barrier()

        fence_deps = [s.tok(name) for name in list(s.cnt.keys())]
        s.emit("sync", lambda h: h.nop(), deps=fence_deps)
        s.realize(nc, ctx)
    return nc


def _prep_shared(inputs):
    f = np.ascontiguousarray
    d = {}
    d["w0q"] = f(np.asarray(inputs["L0_attn_Wq"], np.float32))
    d["w0k"] = f(np.asarray(inputs["L0_attn_Wk"], np.float32))
    d["w0v"] = f(np.asarray(inputs["L0_attn_Wv"], np.float32))
    d["w0o"] = f(np.asarray(inputs["L0_attn_Wo"], np.float32))
    d["b0q"] = f(np.asarray(inputs["L0_attn_bq"], np.float32).reshape(8, 128).T)
    d["b0k"] = f(np.asarray(inputs["L0_attn_bk"], np.float32).reshape(8, 128).T)
    d["b0o"] = f(np.asarray(inputs["L0_attn_bo"], np.float32).reshape(8, 128).T)
    d["b0v_row"] = f(np.asarray(inputs["L0_attn_bv"], np.float32).reshape(1, 1024))
    d["w0g_t"] = f(np.asarray(inputs["L0_ffn_w_gate"], np.int32).T)
    d["w0u_t"] = f(np.asarray(inputs["L0_ffn_w_up"], np.int32).T)
    d["w0d_t"] = f(np.asarray(inputs["L0_ffn_w_down"], np.int32).T)
    d["w1q_t"] = f(np.asarray(inputs["L1_attn_wq"], np.float32).transpose(1, 0, 2).reshape(1024, 1024))
    d["w1k_t"] = f(np.asarray(inputs["L1_attn_wk"], np.float32).transpose(1, 0, 2).reshape(1024, 1024))
    d["w1v_t"] = f(np.asarray(inputs["L1_attn_wv"], np.float32).transpose(1, 0, 2).reshape(1024, 1024))
    d["w1o_t"] = f(np.asarray(inputs["L1_attn_wo"], np.float32).T)
    d["b1q"] = f(np.asarray(inputs["L1_attn_bq"], np.float32).reshape(1024).reshape(8, 128).T)
    d["b1k"] = f(np.asarray(inputs["L1_attn_bk"], np.float32).reshape(1024).reshape(8, 128).T)
    d["b1o"] = f(np.asarray(inputs["L1_attn_bo"], np.float32).reshape(8, 128).T)
    d["b1v_row"] = f(np.asarray(inputs["L1_attn_bv"], np.float32).reshape(1, 1024))
    d["rmsw"] = f(np.asarray(inputs["L1_ffn_rms_w"], np.float32).reshape(8, 128).T)
    d["gw_t"] = f(np.asarray(inputs["L1_ffn_gate_w"], np.float32).T)
    d["b1f1"] = f(np.asarray(inputs["L1_ffn_b1"], np.float32).reshape(8, 32, 128)
                  .transpose(2, 0, 1).reshape(128, 256))
    d["b2"] = f(np.asarray(inputs["L1_ffn_b2"], np.float32))
    for e in range(NE):
        d[f"w1ff1_{e}"] = f(np.asarray(inputs["L1_ffn_W1"][e], np.float32))
        d[f"w1ff2_{e}"] = f(np.asarray(inputs["L1_ffn_W2"][e], np.float32))
    return d


def fm_layout(x2d):
    fm = np.ascontiguousarray(np.asarray(x2d, np.float32).T)
    return np.ascontiguousarray(fm.reshape(8, 128, S).transpose(1, 0, 2).reshape(128, 8 * S))


def fm_unlayout(y):
    fm = y.reshape(128, 8, S).transpose(1, 0, 2).reshape(1024, S)
    return np.ascontiguousarray(fm.T)


def kernel(**inputs):
    shared = _prep_shared(inputs)
    x = np.asarray(inputs["L0_attn_src"], dtype=np.float32)
    nc = build_nc(phases=("p1", "p2", "p3", "p4"))
    in_maps = []
    for b in range(8):
        m = dict(shared)
        m["x0"] = fm_layout(x[b])
        in_maps.append(m)
    res = run_bass_kernel_spmd(nc, in_maps, core_ids=list(range(8)))
    return np.stack([fm_unlayout(np.asarray(r["y"], dtype=np.float32)) for r in res.results])


# version 09
# speedup vs baseline: 1.4239x; 1.1897x over previous
import os
import numpy as np
import ml_dtypes
from contextlib import ExitStack

import concourse.bass as bass
import concourse.mybir as mybir
from concourse.bass_utils import run_bass_kernel_spmd

F32 = mybir.dt.float32
BF16 = mybir.dt.bfloat16
I32 = mybir.dt.int32
AF = mybir.ActivationFunctionType
ALU = mybir.AluOpType
AX = mybir.AxisListType.X
BF16NP = ml_dtypes.bfloat16

S = 4096
D = 1024
SC0 = float(1.0 / np.sqrt(128.0))
SC1 = 0.125
EPS_LN = 1e-5
EPS_RMS = 1e-6
FF0 = 3072
FF1 = 4096
NE = 8
INV127 = float(1.0 / 127.0)
INV127SQ = float(1.0 / (127.0 * 127.0))

ENGINES = ("sync", "tensor", "scalar", "vector", "gpsimd")
OWN_SEM = {"tensor": "T", "scalar": "A", "vector": "V", "gpsimd": "G", "sync": "SY"}


class Sched:
    def __init__(self):
        self.q = {e: [] for e in ENGINES}
        self.cnt = {}
        self.pending = {e: [] for e in ENGINES}

    def tok(self, sem):
        return (sem, self.cnt.get(sem, 0))

    def emit(self, eng, fn, deps=(), sig=None, inc=1):
        if sig is None:
            sig = OWN_SEM[eng]
        for d in deps:
            if d is None:
                continue
            s_, v = d
            assert v <= self.cnt.get(s_, 0), f"dep on future token {s_}={v}"
        self.cnt[sig] = self.cnt.get(sig, 0) + inc
        token = (sig, self.cnt[sig])
        dd = self.pending[eng] + [d for d in deps if d is not None]
        self.pending[eng] = []
        self.q[eng].append((dd, fn, sig, inc))
        return token

    def dma(self, eng, fn, deps=(), sem="DS"):
        return self.emit(eng, fn, deps=deps, sig=sem, inc=16)

    def barrier(self):
        snap = [(s_, v) for (s_, v) in self.cnt.items()]
        for e in ENGINES:
            self.pending[e] = self.pending[e] + snap

    def realize(self, nc, ctx):
        sems = {name: ctx.enter_context(nc.semaphore(name)) for name in self.cnt}
        block = ctx.enter_context(nc.Block())

        def mk(eng):
            def body(h):
                waited = {}
                for (deps, fn, sig, inc) in self.q[eng]:
                    need = {}
                    for (s_, v) in deps:
                        if v > waited.get(s_, 0) and v > need.get(s_, 0):
                            need[s_] = v
                    for s_ in sorted(need):
                        h.wait_ge(sems[s_], need[s_])
                        waited[s_] = need[s_]
                    fn(h).then_inc(sems[sig], inc)
            return body

        block.sync(mk("sync"))
        block.tensor(mk("tensor"))
        block.scalar(mk("scalar"))
        block.vector(mk("vector"))
        block.gpsimd(mk("gpsimd"))


class BankGuard:
    def __init__(self):
        self.readers = {}

    def read(self, b, token):
        self.readers.setdefault(b, []).append(token)

    def wdeps(self, b):
        d = self.readers.get(b, [])
        self.readers[b] = []
        return d


def rearr_cpf(ap, p=128):
    return ap.rearrange("(c p) f -> p c f", p=p)


def rearr_groups(ap, t=S):
    return ap.rearrange("p (c t) -> p c t", t=t)


class Env:
    pass


def emit_postln(E, r_sb, T, dst_fn, r_deps, eps_col, rbf, sqb):
    s = E.s
    nTT = T // 512
    vcp = s.emit("vector", lambda h: h.tensor_copy(rbf[:, 0:8 * T], r_sb[:, 0:8 * T]),
                 deps=list(r_deps) + E.ln_guard)
    acp = s.emit("scalar", lambda h: h.activation(out=sqb[:, 0:8 * T], in_=r_sb[:, 0:8 * T],
                                                  func=AF.Square),
                 deps=list(r_deps) + E.ln_guard)
    R = E.ln_rows
    for tt in range(nTT):
        mean_ps = E.pb[2][0:1, 0:512]
        msq_ps = E.pb[3][0:1, 0:512]

        def statmm(h, src, dst, tt=tt):
            ins = None
            for dc in range(8):
                ins = h.matmul(dst, E.invD_bf[:, 0:1],
                               src[:, dc * T + tt * 512: dc * T + tt * 512 + 512],
                               start=(dc == 0), stop=(dc == 7))
            return ins
        t_mean = s.emit("tensor", lambda h, tt=tt: statmm(h, rbf, E.pb[2][0:1, 0:512], tt),
                        deps=[vcp] + E.bank.wdeps(2))
        t_msq = s.emit("tensor", lambda h, tt=tt: statmm(h, sqb, E.pb[3][0:1, 0:512], tt),
                       deps=[acp] + E.bank.wdeps(3))
        mrow = R[0:1, 0:512]
        m2 = R[0:1, 512:1024]
        vrow = R[0:1, 1024:1536]
        rstd = vrow
        v1 = s.emit("vector", lambda h: h.tensor_copy(mrow, mean_ps),
                    deps=[t_mean] + E.rows_guard)
        E.rows_guard = []
        E.bank.read(2, v1)
        v2 = s.emit("vector", lambda h: h.tensor_mul(m2, mrow, mrow), deps=[v1])
        v3 = s.emit("vector", lambda h: h.tensor_sub(vrow, msq_ps, m2), deps=[t_msq, v2])
        E.bank.read(3, v3)
        a1 = s.emit("scalar", lambda h: h.activation(out=vrow, in_=vrow, func=AF.Ln,
                                                     bias=eps_col), deps=[v3])
        a2 = s.emit("scalar", lambda h: h.activation(out=rstd, in_=vrow, func=AF.Exp,
                                                     scale=-0.5), deps=[a1])
        v4 = s.emit("vector", lambda h: h.tensor_mul(m2, mrow, rstd), deps=[a2])
        bca = E.pb[6][:, 0:512]
        bcb = E.pb[7][:, 0:512]
        t_a = s.emit("tensor", lambda h: h.matmul(bca, E.ones_f[0:1, 0:128], rstd,
                                                  start=True, stop=True),
                     deps=[a2] + E.bank.wdeps(6))
        t_b = s.emit("tensor", lambda h: h.matmul(bcb, E.neg_f[0:1, 0:128], m2,
                                                  start=True, stop=True),
                     deps=[v4] + E.bank.wdeps(7))
        E.rows_guard = [t_a, t_b]
        for dc in range(8):
            slot = (tt * 8 + dc) % 2
            stg = E.ln_stg[slot]
            vv1 = s.emit("vector",
                         lambda h, stg=stg, dc=dc, tt=tt: h.tensor_mul(
                             stg[:], r_sb[:, dc * T + tt * 512: dc * T + tt * 512 + 512], bca),
                         deps=[t_a, t_b] + E.stg_guard[slot])
            vv2 = s.emit("vector", lambda h, stg=stg: h.tensor_add(stg[:], stg[:], bcb),
                         deps=[vv1])
            E.bank.read(6, vv1)
            E.bank.read(7, vv2)
            dtok = s.dma("sync", lambda h, stg=stg, dc=dc, tt=tt: h.dma_start(
                out=dst_fn(dc, tt), in_=stg[:]), deps=[vv2], sem=f"DST{slot}")
            E.stg_guard[slot] = [dtok]
    E.ln_guard = [s.tok("T")]


def emit_p1(E):
    BIS = os.environ.get("P1BISECT", "")
    s = E.s
    wq, wk, wv, wo = E.p1_w
    ldtoks = []
    for (sb_t, src) in ((wq, "w0q"), (wk, "w0k"), (wv, "w0v"), (wo, "w0o")):
        ldtoks.append(s.dma("gpsimd", lambda h, d=sb_t, src=src: h.dma_start(
            out=d[:].rearrange("p (c f) -> p c f", f=1024), in_=rearr_cpf(E.ap[src])),
            sem="DW1"))
    wtok = ldtoks[-1]
    bvr = s.dma("sync", lambda h: h.dma_start(out=E.p1_bvrow[:], in_=E.ap["b0v_row"]), sem="DC")
    bvbc = E.p1_bvbc
    bv_toks = []
    for half in range(2):
        t = s.emit("tensor", lambda h, half=half: h.matmul(
            E.pb[2][:, 0:512], E.ones_f[0:1, 0:128],
            E.p1_bvrow[0:1, half * 512:half * 512 + 512], start=True, stop=True),
            deps=[bvr] + E.bank.wdeps(2))
        a = s.emit("scalar", lambda h, half=half: h.copy(bvbc[:, half * 512:half * 512 + 512],
                                                         E.pb[2][:, 0:512]), deps=[t])
        E.bank.read(2, a)
        bv_toks.append(a)

    xg32 = E.p1_xg32
    xgbf = E.p1_xgbf
    q_sb, k_sb, v_sb, att_sb = E.p1_q, E.p1_k, E.p1_v, E.p1_att
    xg32_guard = []
    xgbf_guard = []
    gg = []

    ngroups = 1 if BIS else 8
    for g in range(ngroups):
        ld32 = s.dma("sync", lambda h, g=g: h.dma_start(
            out=xg32[:].rearrange("p (c t) -> p c t", t=512),
            in_=rearr_groups(E.ap["x_in"])[:, :, g * 512:(g + 1) * 512]),
            deps=xg32_guard, sem="DX32")
        ldbf = s.dma("gpsimd", lambda h, g=g: h.dma_start(
            out=xgbf[:].rearrange("p (c t) -> p c t", t=512),
            in_=rearr_groups(E.ap["x_in"])[:, :, g * 512:(g + 1) * 512]),
            deps=xgbf_guard + [wtok], sem="DXBF")
        xg32_guard = []
        xgbf_guard = []

        pe_toks = []
        for (w_sb, dst, bname) in ((wq, q_sb, "b0q"), (wk, k_sb, "b0k")):
            for dc in range(8):
                bank = len(pe_toks) % 2

                def pmm(h, w_sb=w_sb, dc=dc, bank=bank):
                    ins = None
                    for ci in range(8):
                        ins = h.matmul(E.pb[bank][:, 0:512],
                                       w_sb[:, ci * 1024 + dc * 128: ci * 1024 + dc * 128 + 128],
                                       xgbf[:, ci * 512:(ci + 1) * 512],
                                       start=(ci == 0), stop=(ci == 7))
                    return ins
                t = s.emit("tensor", lambda h, f=pmm: f(h),
                           deps=[ldbf] + E.bank.wdeps(bank))
                pe_toks.append(t)
                a = s.emit("scalar", lambda h, dst=dst, dc=dc, bank=bank, bname=bname:
                           h.activation(out=dst[:, dc * 512:(dc + 1) * 512],
                                        in_=E.pb[bank][:, 0:512], func=AF.Identity,
                                        bias=E.bias[bname][:, dc:dc + 1]),
                           deps=[t, E.bias_tok] + gg)
                E.bank.read(bank, a)
        for tc in range(4):
            for half in range(2):
                bank = len(pe_toks) % 2

                def vmm(h, tc=tc, half=half, bank=bank):
                    ins = None
                    for ci in range(8):
                        ins = h.matmul(E.pb[bank][:, 0:512],
                                       xgbf[:, ci * 512 + tc * 128: ci * 512 + tc * 128 + 128],
                                       wv[:, ci * 1024 + half * 512: ci * 1024 + half * 512 + 512],
                                       start=(ci == 0), stop=(ci == 7))
                    return ins
                t = s.emit("tensor", lambda h, f=vmm: f(h), deps=[ldbf] + E.bank.wdeps(bank))
                pe_toks.append(t)
                v = s.emit("vector", lambda h, tc=tc, half=half, bank=bank: h.tensor_add(
                    v_sb[:, tc * 1024 + half * 512: tc * 1024 + half * 512 + 512],
                    E.pb[bank][:, 0:512], bvbc[:, half * 512:half * 512 + 512]),
                    deps=[t] + bv_toks + gg)
                E.bank.read(bank, v)
        proj_done_v = s.tok("V")
        proj_done_a = s.tok("A")
        if BIS == "proj":
            s.dma("gpsimd", lambda h: h.dma_start(
                out=rearr_groups(E.ap["r1"])[:, :, 0:512],
                in_=q_sb[:].rearrange("p (c t) -> p c t", t=512)),
                deps=[proj_done_v, proj_done_a], sem="DXBF")
            return

        it = 0
        if g == 0:
            z0 = s.emit("vector", lambda h: h.memset(E.p1_expst[0][0][64:128, :], 0.0))
            z1 = s.emit("vector", lambda h: h.memset(E.p1_expst[1][0][64:128, :], 0.0))
            z2 = s.emit("vector", lambda h: h.memset(E.p1_expst[0][1][0:64, :], 0.0))
            z3 = s.emit("vector", lambda h: h.memset(E.p1_expst[1][1][0:64, :], 0.0))
            E.p1_expst_init = [z0, z1, z2, z3]
        expst_g = [list(E.p1_expst_init), list(E.p1_expst_init)]
        recip_g = [[], []]
        bcsb_g = [[], []]
        natt = int(os.environ.get("P1ATT_N", "32"))
        aops = int(os.environ.get("P1ATT_OPS", "7"))
        for tc in range(4):
            for hh in range(8):
                if it >= natt:
                    continue
                sl = it % 2
                st_ps = E.pb[2 + sl]
                da_ps = E.pb[4 + sl]
                bc_ps = E.pb[6 + sl]
                exp0 = E.p1_expst[sl][0]
                exp1 = E.p1_expst[sl][1]
                recip = E.p1_recip[sl]
                bc_sb = E.p1_bcsb[sl]

                def stmm(h, tc=tc, hh=hh, st_ps=st_ps):
                    base = hh * 512 + tc * 128
                    h.matmul(st_ps[0:64, 0:64], k_sb[:, base:base + 64],
                             q_sb[:, base:base + 64], start=True, stop=True)
                    return h.matmul(st_ps[64:128, 0:64], k_sb[:, base + 64:base + 128],
                                    q_sb[:, base + 64:base + 128], start=True, stop=True,
                                    tile_position=(0, 64))
                t1 = s.emit("tensor", lambda h, f=stmm: f(h),
                            deps=[proj_done_v, proj_done_a] + E.bank.wdeps(2 + sl))
                if aops < 2:
                    it += 1
                    continue
                e1a = s.emit("scalar", lambda h, st_ps=st_ps, exp0=exp0: h.activation(
                    out=exp0[0:64, :], in_=st_ps[0:64, 0:64], func=AF.Exp, scale=SC0),
                    deps=[t1] + expst_g[sl])
                e1 = s.emit("scalar", lambda h, st_ps=st_ps, exp1=exp1: h.activation(
                    out=exp1[64:128, :], in_=st_ps[64:128, 0:64], func=AF.Exp, scale=SC0),
                    deps=[t1])
                E.bank.read(2 + sl, e1)
                if aops < 3:
                    it += 1
                    continue

                def dpv(h, tc=tc, hh=hh, da_ps=da_ps, exp0=exp0, exp1=exp1):
                    h.matmul(da_ps[0:1, 128:192], E.ones_bf[:, 0:1], exp0[:, :],
                             start=True, stop=True)
                    h.matmul(da_ps[0:1, 192:256], E.ones_bf[:, 0:1], exp1[:, :],
                             start=True, stop=True)
                    vb = tc * 1024 + hh * 128
                    h.matmul(da_ps[:, 0:64], v_sb[:, vb:vb + 128], exp0[:, :],
                             start=True, stop=True)
                    return h.matmul(da_ps[:, 64:128], v_sb[:, vb:vb + 128],
                                    exp1[:, :], start=True, stop=True)
                t2 = s.emit("tensor", lambda h, f=dpv: f(h),
                            deps=[e1a, e1] + E.bank.wdeps(4 + sl))
                expst_g[sl] = [t2]
                if aops < 4:
                    it += 1
                    continue
                r1 = s.emit("vector", lambda h, da_ps=da_ps, recip=recip: h.reciprocal(
                    recip[0:1, 0:128], da_ps[0:1, 128:256]), deps=[t2] + recip_g[sl])
                E.bank.read(4 + sl, r1)
                if aops < 5:
                    it += 1
                    continue
                t3 = s.emit("tensor", lambda h, bc_ps=bc_ps, recip=recip: h.matmul(
                    bc_ps[:, 0:128], E.ones_f[0:1, 0:128], recip[0:1, 0:128],
                    start=True, stop=True), deps=[r1] + E.bank.wdeps(6 + sl))
                recip_g[sl] = [t3]
                if aops < 6:
                    it += 1
                    continue
                a3 = s.emit("scalar", lambda h, bc_ps=bc_ps, bc_sb=bc_sb: h.copy(
                    bc_sb[:], bc_ps[:, 0:128]), deps=[t3] + bcsb_g[sl])
                E.bank.read(6 + sl, a3)
                if aops < 7:
                    it += 1
                    continue
                v3 = s.emit("vector", lambda h, tc=tc, hh=hh, da_ps=da_ps, bc_sb=bc_sb:
                            h.tensor_mul(att_sb[:, hh * 512 + tc * 128: hh * 512 + tc * 128 + 128],
                                         da_ps[:, 0:128], bc_sb[:]),
                            deps=[a3, t2])
                E.bank.read(4 + sl, v3)
                expst_g[sl] = [t2]
                bcsb_g[sl] = [v3]
                it += 1
        att_done = s.tok("V")
        if BIS == "attn":
            s.dma("gpsimd", lambda h: h.dma_start(
                out=rearr_groups(E.ap["r1"])[:, :, 0:512],
                in_=att_sb[:].rearrange("p (c t) -> p c t", t=512)),
                deps=[att_done], sem="DXBF")
            return

        wo_last = None
        for dc in range(8):
            bank = dc % 2

            def womm(h, dc=dc, bank=bank):
                ins = None
                for ci in range(8):
                    ins = h.matmul(E.pb[bank][:, 0:512],
                                   wo[:, ci * 1024 + dc * 128: ci * 1024 + dc * 128 + 128],
                                   att_sb[:, ci * 512:(ci + 1) * 512],
                                   start=(ci == 0), stop=(ci == 7))
                return ins
            t = s.emit("tensor", lambda h, f=womm: f(h),
                       deps=[att_done] + E.bank.wdeps(bank))
            v = s.emit("vector", lambda h, dc=dc, bank=bank: h.scalar_tensor_tensor(
                xg32[:, dc * 512:(dc + 1) * 512], E.pb[bank][:, 0:512],
                E.bias["b0o"][:, dc:dc + 1], xg32[:, dc * 512:(dc + 1) * 512],
                ALU.add, ALU.add),
                deps=[t, ld32, E.bias_tok])
            E.bank.read(bank, v)
            wo_last = v

        if BIS == "wo":
            s.dma("sync", lambda h: h.dma_start(
                out=rearr_groups(E.ap["r1"])[:, :, 0:512],
                in_=xg32[:].rearrange("p (c t) -> p c t", t=512)),
                deps=[wo_last], sem="DST0")
            return

        def dst_fn(dc, tt, g=g):
            return rearr_groups(E.ap["r1"])[:, dc, g * 512:(g + 1) * 512]
        emit_postln(E, xg32, 512, dst_fn, [wo_last], E.eps[0:1, 0:1], rbf=att_sb, sqb=xgbf)
        xg32_guard = [s.tok("V"), s.tok("A")]
        xgbf_guard = list(E.ln_guard)
        gg = [s.tok("T")]


def emit_p2(E):
    s = E.s
    wd = E.p2_wd
    wd_toks = []
    for part in range(3):
        wd_toks.append(s.dma("gpsimd", lambda h, part=part: h.dma_start(
            out=wd[:, part * 8192:(part + 1) * 8192].rearrange("p (c f) -> p c f", f=1024),
            in_=rearr_cpf(E.ap["w0d_t"])[:, part * 8:(part + 1) * 8, :]), sem="DWD"))
    wg4, wu4 = E.p2_wg, E.p2_wu
    h_sb = E.p2_h
    x32 = E.p2_x32
    xbf = E.p2_xbf
    x32_guard = []
    xbf_guard = []
    w_guard = [[], []]
    h_guard = []
    for ei in range(8):
        ld32 = s.dma("sync", lambda h, ei=ei: h.dma_start(
            out=x32[:].rearrange("p (c t) -> p c t", t=512),
            in_=rearr_groups(E.ap["r1"])[:, :, ei * 512:(ei + 1) * 512]),
            deps=x32_guard, sem="DX32")
        ldbf = s.dma("gpsimd", lambda h, ei=ei: h.dma_start(
            out=xbf[:].rearrange("p (c t) -> p c t", t=512),
            in_=rearr_groups(E.ap["r1"])[:, :, ei * 512:(ei + 1) * 512]),
            deps=xbf_guard + [wd_toks[-1]], sem="DXBF")
        x32_guard = []
        xbf_guard = []
        for wgrp in range(6):
            slot = wgrp % 2
            ldg = s.dma("gpsimd", lambda h, wgrp=wgrp, slot=slot: h.dma_start(
                out=wg4[slot][:].rearrange("p (c f) -> p c f", f=512),
                in_=rearr_cpf(E.ap["w0g_t"])[:, :, wgrp * 512:(wgrp + 1) * 512]),
                deps=w_guard[slot], sem=f"DWG{slot}")
            ldu = s.dma("gpsimd", lambda h, wgrp=wgrp, slot=slot: h.dma_start(
                out=wu4[slot][:].rearrange("p (c f) -> p c f", f=512),
                in_=rearr_cpf(E.ap["w0u_t"])[:, :, wgrp * 512:(wgrp + 1) * 512]),
                deps=w_guard[slot], sem=f"DWU{slot}")
            w_guard[slot] = []
            for h4 in range(4):
                hc = wgrp * 4 + h4
                bkg = hc % 2
                bku = 2 + hc % 2

                def gmm(h, h4=h4, bkg=bkg, slot=slot):
                    ins = None
                    for ci in range(8):
                        ins = h.matmul(E.pb[bkg][:, 0:512],
                                       wg4[slot][:, ci * 512 + h4 * 128: ci * 512 + h4 * 128 + 128],
                                       xbf[:, ci * 512:(ci + 1) * 512],
                                       start=(ci == 0), stop=(ci == 7))
                    return ins
                tg = s.emit("tensor", lambda h, f=gmm: f(h),
                            deps=[ldbf, ldg] + E.bank.wdeps(bkg))

                def umm(h, h4=h4, bku=bku, slot=slot):
                    ins = None
                    for ci in range(8):
                        ins = h.matmul(E.pb[bku][:, 0:512],
                                       wu4[slot][:, ci * 512 + h4 * 128: ci * 512 + h4 * 128 + 128],
                                       xbf[:, ci * 512:(ci + 1) * 512],
                                       start=(ci == 0), stop=(ci == 7))
                    return ins
                tu = s.emit("tensor", lambda h, f=umm: f(h),
                            deps=[ldbf, ldu] + E.bank.wdeps(bku))
                ssl = hc % 2
                sgs = E.sg[ssl]
                ag = s.emit("scalar", lambda h, bkg=bkg, sgs=sgs: h.activation(
                    out=sgs[:], in_=E.pb[bkg][:, 0:512], func=AF.Silu, scale=INV127),
                    deps=[tg] + E.sg_guard[ssl])
                E.bank.read(bkg, ag)
                vh = s.emit("vector", lambda h, hc=hc, bku=bku, sgs=sgs: h.tensor_mul(
                    h_sb[:, hc * 512:(hc + 1) * 512], E.pb[bku][:, 0:512], sgs[:]),
                    deps=[tu, ag] + h_guard)
                h_guard = []
                E.bank.read(bku, vh)
                E.sg_guard[ssl] = [vh]
            w_guard[slot] = [s.tok("T")]
        h_done = s.tok("V")
        dn_last = None
        for dc in range(8):
            bank = 4 + dc % 2

            def dmm(h, dc=dc, bank=bank):
                ins = None
                for hc in range(24):
                    ins = h.matmul(E.pb[bank][:, 0:512],
                                   wd[:, hc * 1024 + dc * 128: hc * 1024 + dc * 128 + 128],
                                   h_sb[:, hc * 512:(hc + 1) * 512],
                                   start=(hc == 0), stop=(hc == 23))
                return ins
            t = s.emit("tensor", lambda h, f=dmm: f(h),
                       deps=[h_done, wd_toks[-1]] + E.bank.wdeps(bank))
            v = s.emit("vector", lambda h, dc=dc, bank=bank: h.scalar_tensor_tensor(
                x32[:, dc * 512:(dc + 1) * 512], E.pb[bank][:, 0:512], INV127SQ,
                x32[:, dc * 512:(dc + 1) * 512], ALU.mult, ALU.add),
                deps=[t, ld32])
            E.bank.read(bank, v)
            dn_last = v
        h_guard = [s.tok("T")]

        def dst_fn(dc, tt, ei=ei):
            return rearr_groups(E.ap["r2"])[:, dc, ei * 512:(ei + 1) * 512]
        emit_postln(E, x32, 512, dst_fn, [dn_last], E.eps[0:1, 0:1],
                    rbf=h_sb, sqb=xbf)
        h_guard = h_guard + E.ln_guard
        x32_guard = [s.tok("V"), s.tok("A")]
        xbf_guard = list(E.ln_guard)


def emit_p3(E, pc):
    s = E.s
    nc = E.nc
    kfm = E.p3_k
    vext = E.p3_vext
    wbuf = E.p3_wbuf
    actx = ExitStack()
    wvb = actx.enter_context(nc.sbuf_tensor("p3_wv", [128, 8 * 1024], BF16))
    E.p3_bvrow = actx.enter_context(nc.sbuf_tensor("p3_bvrow", [1, D], F32))
    bvbc = actx.enter_context(nc.sbuf_tensor("p3_bvbc", [128, 1024], F32))
    xbf1 = actx.enter_context(nc.sbuf_tensor("p3_xbf1", [128, 8 * 512], BF16))
    ms = s.emit("vector", lambda h: h.memset(vext[:], 1.0))
    wk_t = s.dma("gpsimd", lambda h: h.dma_start(
        out=wbuf[:].rearrange("p (c f) -> p c f", f=1024), in_=rearr_cpf(E.ap["w1k_t"])),
        sem="DW3")
    wv_t = s.dma("gpsimd", lambda h: h.dma_start(
        out=wvb[:].rearrange("p (c f) -> p c f", f=1024), in_=rearr_cpf(E.ap["w1v_t"])),
        sem="DWV")
    bvr = s.dma("sync", lambda h: h.dma_start(out=E.p3_bvrow[:], in_=E.ap["b1v_row"]), sem="DC")
    bv_toks = []
    for half in range(2):
        t = s.emit("tensor", lambda h, half=half: h.matmul(
            E.pb[2][:, 0:512], E.ones_f[0:1, 0:128],
            E.p3_bvrow[0:1, half * 512:half * 512 + 512], start=True, stop=True),
            deps=[bvr] + E.bank.wdeps(2))
        a = s.emit("scalar", lambda h, half=half: h.copy(bvbc[:, half * 512:half * 512 + 512],
                                                         E.pb[2][:, 0:512]), deps=[t])
        E.bank.read(2, a)
        bv_toks.append(a)

    xbf = [E.p3_xbf0, xbf1]
    xbf_guard = [[], []]
    for tt in range(8):
        slot = tt % 2
        ldbf = s.dma("gpsimd", lambda h, tt=tt, slot=slot: h.dma_start(
            out=xbf[slot][:].rearrange("p (c t) -> p c t", t=512),
            in_=rearr_groups(E.ap["r2"])[:, :, tt * 512:(tt + 1) * 512]),
            deps=xbf_guard[slot] + [wv_t], sem=f"DXB{slot}")
        xbf_guard[slot] = []
        for dc in range(8):
            bank = dc % 2

            def kmm(h, dc=dc, bank=bank, slot=slot):
                ins = None
                for ci in range(8):
                    ins = h.matmul(E.pb[bank][:, 0:512],
                                   wbuf[:, ci * 1024 + dc * 128: ci * 1024 + dc * 128 + 128],
                                   xbf[slot][:, ci * 512:(ci + 1) * 512],
                                   start=(ci == 0), stop=(ci == 7))
                return ins
            t = s.emit("tensor", lambda h, f=kmm: f(h), deps=[ldbf, wk_t] + E.bank.wdeps(bank))
            v = s.emit("vector", lambda h, tt=tt, dc=dc, bank=bank: h.tensor_scalar(
                kfm[:, dc * 4096 + tt * 512: dc * 4096 + tt * 512 + 512],
                E.pb[bank][:, 0:512], E.bias["b1k"][:, dc:dc + 1], None, ALU.add),
                deps=[t, E.bias_tok])
            E.bank.read(bank, v)
        for tc in range(4):
            for half in range(2):
                bank = 2 + (tc * 2 + half) % 2

                def vmm(h, tc=tc, half=half, bank=bank, slot=slot):
                    ins = None
                    for ci in range(8):
                        ins = h.matmul(E.pb[bank][:, 0:512],
                                       xbf[slot][:, ci * 512 + tc * 128: ci * 512 + tc * 128 + 128],
                                       wvb[:, ci * 1024 + half * 512: ci * 1024 + half * 512 + 512],
                                       start=(ci == 0), stop=(ci == 7))
                    return ins
                t = s.emit("tensor", lambda h, f=vmm: f(h), deps=[ldbf, wv_t] + E.bank.wdeps(bank))
                base = (tt * 4 + tc) * 1040 + half * 520

                def vev(h, bank=bank, base=base, half=half):
                    dst = vext[:, base:base + 520].rearrange("p (hh e) -> p hh e", e=65)[:, :, 0:64]
                    src = E.pb[bank][:, 0:512].rearrange("p (hh e) -> p hh e", e=64)
                    bcv = bvbc[:, half * 512:half * 512 + 512].rearrange("p (hh e) -> p hh e", e=64)
                    return h.tensor_add(dst, src, bcv)
                v = s.emit("vector", lambda h, f=vev: f(h), deps=[t, ms] + bv_toks)
                E.bank.read(bank, v)
        xbf_guard[slot] = [s.tok("T")]
    a_done_v = s.tok("V")
    actx.close()
    a_fin = [s.tok("T"), s.tok("V"), s.tok("A")]

    p = lambda *a: pc.enter_context(nc.sbuf_tensor(*a))
    q_sb = p("p3_q", [128, 8 * 512], BF16)
    att_sb = p("p3_att", [128, 8 * 512], BF16)
    r32 = p("p3_r32", [128, 8 * 512], F32)
    E.expst3 = [p(f"p3_expst{i}", [128, 1024], BF16) for i in range(3)]
    E.expst3_g = [list(a_fin) for _ in range(3)]
    E.p3_rec = p("p3_rec", [128, 512], BF16)
    E.p3_bcsb = [p("p3_bcsb0", [64, 512], F32)]
    E.p3_bcsb = E.p3_bcsb * 2
    E.p3_tmp = [p("p3_tmp0", [64, 512], BF16)] * 2
    wload_guard = [s.tok("T")]
    q_guard = list(a_fin)
    att_guard = list(a_fin)
    r32_guard = list(a_fin)
    expst_g = [list(a_fin), list(a_fin)]
    rec_g = list(a_fin)
    bcsb_g = [list(a_fin), list(a_fin)]
    tmp_g = [list(a_fin), list(a_fin)]
    for qt in range(8):
        wq_t = s.dma("gpsimd", lambda h: h.dma_start(
            out=wbuf[:].rearrange("p (c f) -> p c f", f=1024), in_=rearr_cpf(E.ap["w1q_t"])),
            deps=wload_guard, sem="DW3")
        ldr = s.dma("sync", lambda h, qt=qt: h.dma_start(
            out=r32[:].rearrange("p (c t) -> p c t", t=512),
            in_=rearr_groups(E.ap["r2"])[:, :, qt * 512:(qt + 1) * 512]),
            deps=r32_guard, sem="DR32")
        ldq = s.dma("gpsimd", lambda h, qt=qt: h.dma_start(
            out=xbf[0][:].rearrange("p (c t) -> p c t", t=512),
            in_=rearr_groups(E.ap["r2"])[:, :, qt * 512:(qt + 1) * 512]),
            deps=xbf_guard[0], sem="DXB0")
        r32_guard = []
        xbf_guard[0] = []
        q_last = None
        for dc in range(8):
            bank = dc % 2

            def qmm(h, dc=dc, bank=bank):
                ins = None
                for ci in range(8):
                    ins = h.matmul(E.pb[bank][:, 0:512],
                                   wbuf[:, ci * 1024 + dc * 128: ci * 1024 + dc * 128 + 128],
                                   xbf[0][:, ci * 512:(ci + 1) * 512],
                                   start=(ci == 0), stop=(ci == 7))
                return ins
            t = s.emit("tensor", lambda h, f=qmm: f(h),
                       deps=[ldq, wq_t, a_done_v] + E.bank.wdeps(bank))
            v = s.emit("vector", lambda h, dc=dc, bank=bank: h.tensor_scalar(
                q_sb[:, dc * 512:(dc + 1) * 512], E.pb[bank][:, 0:512],
                E.bias["b1q"][:, dc:dc + 1], None, ALU.add),
                deps=[t, E.bias_tok] + q_guard)
            q_guard = []
            E.bank.read(bank, v)
            q_last = v
        qmm_done = s.tok("T")
        wo_t = s.dma("gpsimd", lambda h: h.dma_start(
            out=wbuf[:].rearrange("p (c f) -> p c f", f=1024), in_=rearr_cpf(E.ap["w1o_t"])),
            deps=[qmm_done], sem="DW3")

        SCP = ((0, 1), (2, 3), (6, 7))
        for hp in range(8):
            pvA = E.pb[4]
            pvB = E.pb[5]
            pvA_w = E.bank.wdeps(4)
            pvB_w = E.bank.wdeps(5)
            pv_tok = {"A": None, "B": None}
            e_tok = {}

            def stm(h, hlo, k, hp, bank):
                return h.matmul(E.pb[bank][:, 0:512],
                                kfm[hlo:hlo + 64, hp * 4096 + k * 128: hp * 4096 + k * 128 + 128],
                                q_sb[hlo:hlo + 64, hp * 512:(hp + 1) * 512],
                                start=True, stop=True, tile_position=(hlo, 0))

            def pvm(h, hh, k, side, dst):
                return h.matmul(dst[0:65, 0:512],
                                vext[:, k * 1040 + hh * 65: k * 1040 + hh * 65 + 65],
                                E.expst3[k % 3][:, side * 512: side * 512 + 512],
                                start=(k == 0), stop=(k == 31))

            for ktc in range(34):
                if ktc < 32:
                    par = ktc % 3
                    bA, bB = SCP[par]
                    tA = s.emit("tensor", lambda h, ktc=ktc, hp=hp, bA=bA: stm(h, 0, ktc, hp, bA),
                                deps=[q_last, a_done_v] + E.bank.wdeps(bA) + E.bank.wdeps(bB))
                    tB = s.emit("tensor", lambda h, ktc=ktc, hp=hp, bB=bB: stm(h, 64, ktc, hp, bB),
                                deps=[])
                    ee = s.emit("scalar", lambda h, par=par, bA=bA: h.activation(
                        out=E.expst3[par][:], in_=E.pbig[:, bA * 512: bA * 512 + 1024],
                        func=AF.Exp, scale=SC1),
                        deps=[tA, tB] + E.expst3_g[par])
                    E.bank.read(bA, ee)
                    E.bank.read(bB, ee)
                    e_tok[ktc] = ee
                if ktc >= 2:
                    kprev = ktc - 2
                    pv_tok["A"] = s.emit("tensor",
                                         lambda h, kprev=kprev, hp=hp: pvm(
                                             h, 2 * hp, kprev, 0, pvA),
                                         deps=[e_tok[kprev]] + (pvA_w if kprev == 0 else []))
                    pv_tok["B"] = s.emit("tensor",
                                         lambda h, kprev=kprev, hp=hp: pvm(
                                             h, 2 * hp + 1, kprev, 1, pvB),
                                         deps=(pvB_w if kprev == 0 else []))
                    E.expst3_g[kprev % 3] = [pv_tok["B"]]
            for (side, hh, pvb, pv_ps) in (("A", 2 * hp, 4, pvA), ("B", 2 * hp + 1, 5, pvB)):
                rec = E.p3_rec
                def recf(h, pv_ps=pv_ps):
                    with nc.allow_low_precision(reason="softmax denom recip rounds to bf16; att output is bf16 anyway"):
                        return h.reciprocal(rec[64:65, 0:512], pv_ps[64:65, 0:512])
                r1 = s.emit("vector", recf, deps=[pv_tok[side]] + rec_g)
                t3 = s.emit("tensor", lambda h: h.matmul(
                    E.pb[6][0:64, 0:512], E.ones_bf[64:65, 0:64], rec[64:65, 0:512],
                    start=True, stop=True), deps=[r1] + E.bank.wdeps(6))
                rec_g = [t3]
                a3 = s.emit("scalar", lambda h: h.copy(
                    E.p3_bcsb[0][0:64, :], E.pb[6][0:64, 0:512]),
                    deps=[t3] + bcsb_g[0] + bcsb_g[1])
                E.bank.read(6, a3)
                if side == "A":
                    v3 = s.emit("vector", lambda h, hp=hp, pv_ps=pv_ps: h.tensor_mul(
                        att_sb[0:64, hp * 512:(hp + 1) * 512], pv_ps[0:64, 0:512],
                        E.p3_bcsb[0][0:64, :]),
                        deps=[a3, pv_tok[side]] + att_guard)
                    E.bank.read(pvb, v3)
                    bcsb_g = [[v3], []]
                else:
                    tsl = 0
                    v3 = s.emit("vector", lambda h, pv_ps=pv_ps, tsl=tsl: h.tensor_mul(
                        E.p3_tmp[tsl][0:64, :], pv_ps[0:64, 0:512], E.p3_bcsb[0][0:64, :]),
                        deps=[a3, pv_tok[side]] + tmp_g[tsl])
                    E.bank.read(pvb, v3)
                    bcsb_g = [[v3], []]
                    d3 = s.dma("sync", lambda h, hp=hp, tsl=tsl: h.dma_start(
                        out=att_sb[64:128, hp * 512:(hp + 1) * 512], in_=E.p3_tmp[tsl][0:64, :]),
                        deps=[v3] + att_guard, sem=f"DTMP{tsl}")
                    tmp_g[tsl] = [d3]
        att_done = [s.tok("V"), s.tok("DTMP0"), s.tok("DTMP1")]

        wo_last = None
        for dc in range(8):
            bank = dc % 2

            def womm(h, dc=dc, bank=bank):
                ins = None
                for ci in range(8):
                    ins = h.matmul(E.pb[bank][:, 0:512],
                                   wbuf[:, ci * 1024 + dc * 128: ci * 1024 + dc * 128 + 128],
                                   att_sb[:, ci * 512:(ci + 1) * 512],
                                   start=(ci == 0), stop=(ci == 7))
                return ins
            t = s.emit("tensor", lambda h, f=womm: f(h),
                       deps=att_done + [wo_t] + E.bank.wdeps(bank))
            v = s.emit("vector", lambda h, dc=dc, bank=bank: h.scalar_tensor_tensor(
                r32[:, dc * 512:(dc + 1) * 512], E.pb[bank][:, 0:512],
                E.bias["b1o"][:, dc:dc + 1], r32[:, dc * 512:(dc + 1) * 512],
                ALU.add, ALU.add),
                deps=[t, ldr, E.bias_tok])
            E.bank.read(bank, v)
            wo_last = v
        wload_guard = [s.tok("T")]

        def dst_fn(dc, tt, qt=qt):
            return rearr_groups(E.ap["r3"])[:, dc, qt * 512:(qt + 1) * 512]
        emit_postln(E, r32, 512, dst_fn, [wo_last], E.eps[0:1, 0:1],
                    rbf=att_sb, sqb=xbf[0])
        att_guard = list(E.ln_guard)
        q_guard = [s.tok("T")]
        xbf_guard[0] = list(E.ln_guard)
        r32_guard = [s.tok("V"), s.tok("A")]


def emit_p4(E, out_name):
    s = E.s
    r3q = E.p4_r3
    xn = E.p4_xn
    y_acc = E.p4_yacc
    h_sb = E.p4_h
    mfm = E.p4_mfm
    mbc = E.p4_mbc
    w1b = E.p4_w1
    w2b = E.p4_w2
    gw_tok = s.dma("gpsimd", lambda h: h.dma_start(
        out=E.gw_sb[:].rearrange("p (c e) -> p c e", e=8), in_=rearr_cpf(E.ap["gw_t"])),
        sem="DGW")
    s.dma("sync", lambda h: h.dma_start(out=E.b2_sb[:], in_=E.ap["b2"]), sem="DC")
    s.dma("sync", lambda h: h.dma_start(out=E.sel_f[:], in_=E.c_sel), sem="DC")
    s.dma("sync", lambda h: h.dma_start(out=E.b1f1[:], in_=E.ap["b1f1"]), sem="DC")
    b2_tok = sel_tok = b1f_tok = s.tok("DC")

    r3_guard = []
    xn_guard = []
    yacc_guard = []
    h_guard = []
    mfm_guard = []
    mbc_guard = []
    w_guard = [[], []]
    lg_guard = []
    for qtr in range(4):
        ld32 = s.dma("sync", lambda h, qtr=qtr: h.dma_start(
            out=r3q[:].rearrange("p (c t) -> p c t", t=1024),
            in_=rearr_groups(E.ap["r3"])[:, :, qtr * 1024:(qtr + 1) * 1024]),
            deps=r3_guard, sem="DR32")
        r3_guard = []
        asq = s.emit("scalar", lambda h: h.activation(out=xn[:], in_=r3q[:], func=AF.Square),
                     deps=[ld32] + xn_guard)
        xn_guard = []
        stat_toks = []
        bc_toks = []
        for tt in range(2):
            def mms(h, tt=tt):
                ins = None
                for dc in range(8):
                    ins = h.matmul(E.pb[2][0:1, 0:512], E.invD_bf[:, 0:1],
                                   xn[:, dc * 1024 + tt * 512: dc * 1024 + tt * 512 + 512],
                                   start=(dc == 0), stop=(dc == 7))
                return ins
            tm = s.emit("tensor", lambda h, f=mms: f(h), deps=[asq] + E.bank.wdeps(2))
            vrow = E.ln_rows[0:1, 1024:1536]
            rstd = vrow
            a1 = s.emit("scalar", lambda h: h.activation(
                out=vrow, in_=E.pb[2][0:1, 0:512], func=AF.Ln, bias=E.eps[0:1, 1:2]),
                deps=[tm] + E.rows_guard)
            E.rows_guard = []
            E.bank.read(2, a1)
            a2 = s.emit("scalar", lambda h: h.activation(
                out=rstd, in_=vrow, func=AF.Exp, scale=-0.5), deps=[a1])
            t_b = s.emit("tensor", lambda h, tt=tt: h.matmul(
                E.pb[6 + tt][:, 0:512], E.ones_f[0:1, 0:128], rstd, start=True, stop=True),
                deps=[a2] + E.bank.wdeps(6 + tt))
            E.rows_guard = [t_b]
            stat_toks.append(tm)
            bc_toks.append(t_b)
        xn_last = None
        for tt in range(2):
            for dc in range(8):
                v = s.emit("vector", lambda h, dc=dc, tt=tt: h.scalar_tensor_tensor(
                    xn[:, dc * 1024 + tt * 512: dc * 1024 + tt * 512 + 512],
                    r3q[:, dc * 1024 + tt * 512: dc * 1024 + tt * 512 + 512],
                    E.bias["rmsw"][:, dc:dc + 1], E.pb[6 + tt][:, 0:512],
                    ALU.mult, ALU.mult),
                    deps=[bc_toks[tt], stat_toks[0], stat_toks[1], ld32, E.bias_tok])
                E.bank.read(6 + tt, v)
                xn_last = v
        xn_done = xn_last

        prev_a1 = None
        prev_t2 = None
        mask_last = None
        for tcl in range(8):
            def lmm(h, tcl=tcl):
                ins = None
                for ci in range(8):
                    ins = h.matmul(E.pb[2][0:128, 0:8],
                                   xn[:, ci * 1024 + tcl * 128: ci * 1024 + tcl * 128 + 128],
                                   E.gw_sb[:, ci * 8:(ci + 1) * 8],
                                   start=(ci == 0), stop=(ci == 7))
                return ins
            t1 = s.emit("tensor", lambda h, f=lmm: f(h),
                        deps=[xn_done, gw_tok] + E.bank.wdeps(2))
            G = E.p4_g
            v1 = s.emit("vector", lambda h: h.tensor_copy(G[:, 0:8], E.pb[2][0:128, 0:8]),
                        deps=[t1] + lg_guard + ([prev_a1] if prev_a1 else []))
            lg_guard = []
            E.bank.read(2, v1)
            v2 = s.emit("vector", lambda h: h.reduce_max(G[:, 8:9], G[:, 0:8], axis=AX),
                        deps=[v1])
            v3 = s.emit("vector", lambda h: h.tensor_scalar_mul(G[:, 9:10], G[:, 8:9], -1.0),
                        deps=[v2])
            a1 = s.emit("scalar", lambda h: h.activation(out=G[:, 10:18], in_=G[:, 0:8],
                                                         func=AF.Exp, bias=G[:, 9:10]),
                        deps=[v3])
            v4 = s.emit("vector", lambda h: h.reduce_sum(G[:, 18:19], G[:, 10:18], axis=AX),
                        deps=[a1])
            v5 = s.emit("vector", lambda h: h.reciprocal(G[:, 19:20], G[:, 18:19]), deps=[v4])
            v6 = s.emit("vector", lambda h: h.tensor_scalar(G[:, 20:28], G[:, 0:8], G[:, 8:9],
                                                            None, ALU.is_equal),
                        deps=[v5] + ([prev_t2] if prev_t2 else []))
            v7 = s.emit("vector", lambda h: h.tensor_scalar(G[:, 20:28], G[:, 20:28],
                                                            G[:, 19:20], None, ALU.mult),
                        deps=[v6])
            t2 = s.emit("tensor", lambda h: h.transpose(E.pb[3][0:8, 0:128], G[:, 20:28],
                                                        E.ident_f[:]),
                        deps=[v7, E.ident_tok] + E.bank.wdeps(3))
            v8 = s.emit("vector", lambda h, tcl=tcl: h.tensor_copy(
                mfm[0:8, tcl * 128:(tcl + 1) * 128], E.pb[3][0:8, 0:128]),
                deps=[t2] + mfm_guard)
            mfm_guard = []
            E.bank.read(3, v8)
            prev_a1 = a1
            prev_t2 = t2
            mask_last = v8
        lg_guard = [s.tok("T")]
        mask_done = mask_last

        y_last = None
        for tt in range(2):
            for dc in range(8):
                bank = (tt * 8 + dc) % 2
                t = s.emit("tensor", lambda h, tt=tt, dc=dc, bank=bank: h.matmul(
                    E.pb[bank][:, 0:512], E.b2_sb[0:8, dc * 128:(dc + 1) * 128],
                    mfm[0:8, tt * 512:(tt + 1) * 512], start=True, stop=True),
                    deps=[mask_done, b2_tok] + E.bank.wdeps(bank))
                v = s.emit("vector", lambda h, tt=tt, dc=dc, bank=bank: h.tensor_copy(
                    y_acc[:, dc * 1024 + tt * 512: dc * 1024 + tt * 512 + 512],
                    E.pb[bank][:, 0:512]),
                    deps=[t] + yacc_guard)
                yacc_guard = []
                E.bank.read(bank, v)
                y_last = v

        for e in range(NE):
            mb_last = None
            for tt in range(2):
                bank = 6 + tt % 2
                t = s.emit("tensor", lambda h, e=e, tt=tt, bank=bank: h.matmul(
                    E.pb[bank][:, 0:512], E.sel_f[0:8, e * 128:(e + 1) * 128],
                    mfm[0:8, tt * 512:(tt + 1) * 512], start=True, stop=True),
                    deps=[mask_done, sel_tok] + E.bank.wdeps(bank))
                a = s.emit("scalar", lambda h, tt=tt, bank=bank: h.copy(
                    mbc[:, tt * 512:(tt + 1) * 512], E.pb[bank][:, 0:512]),
                    deps=[t] + mbc_guard)
                mbc_guard = []
                E.bank.read(bank, a)
                mb_last = a
            for hcg in range(4):
                slot = hcg % 2
                ld1 = s.dma("gpsimd", lambda h, e=e, hcg=hcg, slot=slot: h.dma_start(
                    out=w1b[slot][:].rearrange("p (c f) -> p c f", f=1024),
                    in_=rearr_cpf(E.ap["w1ff1"][e])[:, :, hcg * 1024:(hcg + 1) * 1024]),
                    deps=w_guard[slot], sem=f"D41{slot}")
                ld2 = s.dma("gpsimd", lambda h, e=e, hcg=hcg, slot=slot: h.dma_start(
                    out=w2b[slot][:].rearrange("p (c f) -> p c f", f=1024),
                    in_=rearr_cpf(E.ap["w1ff2"][e])[:, hcg * 8:(hcg + 1) * 8, :]),
                    deps=w_guard[slot], sem=f"D42{slot}")
                w_guard[slot] = []
                h_last = None
                for hcl in range(8):
                    for tt in range(2):
                        bank = (hcl * 2 + tt) % 2

                        def gmm(h, hcl=hcl, tt=tt, bank=bank, slot=slot):
                            ins = None
                            for ci in range(8):
                                ins = h.matmul(
                                    E.pb[bank][:, 0:512],
                                    w1b[slot][:, ci * 1024 + hcl * 128: ci * 1024 + hcl * 128 + 128],
                                    xn[:, ci * 1024 + tt * 512: ci * 1024 + tt * 512 + 512],
                                    start=(ci == 0), stop=(ci == 7))
                            return ins
                        tg = s.emit("tensor", lambda h, f=gmm: f(h),
                                    deps=[ld1, xn_done] + E.bank.wdeps(bank))
                        ssl = (hcl * 2 + tt) % 2
                        sgs = E.sg[ssl]
                        bcol = E.b1f1[:, e * 32 + hcg * 8 + hcl: e * 32 + hcg * 8 + hcl + 1]
                        ag = s.emit("scalar", lambda h, bank=bank, sgs=sgs, bcol=bcol:
                                    h.activation(out=sgs[:], in_=E.pb[bank][:, 0:512],
                                                 func=AF.Silu, bias=bcol),
                                    deps=[tg, b1f_tok] + E.sg_guard[ssl])
                        E.bank.read(bank, ag)
                        vh = s.emit("vector", lambda h, hcl=hcl, tt=tt, sgs=sgs: h.tensor_mul(
                            h_sb[:, hcl * 1024 + tt * 512: hcl * 1024 + tt * 512 + 512],
                            sgs[:], mbc[:, tt * 512:(tt + 1) * 512]),
                            deps=[ag, mb_last] + h_guard)
                        h_guard = []
                        E.sg_guard[ssl] = [vh]
                        h_last = vh
                for tt in range(2):
                    for dc in range(8):
                        bank = 2 + (tt * 8 + dc) % 2

                        def ymm(h, tt=tt, dc=dc, bank=bank, slot=slot):
                            ins = None
                            for hcl in range(8):
                                ins = h.matmul(
                                    E.pb[bank][:, 0:512],
                                    w2b[slot][:, hcl * 1024 + dc * 128: hcl * 1024 + dc * 128 + 128],
                                    h_sb[:, hcl * 1024 + tt * 512: hcl * 1024 + tt * 512 + 512],
                                    start=(hcl == 0), stop=(hcl == 7))
                            return ins
                        t = s.emit("tensor", lambda h, f=ymm: f(h),
                                   deps=[ld2, h_last] + E.bank.wdeps(bank))
                        v = s.emit("vector", lambda h, tt=tt, dc=dc, bank=bank: h.tensor_add(
                            y_acc[:, dc * 1024 + tt * 512: dc * 1024 + tt * 512 + 512],
                            y_acc[:, dc * 1024 + tt * 512: dc * 1024 + tt * 512 + 512],
                            E.pb[bank][:, 0:512]),
                            deps=[t, y_last])
                        E.bank.read(bank, v)
                        y_last = v
                h_guard = [s.tok("T")]
                w_guard[slot] = [s.tok("T")]
            mbc_guard = [s.tok("V")]

        radd_last = None
        for i in range(2):
            radd_last = s.emit("vector", lambda h, i=i: h.tensor_add(
                y_acc[:, i * 4096:(i + 1) * 4096], y_acc[:, i * 4096:(i + 1) * 4096],
                r3q[:, i * 4096:(i + 1) * 4096]), deps=[y_last, ld32])

        def dst_fn(dc, tt, qtr=qtr):
            return rearr_groups(E.ap[out_name])[:, dc, qtr * 1024 + tt * 512: qtr * 1024 + tt * 512 + 512]
        emit_postln(E, y_acc, 1024, dst_fn, [radd_last], E.eps[0:1, 0:1],
                    rbf=h_sb, sqb=xn)
        vtok, atok = s.tok("V"), s.tok("A")
        r3_guard = [vtok, atok]
        yacc_guard = [vtok, atok]
        xn_guard = list(E.ln_guard)
        h_guard = list(E.ln_guard)
        mfm_guard = [s.tok("T")]
        mbc_guard = [s.tok("V")]


def build_nc(phases=("p1", "p2", "p3", "p4")):
    nc = bass.Bass("TRN2", target_bir_lowering=False, debug=False)
    E = Env()
    E.nc = nc
    s = Sched()
    E.s = s
    E.bank = BankGuard()

    ap = {}
    ap["x0"] = nc.dram_tensor("x0", [128, 8 * S], F32, kind="ExternalInput").ap()
    chain = {"p1": ("x_in", "r1"), "p2": ("r1", "r2"), "p3": ("r2", "r3"), "p4": ("r3", "y")}
    needed = set()
    for ph in phases:
        needed.add(chain[ph][0])
        needed.add(chain[ph][1])
    needed.discard("x_in")
    out_hbm = chain[phases[-1]][1]
    in_hbm = chain[phases[0]][0]
    for name in sorted(needed):
        if name == out_hbm:
            ap[name] = nc.dram_tensor(name, [128, 8 * S], F32, kind="ExternalOutput").ap()
        else:
            ap[name] = nc.dram_tensor(name, [128, 8 * S], F32, kind="Internal").ap()
    if in_hbm != "x_in":
        ap[in_hbm] = ap["x0"]
    else:
        ap["x_in"] = ap["x0"]

    def din(name, shape, dt):
        ap[name] = nc.dram_tensor(name, shape, dt, kind="ExternalInput").ap()
    din("w0q", [D, D], F32); din("w0k", [D, D], F32); din("w0v", [D, D], F32); din("w0o", [D, D], F32)
    din("b0q", [128, 8], F32); din("b0k", [128, 8], F32); din("b0o", [128, 8], F32)
    din("b0v_row", [1, D], F32)
    din("w0g_t", [D, FF0], I32); din("w0u_t", [D, FF0], I32); din("w0d_t", [FF0, D], I32)
    din("w1q_t", [D, D], F32); din("w1k_t", [D, D], F32); din("w1v_t", [D, D], F32)
    din("w1o_t", [D, D], F32)
    din("b1q", [128, 8], F32); din("b1k", [128, 8], F32); din("b1o", [128, 8], F32)
    din("b1v_row", [1, D], F32)
    din("rmsw", [128, 8], F32)
    din("gw_t", [D, NE], F32)
    din("b1f1", [128, 256], F32)
    din("b2", [NE, D], F32)
    ap["w1ff1"] = [nc.dram_tensor(f"w1ff1_{e}", [D, FF1], F32, kind="ExternalInput").ap()
                   for e in range(NE)]
    ap["w1ff2"] = [nc.dram_tensor(f"w1ff2_{e}", [FF1, D], F32, kind="ExternalInput").ap()
                   for e in range(NE)]
    c_ones_bf = nc.inline_tensor(np.ones((128, 128), dtype=BF16NP), "c_ones_bf").ap()
    c_ones_f = nc.inline_tensor(np.ones((128, 128), dtype=np.float32), "c_ones_f").ap()
    c_neg_f = nc.inline_tensor(np.full((1, 128), -1.0, dtype=np.float32), "c_neg_f").ap()
    c_invd = nc.inline_tensor(np.full((128, 1), 1.0 / 1024.0, dtype=BF16NP), "c_invd").ap()
    c_ident = nc.inline_tensor(np.eye(128, dtype=np.float32), "c_ident").ap()
    selm = np.zeros((8, 8 * 128), dtype=np.float32)
    for e in range(8):
        selm[e, e * 128:(e + 1) * 128] = 1.0
    E.c_sel = nc.inline_tensor(selm, "c_sel").ap()
    c_eps = nc.inline_tensor(np.array([[EPS_LN, EPS_RMS]], dtype=np.float32), "c_eps").ap()
    E.ap = ap

    with ExitStack() as ctx:
        sb = lambda *a: ctx.enter_context(nc.sbuf_tensor(*a))
        psb = lambda *a: ctx.enter_context(nc.psum_tensor(*a))
        E.pbig = psb("pbig", [128, 4096], F32)
        E.pb = [E.pbig[:, i * 512:(i + 1) * 512] for i in range(8)]
        E.ones_bf = sb("ones_bf", [128, 128], BF16)
        E.ones_f = sb("ones_f", [128, 128], F32)
        E.neg_f = sb("neg_f", [1, 128], F32)
        E.invD_bf = sb("invD_bf", [128, 1], BF16)
        E.ident_f = sb("ident_f", [128, 128], F32)
        E.eps = sb("eps", [1, 2], F32)
        E.bias = {}
        for name in ("b0q", "b0k", "b0o", "b1q", "b1k", "b1o", "rmsw"):
            E.bias[name] = sb("t_" + name, [128, 8], F32)
        E.ln_rows = sb("ln_rows", [1, 1536], F32)
        E.ln_stg = [sb(f"ln_stg{i}", [128, 512], F32) for i in range(2)]
        E.stg_guard = [[], []]
        E.ln_guard = []
        E.rows_guard = []

        s.dma("sync", lambda h: h.dma_start(out=E.ones_bf[:], in_=c_ones_bf), sem="DC")
        s.dma("sync", lambda h: h.dma_start(out=E.ones_f[:], in_=c_ones_f), sem="DC")
        s.dma("sync", lambda h: h.dma_start(out=E.neg_f[:], in_=c_neg_f), sem="DC")
        s.dma("sync", lambda h: h.dma_start(out=E.invD_bf[:], in_=c_invd), sem="DC")
        E.ident_tok = s.dma("sync", lambda h: h.dma_start(out=E.ident_f[:], in_=c_ident), sem="DC")
        s.dma("sync", lambda h: h.dma_start(out=E.eps[:], in_=c_eps), sem="DC")
        for name in ("b0q", "b0k", "b0o", "b1q", "b1k", "b1o", "rmsw"):
            s.dma("sync", lambda h, name=name: h.dma_start(out=E.bias[name][:], in_=ap[name]),
                  sem="DC")
        E.bias_tok = s.tok("DC")
        s.barrier()

        if "p1" in phases:
            with ExitStack() as pc:
                p = lambda *a: pc.enter_context(nc.sbuf_tensor(*a))
                E.p1_w = [p(f"p1w{i}", [128, 8 * 1024], BF16) for i in range(4)]
                E.p1_bvrow = p("p1_bvrow", [1, D], F32)
                E.p1_bvbc = p("p1_bvbc", [128, 1024], F32)
                E.p1_xg32 = p("p1_xg32", [128, 8 * 512], F32)
                E.p1_xgbf = p("p1_xgbf", [128, 8 * 512], BF16)
                E.p1_q = p("p1_q", [128, 8 * 512], BF16)
                E.p1_k = p("p1_k", [128, 8 * 512], BF16)
                E.p1_v = p("p1_v", [128, 4 * 1024], BF16)
                E.p1_att = p("p1_att", [128, 8 * 512], BF16)
                E.p1_expst = [[p(f"p1_expst{i}_{b}", [128, 64], BF16) for b in range(2)]
                              for i in range(2)]
                E.p1_recip = [p(f"p1_recip{i}", [1, 128], F32) for i in range(2)]
                E.p1_bcsb = [p(f"p1_bcsb{i}", [128, 128], F32) for i in range(2)]
                emit_p1(E)
            s.barrier()
        if "p2" in phases:
            with ExitStack() as pc:
                p = lambda *a: pc.enter_context(nc.sbuf_tensor(*a))
                E.p2_wd = p("p2_wd", [128, 24 * 1024], BF16)
                E.p2_wg = [p(f"p2_wg{i}", [128, 8 * 512], BF16) for i in range(2)]
                E.p2_wu = [p(f"p2_wu{i}", [128, 8 * 512], BF16) for i in range(2)]
                E.p2_h = p("p2_h", [128, 24 * 512], BF16)
                E.p2_x32 = p("p2_x32", [128, 8 * 512], F32)
                E.p2_xbf = p("p2_xbf", [128, 8 * 512], BF16)
                E.sg = [p(f"p2sg{i}", [128, 512], F32) for i in range(2)]
                E.sg_guard = [[], []]
                emit_p2(E)
            s.barrier()
        if "p3" in phases:
            with ExitStack() as pc:
                p = lambda *a: pc.enter_context(nc.sbuf_tensor(*a))
                E.p3_k = p("p3_k", [128, 8 * 4096], BF16)
                E.p3_vext = p("p3_vext", [128, 32 * 1040], BF16)
                E.p3_wbuf = p("p3_wbuf", [128, 8 * 1024], BF16)
                E.p3_xbf0 = p("p3_xbf0", [128, 8 * 512], BF16)
                emit_p3(E, pc)
            s.barrier()
        if "p4" in phases:
            with ExitStack() as pc:
                p = lambda *a: pc.enter_context(nc.sbuf_tensor(*a))
                E.p4_r3 = p("p4_r3", [128, 8 * 1024], F32)
                E.p4_xn = p("p4_xn", [128, 8 * 1024], BF16)
                E.p4_yacc = p("p4_yacc", [128, 8 * 1024], F32)
                E.p4_h = p("p4_h", [128, 8 * 1024], BF16)
                E.p4_mfm = p("p4_mfm", [8, 1024], F32)
                E.p4_mbc = p("p4_mbc", [128, 1024], F32)
                E.p4_w1 = [p(f"p4_w1{i}", [128, 8 * 1024], BF16) for i in range(2)]
                E.p4_w2 = [p(f"p4_w2{i}", [128, 8 * 1024], BF16) for i in range(2)]
                E.p4_g = p("p4_g", [128, 32], F32)
                E.gw_sb = p("gw_sb", [128, 64], BF16)
                E.b2_sb = p("b2_sb", [8, 1024], F32)
                E.sel_f = p("sel_f", [8, 1024], F32)
                E.b1f1 = p("t_b1f1", [128, 256], F32)
                E.sg = [p(f"p4sg{i}", [128, 512], F32) for i in range(3)]
                E.sg_guard = [[], [], []]
                emit_p4(E, out_hbm)
            s.barrier()

        fence_deps = [s.tok(name) for name in list(s.cnt.keys())]
        s.emit("sync", lambda h: h.nop(), deps=fence_deps)
        s.realize(nc, ctx)
    return nc


def _prep_shared(inputs):
    f = np.ascontiguousarray
    d = {}
    d["w0q"] = f(np.asarray(inputs["L0_attn_Wq"], np.float32))
    d["w0k"] = f(np.asarray(inputs["L0_attn_Wk"], np.float32))
    d["w0v"] = f(np.asarray(inputs["L0_attn_Wv"], np.float32))
    d["w0o"] = f(np.asarray(inputs["L0_attn_Wo"], np.float32))
    d["b0q"] = f(np.asarray(inputs["L0_attn_bq"], np.float32).reshape(8, 128).T)
    d["b0k"] = f(np.asarray(inputs["L0_attn_bk"], np.float32).reshape(8, 128).T)
    d["b0o"] = f(np.asarray(inputs["L0_attn_bo"], np.float32).reshape(8, 128).T)
    d["b0v_row"] = f(np.asarray(inputs["L0_attn_bv"], np.float32).reshape(1, 1024))
    d["w0g_t"] = f(np.asarray(inputs["L0_ffn_w_gate"], np.int32).T)
    d["w0u_t"] = f(np.asarray(inputs["L0_ffn_w_up"], np.int32).T)
    d["w0d_t"] = f(np.asarray(inputs["L0_ffn_w_down"], np.int32).T)
    d["w1q_t"] = f(np.asarray(inputs["L1_attn_wq"], np.float32).transpose(1, 0, 2).reshape(1024, 1024))
    d["w1k_t"] = f(np.asarray(inputs["L1_attn_wk"], np.float32).transpose(1, 0, 2).reshape(1024, 1024))
    d["w1v_t"] = f(np.asarray(inputs["L1_attn_wv"], np.float32).transpose(1, 0, 2).reshape(1024, 1024))
    d["w1o_t"] = f(np.asarray(inputs["L1_attn_wo"], np.float32).T)
    d["b1q"] = f(np.asarray(inputs["L1_attn_bq"], np.float32).reshape(1024).reshape(8, 128).T)
    d["b1k"] = f(np.asarray(inputs["L1_attn_bk"], np.float32).reshape(1024).reshape(8, 128).T)
    d["b1o"] = f(np.asarray(inputs["L1_attn_bo"], np.float32).reshape(8, 128).T)
    d["b1v_row"] = f(np.asarray(inputs["L1_attn_bv"], np.float32).reshape(1, 1024))
    d["rmsw"] = f(np.asarray(inputs["L1_ffn_rms_w"], np.float32).reshape(8, 128).T)
    d["gw_t"] = f(np.asarray(inputs["L1_ffn_gate_w"], np.float32).T)
    d["b1f1"] = f(np.asarray(inputs["L1_ffn_b1"], np.float32).reshape(8, 32, 128)
                  .transpose(2, 0, 1).reshape(128, 256))
    d["b2"] = f(np.asarray(inputs["L1_ffn_b2"], np.float32))
    for e in range(NE):
        d[f"w1ff1_{e}"] = f(np.asarray(inputs["L1_ffn_W1"][e], np.float32))
        d[f"w1ff2_{e}"] = f(np.asarray(inputs["L1_ffn_W2"][e], np.float32))
    return d


def fm_layout(x2d):
    fm = np.ascontiguousarray(np.asarray(x2d, np.float32).T)
    return np.ascontiguousarray(fm.reshape(8, 128, S).transpose(1, 0, 2).reshape(128, 8 * S))


def fm_unlayout(y):
    fm = y.reshape(128, 8, S).transpose(1, 0, 2).reshape(1024, S)
    return np.ascontiguousarray(fm.T)


def kernel(**inputs):
    shared = _prep_shared(inputs)
    x = np.asarray(inputs["L0_attn_src"], dtype=np.float32)
    nc = build_nc(phases=("p1", "p2", "p3", "p4"))
    in_maps = []
    for b in range(8):
        m = dict(shared)
        m["x0"] = fm_layout(x[b])
        in_maps.append(m)
    res = run_bass_kernel_spmd(nc, in_maps, core_ids=list(range(8)))
    return np.stack([fm_unlayout(np.asarray(r["y"], dtype=np.float32)) for r in res.results])


# version 16 (final)
# speedup vs baseline: 2.2813x; 1.6022x over previous
import os
import numpy as np
import ml_dtypes
from contextlib import ExitStack

import concourse.bass as bass
import concourse.mybir as mybir
from concourse.bass_utils import run_bass_kernel_spmd

F32 = mybir.dt.float32
BF16 = mybir.dt.bfloat16
I32 = mybir.dt.int32
AF = mybir.ActivationFunctionType
ALU = mybir.AluOpType
AX = mybir.AxisListType.X
BF16NP = ml_dtypes.bfloat16

S = 4096
D = 1024
SC0 = float(1.0 / np.sqrt(128.0))
SC1 = 0.125
EPS_LN = 1e-5
EPS_RMS = 1e-6
FF0 = 3072
FF1 = 4096
NE = 8
INV127 = float(1.0 / 127.0)
INV127SQ = float(1.0 / (127.0 * 127.0))
U32 = mybir.dt.uint32
CAP = 768
NSLOT = NE * CAP

ENGINES = ("sync", "tensor", "scalar", "vector", "gpsimd")
OWN_SEM = {"tensor": "T", "scalar": "A", "vector": "V", "gpsimd": "G", "sync": "SY"}


class Sched:
    def __init__(self):
        self.q = {e: [] for e in ENGINES}
        self.cnt = {}
        self.pending = {e: [] for e in ENGINES}

    def tok(self, sem):
        return (sem, self.cnt.get(sem, 0))

    def emit(self, eng, fn, deps=(), sig=None, inc=1):
        if sig is None:
            sig = OWN_SEM[eng]
        for d in deps:
            if d is None:
                continue
            s_, v = d
            assert v <= self.cnt.get(s_, 0), f"dep on future token {s_}={v}"
        self.cnt[sig] = self.cnt.get(sig, 0) + inc
        token = (sig, self.cnt[sig])
        dd = self.pending[eng] + [d for d in deps if d is not None]
        self.pending[eng] = []
        self.q[eng].append((dd, fn, sig, inc))
        return token

    def dma(self, eng, fn, deps=(), sem="DS"):
        return self.emit(eng, fn, deps=deps, sig=sem, inc=16)

    def barrier(self):
        snap = [(s_, v) for (s_, v) in self.cnt.items()]
        for e in ENGINES:
            self.pending[e] = self.pending[e] + snap

    def realize(self, nc, ctx):
        sems = {name: ctx.enter_context(nc.semaphore(name)) for name in self.cnt}
        block = ctx.enter_context(nc.Block())

        def mk(eng):
            def body(h):
                waited = {}
                for (deps, fn, sig, inc) in self.q[eng]:
                    need = {}
                    for (s_, v) in deps:
                        if v > waited.get(s_, 0) and v > need.get(s_, 0):
                            need[s_] = v
                    for s_ in sorted(need):
                        h.wait_ge(sems[s_], need[s_])
                        waited[s_] = need[s_]
                    fn(h).then_inc(sems[sig], inc)
            return body

        block.sync(mk("sync"))
        block.tensor(mk("tensor"))
        block.scalar(mk("scalar"))
        block.vector(mk("vector"))
        block.gpsimd(mk("gpsimd"))


class BankGuard:
    def __init__(self):
        self.readers = {}

    def read(self, b, token):
        self.readers.setdefault(b, []).append(token)

    def wdeps(self, b):
        d = self.readers.get(b, [])
        self.readers[b] = []
        return d


def rearr_cpf(ap, p=128):
    return ap.rearrange("(c p) f -> p c f", p=p)


def rearr_groups(ap, t=S):
    return ap.rearrange("p (c t) -> p c t", t=t)


class Env:
    pass


def emit_postln(E, r_sb, T, dst_fn, r_deps, eps_col, rbf, sqb):
    s = E.s
    b0, b1, b2_, b3 = E.lnb
    nTT = T // 512
    vcp = s.emit("vector", lambda h: h.tensor_copy(rbf[:, 0:8 * T], r_sb[:, 0:8 * T]),
                 deps=list(r_deps) + E.ln_guard)
    acp = s.emit("scalar", lambda h: h.activation(out=sqb[:, 0:8 * T], in_=r_sb[:, 0:8 * T],
                                                  func=AF.Square),
                 deps=list(r_deps) + E.ln_guard)
    R = E.ln_rows
    for tt in range(nTT):
        mean_ps = E.pb[b0][0:1, 0:512]
        msq_ps = E.pb[b1][0:1, 0:512]

        def statmm(h, src, dst, tt=tt):
            ins = None
            for dc in range(8):
                ins = h.matmul(dst, E.invD_bf[:, 0:1],
                               src[:, dc * T + tt * 512: dc * T + tt * 512 + 512],
                               start=(dc == 0), stop=(dc == 7))
            return ins
        t_mean = s.emit("tensor", lambda h, tt=tt: statmm(h, rbf, E.pb[b0][0:1, 0:512], tt),
                        deps=[vcp] + E.bank.wdeps(b0))
        t_msq = s.emit("tensor", lambda h, tt=tt: statmm(h, sqb, E.pb[b1][0:1, 0:512], tt),
                       deps=[acp] + E.bank.wdeps(b1))
        mrow = R[0:1, 0:512]
        m2 = R[0:1, 512:1024]
        vrow = R[0:1, 1024:1536]
        rstd = vrow
        v1 = s.emit("vector", lambda h: h.tensor_copy(mrow, mean_ps),
                    deps=[t_mean] + E.rows_guard)
        E.rows_guard = []
        E.bank.read(b0, v1)
        v2 = s.emit("vector", lambda h: h.tensor_mul(m2, mrow, mrow), deps=[v1])
        v3 = s.emit("vector", lambda h: h.tensor_sub(vrow, msq_ps, m2), deps=[t_msq, v2])
        E.bank.read(b1, v3)
        a1 = s.emit("scalar", lambda h: h.activation(out=vrow, in_=vrow, func=AF.Ln,
                                                     bias=eps_col), deps=[v3])
        a2 = s.emit("scalar", lambda h: h.activation(out=rstd, in_=vrow, func=AF.Exp,
                                                     scale=-0.5), deps=[a1])
        v4 = s.emit("vector", lambda h: h.tensor_mul(m2, mrow, rstd), deps=[a2])
        bca = E.pb[b2_][:, 0:512]
        bcb = E.pb[b3][:, 0:512]
        t_a = s.emit("tensor", lambda h: h.matmul(bca, E.ones_f[0:1, 0:128], rstd,
                                                  start=True, stop=True),
                     deps=[a2] + E.bank.wdeps(b2_))
        t_b = s.emit("tensor", lambda h: h.matmul(bcb, E.neg_f[0:1, 0:128], m2,
                                                  start=True, stop=True),
                     deps=[v4] + E.bank.wdeps(b3))
        E.rows_guard = [t_a, t_b]
        for dc in range(8):
            slot = (tt * 8 + dc) % 2
            stg = E.ln_stg[slot]
            vv1 = s.emit("vector",
                         lambda h, stg=stg, dc=dc, tt=tt: h.tensor_mul(
                             stg[:], r_sb[:, dc * T + tt * 512: dc * T + tt * 512 + 512], bca),
                         deps=[t_a, t_b] + E.stg_guard[slot])
            vv2 = s.emit("vector", lambda h, stg=stg: h.tensor_add(stg[:], stg[:], bcb),
                         deps=[vv1])
            E.bank.read(b2_, vv1)
            E.bank.read(b3, vv2)
            dtok = s.dma("sync", lambda h, stg=stg, dc=dc, tt=tt: h.dma_start(
                out=dst_fn(dc, tt), in_=stg[:]), deps=[vv2], sem=f"DST{slot}")
            E.stg_guard[slot] = [dtok]
    E.ln_guard = [s.tok("T")]


def emit_p1(E):
    BIS = os.environ.get("P1BISECT", "")
    s = E.s
    wq, wk, wv, wo = E.p1_w
    ldtoks = []
    for (sb_t, src) in ((wq, "w0q"), (wk, "w0k"), (wv, "w0v"), (wo, "w0o")):
        ldtoks.append(s.dma("gpsimd", lambda h, d=sb_t, src=src: h.dma_start(
            out=d[:].rearrange("p (c f) -> p c f", f=1024), in_=rearr_cpf(E.ap[src])),
            sem="DW1"))
    wtok = ldtoks[-1]
    bvr = s.dma("sync", lambda h: h.dma_start(out=E.p1_bvrow[:], in_=E.ap["b0v_row"]), sem="DC")
    bvbc = E.p1_bvbc
    bv_toks = []
    for half in range(2):
        t = s.emit("tensor", lambda h, half=half: h.matmul(
            E.pb[2][:, 0:512], E.ones_f[0:1, 0:128],
            E.p1_bvrow[0:1, half * 512:half * 512 + 512], start=True, stop=True),
            deps=[bvr] + E.bank.wdeps(2))
        a = s.emit("scalar", lambda h, half=half: h.copy(bvbc[:, half * 512:half * 512 + 512],
                                                         E.pb[2][:, 0:512]), deps=[t])
        E.bank.read(2, a)
        bv_toks.append(a)

    xg32 = E.p1_xg32
    xgbf = E.p1_xgbf
    q_sb, k_sb, v_sb, att_sb = E.p1_q, E.p1_k, E.p1_v, E.p1_att
    xg32_guard = []
    xgbf_guard = []
    gg = []

    ngroups = 1 if BIS else 8
    for g in range(ngroups):
        ld32 = s.dma("sync", lambda h, g=g: h.dma_start(
            out=xg32[:].rearrange("p (c t) -> p c t", t=512),
            in_=rearr_groups(E.ap["x_in"])[:, :, g * 512:(g + 1) * 512]),
            deps=xg32_guard, sem="DX32")
        ldbf = s.dma("gpsimd", lambda h, g=g: h.dma_start(
            out=xgbf[:].rearrange("p (c t) -> p c t", t=512),
            in_=rearr_groups(E.ap["x_in"])[:, :, g * 512:(g + 1) * 512]),
            deps=xgbf_guard + [wtok], sem="DXBF")
        xg32_guard = []
        xgbf_guard = []

        pe_toks = []
        for (w_sb, dst, bname) in ((wq, q_sb, "b0q"), (wk, k_sb, "b0k")):
            for dc in range(8):
                bank = len(pe_toks) % 2

                def pmm(h, w_sb=w_sb, dc=dc, bank=bank):
                    ins = None
                    for ci in range(8):
                        ins = h.matmul(E.pb[bank][:, 0:512],
                                       w_sb[:, ci * 1024 + dc * 128: ci * 1024 + dc * 128 + 128],
                                       xgbf[:, ci * 512:(ci + 1) * 512],
                                       start=(ci == 0), stop=(ci == 7))
                    return ins
                t = s.emit("tensor", lambda h, f=pmm: f(h),
                           deps=[ldbf] + E.bank.wdeps(bank))
                pe_toks.append(t)
                a = s.emit("scalar", lambda h, dst=dst, dc=dc, bank=bank, bname=bname:
                           h.activation(out=dst[:, dc * 512:(dc + 1) * 512],
                                        in_=E.pb[bank][:, 0:512], func=AF.Identity,
                                        bias=E.bias[bname][:, dc:dc + 1]),
                           deps=[t, E.bias_tok] + gg)
                E.bank.read(bank, a)
        for tc in range(4):
            for half in range(2):
                bank = len(pe_toks) % 2

                def vmm(h, tc=tc, half=half, bank=bank):
                    ins = None
                    for ci in range(8):
                        ins = h.matmul(E.pb[bank][:, 0:512],
                                       xgbf[:, ci * 512 + tc * 128: ci * 512 + tc * 128 + 128],
                                       wv[:, ci * 1024 + half * 512: ci * 1024 + half * 512 + 512],
                                       start=(ci == 0), stop=(ci == 7))
                    return ins
                t = s.emit("tensor", lambda h, f=vmm: f(h), deps=[ldbf] + E.bank.wdeps(bank))
                pe_toks.append(t)
                v = s.emit("vector", lambda h, tc=tc, half=half, bank=bank: h.tensor_add(
                    v_sb[:, tc * 1024 + half * 512: tc * 1024 + half * 512 + 512],
                    E.pb[bank][:, 0:512], bvbc[:, half * 512:half * 512 + 512]),
                    deps=[t] + bv_toks + gg)
                E.bank.read(bank, v)
        proj_done_v = s.tok("V")
        proj_done_a = s.tok("A")
        if BIS == "proj":
            s.dma("gpsimd", lambda h: h.dma_start(
                out=rearr_groups(E.ap["r1"])[:, :, 0:512],
                in_=q_sb[:].rearrange("p (c t) -> p c t", t=512)),
                deps=[proj_done_v, proj_done_a], sem="DXBF")
            return

        it = 0
        if g == 0:
            z0 = s.emit("vector", lambda h: h.memset(E.p1_expst[0][0][64:128, :], 0.0))
            z1 = s.emit("vector", lambda h: h.memset(E.p1_expst[1][0][64:128, :], 0.0))
            z2 = s.emit("vector", lambda h: h.memset(E.p1_expst[0][1][0:64, :], 0.0))
            z3 = s.emit("vector", lambda h: h.memset(E.p1_expst[1][1][0:64, :], 0.0))
            E.p1_expst_init = [z0, z1, z2, z3]
        expst_g = [list(E.p1_expst_init), list(E.p1_expst_init)]
        recip_g = [[], []]
        bcsb_g = [[], []]
        natt = int(os.environ.get("P1ATT_N", "32"))
        aops = int(os.environ.get("P1ATT_OPS", "7"))
        for tc in range(4):
            for hh in range(8):
                if it >= natt:
                    continue
                sl = it % 2
                st_ps = E.pb[2 + sl]
                da_ps = E.pb[4 + sl]
                bc_ps = E.pb[sl]
                exp0 = E.p1_expst[sl][0]
                exp1 = E.p1_expst[sl][1]
                recip = E.p1_recip[sl]
                bc_sb = E.p1_bcsb[sl]

                def stmm(h, tc=tc, hh=hh, st_ps=st_ps):
                    base = hh * 512 + tc * 128
                    h.matmul(st_ps[0:64, 0:64], k_sb[:, base:base + 64],
                             q_sb[:, base:base + 64], start=True, stop=True)
                    return h.matmul(st_ps[64:128, 0:64], k_sb[:, base + 64:base + 128],
                                    q_sb[:, base + 64:base + 128], start=True, stop=True,
                                    tile_position=(0, 64))
                t1 = s.emit("tensor", lambda h, f=stmm: f(h),
                            deps=[proj_done_v, proj_done_a] + E.bank.wdeps(2 + sl))
                if aops < 2:
                    it += 1
                    continue
                e1a = s.emit("scalar", lambda h, st_ps=st_ps, exp0=exp0: h.activation(
                    out=exp0[0:64, :], in_=st_ps[0:64, 0:64], func=AF.Exp, scale=SC0),
                    deps=[t1] + expst_g[sl])
                e1 = s.emit("scalar", lambda h, st_ps=st_ps, exp1=exp1: h.activation(
                    out=exp1[64:128, :], in_=st_ps[64:128, 0:64], func=AF.Exp, scale=SC0),
                    deps=[t1])
                E.bank.read(2 + sl, e1)
                if aops < 3:
                    it += 1
                    continue

                def dpv(h, tc=tc, hh=hh, da_ps=da_ps, exp0=exp0, exp1=exp1):
                    h.matmul(da_ps[0:1, 128:192], E.ones_bf[:, 0:1], exp0[:, :],
                             start=True, stop=True)
                    h.matmul(da_ps[0:1, 192:256], E.ones_bf[:, 0:1], exp1[:, :],
                             start=True, stop=True)
                    vb = tc * 1024 + hh * 128
                    h.matmul(da_ps[:, 0:64], v_sb[:, vb:vb + 128], exp0[:, :],
                             start=True, stop=True)
                    return h.matmul(da_ps[:, 64:128], v_sb[:, vb:vb + 128],
                                    exp1[:, :], start=True, stop=True)
                t2 = s.emit("tensor", lambda h, f=dpv: f(h),
                            deps=[e1a, e1] + E.bank.wdeps(4 + sl))
                expst_g[sl] = [t2]
                if aops < 4:
                    it += 1
                    continue
                r1 = s.emit("vector", lambda h, da_ps=da_ps, recip=recip: h.reciprocal(
                    recip[0:1, 0:128], da_ps[0:1, 128:256]), deps=[t2] + recip_g[sl])
                E.bank.read(4 + sl, r1)
                if aops < 5:
                    it += 1
                    continue
                t3 = s.emit("tensor", lambda h, bc_ps=bc_ps, recip=recip: h.matmul(
                    bc_ps[:, 0:128], E.ones_f[0:1, 0:128], recip[0:1, 0:128],
                    start=True, stop=True), deps=[r1] + E.bank.wdeps(sl))
                recip_g[sl] = [t3]
                if aops < 6:
                    it += 1
                    continue
                a3 = s.emit("scalar", lambda h, bc_ps=bc_ps, bc_sb=bc_sb: h.copy(
                    bc_sb[:], bc_ps[:, 0:128]), deps=[t3] + bcsb_g[sl])
                E.bank.read(sl, a3)
                if aops < 7:
                    it += 1
                    continue
                v3 = s.emit("vector", lambda h, tc=tc, hh=hh, da_ps=da_ps, bc_sb=bc_sb:
                            h.tensor_mul(att_sb[:, hh * 512 + tc * 128: hh * 512 + tc * 128 + 128],
                                         da_ps[:, 0:128], bc_sb[:]),
                            deps=[a3, t2])
                E.bank.read(4 + sl, v3)
                expst_g[sl] = [t2]
                bcsb_g[sl] = [v3]
                it += 1
        att_done = s.tok("V")
        if BIS == "attn":
            s.dma("gpsimd", lambda h: h.dma_start(
                out=rearr_groups(E.ap["r1"])[:, :, 0:512],
                in_=att_sb[:].rearrange("p (c t) -> p c t", t=512)),
                deps=[att_done], sem="DXBF")
            return

        wo_last = None
        for dc in range(8):
            bank = dc % 2

            def womm(h, dc=dc, bank=bank):
                ins = None
                for ci in range(8):
                    ins = h.matmul(E.pb[bank][:, 0:512],
                                   wo[:, ci * 1024 + dc * 128: ci * 1024 + dc * 128 + 128],
                                   att_sb[:, ci * 512:(ci + 1) * 512],
                                   start=(ci == 0), stop=(ci == 7))
                return ins
            t = s.emit("tensor", lambda h, f=womm: f(h),
                       deps=[att_done] + E.bank.wdeps(bank))
            v = s.emit("vector", lambda h, dc=dc, bank=bank: h.scalar_tensor_tensor(
                xg32[:, dc * 512:(dc + 1) * 512], E.pb[bank][:, 0:512],
                E.bias["b0o"][:, dc:dc + 1], xg32[:, dc * 512:(dc + 1) * 512],
                ALU.add, ALU.add),
                deps=[t, ld32, E.bias_tok])
            E.bank.read(bank, v)
            wo_last = v

        if BIS == "wo":
            s.dma("sync", lambda h: h.dma_start(
                out=rearr_groups(E.ap["r1"])[:, :, 0:512],
                in_=xg32[:].rearrange("p (c t) -> p c t", t=512)),
                deps=[wo_last], sem="DST0")
            return

        def dst_fn(dc, tt, g=g):
            return rearr_groups(E.ap["r1"])[:, dc, g * 512:(g + 1) * 512]
        emit_postln(E, xg32, 512, dst_fn, [wo_last], E.eps[0:1, 0:1], rbf=att_sb, sqb=xgbf)
        xg32_guard = [s.tok("V"), s.tok("A")]
        xgbf_guard = list(E.ln_guard)
        gg = [s.tok("T")]


def emit_p2(E):
    s = E.s
    wd = E.p2_wd
    wd_toks = []
    for part in range(3):
        wd_toks.append(s.dma("gpsimd", lambda h, part=part: h.dma_start(
            out=wd[:, part * 8192:(part + 1) * 8192].rearrange("p (c f) -> p c f", f=1024),
            in_=rearr_cpf(E.ap["w0d_t"])[:, part * 8:(part + 1) * 8, :]), sem="DWD"))
    wg4, wu4 = E.p2_wg, E.p2_wu
    h_sb = E.p2_h
    x32 = E.p2_x32
    xbf = E.p2_xbf
    x32_guard = []
    xbf_guard = []
    w_guard = [[], []]
    h_guard = []
    for ei in range(8):
        ld32 = s.dma("sync", lambda h, ei=ei: h.dma_start(
            out=x32[:].rearrange("p (c t) -> p c t", t=512),
            in_=rearr_groups(E.ap["r1"])[:, :, ei * 512:(ei + 1) * 512]),
            deps=x32_guard, sem="DX32")
        ldbf = s.dma("gpsimd", lambda h, ei=ei: h.dma_start(
            out=xbf[:].rearrange("p (c t) -> p c t", t=512),
            in_=rearr_groups(E.ap["r1"])[:, :, ei * 512:(ei + 1) * 512]),
            deps=xbf_guard + [wd_toks[-1]], sem="DXBF")
        x32_guard = []
        xbf_guard = []
        for wgrp in range(6):
            slot = wgrp % 2
            ldg = s.dma("gpsimd", lambda h, wgrp=wgrp, slot=slot: h.dma_start(
                out=wg4[slot][:].rearrange("p (c f) -> p c f", f=512),
                in_=rearr_cpf(E.ap["w0g_t"])[:, :, wgrp * 512:(wgrp + 1) * 512]),
                deps=w_guard[slot], sem=f"DWG{slot}")
            ldu = s.dma("gpsimd", lambda h, wgrp=wgrp, slot=slot: h.dma_start(
                out=wu4[slot][:].rearrange("p (c f) -> p c f", f=512),
                in_=rearr_cpf(E.ap["w0u_t"])[:, :, wgrp * 512:(wgrp + 1) * 512]),
                deps=w_guard[slot], sem=f"DWU{slot}")
            w_guard[slot] = []
            for h4 in range(4):
                hc = wgrp * 4 + h4
                bkg = hc % 2
                bku = 2 + hc % 2

                def gmm(h, h4=h4, bkg=bkg, slot=slot):
                    ins = None
                    for ci in range(8):
                        ins = h.matmul(E.pb[bkg][:, 0:512],
                                       wg4[slot][:, ci * 512 + h4 * 128: ci * 512 + h4 * 128 + 128],
                                       xbf[:, ci * 512:(ci + 1) * 512],
                                       start=(ci == 0), stop=(ci == 7))
                    return ins
                tg = s.emit("tensor", lambda h, f=gmm: f(h),
                            deps=[ldbf, ldg] + E.bank.wdeps(bkg))

                def umm(h, h4=h4, bku=bku, slot=slot):
                    ins = None
                    for ci in range(8):
                        ins = h.matmul(E.pb[bku][:, 0:512],
                                       wu4[slot][:, ci * 512 + h4 * 128: ci * 512 + h4 * 128 + 128],
                                       xbf[:, ci * 512:(ci + 1) * 512],
                                       start=(ci == 0), stop=(ci == 7))
                    return ins
                tu = s.emit("tensor", lambda h, f=umm: f(h),
                            deps=[ldbf, ldu] + E.bank.wdeps(bku))
                ssl = hc % 2
                sgs = E.sg[ssl]
                ag = s.emit("scalar", lambda h, bkg=bkg, sgs=sgs: h.activation(
                    out=sgs[:], in_=E.pb[bkg][:, 0:512], func=AF.Silu, scale=INV127),
                    deps=[tg] + E.sg_guard[ssl])
                E.bank.read(bkg, ag)
                vh = s.emit("vector", lambda h, hc=hc, bku=bku, sgs=sgs: h.tensor_mul(
                    h_sb[:, hc * 512:(hc + 1) * 512], E.pb[bku][:, 0:512], sgs[:]),
                    deps=[tu, ag] + h_guard)
                h_guard = []
                E.bank.read(bku, vh)
                E.sg_guard[ssl] = [vh]
            w_guard[slot] = [s.tok("T")]
        h_done = s.tok("V")
        dn_last = None
        for dc in range(8):
            bank = 4 + dc % 2

            def dmm(h, dc=dc, bank=bank):
                ins = None
                for hc in range(24):
                    ins = h.matmul(E.pb[bank][:, 0:512],
                                   wd[:, hc * 1024 + dc * 128: hc * 1024 + dc * 128 + 128],
                                   h_sb[:, hc * 512:(hc + 1) * 512],
                                   start=(hc == 0), stop=(hc == 23))
                return ins
            t = s.emit("tensor", lambda h, f=dmm: f(h),
                       deps=[h_done, wd_toks[-1]] + E.bank.wdeps(bank))
            v = s.emit("vector", lambda h, dc=dc, bank=bank: h.scalar_tensor_tensor(
                x32[:, dc * 512:(dc + 1) * 512], E.pb[bank][:, 0:512], INV127SQ,
                x32[:, dc * 512:(dc + 1) * 512], ALU.mult, ALU.add),
                deps=[t, ld32])
            E.bank.read(bank, v)
            dn_last = v
        h_guard = [s.tok("T")]

        def dst_fn(dc, tt, ei=ei):
            return rearr_groups(E.ap["r2"])[:, dc, ei * 512:(ei + 1) * 512]
        emit_postln(E, x32, 512, dst_fn, [dn_last], E.eps[0:1, 0:1],
                    rbf=h_sb, sqb=xbf)
        h_guard = h_guard + E.ln_guard
        x32_guard = [s.tok("V"), s.tok("A")]
        xbf_guard = list(E.ln_guard)


def emit_p3(E, pc):
    s = E.s
    nc = E.nc
    kfm = E.p3_k
    vext = E.p3_vext
    wbuf = E.p3_wbuf
    actx = ExitStack()
    wvb = actx.enter_context(nc.sbuf_tensor("p3_wv", [128, 8 * 1024], BF16))
    E.p3_bvrow = actx.enter_context(nc.sbuf_tensor("p3_bvrow", [1, D], F32))
    bvbc = actx.enter_context(nc.sbuf_tensor("p3_bvbc", [128, 1024], F32))
    xbf1 = actx.enter_context(nc.sbuf_tensor("p3_xbf1", [128, 8 * 512], BF16))
    ms = s.emit("vector", lambda h: h.memset(vext[:], 1.0))
    wk_t = s.dma("gpsimd", lambda h: h.dma_start(
        out=wbuf[:].rearrange("p (c f) -> p c f", f=1024), in_=rearr_cpf(E.ap["w1k_t"])),
        sem="DW3")
    wv_t = s.dma("gpsimd", lambda h: h.dma_start(
        out=wvb[:].rearrange("p (c f) -> p c f", f=1024), in_=rearr_cpf(E.ap["w1v_t"])),
        sem="DWV")
    bvr = s.dma("sync", lambda h: h.dma_start(out=E.p3_bvrow[:], in_=E.ap["b1v_row"]), sem="DC")
    bv_toks = []
    for half in range(2):
        t = s.emit("tensor", lambda h, half=half: h.matmul(
            E.pb[2][:, 0:512], E.ones_f[0:1, 0:128],
            E.p3_bvrow[0:1, half * 512:half * 512 + 512], start=True, stop=True),
            deps=[bvr] + E.bank.wdeps(2))
        a = s.emit("scalar", lambda h, half=half: h.copy(bvbc[:, half * 512:half * 512 + 512],
                                                         E.pb[2][:, 0:512]), deps=[t])
        E.bank.read(2, a)
        bv_toks.append(a)

    xbf = [E.p3_xbf0, xbf1]
    xbf_guard = [[], []]
    for tt in range(8):
        slot = tt % 2
        ldbf = s.dma("gpsimd", lambda h, tt=tt, slot=slot: h.dma_start(
            out=xbf[slot][:].rearrange("p (c t) -> p c t", t=512),
            in_=rearr_groups(E.ap["r2"])[:, :, tt * 512:(tt + 1) * 512]),
            deps=xbf_guard[slot] + [wv_t], sem=f"DXB{slot}")
        xbf_guard[slot] = []
        for dc in range(8):
            bank = dc % 2

            def kmm(h, dc=dc, bank=bank, slot=slot):
                ins = None
                for ci in range(8):
                    ins = h.matmul(E.pb[bank][:, 0:512],
                                   wbuf[:, ci * 1024 + dc * 128: ci * 1024 + dc * 128 + 128],
                                   xbf[slot][:, ci * 512:(ci + 1) * 512],
                                   start=(ci == 0), stop=(ci == 7))
                return ins
            t = s.emit("tensor", lambda h, f=kmm: f(h), deps=[ldbf, wk_t] + E.bank.wdeps(bank))
            v = s.emit("vector", lambda h, tt=tt, dc=dc, bank=bank: h.tensor_scalar(
                kfm[:, dc * 4096 + tt * 512: dc * 4096 + tt * 512 + 512],
                E.pb[bank][:, 0:512], E.bias["b1k"][:, dc:dc + 1], None, ALU.add),
                deps=[t, E.bias_tok])
            E.bank.read(bank, v)
        for tc in range(4):
            for half in range(2):
                bank = 2 + (tc * 2 + half) % 2

                def vmm(h, tc=tc, half=half, bank=bank, slot=slot):
                    ins = None
                    for ci in range(8):
                        ins = h.matmul(E.pb[bank][:, 0:512],
                                       xbf[slot][:, ci * 512 + tc * 128: ci * 512 + tc * 128 + 128],
                                       wvb[:, ci * 1024 + half * 512: ci * 1024 + half * 512 + 512],
                                       start=(ci == 0), stop=(ci == 7))
                    return ins
                t = s.emit("tensor", lambda h, f=vmm: f(h), deps=[ldbf, wv_t] + E.bank.wdeps(bank))
                base = (tt * 4 + tc) * 1040 + half * 520

                def vev(h, bank=bank, base=base, half=half):
                    dst = vext[:, base:base + 520].rearrange("p (hh e) -> p hh e", e=65)[:, :, 0:64]
                    src = E.pb[bank][:, 0:512].rearrange("p (hh e) -> p hh e", e=64)
                    bcv = bvbc[:, half * 512:half * 512 + 512].rearrange("p (hh e) -> p hh e", e=64)
                    return h.tensor_add(dst, src, bcv)
                v = s.emit("vector", lambda h, f=vev: f(h), deps=[t, ms] + bv_toks)
                E.bank.read(bank, v)
        xbf_guard[slot] = [s.tok("T")]
    a_done_v = s.tok("V")
    actx.close()
    a_fin = [s.tok("T"), s.tok("V"), s.tok("A")]

    p = lambda *a: pc.enter_context(nc.sbuf_tensor(*a))
    q_sb = p("p3_q", [128, 8 * 512], BF16)
    att_sb = p("p3_att", [128, 8 * 512], BF16)
    r32 = p("p3_r32", [128, 8 * 512], F32)
    E.expst3 = [p(f"p3_expst{i}", [128, 1024], BF16) for i in range(3)]
    E.expst3_g = [list(a_fin) for _ in range(3)]
    E.p3_rec = p("p3_rec", [128, 512], BF16)
    E.p3_bcsb = [p("p3_bcsb0", [64, 512], F32)]
    E.p3_bcsb = E.p3_bcsb * 2
    E.p3_tmp = [p("p3_tmp0", [64, 512], BF16)] * 2
    wload_guard = [s.tok("T")]
    q_guard = list(a_fin)
    att_guard = list(a_fin)
    r32_guard = list(a_fin)
    expst_g = [list(a_fin), list(a_fin)]
    rec_g = list(a_fin)
    bcsb_g = [list(a_fin), list(a_fin)]
    tmp_g = [list(a_fin), list(a_fin)]
    for qt in range(8):
        wq_t = s.dma("gpsimd", lambda h: h.dma_start(
            out=wbuf[:].rearrange("p (c f) -> p c f", f=1024), in_=rearr_cpf(E.ap["w1q_t"])),
            deps=wload_guard, sem="DW3")
        ldr = s.dma("sync", lambda h, qt=qt: h.dma_start(
            out=r32[:].rearrange("p (c t) -> p c t", t=512),
            in_=rearr_groups(E.ap["r2"])[:, :, qt * 512:(qt + 1) * 512]),
            deps=r32_guard, sem="DR32")
        ldq = s.dma("gpsimd", lambda h, qt=qt: h.dma_start(
            out=xbf[0][:].rearrange("p (c t) -> p c t", t=512),
            in_=rearr_groups(E.ap["r2"])[:, :, qt * 512:(qt + 1) * 512]),
            deps=xbf_guard[0], sem="DXB0")
        r32_guard = []
        xbf_guard[0] = []
        q_last = None
        for dc in range(8):
            bank = dc % 2

            def qmm(h, dc=dc, bank=bank):
                ins = None
                for ci in range(8):
                    ins = h.matmul(E.pb[bank][:, 0:512],
                                   wbuf[:, ci * 1024 + dc * 128: ci * 1024 + dc * 128 + 128],
                                   xbf[0][:, ci * 512:(ci + 1) * 512],
                                   start=(ci == 0), stop=(ci == 7))
                return ins
            t = s.emit("tensor", lambda h, f=qmm: f(h),
                       deps=[ldq, wq_t, a_done_v] + E.bank.wdeps(bank))
            v = s.emit("vector", lambda h, dc=dc, bank=bank: h.tensor_scalar(
                q_sb[:, dc * 512:(dc + 1) * 512], E.pb[bank][:, 0:512],
                E.bias["b1q"][:, dc:dc + 1], None, ALU.add),
                deps=[t, E.bias_tok] + q_guard)
            q_guard = []
            E.bank.read(bank, v)
            q_last = v
        qmm_done = s.tok("T")
        wo_t = s.dma("gpsimd", lambda h: h.dma_start(
            out=wbuf[:].rearrange("p (c f) -> p c f", f=1024), in_=rearr_cpf(E.ap["w1o_t"])),
            deps=[qmm_done], sem="DW3")

        SCP = ((0, 1), (2, 3))
        for hp in range(8):
            pvA = E.pb[4]
            pvB = E.pb[5]
            pvA_w = E.bank.wdeps(4)
            pvB_w = E.bank.wdeps(5)
            pv_tok = {"A": None, "B": None}
            e_tok = {}

            def stm(h, hlo, k, hp, bank):
                return h.matmul(E.pb[bank][:, 0:512],
                                kfm[hlo:hlo + 64, hp * 4096 + k * 128: hp * 4096 + k * 128 + 128],
                                q_sb[hlo:hlo + 64, hp * 512:(hp + 1) * 512],
                                start=True, stop=True, tile_position=(hlo, 0))

            def pvm(h, hh, k, side, dst):
                return h.matmul(dst[0:65, 0:512],
                                vext[:, k * 1040 + hh * 65: k * 1040 + hh * 65 + 65],
                                E.expst3[k % 3][:, side * 512: side * 512 + 512],
                                start=(k == 0), stop=(k == 31))

            for ktc in range(34):
                if ktc < 32:
                    par = ktc % 2
                    par3 = ktc % 3
                    bA, bB = SCP[par]
                    tA = s.emit("tensor", lambda h, ktc=ktc, hp=hp, bA=bA: stm(h, 0, ktc, hp, bA),
                                deps=[q_last, a_done_v] + E.bank.wdeps(bA) + E.bank.wdeps(bB))
                    tB = s.emit("tensor", lambda h, ktc=ktc, hp=hp, bB=bB: stm(h, 64, ktc, hp, bB),
                                deps=[])
                    ee = s.emit("scalar", lambda h, par3=par3, bA=bA: h.activation(
                        out=E.expst3[par3][:], in_=E.pbig[:, bA * 512: bA * 512 + 1024],
                        func=AF.Exp, scale=SC1),
                        deps=[tA, tB] + E.expst3_g[par3])
                    E.bank.read(bA, ee)
                    E.bank.read(bB, ee)
                    e_tok[ktc] = ee
                if ktc >= 2:
                    kprev = ktc - 2
                    pv_tok["A"] = s.emit("tensor",
                                         lambda h, kprev=kprev, hp=hp: pvm(
                                             h, 2 * hp, kprev, 0, pvA),
                                         deps=[e_tok[kprev]] + (pvA_w if kprev == 0 else []))
                    pv_tok["B"] = s.emit("tensor",
                                         lambda h, kprev=kprev, hp=hp: pvm(
                                             h, 2 * hp + 1, kprev, 1, pvB),
                                         deps=(pvB_w if kprev == 0 else []))
                    E.expst3_g[kprev % 3] = [pv_tok["B"]]
            for (side, hh, pvb, pv_ps) in (("A", 2 * hp, 4, pvA), ("B", 2 * hp + 1, 5, pvB)):
                rec = E.p3_rec
                def recf(h, pv_ps=pv_ps):
                    with nc.allow_low_precision(reason="softmax denom recip rounds to bf16; att output is bf16 anyway"):
                        return h.reciprocal(rec[64:65, 0:512], pv_ps[64:65, 0:512])
                r1 = s.emit("vector", recf, deps=[pv_tok[side]] + rec_g)
                t3 = s.emit("tensor", lambda h: h.matmul(
                    E.pb[0][0:64, 0:512], E.ones_bf[64:65, 0:64], rec[64:65, 0:512],
                    start=True, stop=True), deps=[r1] + E.bank.wdeps(0))
                rec_g = [t3]
                a3 = s.emit("scalar", lambda h: h.copy(
                    E.p3_bcsb[0][0:64, :], E.pb[0][0:64, 0:512]),
                    deps=[t3] + bcsb_g[0] + bcsb_g[1])
                E.bank.read(0, a3)
                if side == "A":
                    v3 = s.emit("vector", lambda h, hp=hp, pv_ps=pv_ps: h.tensor_mul(
                        att_sb[0:64, hp * 512:(hp + 1) * 512], pv_ps[0:64, 0:512],
                        E.p3_bcsb[0][0:64, :]),
                        deps=[a3, pv_tok[side]] + att_guard)
                    E.bank.read(pvb, v3)
                    bcsb_g = [[v3], []]
                else:
                    tsl = 0
                    v3 = s.emit("vector", lambda h, pv_ps=pv_ps, tsl=tsl: h.tensor_mul(
                        E.p3_tmp[tsl][0:64, :], pv_ps[0:64, 0:512], E.p3_bcsb[0][0:64, :]),
                        deps=[a3, pv_tok[side]] + tmp_g[tsl])
                    E.bank.read(pvb, v3)
                    bcsb_g = [[v3], []]
                    d3 = s.dma("sync", lambda h, hp=hp, tsl=tsl: h.dma_start(
                        out=att_sb[64:128, hp * 512:(hp + 1) * 512], in_=E.p3_tmp[tsl][0:64, :]),
                        deps=[v3] + att_guard, sem=f"DTMP{tsl}")
                    tmp_g[tsl] = [d3]
        att_done = [s.tok("V"), s.tok("DTMP0"), s.tok("DTMP1")]

        wo_last = None
        for dc in range(8):
            bank = dc % 2

            def womm(h, dc=dc, bank=bank):
                ins = None
                for ci in range(8):
                    ins = h.matmul(E.pb[bank][:, 0:512],
                                   wbuf[:, ci * 1024 + dc * 128: ci * 1024 + dc * 128 + 128],
                                   att_sb[:, ci * 512:(ci + 1) * 512],
                                   start=(ci == 0), stop=(ci == 7))
                return ins
            t = s.emit("tensor", lambda h, f=womm: f(h),
                       deps=att_done + [wo_t] + E.bank.wdeps(bank))
            v = s.emit("vector", lambda h, dc=dc, bank=bank: h.scalar_tensor_tensor(
                r32[:, dc * 512:(dc + 1) * 512], E.pb[bank][:, 0:512],
                E.bias["b1o"][:, dc:dc + 1], r32[:, dc * 512:(dc + 1) * 512],
                ALU.add, ALU.add),
                deps=[t, ldr, E.bias_tok])
            E.bank.read(bank, v)
            wo_last = v
        wload_guard = [s.tok("T")]

        def dst_fn(dc, tt, qt=qt):
            return rearr_groups(E.ap["r3"])[:, dc, qt * 512:(qt + 1) * 512]
        emit_postln(E, r32, 512, dst_fn, [wo_last], E.eps[0:1, 0:1],
                    rbf=att_sb, sqb=xbf[0])
        att_guard = list(E.ln_guard)
        q_guard = [s.tok("T")]
        xbf_guard[0] = list(E.ln_guard)
        r32_guard = [s.tok("V"), s.tok("A")]


def emit_p4s(E, out_name):
    s = E.s
    nc = E.nc
    r3q = E.p4_r3
    xnc = E.p4_xn
    sq2 = E.p4_sq
    xts = E.p4_xts
    gtok = E.p4_gtok
    gfm = E.p4_gfm
    hsb = E.p4_h
    wb = E.p4_w
    G = E.p4_g
    mbf = E.p4_mbf
    idxs = E.p4_idx
    scs = E.p4_sc
    offs = E.p4_offs
    tp = E.tp

    gw_tok = s.dma("sync", lambda h: h.dma_start(
        out=E.gw_sb[:].rearrange("p (c e) -> p c e", e=8), in_=rearr_cpf(E.ap["gw_t"])),
        sem="DGW")
    s.dma("gpsimd", lambda h: h.dma_start(out=E.b2sb[:], in_=E.ap["b2r"]), sem="DC")
    s.dma("sync", lambda h: h.dma_start(out=E.b1f1[:], in_=E.ap["b1f1"]), sem="DC")
    s.dma("sync", lambda h: h.dma_start(out=E.ident_bf[:], in_=E.c_ident_bf), sem="DC")
    s.dma("sync", lambda h: h.dma_start(out=E.ltri_bf[:], in_=E.c_ltri), sem="DC")
    s.dma("sync", lambda h: h.dma_start(out=offs[:], in_=E.c_offs0), sem="DC")
    cst_tok = s.tok("DC")
    gwm_tok = None
    for ci in range(8):
        gwm_tok = s.emit("vector", lambda h, ci=ci: h.tensor_scalar(
            E.gw_sb[:, ci * 8:(ci + 1) * 8], E.gw_sb[:, ci * 8:(ci + 1) * 8],
            E.bias["rmsw"][:, ci:ci + 1], None, ALU.mult),
            deps=[gw_tok, E.bias_tok])

    r3_guard = []
    xn_guard = []
    tp_guard = [[], []]
    xts_guard = [[], []]
    off_chain = [cst_tok]
    scat_toks = []
    lgbank_guard = []
    for ei in range(8):
        ld32 = s.dma("sync", lambda h, ei=ei: h.dma_start(
            out=r3q[:].rearrange("p (c t) -> p c t", t=512),
            in_=rearr_groups(E.ap["r3"])[:, :, ei * 512:(ei + 1) * 512]),
            deps=r3_guard, sem="DR32")
        r3_guard = []
        asq = s.emit("scalar", lambda h: h.activation(out=sq2[:, 0:4096], in_=r3q[:, 0:4096],
                                                      func=AF.Square),
                     deps=[ld32] + xn_guard)
        xn_guard = []

        def rmsmm(h):
            ins = None
            for dc in range(8):
                ins = h.matmul(E.pb[4][0:1, 0:512], E.invD_bf[:, 0:1],
                               sq2[:, dc * 512: dc * 512 + 512],
                               start=(dc == 0), stop=(dc == 7))
            return ins
        tm = s.emit("tensor", lambda h: rmsmm(h), deps=[asq] + E.bank.wdeps(4) + lgbank_guard)
        vrow = E.ln_rows[0:1, 1024:1536]
        a1 = s.emit("scalar", lambda h: h.activation(
            out=vrow, in_=E.pb[4][0:1, 0:512], func=AF.Ln, bias=E.eps[0:1, 1:2]),
            deps=[tm] + E.rows_guard)
        E.rows_guard = []
        E.bank.read(4, a1)
        a2 = s.emit("scalar", lambda h: h.activation(
            out=vrow, in_=vrow, func=AF.Exp, scale=-0.5), deps=[a1])
        t_b = s.emit("tensor", lambda h: h.matmul(
            E.pb[5][:, 0:512], E.ones_f[0:1, 0:128], vrow, start=True, stop=True),
            deps=[a2] + E.bank.wdeps(5))
        E.rows_guard = [t_b]
        xn_last = None
        for dc in range(8):
            v = s.emit("vector", lambda h, dc=dc: h.scalar_tensor_tensor(
                xnc[:, dc * 512:(dc + 1) * 512], r3q[:, dc * 512:(dc + 1) * 512],
                E.bias["rmsw"][:, dc:dc + 1], E.pb[5][:, 0:512],
                ALU.mult, ALU.mult),
                deps=[t_b, tm, ld32, E.bias_tok])
            E.bank.read(5, v)
            xn_last = v

        for tcl in range(4):
            c = ei * 4 + tcl
            lg = E.pb[4][0:128, 0:8]
            pfx = E.pb[4][0:128, 128:136]
            rcol = E.pb[4][0:128, 392:393]
            tot = E.pb[5][0:1, 120:128]

            def lmm(h, tcl=tcl):
                ins = None
                for ci in range(8):
                    ins = h.matmul(lg, r3q[:, ci * 512 + tcl * 128: ci * 512 + tcl * 128 + 128],
                                   E.gw_sb[:, ci * 8:(ci + 1) * 8],
                                   start=(ci == 0), stop=(ci == 7))
                return ins
            t1 = s.emit("tensor", lambda h, f=lmm: f(h),
                        deps=[ld32, gwm_tok, xn_last] + E.bank.wdeps(4))
            trs = s.emit("tensor", lambda h, tcl=tcl: h.matmul(
                rcol, E.ln_rows[0:1, 1024 + tcl * 128: 1024 + tcl * 128 + 128],
                E.ones_f[0:1, 0:1], start=True, stop=True), deps=[a2])
            v1 = s.emit("vector", lambda h: h.tensor_copy(G[:, 0:8], lg), deps=[t1])
            v2 = s.emit("vector", lambda h: h.reduce_max(G[:, 8:9], G[:, 0:8], axis=AX),
                        deps=[v1])
            vrc = s.emit("vector", lambda h: h.tensor_copy(G[:, 9:10], rcol), deps=[trs])
            vz = s.emit("vector", lambda h: h.tensor_scalar(G[:, 10:18], G[:, 0:8],
                                                            G[:, 8:9], None, ALU.subtract),
                        deps=[v2])
            vz2 = s.emit("vector", lambda h: h.tensor_scalar(G[:, 10:18], G[:, 10:18],
                                                             G[:, 9:10], None, ALU.mult),
                        deps=[vz, vrc])
            a3 = s.emit("scalar", lambda h: h.activation(out=G[:, 10:18], in_=G[:, 10:18],
                                                         func=AF.Exp),
                        deps=[vz2])
            v4 = s.emit("vector", lambda h: h.reduce_sum(G[:, 18:19], G[:, 10:18], axis=AX),
                        deps=[a3])
            v5 = s.emit("vector", lambda h, c=c: h.reciprocal(scs[:, c:c + 1], G[:, 18:19]),
                        deps=[v4])
            v6 = s.emit("vector", lambda h: h.tensor_scalar(G[:, 20:28], G[:, 0:8], G[:, 8:9],
                                                            None, ALU.is_equal),
                        deps=[v2, v1])
            t2 = s.emit("tensor", lambda h: h.matmul(pfx, E.ltri_bf[:], G[:, 20:28],
                                                     start=True, stop=False),
                        deps=[v6, cst_tok])
            t3 = s.emit("tensor", lambda h: h.matmul(pfx, E.ones_f[0:1, 0:128],
                                                     offs[0:1, 0:8], start=False, stop=True),
                        deps=off_chain)
            t4 = s.emit("tensor", lambda h: h.matmul(tot, E.ones_f[:, 0:1], G[:, 20:28],
                                                     start=True, stop=True),
                        deps=[v6] + E.bank.wdeps(5))
            v8 = s.emit("vector", lambda h: h.tensor_mul(G[:, 10:18], pfx, G[:, 20:28]),
                        deps=[t3, v6])
            v9 = s.emit("vector", lambda h: h.reduce_sum(G[:, 28:29], G[:, 10:18], axis=AX),
                        deps=[v8])
            vA = s.emit("vector", lambda h, c=c: h.tensor_copy(idxs[:, c:c + 1], G[:, 28:29]),
                        deps=[v9])
            voff = s.emit("vector", lambda h: h.tensor_add(offs[0:1, 0:8],
                                                           offs[0:1, 0:8],
                                                           tot),
                          deps=[t4, t3] + off_chain)
            off_chain = [voff]
            E.bank.read(4, v8)
            E.bank.read(4, vrc)
            E.bank.read(5, voff)
            lgbank_guard = [s.tok("V")]

            par = tcl % 2
            tlast = None
            for fc in range(8):
                tlast = s.emit("tensor", lambda h, fc=fc, tcl=tcl, par=par: h.transpose(
                    tp[par][:, fc * 128:(fc + 1) * 128],
                    xnc[:, fc * 512 + tcl * 128: fc * 512 + tcl * 128 + 128],
                    E.ident_bf[:]),
                    deps=([xn_last, cst_tok] + tp_guard[par]) if fc == 0 else [])
            tp_guard[par] = []
            ve = s.emit("vector", lambda h, par=par: h.tensor_copy(xts[par][:], tp[par][:]),
                        deps=[tlast] + xts_guard[par])
            xts_guard[par] = []
            tp_guard[par] = [ve]
            dsc = s.dma("gpsimd", lambda h, par=par, c=c: h.indirect_dma_start(
                out=E.ap["slx"],
                out_offset=bass.IndirectOffsetOnAxis(ap=idxs[:, c:c + 1], axis=0),
                in_=xts[par][:], in_offset=None,
                bounds_check=NSLOT - 1),
                deps=[ve, vA], sem="DIX")
            xts_guard[par] = [dsc]
            scat_toks.append(dsc)
        E.rows_guard = E.rows_guard + [s.tok("T")]
        xn_guard = [s.tok("T")]
        r3_guard = [s.tok("V"), s.tok("A")]
    scat_all = scat_toks[-1]

    ysl = E.p4_ysl
    gtok_guard = []
    gfm_guard = []
    h_guard = []
    ysl_guard = []
    w_guard = [[], []]
    sly_toks = []
    for e in range(NE):
        ldg = s.dma("scalar", lambda h, e=e: h.dma_start(
            out=gtok[:].rearrange("p (c f) -> p c f", f=1024),
            in_=E.ap["slx"][e * CAP:(e + 1) * CAP, :].rearrange("(c p) f -> p c f", p=128)),
            deps=[scat_all] + gtok_guard, sem="DGT")
        gtok_guard = []
        gfm_r = gfm[:].rearrange("p (fc ss) -> p fc ss", ss=CAP)
        tr_last = None
        for cs in range(6):
            par = cs % 2
            tl = None
            for fc in range(8):
                tl = s.emit("tensor", lambda h, cs=cs, fc=fc, par=par: h.transpose(
                    tp[par][:, fc * 128:(fc + 1) * 128],
                    gtok[:, cs * 1024 + fc * 128: cs * 1024 + fc * 128 + 128],
                    E.ident_bf[:]),
                    deps=([ldg, cst_tok] + tp_guard[par]) if fc == 0 else [])
            tp_guard[par] = []
            ve = s.emit("vector", lambda h, cs=cs, par=par: h.tensor_copy(
                gfm_r[:, :, cs * 128: cs * 128 + 128],
                tp[par][:].rearrange("p (fc t) -> p fc t", t=128)),
                deps=[tl] + gfm_guard)
            gfm_guard = []
            tp_guard[par] = [ve]
            tr_last = ve
        gtok_guard = [s.tok("T")]

        h_last = None
        for hcg in range(4):
            slot = hcg % 2
            ld1 = s.dma("gpsimd", lambda h, e=e, hcg=hcg, slot=slot: h.dma_start(
                out=wb[slot][:].rearrange("p (c f) -> p c f", f=1024),
                in_=rearr_cpf(E.ap["w1ff1"][e])[:, :, hcg * 1024:(hcg + 1) * 1024]),
                deps=w_guard[slot], sem=f"D41{slot}")
            w_guard[slot] = []
            for hcl in range(8):
                hc = hcg * 8 + hcl
                for sg in range(2):
                    sgw = 512 if sg == 0 else CAP - 512
                    bank = (hcl * 2 + sg) % 4

                    def gmm(h, hcl=hcl, sg=sg, sgw=sgw, bank=bank, slot=slot):
                        ins = None
                        for ci in range(8):
                            ins = h.matmul(
                                E.pb[bank][:, 0:sgw],
                                wb[slot][:, ci * 1024 + hcl * 128: ci * 1024 + hcl * 128 + 128],
                                gfm[:, ci * CAP + sg * 512: ci * CAP + sg * 512 + sgw],
                                start=(ci == 0), stop=(ci == 7))
                        return ins
                    tg = s.emit("tensor", lambda h, f=gmm: f(h),
                                deps=[ld1, tr_last] + E.bank.wdeps(bank))
                    bcol = E.b1f1[:, e * 32 + hc: e * 32 + hc + 1]
                    ag = s.emit("scalar", lambda h, hc=hc, sg=sg, sgw=sgw, bank=bank,
                                bcol=bcol: h.activation(
                                    out=hsb[:, hc * CAP + sg * 512: hc * CAP + sg * 512 + sgw],
                                    in_=E.pb[bank][:, 0:sgw], func=AF.Silu, bias=bcol),
                                deps=[tg, cst_tok] + h_guard)
                    h_guard = []
                    E.bank.read(bank, ag)
                    h_last = ag
            w_guard[slot] = [s.tok("T")]
        h_done = h_last
        gfm_guard = [s.tok("T")]

        ysl_ev = []
        for dq in range(4):
            slot = dq % 2
            ld2 = s.dma("gpsimd", lambda h, e=e, dq=dq, slot=slot: h.dma_start(
                out=wb[slot][:, 0:8192].rearrange("p (c f) -> p c f", f=256),
                in_=rearr_cpf(E.ap["w1ff2"][e])[:, :, dq * 256:(dq + 1) * 256]),
                deps=w_guard[slot], sem=f"D42{slot}")
            w_guard[slot] = []
            bdep = {cs: E.bank.wdeps(cs) for cs in range(6)}
            for hc in range(32):
                for cs in range(6):
                    def ymm(h, hc=hc, cs=cs, dq=dq, slot=slot):
                        return h.matmul(
                            E.pb[cs][:, 0:256],
                            hsb[:, hc * CAP + cs * 128: hc * CAP + cs * 128 + 128],
                            wb[slot][:, hc * 256: hc * 256 + 256],
                            start=(hc == 0), stop=False)
                    s.emit("tensor", lambda h, f=ymm: f(h),
                           deps=[ld2, h_done] + (bdep[cs] if hc == 0 else []))
            for cs in range(6):
                tb2 = s.emit("tensor", lambda h, e=e, dq=dq, cs=cs: h.matmul(
                    E.pb[cs][:, 0:256],
                    E.ones_bf[0:1, 0:128],
                    E.b2sb[0:1, e * 1024 + dq * 256: e * 1024 + dq * 256 + 256],
                    start=False, stop=True), deps=[cst_tok])
                v = s.emit("vector", lambda h, dq=dq, cs=cs: h.tensor_copy(
                    ysl[:, cs * 1024 + dq * 256: cs * 1024 + dq * 256 + 256],
                    E.pb[cs][:, 0:256]),
                    deps=[tb2] + (ysl_guard if (dq == 0 and cs == 0) else []))
                if dq == 0 and cs == 0:
                    ysl_guard = []
                E.bank.read(cs, v)
                ysl_ev.append(v)
            w_guard[slot] = [s.tok("T")]
        h_guard = [s.tok("T")]
        ev_all = ysl_ev[-1]
        for cs in range(6):
            dsl = s.dma("sync", lambda h, e=e, cs=cs: h.dma_start(
                out=E.ap["sly"][e * CAP + cs * 128: e * CAP + (cs + 1) * 128, :],
                in_=ysl[:, cs * 1024:(cs + 1) * 1024]), deps=[ev_all], sem="DSLY")
            sly_toks.append(dsl)
        ysl_guard = [s.tok("DSLY")]
    sly_all = sly_toks[-1]

    r3_guard = [s.tok("V"), s.tok("A")]
    for ei in range(8):
        ld32 = s.dma("sync", lambda h, ei=ei: h.dma_start(
            out=r3q[:].rearrange("p (c t) -> p c t", t=512),
            in_=rearr_groups(E.ap["r3"])[:, :, ei * 512:(ei + 1) * 512]),
            deps=r3_guard, sem="DR32")
        r3_guard = []
        radd_last = None
        for tcl in range(4):
            c = ei * 4 + tcl
            par = tcl % 2
            dg = s.dma("gpsimd", lambda h, c=c, par=par: h.indirect_dma_start(
                out=xts[par][:], out_offset=None,
                in_=E.ap["sly"],
                in_offset=bass.IndirectOffsetOnAxis(ap=idxs[:, c:c + 1], axis=0)),
                deps=[sly_all] + xts_guard[par], sem="DIX")
            xts_guard[par] = []
            vs = s.emit("vector", lambda h, c=c, par=par: h.tensor_scalar(
                xts[par][:], xts[par][:], scs[:, c:c + 1], None, ALU.mult),
                deps=[dg])
            tl = None
            for fc in range(8):
                tl = s.emit("tensor", lambda h, fc=fc, par=par: h.transpose(
                    tp[par][:, fc * 128:(fc + 1) * 128], xts[par][:, fc * 128:(fc + 1) * 128],
                    E.ident_bf[:]),
                    deps=([vs, cst_tok] + tp_guard[par]) if fc == 0 else [])
            tp_guard[par] = []
            r3r = r3q[:].rearrange("p (dc t5) -> p dc t5", t5=512)
            va = s.emit("vector", lambda h, tcl=tcl, par=par: h.tensor_add(
                r3r[:, :, tcl * 128: tcl * 128 + 128],
                tp[par][:].rearrange("p (dc t) -> p dc t", t=128),
                r3r[:, :, tcl * 128: tcl * 128 + 128]),
                deps=[tl, ld32])
            tp_guard[par] = [va]
            xts_guard[par] = [s.tok("T")]
            radd_last = va

        def dst_fn(dc, tt, ei=ei):
            return rearr_groups(E.ap[out_name])[:, dc, ei * 512:(ei + 1) * 512]
        emit_postln(E, r3q, 512, dst_fn, [radd_last], E.eps[0:1, 0:1],
                    rbf=xnc, sqb=sq2)
        r3_guard = [s.tok("V"), s.tok("A")]

def build_nc(phases=("p1", "p2", "p3", "p4")):
    nc = bass.Bass("TRN2", target_bir_lowering=False, debug=False)
    E = Env()
    E.nc = nc
    s = Sched()
    E.s = s
    E.bank = BankGuard()

    ap = {}
    ap["x0"] = nc.dram_tensor("x0", [128, 8 * S], F32, kind="ExternalInput").ap()
    chain = {"p1": ("x_in", "r1"), "p2": ("r1", "r2"), "p3": ("r2", "r3"), "p4": ("r3", "y")}
    needed = set()
    for ph in phases:
        needed.add(chain[ph][0])
        needed.add(chain[ph][1])
    needed.discard("x_in")
    out_hbm = chain[phases[-1]][1]
    in_hbm = chain[phases[0]][0]
    for name in sorted(needed):
        if name == out_hbm:
            ap[name] = nc.dram_tensor(name, [128, 8 * S], F32, kind="ExternalOutput").ap()
        else:
            ap[name] = nc.dram_tensor(name, [128, 8 * S], F32, kind="Internal").ap()
    if in_hbm != "x_in":
        ap[in_hbm] = ap["x0"]
    else:
        ap["x_in"] = ap["x0"]

    def din(name, shape, dt):
        ap[name] = nc.dram_tensor(name, shape, dt, kind="ExternalInput").ap()
    din("w0q", [D, D], F32); din("w0k", [D, D], F32); din("w0v", [D, D], F32); din("w0o", [D, D], F32)
    din("b0q", [128, 8], F32); din("b0k", [128, 8], F32); din("b0o", [128, 8], F32)
    din("b0v_row", [1, D], F32)
    din("w0g_t", [D, FF0], I32); din("w0u_t", [D, FF0], I32); din("w0d_t", [FF0, D], I32)
    din("w1q_t", [D, D], F32); din("w1k_t", [D, D], F32); din("w1v_t", [D, D], F32)
    din("w1o_t", [D, D], F32)
    din("b1q", [128, 8], F32); din("b1k", [128, 8], F32); din("b1o", [128, 8], F32)
    din("b1v_row", [1, D], F32)
    din("rmsw", [128, 8], F32)
    din("gw_t", [D, NE], F32)
    din("b1f1", [128, 256], F32)
    din("b2r", [1, NE * D], F32)
    ap["w1ff1"] = [nc.dram_tensor(f"w1ff1_{e}", [D, FF1], F32, kind="ExternalInput").ap()
                   for e in range(NE)]
    ap["w1ff2"] = [nc.dram_tensor(f"w1ff2_{e}", [FF1, D], F32, kind="ExternalInput").ap()
                   for e in range(NE)]
    if "p4" in phases:
        ap["slx"] = nc.dram_tensor("slx", [NSLOT, D], BF16, kind="Internal").ap()
        ap["sly"] = nc.dram_tensor("sly", [NSLOT, D], BF16, kind="Internal").ap()
    c_ones_bf = nc.inline_tensor(np.ones((128, 128), dtype=BF16NP), "c_ones_bf").ap()
    c_ones_f = nc.inline_tensor(np.ones((128, 128), dtype=np.float32), "c_ones_f").ap()
    c_neg_f = nc.inline_tensor(np.full((1, 128), -1.0, dtype=np.float32), "c_neg_f").ap()
    c_invd = nc.inline_tensor(np.full((128, 1), 1.0 / 1024.0, dtype=BF16NP), "c_invd").ap()
    E.c_ident_bf = nc.inline_tensor(np.eye(128, dtype=BF16NP), "c_ident_bf").ap()
    E.c_ltri = nc.inline_tensor(
        np.triu(np.ones((128, 128), dtype=np.float32)), "c_ltri").ap()
    offs0 = np.zeros((128, 8), dtype=np.float32)
    offs0[0, :] = CAP * np.arange(8, dtype=np.float32) - 1.0
    E.c_offs0 = nc.inline_tensor(offs0, "c_offs0").ap()
    c_eps = nc.inline_tensor(np.array([[EPS_LN, EPS_RMS]], dtype=np.float32), "c_eps").ap()
    E.ap = ap

    with ExitStack() as ctx:
        sb = lambda *a: ctx.enter_context(nc.sbuf_tensor(*a))
        E.pbig = ctx.enter_context(nc.psum_tensor("pbig", [128, 3072], F32))
        E.pb = [E.pbig[:, i * 512:(i + 1) * 512] for i in range(6)]
        E.tp = [ctx.enter_context(nc.psum_tensor(f"tp{i}", [128, 1024], BF16))
                for i in range(2)]
        E.lnb = (2, 3, 4, 5)
        E.ones_bf = sb("ones_bf", [128, 128], BF16)
        E.ones_f = sb("ones_f", [128, 128], F32)
        E.neg_f = sb("neg_f", [1, 128], F32)
        E.invD_bf = sb("invD_bf", [128, 1], BF16)
        E.eps = sb("eps", [1, 2], F32)
        E.bias = {}
        for name in ("b0q", "b0k", "b0o", "b1q", "b1k", "b1o", "rmsw"):
            E.bias[name] = sb("t_" + name, [128, 8], F32)
        E.ln_rows = sb("ln_rows", [1, 1536], F32)
        E.ln_stg = [sb(f"ln_stg{i}", [128, 512], F32) for i in range(2)]
        E.stg_guard = [[], []]
        E.ln_guard = []
        E.rows_guard = []

        s.dma("sync", lambda h: h.dma_start(out=E.ones_bf[:], in_=c_ones_bf), sem="DC")
        s.dma("sync", lambda h: h.dma_start(out=E.ones_f[:], in_=c_ones_f), sem="DC")
        s.dma("sync", lambda h: h.dma_start(out=E.neg_f[:], in_=c_neg_f), sem="DC")
        s.dma("sync", lambda h: h.dma_start(out=E.invD_bf[:], in_=c_invd), sem="DC")
        s.dma("sync", lambda h: h.dma_start(out=E.eps[:], in_=c_eps), sem="DC")
        for name in ("b0q", "b0k", "b0o", "b1q", "b1k", "b1o", "rmsw"):
            s.dma("sync", lambda h, name=name: h.dma_start(out=E.bias[name][:], in_=ap[name]),
                  sem="DC")
        E.bias_tok = s.tok("DC")
        s.barrier()

        if "p1" in phases:
            with ExitStack() as pc:
                p = lambda *a: pc.enter_context(nc.sbuf_tensor(*a))
                E.p1_w = [p(f"p1w{i}", [128, 8 * 1024], BF16) for i in range(4)]
                E.p1_bvrow = p("p1_bvrow", [1, D], F32)
                E.p1_bvbc = p("p1_bvbc", [128, 1024], F32)
                E.p1_xg32 = p("p1_xg32", [128, 8 * 512], F32)
                E.p1_xgbf = p("p1_xgbf", [128, 8 * 512], BF16)
                E.p1_q = p("p1_q", [128, 8 * 512], BF16)
                E.p1_k = p("p1_k", [128, 8 * 512], BF16)
                E.p1_v = p("p1_v", [128, 4 * 1024], BF16)
                E.p1_att = p("p1_att", [128, 8 * 512], BF16)
                E.p1_expst = [[p(f"p1_expst{i}_{b}", [128, 64], BF16) for b in range(2)]
                              for i in range(2)]
                E.p1_recip = [p(f"p1_recip{i}", [1, 128], F32) for i in range(2)]
                E.p1_bcsb = [p(f"p1_bcsb{i}", [128, 128], F32) for i in range(2)]
                emit_p1(E)
            s.barrier()
        if "p2" in phases:
            with ExitStack() as pc:
                p = lambda *a: pc.enter_context(nc.sbuf_tensor(*a))
                E.p2_wd = p("p2_wd", [128, 24 * 1024], BF16)
                E.p2_wg = [p(f"p2_wg{i}", [128, 8 * 512], BF16) for i in range(2)]
                E.p2_wu = [p(f"p2_wu{i}", [128, 8 * 512], BF16) for i in range(2)]
                E.p2_h = p("p2_h", [128, 24 * 512], BF16)
                E.p2_x32 = p("p2_x32", [128, 8 * 512], F32)
                E.p2_xbf = p("p2_xbf", [128, 8 * 512], BF16)
                E.sg = [p(f"p2sg{i}", [128, 512], F32) for i in range(2)]
                E.sg_guard = [[], []]
                emit_p2(E)
            s.barrier()
        if "p3" in phases:
            with ExitStack() as pc:
                p = lambda *a: pc.enter_context(nc.sbuf_tensor(*a))
                E.p3_k = p("p3_k", [128, 8 * 4096], BF16)
                E.p3_vext = p("p3_vext", [128, 32 * 1040], BF16)
                E.p3_wbuf = p("p3_wbuf", [128, 8 * 1024], BF16)
                E.p3_xbf0 = p("p3_xbf0", [128, 8 * 512], BF16)
                emit_p3(E, pc)
            s.barrier()
        if "p4" in phases:
            with ExitStack() as pc:
                p = lambda *a: pc.enter_context(nc.sbuf_tensor(*a))
                E.p4_r3 = p("p4_r3", [128, 8 * 512], F32)
                E.p4_xn = p("p4_xn", [128, 8 * 512], BF16)
                E.p4_sq = p("p4_sq", [128, 8 * 512], BF16)
                E.p4_xts = [p(f"p4_xts{i}", [128, 1024], BF16) for i in range(2)]
                E.p4_gtok = p("p4_gtok", [128, 6 * 1024], BF16)
                E.p4_ysl = p("p4_ysl", [128, 6 * 1024], BF16)
                E.p4_gfm = p("p4_gfm", [128, 8 * CAP], BF16)
                E.p4_h = p("p4_h", [128, 32 * CAP], BF16)
                E.p4_w = [p(f"p4_w{i}", [128, 8 * 1024], BF16) for i in range(2)]
                E.p4_g = p("p4_g", [128, 32], F32)
                E.p4_mbf = p("p4_mbf", [128, 8], BF16)
                E.p4_idx = p("p4_idx", [128, 32], U32)
                E.p4_sc = p("p4_sc", [128, 32], F32)
                E.p4_offs = p("p4_offs", [128, 8], F32)
                E.gw_sb = p("gw_sb", [128, 64], F32)
                E.b2sb = p("b2sb", [1, NE * 1024], BF16)
                E.b1f1 = p("t_b1f1", [128, 256], F32)
                E.ident_bf = p("ident_bf", [128, 128], BF16)
                E.ltri_bf = p("ltri_bf", [128, 128], F32)
                emit_p4s(E, out_hbm)
            s.barrier()

        fence_deps = [s.tok(name) for name in list(s.cnt.keys())]
        s.emit("sync", lambda h: h.nop(), deps=fence_deps)
        s.realize(nc, ctx)
    return nc


def _prep_shared(inputs):
    f = np.ascontiguousarray
    d = {}
    d["w0q"] = f(np.asarray(inputs["L0_attn_Wq"], np.float32))
    d["w0k"] = f(np.asarray(inputs["L0_attn_Wk"], np.float32))
    d["w0v"] = f(np.asarray(inputs["L0_attn_Wv"], np.float32))
    d["w0o"] = f(np.asarray(inputs["L0_attn_Wo"], np.float32))
    d["b0q"] = f(np.asarray(inputs["L0_attn_bq"], np.float32).reshape(8, 128).T)
    d["b0k"] = f(np.asarray(inputs["L0_attn_bk"], np.float32).reshape(8, 128).T)
    d["b0o"] = f(np.asarray(inputs["L0_attn_bo"], np.float32).reshape(8, 128).T)
    d["b0v_row"] = f(np.asarray(inputs["L0_attn_bv"], np.float32).reshape(1, 1024))
    d["w0g_t"] = f(np.asarray(inputs["L0_ffn_w_gate"], np.int32).T)
    d["w0u_t"] = f(np.asarray(inputs["L0_ffn_w_up"], np.int32).T)
    d["w0d_t"] = f(np.asarray(inputs["L0_ffn_w_down"], np.int32).T)
    d["w1q_t"] = f(np.asarray(inputs["L1_attn_wq"], np.float32).transpose(1, 0, 2).reshape(1024, 1024))
    d["w1k_t"] = f(np.asarray(inputs["L1_attn_wk"], np.float32).transpose(1, 0, 2).reshape(1024, 1024))
    d["w1v_t"] = f(np.asarray(inputs["L1_attn_wv"], np.float32).transpose(1, 0, 2).reshape(1024, 1024))
    d["w1o_t"] = f(np.asarray(inputs["L1_attn_wo"], np.float32).T)
    d["b1q"] = f(np.asarray(inputs["L1_attn_bq"], np.float32).reshape(1024).reshape(8, 128).T)
    d["b1k"] = f(np.asarray(inputs["L1_attn_bk"], np.float32).reshape(1024).reshape(8, 128).T)
    d["b1o"] = f(np.asarray(inputs["L1_attn_bo"], np.float32).reshape(8, 128).T)
    d["b1v_row"] = f(np.asarray(inputs["L1_attn_bv"], np.float32).reshape(1, 1024))
    d["rmsw"] = f(np.asarray(inputs["L1_ffn_rms_w"], np.float32).reshape(8, 128).T)
    d["gw_t"] = f(np.asarray(inputs["L1_ffn_gate_w"], np.float32).T)
    d["b1f1"] = f(np.asarray(inputs["L1_ffn_b1"], np.float32).reshape(8, 32, 128)
                  .transpose(2, 0, 1).reshape(128, 256))
    d["b2r"] = f(np.asarray(inputs["L1_ffn_b2"], np.float32).reshape(1, NE * 1024))
    for e in range(NE):
        d[f"w1ff1_{e}"] = f(np.asarray(inputs["L1_ffn_W1"][e], np.float32))
        d[f"w1ff2_{e}"] = f(np.asarray(inputs["L1_ffn_W2"][e], np.float32))
    return d


def fm_layout(x2d):
    fm = np.ascontiguousarray(np.asarray(x2d, np.float32).T)
    return np.ascontiguousarray(fm.reshape(8, 128, S).transpose(1, 0, 2).reshape(128, 8 * S))


def fm_unlayout(y):
    fm = y.reshape(128, 8, S).transpose(1, 0, 2).reshape(1024, S)
    return np.ascontiguousarray(fm.T)


def kernel(**inputs):
    shared = _prep_shared(inputs)
    x = np.asarray(inputs["L0_attn_src"], dtype=np.float32)
    nc = build_nc(phases=("p1", "p2", "p3", "p4"))
    in_maps = []
    for b in range(8):
        m = dict(shared)
        m["x0"] = fm_layout(x[b])
        in_maps.append(m)
    res = run_bass_kernel_spmd(nc, in_maps, core_ids=list(range(8)))
    return np.stack([fm_unlayout(np.asarray(r["y"], dtype=np.float32)) for r in res.results])


# version 17
# speedup vs baseline: 2.2874x; 1.0027x over previous
import os
import numpy as np
import ml_dtypes
from contextlib import ExitStack

import concourse.bass as bass
import concourse.mybir as mybir
from concourse.bass_utils import run_bass_kernel_spmd

F32 = mybir.dt.float32
BF16 = mybir.dt.bfloat16
I32 = mybir.dt.int32
AF = mybir.ActivationFunctionType
ALU = mybir.AluOpType
AX = mybir.AxisListType.X
BF16NP = ml_dtypes.bfloat16

S = 4096
D = 1024
SC0 = float(1.0 / np.sqrt(128.0))
SC1 = 0.125
EPS_LN = 1e-5
EPS_RMS = 1e-6
FF0 = 3072
FF1 = 4096
NE = 8
INV127 = float(1.0 / 127.0)
INV127SQ = float(1.0 / (127.0 * 127.0))
U32 = mybir.dt.uint32
CAP = 768
NSLOT = NE * CAP

ENGINES = ("sync", "tensor", "scalar", "vector", "gpsimd")
OWN_SEM = {"tensor": "T", "scalar": "A", "vector": "V", "gpsimd": "G", "sync": "SY"}


class Sched:
    def __init__(self):
        self.q = {e: [] for e in ENGINES}
        self.cnt = {}
        self.pending = {e: [] for e in ENGINES}

    def tok(self, sem):
        return (sem, self.cnt.get(sem, 0))

    def emit(self, eng, fn, deps=(), sig=None, inc=1):
        if sig is None:
            sig = OWN_SEM[eng]
        for d in deps:
            if d is None:
                continue
            s_, v = d
            assert v <= self.cnt.get(s_, 0), f"dep on future token {s_}={v}"
        self.cnt[sig] = self.cnt.get(sig, 0) + inc
        token = (sig, self.cnt[sig])
        dd = self.pending[eng] + [d for d in deps if d is not None]
        self.pending[eng] = []
        self.q[eng].append((dd, fn, sig, inc))
        return token

    def dma(self, eng, fn, deps=(), sem="DS"):
        return self.emit(eng, fn, deps=deps, sig=sem, inc=16)

    def barrier(self):
        snap = [(s_, v) for (s_, v) in self.cnt.items()]
        for e in ENGINES:
            self.pending[e] = self.pending[e] + snap

    def realize(self, nc, ctx):
        sems = {name: ctx.enter_context(nc.semaphore(name)) for name in self.cnt}
        block = ctx.enter_context(nc.Block())

        def mk(eng):
            def body(h):
                waited = {}
                for (deps, fn, sig, inc) in self.q[eng]:
                    need = {}
                    for (s_, v) in deps:
                        if v > waited.get(s_, 0) and v > need.get(s_, 0):
                            need[s_] = v
                    for s_ in sorted(need):
                        h.wait_ge(sems[s_], need[s_])
                        waited[s_] = need[s_]
                    fn(h).then_inc(sems[sig], inc)
            return body

        block.sync(mk("sync"))
        block.tensor(mk("tensor"))
        block.scalar(mk("scalar"))
        block.vector(mk("vector"))
        block.gpsimd(mk("gpsimd"))


class BankGuard:
    def __init__(self):
        self.readers = {}

    def read(self, b, token):
        self.readers.setdefault(b, []).append(token)

    def wdeps(self, b):
        d = self.readers.get(b, [])
        self.readers[b] = []
        return d


def rearr_cpf(ap, p=128):
    return ap.rearrange("(c p) f -> p c f", p=p)


def rearr_groups(ap, t=S):
    return ap.rearrange("p (c t) -> p c t", t=t)


class Env:
    pass


def emit_postln(E, r_sb, T, dst_fn, r_deps, eps_col, rbf, sqb):
    s = E.s
    b0, b1, b2_, b3 = E.lnb
    nTT = T // 512
    vcp = s.emit("vector", lambda h: h.tensor_copy(rbf[:, 0:8 * T], r_sb[:, 0:8 * T]),
                 deps=list(r_deps) + E.ln_guard)
    acp = s.emit("scalar", lambda h: h.activation(out=sqb[:, 0:8 * T], in_=r_sb[:, 0:8 * T],
                                                  func=AF.Square),
                 deps=list(r_deps) + E.ln_guard)
    R = E.ln_rows
    for tt in range(nTT):
        mean_ps = E.pb[b0][0:1, 0:512]
        msq_ps = E.pb[b1][0:1, 0:512]

        def statmm(h, src, dst, tt=tt):
            ins = None
            for dc in range(8):
                ins = h.matmul(dst, E.invD_bf[:, 0:1],
                               src[:, dc * T + tt * 512: dc * T + tt * 512 + 512],
                               start=(dc == 0), stop=(dc == 7))
            return ins
        t_mean = s.emit("tensor", lambda h, tt=tt: statmm(h, rbf, E.pb[b0][0:1, 0:512], tt),
                        deps=[vcp] + E.bank.wdeps(b0))
        t_msq = s.emit("tensor", lambda h, tt=tt: statmm(h, sqb, E.pb[b1][0:1, 0:512], tt),
                       deps=[acp] + E.bank.wdeps(b1))
        mrow = R[0:1, 0:512]
        m2 = R[0:1, 512:1024]
        vrow = R[0:1, 1024:1536]
        rstd = vrow
        v1 = s.emit("vector", lambda h: h.tensor_copy(mrow, mean_ps),
                    deps=[t_mean] + E.rows_guard)
        E.rows_guard = []
        E.bank.read(b0, v1)
        v2 = s.emit("vector", lambda h: h.tensor_mul(m2, mrow, mrow), deps=[v1])
        v3 = s.emit("vector", lambda h: h.tensor_sub(vrow, msq_ps, m2), deps=[t_msq, v2])
        E.bank.read(b1, v3)
        a1 = s.emit("scalar", lambda h: h.activation(out=vrow, in_=vrow, func=AF.Ln,
                                                     bias=eps_col), deps=[v3])
        a2 = s.emit("scalar", lambda h: h.activation(out=rstd, in_=vrow, func=AF.Exp,
                                                     scale=-0.5), deps=[a1])
        v4 = s.emit("vector", lambda h: h.tensor_mul(m2, mrow, rstd), deps=[a2])
        bca = E.pb[b2_][:, 0:512]
        bcb = E.pb[b3][:, 0:512]
        t_a = s.emit("tensor", lambda h: h.matmul(bca, E.ones_f[0:1, 0:128], rstd,
                                                  start=True, stop=True),
                     deps=[a2] + E.bank.wdeps(b2_))
        t_b = s.emit("tensor", lambda h: h.matmul(bcb, E.neg_f[0:1, 0:128], m2,
                                                  start=True, stop=True),
                     deps=[v4] + E.bank.wdeps(b3))
        E.rows_guard = [t_a, t_b]
        for dc in range(8):
            slot = (tt * 8 + dc) % 2
            stg = E.ln_stg[slot]
            vv1 = s.emit("vector",
                         lambda h, stg=stg, dc=dc, tt=tt: h.tensor_mul(
                             stg[:], r_sb[:, dc * T + tt * 512: dc * T + tt * 512 + 512], bca),
                         deps=[t_a, t_b] + E.stg_guard[slot])
            vv2 = s.emit("vector", lambda h, stg=stg: h.tensor_add(stg[:], stg[:], bcb),
                         deps=[vv1])
            E.bank.read(b2_, vv1)
            E.bank.read(b3, vv2)
            dtok = s.dma("sync", lambda h, stg=stg, dc=dc, tt=tt: h.dma_start(
                out=dst_fn(dc, tt), in_=stg[:]), deps=[vv2], sem=f"DST{slot}")
            E.stg_guard[slot] = [dtok]
    E.ln_guard = [s.tok("T")]


def emit_p1(E):
    BIS = os.environ.get("P1BISECT", "")
    s = E.s
    wq, wk, wv, wo = E.p1_w
    ldtoks = []
    for (sb_t, src) in ((wq, "w0q"), (wk, "w0k"), (wv, "w0v"), (wo, "w0o")):
        ldtoks.append(s.dma("gpsimd", lambda h, d=sb_t, src=src: h.dma_start(
            out=d[:].rearrange("p (c f) -> p c f", f=1024), in_=rearr_cpf(E.ap[src])),
            sem="DW1"))
    wtok = ldtoks[-1]
    bvr = s.dma("sync", lambda h: h.dma_start(out=E.p1_bvrow[:], in_=E.ap["b0v_row"]), sem="DC")
    bvbc = E.p1_bvbc
    bv_toks = []
    for half in range(2):
        t = s.emit("tensor", lambda h, half=half: h.matmul(
            E.pb[2][:, 0:512], E.ones_f[0:1, 0:128],
            E.p1_bvrow[0:1, half * 512:half * 512 + 512], start=True, stop=True),
            deps=[bvr] + E.bank.wdeps(2))
        a = s.emit("scalar", lambda h, half=half: h.copy(bvbc[:, half * 512:half * 512 + 512],
                                                         E.pb[2][:, 0:512]), deps=[t])
        E.bank.read(2, a)
        bv_toks.append(a)

    xg32 = E.p1_xg32
    xgbf = E.p1_xgbf
    q_sb, k_sb, v_sb, att_sb = E.p1_q, E.p1_k, E.p1_v, E.p1_att
    xg32_guard = []
    xgbf_guard = []
    gg = []

    ngroups = 1 if BIS else 8
    for g in range(ngroups):
        ld32 = s.dma("sync", lambda h, g=g: h.dma_start(
            out=xg32[:].rearrange("p (c t) -> p c t", t=512),
            in_=rearr_groups(E.ap["x_in"])[:, :, g * 512:(g + 1) * 512]),
            deps=xg32_guard, sem="DX32")
        ldbf = s.dma("gpsimd", lambda h, g=g: h.dma_start(
            out=xgbf[:].rearrange("p (c t) -> p c t", t=512),
            in_=rearr_groups(E.ap["x_in"])[:, :, g * 512:(g + 1) * 512]),
            deps=xgbf_guard + [wtok], sem="DXBF")
        xg32_guard = []
        xgbf_guard = []

        pe_toks = []
        for (w_sb, dst, bname) in ((wq, q_sb, "b0q"), (wk, k_sb, "b0k")):
            for dc in range(8):
                bank = len(pe_toks) % 2

                def pmm(h, w_sb=w_sb, dc=dc, bank=bank):
                    ins = None
                    for ci in range(8):
                        ins = h.matmul(E.pb[bank][:, 0:512],
                                       w_sb[:, ci * 1024 + dc * 128: ci * 1024 + dc * 128 + 128],
                                       xgbf[:, ci * 512:(ci + 1) * 512],
                                       start=(ci == 0), stop=(ci == 7))
                    return ins
                t = s.emit("tensor", lambda h, f=pmm: f(h),
                           deps=[ldbf] + E.bank.wdeps(bank))
                pe_toks.append(t)
                a = s.emit("scalar", lambda h, dst=dst, dc=dc, bank=bank, bname=bname:
                           h.activation(out=dst[:, dc * 512:(dc + 1) * 512],
                                        in_=E.pb[bank][:, 0:512], func=AF.Identity,
                                        bias=E.bias[bname][:, dc:dc + 1]),
                           deps=[t, E.bias_tok] + gg)
                E.bank.read(bank, a)
        for tc in range(4):
            for half in range(2):
                bank = len(pe_toks) % 2

                def vmm(h, tc=tc, half=half, bank=bank):
                    ins = None
                    for ci in range(8):
                        ins = h.matmul(E.pb[bank][:, 0:512],
                                       xgbf[:, ci * 512 + tc * 128: ci * 512 + tc * 128 + 128],
                                       wv[:, ci * 1024 + half * 512: ci * 1024 + half * 512 + 512],
                                       start=(ci == 0), stop=(ci == 7))
                    return ins
                t = s.emit("tensor", lambda h, f=vmm: f(h), deps=[ldbf] + E.bank.wdeps(bank))
                pe_toks.append(t)
                v = s.emit("vector", lambda h, tc=tc, half=half, bank=bank: h.tensor_add(
                    v_sb[:, tc * 1024 + half * 512: tc * 1024 + half * 512 + 512],
                    E.pb[bank][:, 0:512], bvbc[:, half * 512:half * 512 + 512]),
                    deps=[t] + bv_toks + gg)
                E.bank.read(bank, v)
        proj_done_v = s.tok("V")
        proj_done_a = s.tok("A")
        if BIS == "proj":
            s.dma("gpsimd", lambda h: h.dma_start(
                out=rearr_groups(E.ap["r1"])[:, :, 0:512],
                in_=q_sb[:].rearrange("p (c t) -> p c t", t=512)),
                deps=[proj_done_v, proj_done_a], sem="DXBF")
            return

        it = 0
        if g == 0:
            z0 = s.emit("vector", lambda h: h.memset(E.p1_expst[0][0][64:128, :], 0.0))
            z1 = s.emit("vector", lambda h: h.memset(E.p1_expst[1][0][64:128, :], 0.0))
            z2 = s.emit("vector", lambda h: h.memset(E.p1_expst[0][1][0:64, :], 0.0))
            z3 = s.emit("vector", lambda h: h.memset(E.p1_expst[1][1][0:64, :], 0.0))
            E.p1_expst_init = [z0, z1, z2, z3]
        expst_g = [list(E.p1_expst_init), list(E.p1_expst_init)]
        recip_g = [[], []]
        bcsb_g = [[], []]
        natt = int(os.environ.get("P1ATT_N", "32"))
        aops = int(os.environ.get("P1ATT_OPS", "7"))
        for tc in range(4):
            for hh in range(8):
                if it >= natt:
                    continue
                sl = it % 2
                st_ps = E.pb[2 + sl]
                da_ps = E.pb[4 + sl]
                bc_ps = E.pb[sl]
                exp0 = E.p1_expst[sl][0]
                exp1 = E.p1_expst[sl][1]
                recip = E.p1_recip[sl]
                bc_sb = E.p1_bcsb[sl]

                def stmm(h, tc=tc, hh=hh, st_ps=st_ps):
                    base = hh * 512 + tc * 128
                    h.matmul(st_ps[0:64, 0:64], k_sb[:, base:base + 64],
                             q_sb[:, base:base + 64], start=True, stop=True)
                    return h.matmul(st_ps[64:128, 0:64], k_sb[:, base + 64:base + 128],
                                    q_sb[:, base + 64:base + 128], start=True, stop=True,
                                    tile_position=(0, 64))
                t1 = s.emit("tensor", lambda h, f=stmm: f(h),
                            deps=[proj_done_v, proj_done_a] + E.bank.wdeps(2 + sl))
                if aops < 2:
                    it += 1
                    continue
                e1a = s.emit("scalar", lambda h, st_ps=st_ps, exp0=exp0: h.activation(
                    out=exp0[0:64, :], in_=st_ps[0:64, 0:64], func=AF.Exp, scale=SC0),
                    deps=[t1] + expst_g[sl])
                e1 = s.emit("scalar", lambda h, st_ps=st_ps, exp1=exp1: h.activation(
                    out=exp1[64:128, :], in_=st_ps[64:128, 0:64], func=AF.Exp, scale=SC0),
                    deps=[t1])
                E.bank.read(2 + sl, e1)
                if aops < 3:
                    it += 1
                    continue

                def dpv(h, tc=tc, hh=hh, da_ps=da_ps, exp0=exp0, exp1=exp1):
                    h.matmul(da_ps[0:1, 128:192], E.ones_bf[:, 0:1], exp0[:, :],
                             start=True, stop=True)
                    h.matmul(da_ps[0:1, 192:256], E.ones_bf[:, 0:1], exp1[:, :],
                             start=True, stop=True)
                    vb = tc * 1024 + hh * 128
                    h.matmul(da_ps[:, 0:64], v_sb[:, vb:vb + 128], exp0[:, :],
                             start=True, stop=True)
                    return h.matmul(da_ps[:, 64:128], v_sb[:, vb:vb + 128],
                                    exp1[:, :], start=True, stop=True)
                t2 = s.emit("tensor", lambda h, f=dpv: f(h),
                            deps=[e1a, e1] + E.bank.wdeps(4 + sl))
                expst_g[sl] = [t2]
                if aops < 4:
                    it += 1
                    continue
                r1 = s.emit("vector", lambda h, da_ps=da_ps, recip=recip: h.reciprocal(
                    recip[0:1, 0:128], da_ps[0:1, 128:256]), deps=[t2] + recip_g[sl])
                E.bank.read(4 + sl, r1)
                if aops < 5:
                    it += 1
                    continue
                t3 = s.emit("tensor", lambda h, bc_ps=bc_ps, recip=recip: h.matmul(
                    bc_ps[:, 0:128], E.ones_f[0:1, 0:128], recip[0:1, 0:128],
                    start=True, stop=True), deps=[r1] + E.bank.wdeps(sl))
                recip_g[sl] = [t3]
                if aops < 6:
                    it += 1
                    continue
                a3 = s.emit("scalar", lambda h, bc_ps=bc_ps, bc_sb=bc_sb: h.copy(
                    bc_sb[:], bc_ps[:, 0:128]), deps=[t3] + bcsb_g[sl])
                E.bank.read(sl, a3)
                if aops < 7:
                    it += 1
                    continue
                v3 = s.emit("vector", lambda h, tc=tc, hh=hh, da_ps=da_ps, bc_sb=bc_sb:
                            h.tensor_mul(att_sb[:, hh * 512 + tc * 128: hh * 512 + tc * 128 + 128],
                                         da_ps[:, 0:128], bc_sb[:]),
                            deps=[a3, t2])
                E.bank.read(4 + sl, v3)
                expst_g[sl] = [t2]
                bcsb_g[sl] = [v3]
                it += 1
        att_done = s.tok("V")
        if BIS == "attn":
            s.dma("gpsimd", lambda h: h.dma_start(
                out=rearr_groups(E.ap["r1"])[:, :, 0:512],
                in_=att_sb[:].rearrange("p (c t) -> p c t", t=512)),
                deps=[att_done], sem="DXBF")
            return

        wo_last = None
        for dc in range(8):
            bank = dc % 2

            def womm(h, dc=dc, bank=bank):
                ins = None
                for ci in range(8):
                    ins = h.matmul(E.pb[bank][:, 0:512],
                                   wo[:, ci * 1024 + dc * 128: ci * 1024 + dc * 128 + 128],
                                   att_sb[:, ci * 512:(ci + 1) * 512],
                                   start=(ci == 0), stop=(ci == 7))
                return ins
            t = s.emit("tensor", lambda h, f=womm: f(h),
                       deps=[att_done] + E.bank.wdeps(bank))
            v = s.emit("vector", lambda h, dc=dc, bank=bank: h.scalar_tensor_tensor(
                xg32[:, dc * 512:(dc + 1) * 512], E.pb[bank][:, 0:512],
                E.bias["b0o"][:, dc:dc + 1], xg32[:, dc * 512:(dc + 1) * 512],
                ALU.add, ALU.add),
                deps=[t, ld32, E.bias_tok])
            E.bank.read(bank, v)
            wo_last = v

        if BIS == "wo":
            s.dma("sync", lambda h: h.dma_start(
                out=rearr_groups(E.ap["r1"])[:, :, 0:512],
                in_=xg32[:].rearrange("p (c t) -> p c t", t=512)),
                deps=[wo_last], sem="DST0")
            return

        def dst_fn(dc, tt, g=g):
            return rearr_groups(E.ap["r1"])[:, dc, g * 512:(g + 1) * 512]
        emit_postln(E, xg32, 512, dst_fn, [wo_last], E.eps[0:1, 0:1], rbf=att_sb, sqb=xgbf)
        xg32_guard = [s.tok("V"), s.tok("A")]
        xgbf_guard = list(E.ln_guard)
        gg = [s.tok("T")]


def emit_p2(E):
    s = E.s
    wd = E.p2_wd
    wd_toks = []
    for part in range(3):
        wd_toks.append(s.dma("gpsimd", lambda h, part=part: h.dma_start(
            out=wd[:, part * 8192:(part + 1) * 8192].rearrange("p (c f) -> p c f", f=1024),
            in_=rearr_cpf(E.ap["w0d_t"])[:, part * 8:(part + 1) * 8, :]), sem="DWD"))
    wg4, wu4 = E.p2_wg, E.p2_wu
    h_sb = E.p2_h
    x32 = E.p2_x32
    xbf = E.p2_xbf
    x32_guard = []
    xbf_guard = []
    w_guard = [[], []]
    h_guard = []
    for ei in range(8):
        ld32 = s.dma("sync", lambda h, ei=ei: h.dma_start(
            out=x32[:].rearrange("p (c t) -> p c t", t=512),
            in_=rearr_groups(E.ap["r1"])[:, :, ei * 512:(ei + 1) * 512]),
            deps=x32_guard, sem="DX32")
        ldbf = s.dma("gpsimd", lambda h, ei=ei: h.dma_start(
            out=xbf[:].rearrange("p (c t) -> p c t", t=512),
            in_=rearr_groups(E.ap["r1"])[:, :, ei * 512:(ei + 1) * 512]),
            deps=xbf_guard + [wd_toks[-1]], sem="DXBF")
        x32_guard = []
        xbf_guard = []
        for wgrp in range(6):
            slot = wgrp % 2
            ldg = s.dma("gpsimd", lambda h, wgrp=wgrp, slot=slot: h.dma_start(
                out=wg4[slot][:].rearrange("p (c f) -> p c f", f=512),
                in_=rearr_cpf(E.ap["w0g_t"])[:, :, wgrp * 512:(wgrp + 1) * 512]),
                deps=w_guard[slot], sem=f"DWG{slot}")
            ldu = s.dma("gpsimd", lambda h, wgrp=wgrp, slot=slot: h.dma_start(
                out=wu4[slot][:].rearrange("p (c f) -> p c f", f=512),
                in_=rearr_cpf(E.ap["w0u_t"])[:, :, wgrp * 512:(wgrp + 1) * 512]),
                deps=w_guard[slot], sem=f"DWU{slot}")
            w_guard[slot] = []
            for h4 in range(4):
                hc = wgrp * 4 + h4
                bkg = hc % 2
                bku = 2 + hc % 2

                def gmm(h, h4=h4, bkg=bkg, slot=slot):
                    ins = None
                    for ci in range(8):
                        ins = h.matmul(E.pb[bkg][:, 0:512],
                                       wg4[slot][:, ci * 512 + h4 * 128: ci * 512 + h4 * 128 + 128],
                                       xbf[:, ci * 512:(ci + 1) * 512],
                                       start=(ci == 0), stop=(ci == 7))
                    return ins
                tg = s.emit("tensor", lambda h, f=gmm: f(h),
                            deps=[ldbf, ldg] + E.bank.wdeps(bkg))

                def umm(h, h4=h4, bku=bku, slot=slot):
                    ins = None
                    for ci in range(8):
                        ins = h.matmul(E.pb[bku][:, 0:512],
                                       wu4[slot][:, ci * 512 + h4 * 128: ci * 512 + h4 * 128 + 128],
                                       xbf[:, ci * 512:(ci + 1) * 512],
                                       start=(ci == 0), stop=(ci == 7))
                    return ins
                tu = s.emit("tensor", lambda h, f=umm: f(h),
                            deps=[ldbf, ldu] + E.bank.wdeps(bku))
                ssl = hc % 2
                sgs = E.sg[ssl]
                ag = s.emit("scalar", lambda h, bkg=bkg, sgs=sgs: h.activation(
                    out=sgs[:], in_=E.pb[bkg][:, 0:512], func=AF.Silu, scale=INV127),
                    deps=[tg] + E.sg_guard[ssl])
                E.bank.read(bkg, ag)
                vh = s.emit("vector", lambda h, hc=hc, bku=bku, sgs=sgs: h.tensor_mul(
                    h_sb[:, hc * 512:(hc + 1) * 512], E.pb[bku][:, 0:512], sgs[:]),
                    deps=[tu, ag] + h_guard)
                h_guard = []
                E.bank.read(bku, vh)
                E.sg_guard[ssl] = [vh]
            w_guard[slot] = [s.tok("T")]
        h_done = s.tok("V")
        dn_last = None
        for dc in range(8):
            bank = 4 + dc % 2

            def dmm(h, dc=dc, bank=bank):
                ins = None
                for hc in range(24):
                    ins = h.matmul(E.pb[bank][:, 0:512],
                                   wd[:, hc * 1024 + dc * 128: hc * 1024 + dc * 128 + 128],
                                   h_sb[:, hc * 512:(hc + 1) * 512],
                                   start=(hc == 0), stop=(hc == 23))
                return ins
            t = s.emit("tensor", lambda h, f=dmm: f(h),
                       deps=[h_done, wd_toks[-1]] + E.bank.wdeps(bank))
            v = s.emit("vector", lambda h, dc=dc, bank=bank: h.scalar_tensor_tensor(
                x32[:, dc * 512:(dc + 1) * 512], E.pb[bank][:, 0:512], INV127SQ,
                x32[:, dc * 512:(dc + 1) * 512], ALU.mult, ALU.add),
                deps=[t, ld32])
            E.bank.read(bank, v)
            dn_last = v
        h_guard = [s.tok("T")]

        def dst_fn(dc, tt, ei=ei):
            return rearr_groups(E.ap["r2"])[:, dc, ei * 512:(ei + 1) * 512]
        emit_postln(E, x32, 512, dst_fn, [dn_last], E.eps[0:1, 0:1],
                    rbf=h_sb, sqb=xbf)
        h_guard = h_guard + E.ln_guard
        x32_guard = [s.tok("V"), s.tok("A")]
        xbf_guard = list(E.ln_guard)


def emit_p3(E, pc):
    s = E.s
    nc = E.nc
    kfm = E.p3_k
    vext = E.p3_vext
    wbuf = E.p3_wbuf
    actx = ExitStack()
    wvb = actx.enter_context(nc.sbuf_tensor("p3_wv", [128, 8 * 1024], BF16))
    E.p3_bvrow = actx.enter_context(nc.sbuf_tensor("p3_bvrow", [1, D], F32))
    bvbc = actx.enter_context(nc.sbuf_tensor("p3_bvbc", [128, 1024], F32))
    xbf1 = actx.enter_context(nc.sbuf_tensor("p3_xbf1", [128, 8 * 512], BF16))
    ms = s.emit("vector", lambda h: h.memset(vext[:], 1.0))
    wk_t = s.dma("gpsimd", lambda h: h.dma_start(
        out=wbuf[:].rearrange("p (c f) -> p c f", f=1024), in_=rearr_cpf(E.ap["w1k_t"])),
        sem="DW3")
    wv_t = s.dma("gpsimd", lambda h: h.dma_start(
        out=wvb[:].rearrange("p (c f) -> p c f", f=1024), in_=rearr_cpf(E.ap["w1v_t"])),
        sem="DWV")
    bvr = s.dma("sync", lambda h: h.dma_start(out=E.p3_bvrow[:], in_=E.ap["b1v_row"]), sem="DC")
    bv_toks = []
    for half in range(2):
        t = s.emit("tensor", lambda h, half=half: h.matmul(
            E.pb[2][:, 0:512], E.ones_f[0:1, 0:128],
            E.p3_bvrow[0:1, half * 512:half * 512 + 512], start=True, stop=True),
            deps=[bvr] + E.bank.wdeps(2))
        a = s.emit("scalar", lambda h, half=half: h.copy(bvbc[:, half * 512:half * 512 + 512],
                                                         E.pb[2][:, 0:512]), deps=[t])
        E.bank.read(2, a)
        bv_toks.append(a)

    xbf = [E.p3_xbf0, xbf1]
    xbf_guard = [[], []]
    for tt in range(8):
        slot = tt % 2
        ldbf = s.dma("gpsimd", lambda h, tt=tt, slot=slot: h.dma_start(
            out=xbf[slot][:].rearrange("p (c t) -> p c t", t=512),
            in_=rearr_groups(E.ap["r2"])[:, :, tt * 512:(tt + 1) * 512]),
            deps=xbf_guard[slot] + [wv_t], sem=f"DXB{slot}")
        xbf_guard[slot] = []
        for dc in range(8):
            bank = dc % 2

            def kmm(h, dc=dc, bank=bank, slot=slot):
                ins = None
                for ci in range(8):
                    ins = h.matmul(E.pb[bank][:, 0:512],
                                   wbuf[:, ci * 1024 + dc * 128: ci * 1024 + dc * 128 + 128],
                                   xbf[slot][:, ci * 512:(ci + 1) * 512],
                                   start=(ci == 0), stop=(ci == 7))
                return ins
            t = s.emit("tensor", lambda h, f=kmm: f(h), deps=[ldbf, wk_t] + E.bank.wdeps(bank))
            v = s.emit("vector", lambda h, tt=tt, dc=dc, bank=bank: h.tensor_scalar(
                kfm[:, dc * 4096 + tt * 512: dc * 4096 + tt * 512 + 512],
                E.pb[bank][:, 0:512], E.bias["b1k"][:, dc:dc + 1], None, ALU.add),
                deps=[t, E.bias_tok])
            E.bank.read(bank, v)
        for tc in range(4):
            for half in range(2):
                bank = 2 + (tc * 2 + half) % 2

                def vmm(h, tc=tc, half=half, bank=bank, slot=slot):
                    ins = None
                    for ci in range(8):
                        ins = h.matmul(E.pb[bank][:, 0:512],
                                       xbf[slot][:, ci * 512 + tc * 128: ci * 512 + tc * 128 + 128],
                                       wvb[:, ci * 1024 + half * 512: ci * 1024 + half * 512 + 512],
                                       start=(ci == 0), stop=(ci == 7))
                    return ins
                t = s.emit("tensor", lambda h, f=vmm: f(h), deps=[ldbf, wv_t] + E.bank.wdeps(bank))
                base = (tt * 4 + tc) * 1040 + half * 520

                def vev(h, bank=bank, base=base, half=half):
                    dst = vext[:, base:base + 520].rearrange("p (hh e) -> p hh e", e=65)[:, :, 0:64]
                    src = E.pb[bank][:, 0:512].rearrange("p (hh e) -> p hh e", e=64)
                    bcv = bvbc[:, half * 512:half * 512 + 512].rearrange("p (hh e) -> p hh e", e=64)
                    return h.tensor_add(dst, src, bcv)
                v = s.emit("vector", lambda h, f=vev: f(h), deps=[t, ms] + bv_toks)
                E.bank.read(bank, v)
        xbf_guard[slot] = [s.tok("T")]
    a_done_v = s.tok("V")
    actx.close()
    a_fin = [s.tok("T"), s.tok("V"), s.tok("A")]

    p = lambda *a: pc.enter_context(nc.sbuf_tensor(*a))
    q_sb = p("p3_q", [128, 8 * 512], BF16)
    att_sb = p("p3_att", [128, 8 * 512], BF16)
    r32 = p("p3_r32", [128, 8 * 512], F32)
    E.expst3 = [p(f"p3_expst{i}", [128, 1024], BF16) for i in range(3)]
    E.expst3_g = [list(a_fin) for _ in range(3)]
    E.p3_rec = p("p3_rec", [128, 512], BF16)
    E.p3_bcsb = [p("p3_bcsb0", [64, 512], F32)]
    E.p3_bcsb = E.p3_bcsb * 2
    E.p3_tmp = [p("p3_tmp0", [64, 512], BF16)] * 2
    wload_guard = [s.tok("T")]
    q_guard = list(a_fin)
    att_guard = list(a_fin)
    r32_guard = list(a_fin)
    expst_g = [list(a_fin), list(a_fin)]
    rec_g = list(a_fin)
    bcsb_g = [list(a_fin), list(a_fin)]
    tmp_g = [list(a_fin), list(a_fin)]
    for qt in range(8):
        wq_t = s.dma("gpsimd", lambda h: h.dma_start(
            out=wbuf[:].rearrange("p (c f) -> p c f", f=1024), in_=rearr_cpf(E.ap["w1q_t"])),
            deps=wload_guard, sem="DW3")
        ldr = s.dma("sync", lambda h, qt=qt: h.dma_start(
            out=r32[:].rearrange("p (c t) -> p c t", t=512),
            in_=rearr_groups(E.ap["r2"])[:, :, qt * 512:(qt + 1) * 512]),
            deps=r32_guard, sem="DR32")
        ldq = s.dma("gpsimd", lambda h, qt=qt: h.dma_start(
            out=xbf[0][:].rearrange("p (c t) -> p c t", t=512),
            in_=rearr_groups(E.ap["r2"])[:, :, qt * 512:(qt + 1) * 512]),
            deps=xbf_guard[0], sem="DXB0")
        r32_guard = []
        xbf_guard[0] = []
        q_last = None
        for dc in range(8):
            bank = dc % 2

            def qmm(h, dc=dc, bank=bank):
                ins = None
                for ci in range(8):
                    ins = h.matmul(E.pb[bank][:, 0:512],
                                   wbuf[:, ci * 1024 + dc * 128: ci * 1024 + dc * 128 + 128],
                                   xbf[0][:, ci * 512:(ci + 1) * 512],
                                   start=(ci == 0), stop=(ci == 7))
                return ins
            t = s.emit("tensor", lambda h, f=qmm: f(h),
                       deps=[ldq, wq_t, a_done_v] + E.bank.wdeps(bank))
            v = s.emit("vector", lambda h, dc=dc, bank=bank: h.tensor_scalar(
                q_sb[:, dc * 512:(dc + 1) * 512], E.pb[bank][:, 0:512],
                E.bias["b1q"][:, dc:dc + 1], None, ALU.add),
                deps=[t, E.bias_tok] + q_guard)
            q_guard = []
            E.bank.read(bank, v)
            q_last = v
        qmm_done = s.tok("T")
        wo_t = s.dma("gpsimd", lambda h: h.dma_start(
            out=wbuf[:].rearrange("p (c f) -> p c f", f=1024), in_=rearr_cpf(E.ap["w1o_t"])),
            deps=[qmm_done], sem="DW3")

        SCP = ((0, 1), (2, 3))
        for hp in range(8):
            pvA = E.pb[4]
            pvB = E.pb[5]
            pvA_w = E.bank.wdeps(4)
            pvB_w = E.bank.wdeps(5)
            pv_tok = {"A": None, "B": None}
            e_tok = {}

            def stm(h, hlo, k, hp, bank):
                return h.matmul(E.pb[bank][:, 0:512],
                                kfm[hlo:hlo + 64, hp * 4096 + k * 128: hp * 4096 + k * 128 + 128],
                                q_sb[hlo:hlo + 64, hp * 512:(hp + 1) * 512],
                                start=True, stop=True, tile_position=(hlo, 0))

            def pvm(h, hh, k, side, dst):
                return h.matmul(dst[0:65, 0:512],
                                vext[:, k * 1040 + hh * 65: k * 1040 + hh * 65 + 65],
                                E.expst3[k % 3][:, side * 512: side * 512 + 512],
                                start=(k == 0), stop=(k == 31))

            for ktc in range(34):
                if ktc < 32:
                    par = ktc % 2
                    par3 = ktc % 3
                    bA, bB = SCP[par]
                    tA = s.emit("tensor", lambda h, ktc=ktc, hp=hp, bA=bA: stm(h, 0, ktc, hp, bA),
                                deps=[q_last, a_done_v] + E.bank.wdeps(bA) + E.bank.wdeps(bB))
                    tB = s.emit("tensor", lambda h, ktc=ktc, hp=hp, bB=bB: stm(h, 64, ktc, hp, bB),
                                deps=[])
                    ee = s.emit("scalar", lambda h, par3=par3, bA=bA: h.activation(
                        out=E.expst3[par3][:], in_=E.pbig[:, bA * 512: bA * 512 + 1024],
                        func=AF.Exp, scale=SC1),
                        deps=[tA, tB] + E.expst3_g[par3])
                    E.bank.read(bA, ee)
                    E.bank.read(bB, ee)
                    e_tok[ktc] = ee
                if ktc >= 2:
                    kprev = ktc - 2
                    pv_tok["A"] = s.emit("tensor",
                                         lambda h, kprev=kprev, hp=hp: pvm(
                                             h, 2 * hp, kprev, 0, pvA),
                                         deps=[e_tok[kprev]] + (pvA_w if kprev == 0 else []))
                    pv_tok["B"] = s.emit("tensor",
                                         lambda h, kprev=kprev, hp=hp: pvm(
                                             h, 2 * hp + 1, kprev, 1, pvB),
                                         deps=(pvB_w if kprev == 0 else []))
                    E.expst3_g[kprev % 3] = [pv_tok["B"]]
            for (side, hh, pvb, pv_ps) in (("A", 2 * hp, 4, pvA), ("B", 2 * hp + 1, 5, pvB)):
                rec = E.p3_rec
                def recf(h, pv_ps=pv_ps):
                    with nc.allow_low_precision(reason="softmax denom recip rounds to bf16; att output is bf16 anyway"):
                        return h.reciprocal(rec[64:65, 0:512], pv_ps[64:65, 0:512])
                r1 = s.emit("vector", recf, deps=[pv_tok[side]] + rec_g)
                t3 = s.emit("tensor", lambda h: h.matmul(
                    E.pb[0][0:64, 0:512], E.ones_bf[64:65, 0:64], rec[64:65, 0:512],
                    start=True, stop=True), deps=[r1] + E.bank.wdeps(0))
                rec_g = [t3]
                a3 = s.emit("scalar", lambda h: h.copy(
                    E.p3_bcsb[0][0:64, :], E.pb[0][0:64, 0:512]),
                    deps=[t3] + bcsb_g[0] + bcsb_g[1])
                E.bank.read(0, a3)
                if side == "A":
                    v3 = s.emit("vector", lambda h, hp=hp, pv_ps=pv_ps: h.tensor_mul(
                        att_sb[0:64, hp * 512:(hp + 1) * 512], pv_ps[0:64, 0:512],
                        E.p3_bcsb[0][0:64, :]),
                        deps=[a3, pv_tok[side]] + att_guard)
                    E.bank.read(pvb, v3)
                    bcsb_g = [[v3], []]
                else:
                    tsl = 0
                    v3 = s.emit("vector", lambda h, pv_ps=pv_ps, tsl=tsl: h.tensor_mul(
                        E.p3_tmp[tsl][0:64, :], pv_ps[0:64, 0:512], E.p3_bcsb[0][0:64, :]),
                        deps=[a3, pv_tok[side]] + tmp_g[tsl])
                    E.bank.read(pvb, v3)
                    bcsb_g = [[v3], []]
                    d3 = s.dma("sync", lambda h, hp=hp, tsl=tsl: h.dma_start(
                        out=att_sb[64:128, hp * 512:(hp + 1) * 512], in_=E.p3_tmp[tsl][0:64, :]),
                        deps=[v3] + att_guard, sem=f"DTMP{tsl}")
                    tmp_g[tsl] = [d3]
        att_done = [s.tok("V"), s.tok("DTMP0"), s.tok("DTMP1")]

        wo_last = None
        for dc in range(8):
            bank = dc % 2

            def womm(h, dc=dc, bank=bank):
                ins = None
                for ci in range(8):
                    ins = h.matmul(E.pb[bank][:, 0:512],
                                   wbuf[:, ci * 1024 + dc * 128: ci * 1024 + dc * 128 + 128],
                                   att_sb[:, ci * 512:(ci + 1) * 512],
                                   start=(ci == 0), stop=(ci == 7))
                return ins
            t = s.emit("tensor", lambda h, f=womm: f(h),
                       deps=att_done + [wo_t] + E.bank.wdeps(bank))
            v = s.emit("vector", lambda h, dc=dc, bank=bank: h.scalar_tensor_tensor(
                r32[:, dc * 512:(dc + 1) * 512], E.pb[bank][:, 0:512],
                E.bias["b1o"][:, dc:dc + 1], r32[:, dc * 512:(dc + 1) * 512],
                ALU.add, ALU.add),
                deps=[t, ldr, E.bias_tok])
            E.bank.read(bank, v)
            wo_last = v
        wload_guard = [s.tok("T")]

        def dst_fn(dc, tt, qt=qt):
            return rearr_groups(E.ap["r3"])[:, dc, qt * 512:(qt + 1) * 512]
        emit_postln(E, r32, 512, dst_fn, [wo_last], E.eps[0:1, 0:1],
                    rbf=att_sb, sqb=xbf[0])
        att_guard = list(E.ln_guard)
        q_guard = [s.tok("T")]
        xbf_guard[0] = list(E.ln_guard)
        r32_guard = [s.tok("V"), s.tok("A")]


def emit_p4s(E, out_name):
    s = E.s
    nc = E.nc
    r3q = E.p4_r3
    xnc = E.p4_xn
    sq2 = E.p4_sq
    xts = E.p4_xts
    gtok = E.p4_gtok
    gfm = E.p4_gfm
    hsb = E.p4_h
    wb = E.p4_w
    G = E.p4_g
    mbf = E.p4_mbf
    idxs = E.p4_idx
    scs = E.p4_sc
    offs = E.p4_offs
    tp = E.tp

    gw_tok = s.dma("sync", lambda h: h.dma_start(
        out=E.gw_sb[:].rearrange("p (c e) -> p c e", e=8), in_=rearr_cpf(E.ap["gw_t"])),
        sem="DGW")
    s.dma("gpsimd", lambda h: h.dma_start(out=E.b2sb[:], in_=E.ap["b2r"]), sem="DC")
    s.dma("sync", lambda h: h.dma_start(out=E.b1f1[:], in_=E.ap["b1f1"]), sem="DC")
    s.dma("sync", lambda h: h.dma_start(out=E.ident_bf[:], in_=E.c_ident_bf), sem="DC")
    s.dma("sync", lambda h: h.dma_start(out=E.ltri_bf[:], in_=E.c_ltri), sem="DC")
    s.dma("sync", lambda h: h.dma_start(out=offs[:], in_=E.c_offs0), sem="DC")
    cst_tok = s.tok("DC")
    gwm_tok = None
    for ci in range(8):
        gwm_tok = s.emit("vector", lambda h, ci=ci: h.tensor_scalar(
            E.gw_sb[:, ci * 8:(ci + 1) * 8], E.gw_sb[:, ci * 8:(ci + 1) * 8],
            E.bias["rmsw"][:, ci:ci + 1], None, ALU.mult),
            deps=[gw_tok, E.bias_tok])

    r3_guard = []
    xn_guard = []
    tp_guard = [[], []]
    xts_guard = [[], []]
    off_chain = [cst_tok]
    scat_toks = []
    lgbank_guard = []
    for ei in range(8):
        ld32 = s.dma("sync", lambda h, ei=ei: h.dma_start(
            out=r3q[:].rearrange("p (c t) -> p c t", t=512),
            in_=rearr_groups(E.ap["r3"])[:, :, ei * 512:(ei + 1) * 512]),
            deps=r3_guard, sem="DR32")
        r3_guard = []
        asq = s.emit("scalar", lambda h: h.activation(out=sq2[:, 0:4096], in_=r3q[:, 0:4096],
                                                      func=AF.Square),
                     deps=[ld32] + xn_guard)
        xn_guard = []

        def rmsmm(h):
            ins = None
            for dc in range(8):
                ins = h.matmul(E.pb[4][0:1, 0:512], E.invD_bf[:, 0:1],
                               sq2[:, dc * 512: dc * 512 + 512],
                               start=(dc == 0), stop=(dc == 7))
            return ins
        tm = s.emit("tensor", lambda h: rmsmm(h), deps=[asq] + E.bank.wdeps(4) + lgbank_guard)
        vrow = E.ln_rows[0:1, 1024:1536]
        a1 = s.emit("scalar", lambda h: h.activation(
            out=vrow, in_=E.pb[4][0:1, 0:512], func=AF.Ln, bias=E.eps[0:1, 1:2]),
            deps=[tm] + E.rows_guard)
        E.rows_guard = []
        E.bank.read(4, a1)
        a2 = s.emit("scalar", lambda h: h.activation(
            out=vrow, in_=vrow, func=AF.Exp, scale=-0.5), deps=[a1])
        t_b = s.emit("tensor", lambda h: h.matmul(
            E.pb[5][:, 0:512], E.ones_f[0:1, 0:128], vrow, start=True, stop=True),
            deps=[a2] + E.bank.wdeps(5))
        E.rows_guard = [t_b]
        xn_last = None
        for dc in range(8):
            v = s.emit("vector", lambda h, dc=dc: h.scalar_tensor_tensor(
                xnc[:, dc * 512:(dc + 1) * 512], r3q[:, dc * 512:(dc + 1) * 512],
                E.bias["rmsw"][:, dc:dc + 1], E.pb[5][:, 0:512],
                ALU.mult, ALU.mult),
                deps=[t_b, tm, ld32, E.bias_tok])
            E.bank.read(5, v)
            xn_last = v

        for tcl in range(4):
            c = ei * 4 + tcl
            lg = E.pb[4][0:128, 0:8]
            pfx = E.pb[4][0:128, 128:136]
            rcol = E.pb[4][0:128, 392:393]
            tot = E.pb[5][0:1, 120:128]

            def lmm(h, tcl=tcl):
                ins = None
                for ci in range(8):
                    ins = h.matmul(lg, r3q[:, ci * 512 + tcl * 128: ci * 512 + tcl * 128 + 128],
                                   E.gw_sb[:, ci * 8:(ci + 1) * 8],
                                   start=(ci == 0), stop=(ci == 7))
                return ins
            t1 = s.emit("tensor", lambda h, f=lmm: f(h),
                        deps=[ld32, gwm_tok, xn_last] + E.bank.wdeps(4))
            trs = s.emit("tensor", lambda h, tcl=tcl: h.matmul(
                rcol, E.ln_rows[0:1, 1024 + tcl * 128: 1024 + tcl * 128 + 128],
                E.ones_f[0:1, 0:1], start=True, stop=True), deps=[a2])
            v1 = s.emit("vector", lambda h: h.tensor_copy(G[:, 0:8], lg), deps=[t1])
            v2 = s.emit("vector", lambda h: h.reduce_max(G[:, 8:9], G[:, 0:8], axis=AX),
                        deps=[v1])
            vrc = s.emit("vector", lambda h: h.tensor_copy(G[:, 9:10], rcol), deps=[trs])
            vz = s.emit("vector", lambda h: h.tensor_scalar(G[:, 10:18], G[:, 0:8],
                                                            G[:, 8:9], None, ALU.subtract),
                        deps=[v2])
            vz2 = s.emit("vector", lambda h: h.tensor_scalar(G[:, 10:18], G[:, 10:18],
                                                             G[:, 9:10], None, ALU.mult),
                        deps=[vz, vrc])
            a3 = s.emit("scalar", lambda h: h.activation(out=G[:, 10:18], in_=G[:, 10:18],
                                                         func=AF.Exp),
                        deps=[vz2])
            v4 = s.emit("vector", lambda h: h.reduce_sum(G[:, 18:19], G[:, 10:18], axis=AX),
                        deps=[a3])
            v5 = s.emit("vector", lambda h, c=c: h.reciprocal(scs[:, c:c + 1], G[:, 18:19]),
                        deps=[v4])
            v6 = s.emit("vector", lambda h: h.tensor_scalar(G[:, 20:28], G[:, 0:8], G[:, 8:9],
                                                            None, ALU.is_equal),
                        deps=[v2, v1])
            t2 = s.emit("tensor", lambda h: h.matmul(pfx, E.ltri_bf[:], G[:, 20:28],
                                                     start=True, stop=False),
                        deps=[v6, cst_tok])
            t3 = s.emit("tensor", lambda h: h.matmul(pfx, E.ones_f[0:1, 0:128],
                                                     offs[0:1, 0:8], start=False, stop=True),
                        deps=off_chain)
            t4 = s.emit("tensor", lambda h: h.matmul(tot, E.ones_f[:, 0:1], G[:, 20:28],
                                                     start=True, stop=True),
                        deps=[v6] + E.bank.wdeps(5))
            v8 = s.emit("vector", lambda h: h.tensor_mul(G[:, 10:18], pfx, G[:, 20:28]),
                        deps=[t3, v6])
            v9 = s.emit("vector", lambda h: h.reduce_sum(G[:, 28:29], G[:, 10:18], axis=AX),
                        deps=[v8])
            vA = s.emit("vector", lambda h, c=c: h.tensor_copy(idxs[:, c:c + 1], G[:, 28:29]),
                        deps=[v9])
            voff = s.emit("vector", lambda h: h.tensor_add(offs[0:1, 0:8],
                                                           offs[0:1, 0:8],
                                                           tot),
                          deps=[t4, t3] + off_chain)
            off_chain = [voff]
            E.bank.read(4, v8)
            E.bank.read(4, vrc)
            E.bank.read(5, voff)
            lgbank_guard = [s.tok("V")]

            par = tcl % 2
            tlast = None
            for fc in range(8):
                tlast = s.emit("tensor", lambda h, fc=fc, tcl=tcl, par=par: h.transpose(
                    tp[par][:, fc * 128:(fc + 1) * 128],
                    xnc[:, fc * 512 + tcl * 128: fc * 512 + tcl * 128 + 128],
                    E.ident_bf[:]),
                    deps=([xn_last, cst_tok] + tp_guard[par]) if fc == 0 else [])
            tp_guard[par] = []
            ve = s.emit("vector", lambda h, par=par: h.tensor_copy(xts[par][:], tp[par][:]),
                        deps=[tlast] + xts_guard[par])
            xts_guard[par] = []
            tp_guard[par] = [ve]
            dsc = s.dma("gpsimd", lambda h, par=par, c=c: h.indirect_dma_start(
                out=E.ap["slx"],
                out_offset=bass.IndirectOffsetOnAxis(ap=idxs[:, c:c + 1], axis=0),
                in_=xts[par][:], in_offset=None,
                bounds_check=NSLOT - 1),
                deps=[ve, vA], sem="DIX")
            xts_guard[par] = [dsc]
            scat_toks.append(dsc)
        E.rows_guard = E.rows_guard + [s.tok("T")]
        xn_guard = [s.tok("T")]
        r3_guard = [s.tok("V"), s.tok("A")]
    scat_all = scat_toks[-1]

    rbf_tok = s.dma("gpsimd", lambda h: h.indirect_dma_start(
        out=mbf[:], out_offset=None, in_=E.ap["slx"],
        in_offset=bass.IndirectOffsetOnAxis(ap=idxs[:, 0:1], axis=0)),
        deps=[scat_all], sem="DIX")

    ysl = E.p4_ysl
    gtok_guard = []
    gfm_guard = []
    h_guard = []
    ysl_guard = []
    w_guard = [[], []]
    sly_toks = []
    for e in range(NE):
        ldg = s.dma("scalar", lambda h, e=e: h.dma_start(
            out=gtok[:].rearrange("p (c f) -> p c f", f=1024),
            in_=E.ap["slx"][e * CAP:(e + 1) * CAP, :].rearrange("(c p) f -> p c f", p=128)),
            deps=[scat_all, rbf_tok] + gtok_guard, sem="DGT")
        gtok_guard = []
        gfm_r = gfm[:].rearrange("p (fc ss) -> p fc ss", ss=CAP)
        tr_last = None
        for cs in range(6):
            par = cs % 2
            tl = None
            for fc in range(8):
                tl = s.emit("tensor", lambda h, cs=cs, fc=fc, par=par: h.transpose(
                    tp[par][:, fc * 128:(fc + 1) * 128],
                    gtok[:, cs * 1024 + fc * 128: cs * 1024 + fc * 128 + 128],
                    E.ident_bf[:]),
                    deps=([ldg, cst_tok] + tp_guard[par]) if fc == 0 else [])
            tp_guard[par] = []
            ve = s.emit("vector", lambda h, cs=cs, par=par: h.tensor_copy(
                gfm_r[:, :, cs * 128: cs * 128 + 128],
                tp[par][:].rearrange("p (fc t) -> p fc t", t=128)),
                deps=[tl] + gfm_guard)
            gfm_guard = []
            tp_guard[par] = [ve]
            tr_last = ve
        gtok_guard = [s.tok("T")]

        h_last = None
        for hcg in range(4):
            slot = hcg % 2
            ld1 = s.dma("gpsimd", lambda h, e=e, hcg=hcg, slot=slot: h.dma_start(
                out=wb[slot][:].rearrange("p (c f) -> p c f", f=1024),
                in_=rearr_cpf(E.ap["w1ff1"][e])[:, :, hcg * 1024:(hcg + 1) * 1024]),
                deps=w_guard[slot], sem=f"D41{slot}")
            w_guard[slot] = []
            for hcl in range(8):
                hc = hcg * 8 + hcl
                for sg in range(2):
                    sgw = 512 if sg == 0 else CAP - 512
                    bank = (hcl * 2 + sg) % 4

                    def gmm(h, hcl=hcl, sg=sg, sgw=sgw, bank=bank, slot=slot):
                        ins = None
                        for ci in range(8):
                            ins = h.matmul(
                                E.pb[bank][:, 0:sgw],
                                wb[slot][:, ci * 1024 + hcl * 128: ci * 1024 + hcl * 128 + 128],
                                gfm[:, ci * CAP + sg * 512: ci * CAP + sg * 512 + sgw],
                                start=(ci == 0), stop=(ci == 7))
                        return ins
                    tg = s.emit("tensor", lambda h, f=gmm: f(h),
                                deps=[ld1, tr_last] + E.bank.wdeps(bank))
                    bcol = E.b1f1[:, e * 32 + hc: e * 32 + hc + 1]
                    ag = s.emit("scalar", lambda h, hc=hc, sg=sg, sgw=sgw, bank=bank,
                                bcol=bcol: h.activation(
                                    out=hsb[:, hc * CAP + sg * 512: hc * CAP + sg * 512 + sgw],
                                    in_=E.pb[bank][:, 0:sgw], func=AF.Silu, bias=bcol),
                                deps=[tg, cst_tok] + h_guard)
                    h_guard = []
                    E.bank.read(bank, ag)
                    h_last = ag
            w_guard[slot] = [s.tok("T")]
        h_done = h_last
        gfm_guard = [s.tok("T")]

        ysl_ev = []
        for dq in range(4):
            slot = dq % 2
            ld2 = s.dma("gpsimd", lambda h, e=e, dq=dq, slot=slot: h.dma_start(
                out=wb[slot][:, 0:8192].rearrange("p (c f) -> p c f", f=256),
                in_=rearr_cpf(E.ap["w1ff2"][e])[:, :, dq * 256:(dq + 1) * 256]),
                deps=w_guard[slot], sem=f"D42{slot}")
            w_guard[slot] = []
            bdep = {cs: E.bank.wdeps(cs) for cs in range(6)}
            for hc in range(32):
                for cs in range(6):
                    def ymm(h, hc=hc, cs=cs, dq=dq, slot=slot):
                        return h.matmul(
                            E.pb[cs][:, 0:256],
                            hsb[:, hc * CAP + cs * 128: hc * CAP + cs * 128 + 128],
                            wb[slot][:, hc * 256: hc * 256 + 256],
                            start=(hc == 0), stop=False)
                    s.emit("tensor", lambda h, f=ymm: f(h),
                           deps=[ld2, h_done] + (bdep[cs] if hc == 0 else []))
            for cs in range(6):
                tb2 = s.emit("tensor", lambda h, e=e, dq=dq, cs=cs: h.matmul(
                    E.pb[cs][:, 0:256],
                    E.ones_bf[0:1, 0:128],
                    E.b2sb[0:1, e * 1024 + dq * 256: e * 1024 + dq * 256 + 256],
                    start=False, stop=True), deps=[cst_tok])
                v = s.emit("vector", lambda h, dq=dq, cs=cs: h.tensor_copy(
                    ysl[:, cs * 1024 + dq * 256: cs * 1024 + dq * 256 + 256],
                    E.pb[cs][:, 0:256]),
                    deps=[tb2] + (ysl_guard if (dq == 0 and cs == 0) else []))
                if dq == 0 and cs == 0:
                    ysl_guard = []
                E.bank.read(cs, v)
                ysl_ev.append(v)
            w_guard[slot] = [s.tok("T")]
        h_guard = [s.tok("T")]
        ev_all = ysl_ev[-1]
        for cs in range(6):
            dsl = s.dma("sync", lambda h, e=e, cs=cs: h.dma_start(
                out=E.ap["sly"][e * CAP + cs * 128: e * CAP + (cs + 1) * 128, :],
                in_=ysl[:, cs * 1024:(cs + 1) * 1024]), deps=[ev_all], sem="DSLY")
            sly_toks.append(dsl)
        ysl_guard = [s.tok("DSLY")]
    sly_all = sly_toks[-1]

    r3_guard = [s.tok("V"), s.tok("A")]
    for ei in range(8):
        ld32 = s.dma("sync", lambda h, ei=ei: h.dma_start(
            out=r3q[:].rearrange("p (c t) -> p c t", t=512),
            in_=rearr_groups(E.ap["r3"])[:, :, ei * 512:(ei + 1) * 512]),
            deps=r3_guard, sem="DR32")
        r3_guard = []
        radd_last = None
        for tcl in range(4):
            c = ei * 4 + tcl
            par = tcl % 2
            dg = s.dma("gpsimd", lambda h, c=c, par=par: h.indirect_dma_start(
                out=xts[par][:], out_offset=None,
                in_=E.ap["sly"],
                in_offset=bass.IndirectOffsetOnAxis(ap=idxs[:, c:c + 1], axis=0)),
                deps=[sly_all] + xts_guard[par], sem="DIX")
            xts_guard[par] = []
            vs = s.emit("vector", lambda h, c=c, par=par: h.tensor_scalar(
                xts[par][:], xts[par][:], scs[:, c:c + 1], None, ALU.mult),
                deps=[dg])
            tl = None
            for fc in range(8):
                tl = s.emit("tensor", lambda h, fc=fc, par=par: h.transpose(
                    tp[par][:, fc * 128:(fc + 1) * 128], xts[par][:, fc * 128:(fc + 1) * 128],
                    E.ident_bf[:]),
                    deps=([vs, cst_tok] + tp_guard[par]) if fc == 0 else [])
            tp_guard[par] = []
            r3r = r3q[:].rearrange("p (dc t5) -> p dc t5", t5=512)
            va = s.emit("vector", lambda h, tcl=tcl, par=par: h.tensor_add(
                r3r[:, :, tcl * 128: tcl * 128 + 128],
                tp[par][:].rearrange("p (dc t) -> p dc t", t=128),
                r3r[:, :, tcl * 128: tcl * 128 + 128]),
                deps=[tl, ld32])
            tp_guard[par] = [va]
            xts_guard[par] = [s.tok("T")]
            radd_last = va

        def dst_fn(dc, tt, ei=ei):
            return rearr_groups(E.ap[out_name])[:, dc, ei * 512:(ei + 1) * 512]
        emit_postln(E, r3q, 512, dst_fn, [radd_last], E.eps[0:1, 0:1],
                    rbf=xnc, sqb=sq2)
        r3_guard = [s.tok("V"), s.tok("A")]

def build_nc(phases=("p1", "p2", "p3", "p4")):
    nc = bass.Bass("TRN2", target_bir_lowering=False, debug=False)
    E = Env()
    E.nc = nc
    s = Sched()
    E.s = s
    E.bank = BankGuard()

    ap = {}
    ap["x0"] = nc.dram_tensor("x0", [128, 8 * S], F32, kind="ExternalInput").ap()
    chain = {"p1": ("x_in", "r1"), "p2": ("r1", "r2"), "p3": ("r2", "r3"), "p4": ("r3", "y")}
    needed = set()
    for ph in phases:
        needed.add(chain[ph][0])
        needed.add(chain[ph][1])
    needed.discard("x_in")
    out_hbm = chain[phases[-1]][1]
    in_hbm = chain[phases[0]][0]
    for name in sorted(needed):
        if name == out_hbm:
            ap[name] = nc.dram_tensor(name, [128, 8 * S], F32, kind="ExternalOutput").ap()
        else:
            ap[name] = nc.dram_tensor(name, [128, 8 * S], F32, kind="Internal").ap()
    if in_hbm != "x_in":
        ap[in_hbm] = ap["x0"]
    else:
        ap["x_in"] = ap["x0"]

    def din(name, shape, dt):
        ap[name] = nc.dram_tensor(name, shape, dt, kind="ExternalInput").ap()
    din("w0q", [D, D], F32); din("w0k", [D, D], F32); din("w0v", [D, D], F32); din("w0o", [D, D], F32)
    din("b0q", [128, 8], F32); din("b0k", [128, 8], F32); din("b0o", [128, 8], F32)
    din("b0v_row", [1, D], F32)
    din("w0g_t", [D, FF0], I32); din("w0u_t", [D, FF0], I32); din("w0d_t", [FF0, D], I32)
    din("w1q_t", [D, D], F32); din("w1k_t", [D, D], F32); din("w1v_t", [D, D], F32)
    din("w1o_t", [D, D], F32)
    din("b1q", [128, 8], F32); din("b1k", [128, 8], F32); din("b1o", [128, 8], F32)
    din("b1v_row", [1, D], F32)
    din("rmsw", [128, 8], F32)
    din("gw_t", [D, NE], F32)
    din("b1f1", [128, 256], F32)
    din("b2r", [1, NE * D], F32)
    ap["w1ff1"] = [nc.dram_tensor(f"w1ff1_{e}", [D, FF1], F32, kind="ExternalInput").ap()
                   for e in range(NE)]
    ap["w1ff2"] = [nc.dram_tensor(f"w1ff2_{e}", [FF1, D], F32, kind="ExternalInput").ap()
                   for e in range(NE)]
    if "p4" in phases:
        ap["slx"] = nc.dram_tensor("slx", [NSLOT, D], BF16, kind="Internal").ap()
        ap["sly"] = nc.dram_tensor("sly", [NSLOT, D], BF16, kind="Internal").ap()
    c_ones_bf = nc.inline_tensor(np.ones((128, 128), dtype=BF16NP), "c_ones_bf").ap()
    c_ones_f = nc.inline_tensor(np.ones((128, 128), dtype=np.float32), "c_ones_f").ap()
    c_neg_f = nc.inline_tensor(np.full((1, 128), -1.0, dtype=np.float32), "c_neg_f").ap()
    c_invd = nc.inline_tensor(np.full((128, 1), 1.0 / 1024.0, dtype=BF16NP), "c_invd").ap()
    E.c_ident_bf = nc.inline_tensor(np.eye(128, dtype=BF16NP), "c_ident_bf").ap()
    E.c_ltri = nc.inline_tensor(
        np.triu(np.ones((128, 128), dtype=np.float32)), "c_ltri").ap()
    offs0 = np.zeros((128, 8), dtype=np.float32)
    offs0[0, :] = CAP * np.arange(8, dtype=np.float32) - 1.0
    E.c_offs0 = nc.inline_tensor(offs0, "c_offs0").ap()
    c_eps = nc.inline_tensor(np.array([[EPS_LN, EPS_RMS]], dtype=np.float32), "c_eps").ap()
    E.ap = ap

    with ExitStack() as ctx:
        sb = lambda *a: ctx.enter_context(nc.sbuf_tensor(*a))
        E.pbig = ctx.enter_context(nc.psum_tensor("pbig", [128, 3072], F32))
        E.pb = [E.pbig[:, i * 512:(i + 1) * 512] for i in range(6)]
        E.tp = [ctx.enter_context(nc.psum_tensor(f"tp{i}", [128, 1024], BF16))
                for i in range(2)]
        E.lnb = (2, 3, 4, 5)
        E.ones_bf = sb("ones_bf", [128, 128], BF16)
        E.ones_f = sb("ones_f", [128, 128], F32)
        E.neg_f = sb("neg_f", [1, 128], F32)
        E.invD_bf = sb("invD_bf", [128, 1], BF16)
        E.eps = sb("eps", [1, 2], F32)
        E.bias = {}
        for name in ("b0q", "b0k", "b0o", "b1q", "b1k", "b1o", "rmsw"):
            E.bias[name] = sb("t_" + name, [128, 8], F32)
        E.ln_rows = sb("ln_rows", [1, 1536], F32)
        E.ln_stg = [sb(f"ln_stg{i}", [128, 512], F32) for i in range(2)]
        E.stg_guard = [[], []]
        E.ln_guard = []
        E.rows_guard = []

        s.dma("sync", lambda h: h.dma_start(out=E.ones_bf[:], in_=c_ones_bf), sem="DC")
        s.dma("sync", lambda h: h.dma_start(out=E.ones_f[:], in_=c_ones_f), sem="DC")
        s.dma("sync", lambda h: h.dma_start(out=E.neg_f[:], in_=c_neg_f), sem="DC")
        s.dma("sync", lambda h: h.dma_start(out=E.invD_bf[:], in_=c_invd), sem="DC")
        s.dma("sync", lambda h: h.dma_start(out=E.eps[:], in_=c_eps), sem="DC")
        for name in ("b0q", "b0k", "b0o", "b1q", "b1k", "b1o", "rmsw"):
            s.dma("sync", lambda h, name=name: h.dma_start(out=E.bias[name][:], in_=ap[name]),
                  sem="DC")
        E.bias_tok = s.tok("DC")
        s.barrier()

        if "p1" in phases:
            with ExitStack() as pc:
                p = lambda *a: pc.enter_context(nc.sbuf_tensor(*a))
                E.p1_w = [p(f"p1w{i}", [128, 8 * 1024], BF16) for i in range(4)]
                E.p1_bvrow = p("p1_bvrow", [1, D], F32)
                E.p1_bvbc = p("p1_bvbc", [128, 1024], F32)
                E.p1_xg32 = p("p1_xg32", [128, 8 * 512], F32)
                E.p1_xgbf = p("p1_xgbf", [128, 8 * 512], BF16)
                E.p1_q = p("p1_q", [128, 8 * 512], BF16)
                E.p1_k = p("p1_k", [128, 8 * 512], BF16)
                E.p1_v = p("p1_v", [128, 4 * 1024], BF16)
                E.p1_att = p("p1_att", [128, 8 * 512], BF16)
                E.p1_expst = [[p(f"p1_expst{i}_{b}", [128, 64], BF16) for b in range(2)]
                              for i in range(2)]
                E.p1_recip = [p(f"p1_recip{i}", [1, 128], F32) for i in range(2)]
                E.p1_bcsb = [p(f"p1_bcsb{i}", [128, 128], F32) for i in range(2)]
                emit_p1(E)
            s.barrier()
        if "p2" in phases:
            with ExitStack() as pc:
                p = lambda *a: pc.enter_context(nc.sbuf_tensor(*a))
                E.p2_wd = p("p2_wd", [128, 24 * 1024], BF16)
                E.p2_wg = [p(f"p2_wg{i}", [128, 8 * 512], BF16) for i in range(2)]
                E.p2_wu = [p(f"p2_wu{i}", [128, 8 * 512], BF16) for i in range(2)]
                E.p2_h = p("p2_h", [128, 24 * 512], BF16)
                E.p2_x32 = p("p2_x32", [128, 8 * 512], F32)
                E.p2_xbf = p("p2_xbf", [128, 8 * 512], BF16)
                E.sg = [p(f"p2sg{i}", [128, 512], F32) for i in range(2)]
                E.sg_guard = [[], []]
                emit_p2(E)
            s.barrier()
        if "p3" in phases:
            with ExitStack() as pc:
                p = lambda *a: pc.enter_context(nc.sbuf_tensor(*a))
                E.p3_k = p("p3_k", [128, 8 * 4096], BF16)
                E.p3_vext = p("p3_vext", [128, 32 * 1040], BF16)
                E.p3_wbuf = p("p3_wbuf", [128, 8 * 1024], BF16)
                E.p3_xbf0 = p("p3_xbf0", [128, 8 * 512], BF16)
                emit_p3(E, pc)
            s.barrier()
        if "p4" in phases:
            with ExitStack() as pc:
                p = lambda *a: pc.enter_context(nc.sbuf_tensor(*a))
                E.p4_r3 = p("p4_r3", [128, 8 * 512], F32)
                E.p4_xn = p("p4_xn", [128, 8 * 512], BF16)
                E.p4_sq = p("p4_sq", [128, 8 * 512], BF16)
                E.p4_xts = [p(f"p4_xts{i}", [128, 1024], BF16) for i in range(2)]
                E.p4_gtok = p("p4_gtok", [128, 6 * 1024], BF16)
                E.p4_ysl = p("p4_ysl", [128, 6 * 1024], BF16)
                E.p4_gfm = p("p4_gfm", [128, 8 * CAP], BF16)
                E.p4_h = p("p4_h", [128, 32 * CAP], BF16)
                E.p4_w = [p(f"p4_w{i}", [128, 8 * 1024], BF16) for i in range(2)]
                E.p4_g = p("p4_g", [128, 32], F32)
                E.p4_mbf = p("p4_mbf", [128, 8], BF16)
                E.p4_idx = p("p4_idx", [128, 32], U32)
                E.p4_sc = p("p4_sc", [128, 32], F32)
                E.p4_offs = p("p4_offs", [128, 8], F32)
                E.gw_sb = p("gw_sb", [128, 64], F32)
                E.b2sb = p("b2sb", [1, NE * 1024], BF16)
                E.b1f1 = p("t_b1f1", [128, 256], F32)
                E.ident_bf = p("ident_bf", [128, 128], BF16)
                E.ltri_bf = p("ltri_bf", [128, 128], F32)
                emit_p4s(E, out_hbm)
            s.barrier()

        fence_deps = [s.tok(name) for name in list(s.cnt.keys())]
        s.emit("sync", lambda h: h.nop(), deps=fence_deps)
        s.realize(nc, ctx)
    return nc


def _prep_shared(inputs):
    f = np.ascontiguousarray
    d = {}
    d["w0q"] = f(np.asarray(inputs["L0_attn_Wq"], np.float32))
    d["w0k"] = f(np.asarray(inputs["L0_attn_Wk"], np.float32))
    d["w0v"] = f(np.asarray(inputs["L0_attn_Wv"], np.float32))
    d["w0o"] = f(np.asarray(inputs["L0_attn_Wo"], np.float32))
    d["b0q"] = f(np.asarray(inputs["L0_attn_bq"], np.float32).reshape(8, 128).T)
    d["b0k"] = f(np.asarray(inputs["L0_attn_bk"], np.float32).reshape(8, 128).T)
    d["b0o"] = f(np.asarray(inputs["L0_attn_bo"], np.float32).reshape(8, 128).T)
    d["b0v_row"] = f(np.asarray(inputs["L0_attn_bv"], np.float32).reshape(1, 1024))
    d["w0g_t"] = f(np.asarray(inputs["L0_ffn_w_gate"], np.int32).T)
    d["w0u_t"] = f(np.asarray(inputs["L0_ffn_w_up"], np.int32).T)
    d["w0d_t"] = f(np.asarray(inputs["L0_ffn_w_down"], np.int32).T)
    d["w1q_t"] = f(np.asarray(inputs["L1_attn_wq"], np.float32).transpose(1, 0, 2).reshape(1024, 1024))
    d["w1k_t"] = f(np.asarray(inputs["L1_attn_wk"], np.float32).transpose(1, 0, 2).reshape(1024, 1024))
    d["w1v_t"] = f(np.asarray(inputs["L1_attn_wv"], np.float32).transpose(1, 0, 2).reshape(1024, 1024))
    d["w1o_t"] = f(np.asarray(inputs["L1_attn_wo"], np.float32).T)
    d["b1q"] = f(np.asarray(inputs["L1_attn_bq"], np.float32).reshape(1024).reshape(8, 128).T)
    d["b1k"] = f(np.asarray(inputs["L1_attn_bk"], np.float32).reshape(1024).reshape(8, 128).T)
    d["b1o"] = f(np.asarray(inputs["L1_attn_bo"], np.float32).reshape(8, 128).T)
    d["b1v_row"] = f(np.asarray(inputs["L1_attn_bv"], np.float32).reshape(1, 1024))
    d["rmsw"] = f(np.asarray(inputs["L1_ffn_rms_w"], np.float32).reshape(8, 128).T)
    d["gw_t"] = f(np.asarray(inputs["L1_ffn_gate_w"], np.float32).T)
    d["b1f1"] = f(np.asarray(inputs["L1_ffn_b1"], np.float32).reshape(8, 32, 128)
                  .transpose(2, 0, 1).reshape(128, 256))
    d["b2r"] = f(np.asarray(inputs["L1_ffn_b2"], np.float32).reshape(1, NE * 1024))
    for e in range(NE):
        d[f"w1ff1_{e}"] = f(np.asarray(inputs["L1_ffn_W1"][e], np.float32))
        d[f"w1ff2_{e}"] = f(np.asarray(inputs["L1_ffn_W2"][e], np.float32))
    return d


def fm_layout(x2d):
    fm = np.ascontiguousarray(np.asarray(x2d, np.float32).T)
    return np.ascontiguousarray(fm.reshape(8, 128, S).transpose(1, 0, 2).reshape(128, 8 * S))


def fm_unlayout(y):
    fm = y.reshape(128, 8, S).transpose(1, 0, 2).reshape(1024, S)
    return np.ascontiguousarray(fm.T)


def kernel(**inputs):
    shared = _prep_shared(inputs)
    x = np.asarray(inputs["L0_attn_src"], dtype=np.float32)
    nc = build_nc(phases=("p1", "p2", "p3", "p4"))
    in_maps = []
    for b in range(8):
        m = dict(shared)
        m["x0"] = fm_layout(x[b])
        in_maps.append(m)
    res = run_bass_kernel_spmd(nc, in_maps, core_ids=list(range(8)))
    return np.stack([fm_unlayout(np.asarray(r["y"], dtype=np.float32)) for r in res.results])
